# version 03
# baseline (speedup 1.0000x reference)
import numpy as np
import concourse.bass as bass
import concourse.mybir as mybir
from concourse.bass_utils import run_bass_kernel_spmd
from concourse.bass_core_rust import IndirectOffsetOnAxis

F32 = mybir.dt.float32
BF16 = mybir.dt.bfloat16
I32 = mybir.dt.int32
AF = mybir.ActivationFunctionType
ALU = mybir.AluOpType
AX = mybir.AxisListType

D = 2048
S = 16
NKC = 16
T = 1024
HALO = 64
TT = T + HALO
NLAG = 64
NG = 8
INTER = 8192
NISL = 64
IGS = 4
NIG = NISL // IGS
NTT = 8
BIGC = float(2 ** 22)

ENGS = ("pe", "act", "dve", "pool", "sync")


class Bld:

    def __init__(self, nc):
        self.nc = nc
        self.q = {e: [] for e in ENGS}
        self.tick = {e: 0 for e in ENGS}
        self.sem = {}
        self.dmacnt = {}

    def op(self, eng, fn, waits=()):
        self.tick[eng] += 1
        t = self.tick[eng]
        waits = tuple(w for w in waits if w is not None)

        def emit(eo, waits=waits, t=t, eng=eng):
            for w in waits:
                if w[0] == "e":
                    if w[1] == eng and w[2] <= t - 1:
                        continue
                    eo.wait_ge(self.sem[w[1]], w[2])
                else:
                    eo.wait_ge(self.sem[w[1]], 16 * w[2])
            inst = fn(eo)
            if t > 1:
                inst._wait_ge(self.sem[eng], t - 1)
            inst.then_inc(self.sem[eng], 1)

        self.q[eng].append(emit)
        return ("e", eng, t)

    def dma(self, qeng, stream, fn, waits=()):
        self.dmacnt[stream] = self.dmacnt.get(stream, 0) + 1
        c = self.dmacnt[stream]
        waits = tuple(w for w in waits if w is not None) + (
            (("d", stream, c - 1),) if c >= 2 else ())

        def emit(eo, waits=waits, c=c):
            for w in waits:
                if w[0] == "e":
                    eo.wait_ge(self.sem[w[1]], w[2])
                else:
                    eo.wait_ge(self.sem[w[1]], 16 * w[2])
            fn(eo).then_inc(self.sem[stream], 16)

        self.q[qeng].append(emit)
        return ("d", stream, c)

    def wait(self, eng, waits):
        waits = tuple(w for w in waits if w is not None)

        def emit(eo, waits=waits):
            for w in waits:
                if w[0] == "e":
                    eo.wait_ge(self.sem[w[1]], w[2])
                else:
                    eo.wait_ge(self.sem[w[1]], 16 * w[2])

        self.q[eng].append(emit)


def last(*hs):
    hs = [h for h in hs if h is not None]
    assert all(h[0] == hs[0][0] and h[1] == hs[0][1] for h in hs)
    return max(hs, key=lambda h: h[2])


STAGE = "full"


def build_nc(debug_taps=(), stage=None):
    stage = stage or STAGE
    nc = bass.Bass("TRN2", target_bir_lowering=False)
    b = Bld(nc)

    def din(name, shape, dt=F32):
        return nc.dram_tensor(name, shape, dt, kind="ExternalInput").ap()

    x_t = din("x_t", [D, TT])
    winT = din("winT", [D, D])
    woutT = din("woutT", [D, D])
    bwT = din("bwT", [128, 256])
    cwT = din("cwT", [S, D])
    a_t = din("a_t", [128, 256])
    dp_t = din("dp_t", [128, NKC])
    bin_t = din("bin_t", [128, NKC])
    bout_t = din("bout_t", [128, NKC])
    wq_t = din("wq_t", [128, D])
    mykeys = din("mykeys", [64, 64])
    gate_t = din("gate_t", [INTER, D])
    up_t = din("up_t", [INTER, D])
    down_t = din("down_t", [NIG * 128, IGS * 2048])
    demb = din("demb", [4096, D])
    uemb = din("uemb", [4096, D])
    sxy_rows = din("sxy_rows", [128, 16], I32)
    c_ident = din("c_ident", [128, 128], BF16)
    c_iotam = din("c_iotam", [128, 64])
    c_kpat = din("c_kpat", [128, NLAG * S])
    c_ones_col = din("c_ones_col", [128, 1], BF16)
    c_scal = din("c_scal", [128, 2])
    c_ioteps = din("c_ioteps", [128, 128])
    halo_mask = din("halo_mask", [128, 1])
    c_ones_f32 = din("c_ones_f32", [1, 128])

    out_t = nc.dram_tensor("out_t", [D, T], F32, kind="ExternalOutput").ap()

    cc_in = nc.dram_tensor("cc_in", [2048, 16], F32, kind="Internal").ap()
    cc_out = nc.dram_tensor("cc_out", [8 * 2048, 16], F32, kind="Internal").ap()
    demb_bf = nc.dram_tensor("demb_bf", [4096, D], BF16, kind="Internal").ap()
    uemb_bf = nc.dram_tensor("uemb_bf", [4096, D], BF16, kind="Internal").ap()

    import contextlib
    es = contextlib.ExitStack()
    with es:
        E = es.enter_context

        BUF1 = E(nc.sbuf_tensor("BUF1", [128, NKC * TT], BF16))
        BUF2 = E(nc.sbuf_tensor("BUF2", [128, NKC * TT], BF16))
        BUF3 = E(nc.sbuf_tensor("BUF3", [128, 8192], BF16))
        R1 = E(nc.sbuf_tensor("R1", [128, NKC * T], F32))
        X1 = E(nc.sbuf_tensor("X1", [128, NKC * T], BF16))
        MISC = E(nc.sbuf_tensor("MISC", [128, 2944], F32))
        BNDL = E(nc.sbuf_tensor("BNDL", [128, 4096], BF16))
        RTG = E(nc.sbuf_tensor("RTG", [128, 1024], F32))
        GXY = E(nc.sbuf_tensor("GXY", [128, 256], F32))
        CONS = E(nc.sbuf_tensor("CONS", [128, 384], F32))
        CONSB = E(nc.sbuf_tensor("CONSB", [128, 384], BF16))
        IDT = E(nc.sbuf_tensor("IDT", [128, 128], BF16))
        PACK = E(nc.sbuf_tensor("PACK", [128, 16], F32))
        SXI = E(nc.sbuf_tensor("SXI", [128, 16], I32))
        SMI = E(nc.sbuf_tensor("SMI", [128, 16], I32))
        ONESF = E(nc.sbuf_tensor("ONESF", [128, 128], F32))

        pA = E(nc.psum_tensor("pA", [128, 1024], F32))
        pB = E(nc.psum_tensor("pB", [128, 1024], F32))
        pH_full = E(nc.psum_tensor("pH", [128, 512], F32))
        pH = pH_full[:, 0:128]
        pC = E(nc.psum_tensor("pC", [128, 1024], F32))

        dma_streams = ["c_const", "c_cb", "c_x", "c_tab", "w_a", "w_b", "w_d",
                       "g_de", "g_ue", "x_pack", "g_cc", "st_out", "m_dma", "tr_x", "tr_e"]
        for e_ in ENGS:
            b.sem[e_] = E(nc.semaphore("s_" + e_))
        for s_ in dma_streams:
            b.sem[s_] = E(nc.semaphore("s_" + s_))

        block = E(nc.Block())

        x_bf = [BUF1[:, i * TT:(i + 1) * TT] for i in range(NKC)]
        xp_bf = [BUF2[:, i * TT:(i + 1) * TT] for i in range(NKC)]
        y_bf = [xp_bf[i][:, HALO:TT] for i in range(NKC)]
        r1c = [R1[:, i * T:(i + 1) * T] for i in range(NKC)]
        x1c = [X1[:, i * T:(i + 1) * T] for i in range(NKC)]
        wslab = [X1[:, 0:2048], X1[:, 2048:4096]]
        bndl = [BNDL[:, 0:2048], BNDL[:, 2048:4096]]
        wqs = BNDL[:, 0:2048]
        qbf = BNDL[:, 2048:3072]
        CWT = BNDL[0:16, 0:2048]
        GT = BNDL[0:16, 2048:3072]
        ESTR = BNDL[:, 3072:3328]
        a_grp = BUF2[:, 0:IGS * T]
        wg_sl = [BUF2[:, 4096 + i * 2048:4096 + (i + 1) * 2048] for i in range(2)]
        wu_sl = [BUF2[:, 8192 + i * 2048:8192 + (i + 1) * 2048] for i in range(2)]
        de_b = [BUF1[:, 0:2048], BUF1[:, 2048:4096]]
        ue_b = [BUF1[:, 4096:6144], BUF1[:, 6144:8192]]
        x1tok = BUF1[:, 8192:10240]
        es_acc = BUF1[:, 10240:12288]
        ttrj = BUF1[:, 12288:14336]
        gs_bf = BUF1[:, 14336:15360]
        ln_scr = [BUF1[:, 15360:16384], BUF1[:, 16384:17408]]
        wgb = [BUF3[:, 0:TT], BUF3[:, TT:2 * TT]]
        dlt_rep = MISC[:, 0:TT]
        la_v = MISC[:, 1088:1344]
        sg_v = MISC[:, 1344:1600]
        kbase = MISC[:, 1600:1728]
        kprod = MISC[:, 1728:1856]
        t1_sp = MISC[0:16, 1856:2944]
        eqbuf = MISC[:, 0:512]
        cbuf = MISC[:, 512:1024]
        ew_f = MISC[:, 1024:1032]
        s8 = MISC[:, 1032:1040]
        sm8 = MISC[:, 1040:1048]
        sil8 = MISC[:, 1048:1056]
        ewf = MISC[:, 1056:1064]
        cidxm = MISC[:, 1064:1072]
        ix64 = MISC[:, 1072:1080]
        offs_f = MISC[:, 1080:1088]
        mx1 = MISC[:, 1088:1089]
        sume = MISC[:, 1089:1090]
        rsum = MISC[:, 1090:1091]
        st_mean = MISC[0:1, 1092:2116]
        eps1_sb = MISC[:, 2116:2180]
        eps2_sb = MISC[:, 2180:2244]
        st_rstd = RTG[64:65, 0:1024]
        st_t1 = RTG[0:1, 0:1024]
        a_sb = CONS[:, 0:256]
        hmask_sb = CONS[:, 370:371]
        dp_sb = CONS[:, 256:272]
        bin_sb = CONS[:, 272:288]
        bout_sb = CONS[:, 288:304]
        iotam_sb = CONS[:, 304:368]
        one_col = CONS[:, 368:369]
        eps_col = CONS[:, 369:370]
        ident_sb = IDT[:, :]
        bw_sb = CONSB[:, 0:256]
        keys_sb = CONSB[0:64, 256:320]
        ones_col_sb = CONSB[:, 320:321]

        tap_out = {}
        for name, shape, dt_ in debug_taps:
            tap_out[name] = nc.dram_tensor("tap_" + name, list(shape), dt_, kind="ExternalOutput").ap()

        def emit_tap(name, sb_ap, waits):
            if name in tap_out:
                b.dma("sync", "st_out",
                      lambda e, n=name, s=sb_ap: e.dma_start(out=tap_out[n][:], in_=s),
                      waits=waits)

        for dst, src in [(a_sb, a_t[:]), (dp_sb, dp_t[:]), (bin_sb, bin_t[:]),
                         (bout_sb, bout_t[:]), (iotam_sb, c_iotam[:]),
                         (ONESF[0:1, :], c_ones_f32[:]), (ONESF[64:65, :], c_ones_f32[:]),
                         (CONS[:, 368:370], c_scal[:]), (CONS[:, 370:371], halo_mask[:])]:
            b.dma("sync", "c_const", lambda e, d=dst, s=src: e.dma_start(out=d, in_=s))
        b.dma("sync", "c_const", lambda e: e.dma_start(out=SXI[:], in_=sxy_rows[:]))
        h_const = ("d", "c_const", b.dmacnt["c_const"])
        for dst, src in [(ident_sb, c_ident[:]), (bw_sb, bwT[:]), (keys_sb, mykeys[:]),
                         (CONSB[64:128, 256:320], mykeys[:]),
                         (ones_col_sb, c_ones_col[:])]:
            b.dma("pool", "c_cb", lambda e, d=dst, s=src: e.dma_start(out=d, in_=s))
        h_cb = ("d", "c_cb", b.dmacnt["c_cb"])

        x_view = x_t.rearrange("(kc p) t -> p kc t", p=128)
        buf1_3d = BUF1[:].rearrange("p (kc t) -> p kc t", t=TT)
        for i in range(4):
            b.dma("pool", "c_x",
                  lambda e, i=i: e.dma_start(out=buf1_3d[:, i * 4:(i + 1) * 4, :],
                                             in_=x_view[:, i * 4:(i + 1) * 4, :]))
        h_x = ("d", "c_x", 4)

        h_sg = b.op("act", lambda e: e.activation(sg_v, a_sb, AF.Sigmoid), waits=[h_const, h_cb])
        h_la = b.op("act", lambda e: e.activation(la_v, sg_v, AF.Ln), waits=[h_sg])

        bnst = {"last_hm": None, "last_he": None}

        def build_bundle(g, par, war):
            hk = b.dma("sync", "m_dma",
                       lambda e, g=g: e.dma_start(out=kbase, in_=c_kpat[:, g * 128:(g + 1) * 128]),
                       waits=[war, bnst["last_hm"]])
            h_prev = bnst["last_he"]
            hm = None
            for kc in range(NKC):
                la_rep = la_v[:, kc * S:(kc + 1) * S].unsqueeze(1).to_broadcast([128, 8, S])
                hm = b.op("dve", lambda e, la_rep=la_rep: e.tensor_tensor(
                    out=kprod.rearrange("p (k s) -> p k s", s=S),
                    in0=kbase.rearrange("p (k s) -> p k s", s=S),
                    in1=la_rep, op=ALU.mult), waits=[hk, h_la, h_prev, h_const, h_cb])
                h_prev = b.op("act", lambda e, g=g, kc=kc, par=par: e.activation(
                    bndl[par][:, kc * 128:(kc + 1) * 128], kprod, AF.Exp), waits=[hm, war])
            bnst["last_hm"] = hm
            bnst["last_he"] = h_prev
            return h_prev

        win_view = winT.rearrange("(mt p) k -> mt p k", p=128)
        wout_view = woutT.rearrange("(mt p) k -> mt p k", p=128)
        gate_view = gate_t.rearrange("(mt p) k -> mt p k", p=128)
        up_view = up_t.rearrange("(mt p) k -> mt p k", p=128)
        down_view = down_t.rearrange("(ig p) k -> ig p k", p=128)
        tabsrc = [demb, uemb]
        tabdst = [demb_bf, uemb_bf]
        tab_i = [0]

        def emit_tab_chunk():
            i = tab_i[0]
            if i >= 16:
                return
            tab_i[0] += 1
            ti, ch = i // 8, i % 8
            b.dma("pool", "c_tab",
                  lambda e, ti=ti, ch=ch: e.dma_start(
                      out=tabdst[ti][ch * 512:(ch + 1) * 512, :],
                      in_=tabsrc[ti][ch * 512:(ch + 1) * 512, :]))

        evac_xp = [None] * NKC
        pe_done_win = [None] * NKC
        for mt in range(NKC):
            par = mt % 2
            war = pe_done_win[mt - 2] if mt >= 2 else None
            h_wd = b.dma("pool", "w_a" if par == 0 else "w_b",
                         lambda e, mt=mt, par=par: e.dma_start(out=wslab[par], in_=win_view[mt]),
                         waits=[war])
            emit_tab_chunk()
            ppsum = pA if par == 0 else pB
            war_ps = evac_xp[mt - 2] if mt >= 2 else None
            phs = pH[:, par * 64:(par + 1) * 64]
            hp = None
            for kc in range(NKC):
                hp = b.op("pe", lambda e, kc=kc, par=par, phs=phs: e.matmul(
                    phs, wslab[par][:, kc * 128:(kc + 1) * 128],
                    x_bf[kc][:, 0:64], start=(kc == 0), stop=(kc == NKC - 1)),
                    waits=([h_wd, h_x, war_ps] + ([h_const, h_cb] if mt == 0 else [])
                           if kc == 0 else []))
                for fo in (0, 512):
                    hp = b.op("pe", lambda e, kc=kc, par=par, ppsum=ppsum, fo=fo: e.matmul(
                        ppsum[:, fo:fo + 512], wslab[par][:, kc * 128:(kc + 1) * 128],
                        x_bf[kc][:, 64 + fo:64 + fo + 512],
                        start=(kc == 0), stop=(kc == NKC - 1)))
            pe_done_win[mt] = hp
            b.op("dve", lambda e, mt=mt, phs=phs: e.tensor_scalar(
                out=xp_bf[mt][:, 0:64], in0=phs, scalar1=bin_sb[:, mt:mt + 1],
                scalar2=hmask_sb, op0=ALU.add, op1=ALU.mult), waits=[hp, h_const])
            evac_xp[mt] = b.op("dve", lambda e, mt=mt, ppsum=ppsum: e.tensor_scalar(
                out=xp_bf[mt][:, 64:TT], in0=ppsum[:, 0:1024], scalar1=bin_sb[:, mt:mt + 1],
                scalar2=None, op0=ALU.add), waits=[hp])

        h_b = [None, None]
        h_b[0] = build_bundle(0, 0, None)
        h_b[1] = build_bundle(1, 1, None)

        h_dlt_mm = None
        for kc in range(NKC):
            h_dlt_mm = b.op("pe", lambda e, kc=kc: e.matmul(
                pH[0:16, 0:64], bw_sb[:, kc * S:(kc + 1) * S], xp_bf[kc][:, 0:64],
                start=(kc == 0), stop=(kc == NKC - 1)),
                waits=([evac_xp[14], evac_xp[15], h_const] if kc == 0 else [evac_xp[kc]]))
            for fo in (0, 512):
                h_dlt_mm = b.op("pe", lambda e, kc=kc, fo=fo: e.matmul(
                    pA[0:16, fo:fo + 512], bw_sb[:, kc * S:(kc + 1) * S],
                    xp_bf[kc][:, 64 + fo:64 + fo + 512],
                    start=(kc == 0), stop=(kc == NKC - 1)))
        h1 = b.op("act", lambda e: e.activation(t1_sp[:, 0:64], pH[0:16, 0:64], AF.Abs),
                  waits=[h_dlt_mm])
        h1 = b.op("act", lambda e: e.activation(t1_sp[:, 64:TT], pA[0:16, 0:1024], AF.Abs),
                  waits=[h1])
        h1 = b.op("act", lambda e: e.activation(t1_sp, t1_sp, AF.Exp, scale=-1.0), waits=[h1])
        h1 = b.op("act", lambda e: e.activation(t1_sp, t1_sp, AF.Ln, bias=one_col[0:16, :]),
                  waits=[h1, h_const])
        h2 = b.op("dve", lambda e: e.tensor_scalar(out=MISC[0:16, 0:64], in0=pH[0:16, 0:64],
                                                   scalar1=0.0, scalar2=None, op0=ALU.max),
                  waits=[h_dlt_mm])
        h2 = b.op("dve", lambda e: e.tensor_scalar(out=MISC[0:16, 64:TT], in0=pA[0:16, 0:1024],
                                                   scalar1=0.0, scalar2=None, op0=ALU.max),
                  waits=[h2])
        h_dlt = b.op("dve", lambda e: e.tensor_tensor(out=MISC[0:16, 0:TT], in0=MISC[0:16, 0:TT],
                                                      in1=t1_sp, op=ALU.add), waits=[h1, h2])
        for k in range(1, 8):
            b.dma("sync", "m_dma",
                  lambda e, k=k: e.dma_start(out=MISC[16 * k:16 * (k + 1), 0:TT],
                                             in_=MISC[0:16, 0:TT]), waits=[h_dlt])
        h_rep = ("d", "m_dma", b.dmacnt["m_dma"])

        h_ioteps = b.dma("sync", "m_dma",
                         lambda e: e.dma_start(out=MISC[:, 2116:2244], in_=c_ioteps[:]),
                         waits=[h_dlt])
        h_wg = [None, None]
        h_sel_done = [None, None]
        h_p_done = [None, None]
        for g in range(NG):
            par = g % 2
            ppsum = pA if par == 0 else pB
            phs = pH[:, par * 64:(par + 1) * 64]
            hp = None
            for kc in range(NKC):
                w0 = []
                if kc == 0:
                    w0 = [h_b[par], h_wg[par], (h2 if g == 0 and par == 0 else None),
                          (h1 if g == 0 else None)]
                hp = b.op("pe", lambda e, kc=kc, par=par, phs=phs: e.matmul(
                    phs, bndl[par][:, kc * 128:(kc + 1) * 128],
                    xp_bf[kc][:, 0:64], start=(kc == 0), stop=(kc == NKC - 1)), waits=w0)
                for fo in (0, 512):
                    hp = b.op("pe", lambda e, kc=kc, par=par, ppsum=ppsum, fo=fo: e.matmul(
                        ppsum[:, fo:fo + 512], bndl[par][:, kc * 128:(kc + 1) * 128],
                        xp_bf[kc][:, 64 + fo:64 + fo + 512],
                        start=(kc == 0), stop=(kc == NKC - 1)))
            h_p_done[par] = hp
            b.op("dve", lambda e, par=par, phs=phs: e.tensor_tensor(
                out=wgb[par][:, 0:64], in0=phs, in1=dlt_rep[:, 0:64], op=ALU.mult),
                waits=[hp, h_rep, h_sel_done[par]])
            h_wg[par] = b.op("dve", lambda e, par=par, ppsum=ppsum: e.tensor_tensor(
                out=wgb[par][:, 64:TT], in0=ppsum[:, 0:1024], in1=dlt_rep[:, 64:TT],
                op=ALU.mult))
            if g + 2 < NG:
                h_b[par] = build_bundle(g + 2, par, h_p_done[par])
            hsel = None
            for k in range(8):
                sig = 8 * g + k
                for fo in (0, 512):
                    hsel = b.op("pe", lambda e, k=k, par=par, sig=sig, fo=fo,
                                st0=(g == 0 and k == 0), sp0=(g == NG - 1 and k == 7): e.matmul(
                        pC[0:16, fo:fo + 512], ident_sb[:, k * 16:(k + 1) * 16],
                        wgb[par][:, HALO - sig + fo:HALO - sig + fo + 512],
                        start=st0, stop=sp0), waits=([h_wg[par]] if (k == 0 and fo == 0) else []))
            h_sel_done[par] = hsel
        h_gt = b.op("dve", lambda e: e.tensor_copy(out=GT[:, :], in_=pC[0:16, 0:1024]),
                    waits=[h_sel_done[(NG - 1) % 2]])

        h_cwt = b.dma("pool", "w_d", lambda e: e.dma_start(out=CWT[:, :], in_=cwT[:]),
                      waits=[h_p_done[0]])
        evac_y = [None] * NKC
        for mt in range(NKC):
            par = mt % 2
            ppsum = pA if par == 0 else pB
            war = evac_y[mt - 2] if mt >= 2 else h_wg[par]
            hy = None
            for fo in (0, 512):
                hy = b.op("pe", lambda e, mt=mt, ppsum=ppsum, fo=fo: e.matmul(
                    ppsum[:, fo:fo + 512], CWT[:, mt * 128:(mt + 1) * 128],
                    GT[:, fo:fo + 512], start=True, stop=True),
                    waits=([h_gt, war, h_cwt if mt == 0 else None] if fo == 0 else []))
            evac_y[mt] = b.op("dve", lambda e, mt=mt, ppsum=ppsum: e.scalar_tensor_tensor(
                out=y_bf[mt], in0=y_bf[mt], scalar=dp_sb[:, mt:mt + 1], in1=ppsum[:, 0:1024],
                op0=ALU.mult, op1=ALU.add), waits=[hy])

        evac_r1 = [None] * NKC
        pe_done_wout = [None] * NKC
        for mt in range(NKC):
            par = mt % 2
            ppsum = pA if par == 0 else pB
            war_slab = pe_done_wout[mt - 2] if mt >= 2 else pe_done_win[NKC - 1]
            h_wd = b.dma("pool", "w_a" if par == 0 else "w_b",
                         lambda e, mt=mt, par=par: e.dma_start(out=wslab[par], in_=wout_view[mt]),
                         waits=[war_slab])
            emit_tab_chunk()
            war_ps = evac_r1[mt - 2] if mt >= 2 else evac_y[14 + par]
            hp = None
            for kc in range(NKC):
                for fo in (0, 512):
                    hp = b.op("pe", lambda e, mt=mt, kc=kc, par=par, ppsum=ppsum, fo=fo: e.matmul(
                        ppsum[:, fo:fo + 512], wslab[par][:, kc * 128:(kc + 1) * 128],
                        y_bf[kc][:, fo:fo + 512],
                        start=(kc == 0), stop=(kc == NKC - 1)),
                        waits=([h_wd, war_ps, (evac_y[kc] if mt == 0 else None)]
                               if (kc == 0 and fo == 0)
                               else ([evac_y[kc]] if (mt == 0 and fo == 0) else [])))
            pe_done_wout[mt] = hp
            hr = b.op("dve", lambda e, mt=mt, ppsum=ppsum: e.scalar_tensor_tensor(
                out=r1c[mt], in0=x_bf[mt][:, HALO:TT], scalar=2.0, in1=ppsum[:, 0:1024],
                op0=ALU.mult, op1=ALU.add), waits=[hp])
            evac_r1[mt] = b.op("dve", lambda e, mt=mt: e.tensor_scalar(
                out=r1c[mt], in0=r1c[mt], scalar1=bout_sb[:, mt:mt + 1], scalar2=None,
                op0=ALU.add), waits=[hr])

        def layernorm(rc, out_bf, wait_in, pre_bcast_waits, first_cast_extra):
            h_cast = [None] * NKC
            hmm = None
            for kc in range(NKC):
                pr = kc % 2
                h_cast[kc] = b.op("dve", lambda e, kc=kc, pr=pr: e.tensor_copy(
                    out=ln_scr[pr], in_=rc[kc]),
                    waits=[wait_in[kc], (hmm if kc >= 2 else None)]
                          + ([first_cast_extra] if kc == 0 else []))
                for fo in (0, 512):
                    hmm = b.op("pe", lambda e, kc=kc, pr=pr, fo=fo: e.matmul(
                        pC[0:1, fo:fo + 512], ones_col_sb, ln_scr[pr][:, fo:fo + 512],
                        start=(kc == 0), stop=(kc == NKC - 1)),
                        waits=([h_cast[kc]] + (pre_bcast_waits if kc == 0 else [])
                               if fo == 0 else []))
            h_mean = b.op("dve", lambda e: e.tensor_scalar(
                out=st_mean, in0=pC[0:1, 0:1024], scalar1=1.0 / D, scalar2=None, op0=ALU.mult),
                waits=[hmm])
            h_bm = None
            for lo in (0, 512):
                h_bm = b.op("pe", lambda e, lo=lo: e.matmul(
                    pA[:, lo:lo + 512], ONESF[0:1, :], st_mean[:, lo:lo + 512],
                    start=True, stop=True),
                    waits=[h_mean] + (pre_bcast_waits if lo == 0 else []))
            hmm2 = None
            h_sub = [None] * NKC
            h_sq = [None] * NKC
            for kc in range(NKC):
                pr = kc % 2
                h_sub[kc] = b.op("dve", lambda e, kc=kc: e.tensor_tensor(
                    out=rc[kc], in0=rc[kc], in1=pA[:, 0:1024], op=ALU.subtract),
                    waits=[h_bm, h_cast[kc]])
                h_sq[kc] = b.op("act", lambda e, kc=kc, pr=pr: e.activation(
                    ln_scr[pr], rc[kc], AF.Square),
                    waits=[h_sub[kc], (hmm2 if kc >= 2 else None), (hmm if kc < 2 else None)])
                for fo in (0, 512):
                    hmm2 = b.op("pe", lambda e, kc=kc, pr=pr, fo=fo: e.matmul(
                        pC[0:1, fo:fo + 512], ones_col_sb, ln_scr[pr][:, fo:fo + 512],
                        start=(kc == 0), stop=(kc == NKC - 1)),
                        waits=([h_sq[kc], (h_mean if kc == 0 else None)] if fo == 0 else []))
            hv = b.op("dve", lambda e: e.tensor_scalar(
                out=st_t1, in0=pC[0:1, 0:1024], scalar1=1.0 / D, scalar2=None, op0=ALU.mult),
                waits=[hmm2])
            hv = b.op("act", lambda e: e.activation(st_t1, st_t1, AF.Sqrt,
                                                    bias=eps_col[0:1, :]), waits=[hv])
            hv = b.op("dve", lambda e: e.reciprocal(st_rstd, st_t1), waits=[hv])
            h_br = None
            for lo in (0, 512):
                h_br = b.op("pe", lambda e, lo=lo: e.matmul(
                    pB[:, lo:lo + 512], ONESF[64:65, :], st_rstd[:, lo:lo + 512],
                    start=True, stop=True),
                    waits=[hv if lo == 0 else None])
            f32_h, bf_h = [], []
            for kc in range(NKC):
                hf = b.op("dve", lambda e, kc=kc: e.tensor_tensor(
                    out=rc[kc], in0=rc[kc], in1=pB[:, 0:1024], op=ALU.mult),
                    waits=[h_br, h_sq[kc]])
                f32_h.append(hf)
                if out_bf is not None:
                    bf_h.append(b.op("act", lambda e, kc=kc: e.activation(
                        out_bf[kc], rc[kc], AF.Copy), waits=[hf]))
            return f32_h, bf_h

        x1_f, x1_b = layernorm(r1c, x1c, evac_r1,
                               pre_bcast_waits=[evac_r1[14], evac_r1[15]],
                               first_cast_extra=evac_r1[15])

        emit_tap("xp", BUF2[:, :], [evac_y[15]])
        emit_tap("dlt", MISC[:, 0:TT], [h_rep])
        emit_tap("gt", GT[:, :], [h_gt])
        emit_tap("x1", X1[:, :], [x1_b[15]] if x1_b else [])
        emit_tap("r1", R1[:, :], [x1_f[15]])

        out_view = out_t.rearrange("(kc p) t -> kc p t", p=128)
        if stage == "attn":
            for kc in range(NKC):
                b.dma("sync", "st_out",
                      lambda e, kc=kc: e.dma_start(out=out_view[kc], in_=r1c[kc]),
                      waits=[x1_f[kc]])
            b.wait("pe", [("d", "st_out", b.dmacnt["st_out"])])
            _emit_streams(block, b)
            return nc

        lvls = {"attn": (0, 9), "swiglu": (1, 9), "route1": (2, 9), "route2": (3, 9),
                "ex_a": (4, 1), "ex_b1": (4, 1.2), "ex_b2": (4, 1.5), "ex_b": (4, 2),
                "ex_c": (4, 3), "nocc": (4, 9), "full": (5, 9),
                "ex_b1_t1": (4, 1.2), "ex_b_t1": (4, 2), "ex_b0_t1": (4, 1.1),
                "ex_bx_t1": (4, 1.15)}
        only_tt0 = stage.endswith("_t1")
        lvl, exlvl = lvls[stage]
        do_route = lvl >= 2
        h_wq = b.dma("pool", "c_cb", lambda e: e.dma_start(out=wqs, in_=wq_t[:]),
                     waits=[h_p_done[0]])
        hq = None
        for kc in (range(NKC) if lvl >= 2 else ()):
            for fo in (0, 512):
                hq = b.op("pe", lambda e, kc=kc, fo=fo: e.matmul(
                    pC[:, fo:fo + 512], wqs[:, kc * 128:(kc + 1) * 128],
                    x1c[kc][:, fo:fo + 512],
                    start=(kc == 0), stop=(kc == NKC - 1)),
                    waits=([h_wq, x1_b[kc]] if fo == 0 else []))
        h_qe = b.op("dve", lambda e: e.tensor_copy(out=qbf, in_=pC[:, 0:1024]),
                    waits=[hq, h_p_done[1]])

        h_rt = [None] * 16
        h_pack = [None] * 16
        for half in (range(2) if do_route else ()):
            for pt in range(8):
                idx = half * 8 + pt
                hr = b.op("pe", lambda e, half=half, pt=pt: e.matmul(
                    pC[:, 0:64], qbf[64 * half:64 * half + 64, pt * 128:(pt + 1) * 128],
                    CONSB[64 * half:64 * half + 64, 256:320], start=True, stop=True),
                    waits=[h_qe, (h_rt[idx - 1] if idx > 0 else None)])
                h_rt[idx] = b.op("dve", lambda e, idx=idx: e.tensor_tensor(
                    out=RTG[:, idx * 64:(idx + 1) * 64], in0=pC[:, 0:64], in1=eps1_sb,
                    op=ALU.subtract), waits=[hr, h_ioteps if idx == 0 else None])
        for half in (range(2) if do_route else ()):
            for pt in range(8):
                idx = half * 8 + pt
                sl = RTG[:, idx * 64:(idx + 1) * 64]
                hk8 = b.op("dve", lambda e, sl=sl: e.max(out=PACK[:, 0:8], in_=sl),
                           waits=[h_rt[idx], (h_pack[idx - 1] if idx > 0 else None)])
                sl_rep = sl.unsqueeze(1).to_broadcast([128, 8, 64])
                k8_rep = PACK[:, 0:8].unsqueeze(2).to_broadcast([128, 8, 64])
                heq = b.op("dve", lambda e, sl_rep=sl_rep, k8_rep=k8_rep: e.tensor_tensor(
                    out=eqbuf.rearrange("p (i j) -> p i j", j=64), in0=sl_rep, in1=k8_rep,
                    op=ALU.is_equal), waits=[hk8])
                iot_rep = iotam_sb.unsqueeze(1).to_broadcast([128, 8, 64])
                hc = b.op("dve", lambda e, iot_rep=iot_rep: e.tensor_tensor(
                    out=cbuf.rearrange("p (i j) -> p i j", j=64),
                    in0=eqbuf.rearrange("p (i j) -> p i j", j=64), in1=iot_rep, op=ALU.mult),
                    waits=[heq, h_const])
                hmin = b.op("dve", lambda e: e.tensor_reduce(
                    out=PACK[:, 8:16], in_=cbuf.rearrange("p (i j) -> p i j", j=64),
                    axis=AX.X, op=ALU.min), waits=[hc])
                dst = cc_in.rearrange("(p2 two) v -> two p2 v", two=2)[half, pt * 128:(pt + 1) * 128, :]
                h_pack[idx] = b.dma("sync", "x_pack",
                                    lambda e, dst=dst: e.dma_start(out=dst, in_=PACK[:, :]),
                                    waits=[hmin])
        h_gsl = [None] * NISL
        h_usl = [None] * NISL
        for isl in range(2):
            par = isl % 2
            h_gsl[isl] = b.dma("pool", "w_a" if par == 0 else "w_b",
                               lambda e, isl=isl, par=par: e.dma_start(
                                   out=wg_sl[par], in_=gate_view[isl]),
                               waits=[pe_done_wout[NKC - 1] if isl == 0 else None])
            h_usl[isl] = b.dma("pool", "w_a" if par == 0 else "w_b",
                               lambda e, isl=isl, par=par: e.dma_start(
                                   out=wu_sl[par], in_=up_view[isl]),
                               waits=[pe_done_wout[NKC - 1] if isl == 0 else None])
        h_dsl = [None] * NIG
        h_dsl[0] = b.dma("pool", "w_d", lambda e: e.dma_start(out=BUF3[:, :], in_=down_view[0]),
                         waits=[h_sel_done[0], h_sel_done[1]])

        h_gx = [None] * NTT
        if lvl >= 5:
            h_cc = b.op("pool", lambda e: e.collective_compute(
                "AllGather", ALU.bypass,
                replica_groups=[[0, 1, 2, 3, 4, 5, 6, 7]],
                ins=[cc_in.opt()], outs=[cc_out.opt()]),
                waits=[("d", "x_pack", b.dmacnt["x_pack"])])
        elif lvl >= 3:
            h_cc = b.op("pool", lambda e: e.memset(GXY[:, :], 0.0),
                        waits=[("d", "x_pack", b.dmacnt["x_pack"])])
        if lvl >= 3:
            for tt in range(NTT):
                b.dma("pool", "g_cc", lambda e, tt=tt: e.indirect_dma_start(
                    out=GXY[:, tt * 16:tt * 16 + 16], out_offset=None, in_=cc_out[:],
                    in_offset=IndirectOffsetOnAxis(SXI[:, tt:tt + 1], axis=0)),
                    waits=[h_cc if tt == 0 else None, h_const if tt == 0 else None])
                b.dma("pool", "g_cc", lambda e, tt=tt: e.indirect_dma_start(
                    out=GXY[:, 128 + tt * 16:128 + tt * 16 + 16], out_offset=None, in_=cc_out[:],
                    in_offset=IndirectOffsetOnAxis(SXI[:, 8 + tt:8 + tt + 1], axis=0)))
                h_gx[tt] = ("d", "g_cc", b.dmacnt["g_cc"])

        emit_tap("rtg", RTG[:, :], [h_rt[15]])
        emit_tap("gxy", GXY[:, :], [h_gx[NTT - 1]])

        pe_gate_done = [None] * NISL
        pe_up_done = [None] * NISL
        h_aev = [None] * NISL
        h_sil = [None] * NISL
        pe_down_done = [None] * NIG
        h_devac = [None] * NIG
        h_r2 = {mt: x1_f[mt] for mt in range(NKC)}
        ex = {
            "h_smi_r_de": None,
            "h_smi_r_ue": None,
            "h_sm_reader": None,
            "h_x1tok_reader": None,
            "h_pt_evac": [None, None],
            "h_es_reader": None,
            "h_es_tr": None,
        }
        eq2 = RTG[:, 0:512]
        c2b = RTG[:, 512:1024]

        def emit_expert_full(tt):
            w0 = [h_gx[tt]]
            gxv = GXY[:, tt * 16:tt * 16 + 8]
            gxi = GXY[:, tt * 16 + 8:tt * 16 + 16]
            gyv = GXY[:, 128 + tt * 16:128 + tt * 16 + 8]
            gyi = GXY[:, 128 + tt * 16 + 8:128 + tt * 16 + 16]
            h = None
            for i in range(8):
                h = b.op("dve", lambda e, i=i, gyv=gyv, gxv=gxv: e.tensor_scalar(
                    out=eqbuf[:, i * 8:(i + 1) * 8], in0=gyv, scalar1=gxv[:, i:i + 1],
                    scalar2=None, op0=ALU.add), waits=(w0 if i == 0 else []))
            h_as = b.op("dve", lambda e: e.tensor_tensor(
                out=eqbuf[:, 0:64], in0=eqbuf[:, 0:64], in1=eps2_sb, op=ALU.subtract),
                waits=[h])
            h = b.op("dve", lambda e, gxi=gxi: e.tensor_scalar(
                out=ix64, in0=gxi, scalar1=BIGC, scalar2=64.0, op0=ALU.add, op1=ALU.mult),
                waits=w0)
            for i in range(8):
                h = b.op("dve", lambda e, i=i, gyi=gyi: e.tensor_scalar(
                    out=cbuf[:, i * 8:(i + 1) * 8], in0=gyi, scalar1=ix64[:, i:i + 1],
                    scalar2=None, op0=ALU.add), waits=([h] if i == 0 else []))
            h_ci = h
            h_s8 = b.op("dve", lambda e: e.max(out=s8, in_=eqbuf[:, 0:64]),
                        waits=[h_as, ex["h_sm_reader"]])
            as_rep = eqbuf[:, 0:64].unsqueeze(1).to_broadcast([128, 8, 64])
            s8_rep = s8.unsqueeze(2).to_broadcast([128, 8, 64])
            h_eq = b.op("dve", lambda e, as_rep=as_rep, s8_rep=s8_rep: e.tensor_tensor(
                out=eq2.rearrange("p (i j) -> p i j", j=64), in0=as_rep, in1=s8_rep,
                op=ALU.is_equal), waits=[h_s8])
            ci_rep = cbuf[:, 0:64].unsqueeze(1).to_broadcast([128, 8, 64])
            h_c2 = b.op("dve", lambda e, ci_rep=ci_rep: e.tensor_tensor(
                out=c2b.rearrange("p (i j) -> p i j", j=64),
                in0=eq2.rearrange("p (i j) -> p i j", j=64), in1=ci_rep, op=ALU.mult),
                waits=[h_eq, h_ci])
            h_cm = b.op("dve", lambda e: e.tensor_reduce(
                out=cidxm, in_=c2b.rearrange("p (i j) -> p i j", j=64),
                axis=AX.X, op=ALU.min), waits=[h_c2])
            par_smi = tt % 2
            h_of = b.op("dve", lambda e: e.tensor_scalar(
                out=offs_f, in0=cidxm, scalar1=BIGC, scalar2=4095.0,
                op0=ALU.add, op1=ALU.min), waits=[h_cm])
            h_of = b.op("dve", lambda e: e.tensor_scalar(
                out=offs_f, in0=offs_f, scalar1=0.0, scalar2=None, op0=ALU.max),
                waits=[h_of])
            h_oi = b.op("dve", lambda e, par_smi=par_smi: e.tensor_copy(
                out=SMI[:, par_smi * 8:par_smi * 8 + 8], in_=offs_f),
                waits=[h_of, ex["h_smi_r_de"], ex["h_smi_r_ue"]])
            h_m = b.op("dve", lambda e: e.tensor_reduce(out=mx1, in_=s8, axis=AX.X,
                                                        op=ALU.max, negate=True), waits=[h_s8])
            h_e = b.op("act", lambda e: e.activation(sm8, s8, AF.Exp, bias=mx1[:, 0:1],
                                                     scale=1.0, accum_out=sume),
                       waits=[h_m, ex["h_sm_reader"]])
            h_rc = b.op("dve", lambda e: e.reciprocal(rsum, sume), waits=[h_e])
            h_sm = b.op("dve", lambda e: e.tensor_scalar(
                out=sm8, in0=sm8, scalar1=rsum[:, 0:1], scalar2=None, op0=ALU.mult),
                waits=[h_rc])
            if exlvl < 1.1:
                return
            h_xt = None
            for kc in range(NKC):
                h_xt = b.dma("sync", "tr_x",
                             lambda e, kc=kc, tt=tt: e.dma_start_transpose(
                                 x1tok[:, kc * 128:(kc + 1) * 128],
                                 x1c[kc][:, tt * 128:(tt + 1) * 128]),
                             waits=[(x1_b[kc] if tt == 0 else None),
                                    (ex["h_x1tok_reader"] if kc == 0 else None)])
            if exlvl < 1.2:
                return
            h_xt_all = [h_xt]
            if exlvl < 1.5:
                return
            h_dot = [None] * 8
            h_deg = [None] * 8
            for r in range(8):
                pr = r % 2
                h_deg[r] = b.dma("pool", "g_de", lambda e, r=r, pr=pr, par_smi=par_smi:
                                 e.indirect_dma_start(
                                     out=de_b[pr], out_offset=None, in_=demb_bf[:],
                                     in_offset=IndirectOffsetOnAxis(
                                         SMI[:, par_smi * 8 + r:par_smi * 8 + r + 1], axis=0)),
                                 waits=[(h_oi if r == 0 else None),
                                        (("d", "c_tab", 16) if (tt == 0 and r == 0) else None),
                                        (h_dot[r - 2] if r >= 2 else None)])
                if exlvl < 2:
                    h_dot[r] = h_deg[r]
                    continue
                h_dm = b.op("dve", lambda e, r=r, pr=pr: e.tensor_tensor(
                    out=ttrj, in0=x1tok, in1=de_b[pr], op=ALU.mult),
                    waits=[h_deg[r]] + (h_xt_all if r == 0 else []))
                h_dot[r] = b.op("dve", lambda e, r=r: e.reduce_sum(
                    out=ew_f[:, r:r + 1], in_=ttrj, axis=AX.X), waits=[h_dm])
            ex["h_smi_r_de"] = h_deg[7]
            if exlvl < 2:
                return
            ex["h_x1tok_reader"] = h_dot[7]
            h_si = b.op("act", lambda e: e.activation(sil8, ew_f, AF.Silu), waits=[h_dot[7]])
            h_ewf = b.op("dve", lambda e: e.tensor_tensor(out=ewf, in0=sil8, in1=sm8,
                                                          op=ALU.mult), waits=[h_si, h_sm])
            ex["h_sm_reader"] = h_ewf
            if exlvl < 3:
                return
            h_es_list = [None] * 8
            h_ueg = [None] * 8
            prev_es_reader = ex["h_es_reader"]
            for r in range(8):
                pr = r % 2
                h_ueg[r] = b.dma("pool", "g_ue", lambda e, r=r, pr=pr, par_smi=par_smi:
                                 e.indirect_dma_start(
                                     out=ue_b[pr], out_offset=None, in_=uemb_bf[:],
                                     in_offset=IndirectOffsetOnAxis(
                                         SMI[:, par_smi * 8 + r:par_smi * 8 + r + 1], axis=0)),
                                 waits=[(prev_es_reader if r < 2 else h_es_list[r - 2])])
                if r == 0:
                    h_es_list[r] = b.op("dve", lambda e, pr=pr: e.tensor_scalar(
                        out=es_acc, in0=ue_b[pr], scalar1=ewf[:, 0:1], scalar2=None,
                        op0=ALU.mult),
                        waits=[h_ueg[0], h_ewf, ex["h_es_reader"], ex["h_es_tr"]])
                else:
                    h_es_list[r] = b.op("dve", lambda e, r=r, pr=pr: e.scalar_tensor_tensor(
                        out=es_acc, in0=ue_b[pr], scalar=ewf[:, r:r + 1], in1=es_acc,
                        op0=ALU.mult, op1=ALU.add), waits=[h_ueg[r], h_es_list[r - 1]])
            h_es = h_es_list[7]
            ex["h_smi_r_ue"] = h_ueg[7]
            ex["h_es_reader"] = h_es
            if exlvl < 4:
                return
            h_es_tr = None
            for kc in range(NKC):
                pt_par = kc % 2
                htr = b.dma("sync", "tr_e",
                            lambda e, kc=kc, pt_par=pt_par: e.dma_start_transpose(
                                ESTR[:, pt_par * 128:(pt_par + 1) * 128],
                                es_acc[:, kc * 128:(kc + 1) * 128]),
                            waits=[h_es, ex["h_pt_evac"][pt_par]])
                hadd = b.op("dve", lambda e, kc=kc, tt=tt, pt_par=pt_par: e.tensor_tensor(
                    out=r1c[kc][:, tt * 128:(tt + 1) * 128],
                    in0=r1c[kc][:, tt * 128:(tt + 1) * 128],
                    in1=ESTR[:, pt_par * 128:(pt_par + 1) * 128], op=ALU.add),
                    waits=[htr, h_r2[kc]])
                h_r2[kc] = hadd
                ex["h_pt_evac"][pt_par] = hadd
                h_es_tr = htr
            ex["h_es_tr"] = h_es_tr

        for ig in range(NIG):
            if lvl >= 4 and 2 <= ig < 2 + NTT and not (only_tt0 and ig != 2):
                emit_expert_full(ig - 2)
            for j in range(IGS):
                isl = ig * IGS + j
                par = isl % 2
                if isl >= 2:
                    h_gsl[isl] = b.dma("pool", "w_a" if par == 0 else "w_b",
                                       lambda e, isl=isl, par=par: e.dma_start(
                                           out=wg_sl[par], in_=gate_view[isl]),
                                       waits=[pe_gate_done[isl - 2]])
                    h_usl[isl] = b.dma("pool", "w_a" if par == 0 else "w_b",
                                       lambda e, isl=isl, par=par: e.dma_start(
                                           out=wu_sl[par], in_=up_view[isl]),
                                       waits=[pe_up_done[isl - 2]])
                war_psum = h_aev[isl - 1] if isl >= 1 else x1_f[NKC - 1]
                hp = None
                for kc in range(NKC):
                    for fo in (0, 512):
                        hp = b.op("pe", lambda e, isl=isl, kc=kc, par=par, fo=fo: e.matmul(
                            pA[:, fo:fo + 512], wg_sl[par][:, kc * 128:(kc + 1) * 128],
                            x1c[kc][:, fo:fo + 512],
                            start=(kc == 0), stop=(kc == NKC - 1)),
                            waits=([h_gsl[isl], war_psum,
                                    (x1_b[kc] if isl == 0 else None),
                                    (h_sil[isl - 1] if isl >= 1 else None)]
                                   if (kc == 0 and fo == 0) else []))
                pe_gate_done[isl] = hp
                h_sil[isl] = b.op("act", lambda e: e.activation(gs_bf, pA[:, 0:1024], AF.Silu),
                                  waits=[hp])
                hp = None
                for kc in range(NKC):
                    for fo in (0, 512):
                        hp = b.op("pe", lambda e, isl=isl, kc=kc, par=par, fo=fo: e.matmul(
                            pA[:, fo:fo + 512], wu_sl[par][:, kc * 128:(kc + 1) * 128],
                            x1c[kc][:, fo:fo + 512],
                            start=(kc == 0), stop=(kc == NKC - 1)),
                            waits=([h_usl[isl], h_sil[isl]] if (kc == 0 and fo == 0) else []))
                pe_up_done[isl] = hp
                h_aev[isl] = b.op("dve", lambda e, j=j: e.tensor_tensor(
                    out=a_grp[:, j * 1024:(j + 1) * 1024], in0=pA[:, 0:1024], in1=gs_bf,
                    op=ALU.mult), waits=[hp, h_sil[isl]])
            evs = []
            for mtg in range(8):
                for half_mt in range(2):
                    mt = mtg * 2 + half_mt
                    pp = pB if half_mt == 0 else pC
                    hp = None
                    for kc in range(IGS):
                        w0 = []
                        if kc == 0:
                            w0 = [h_dsl[ig] if mt == 0 else None,
                                  (evs[mt - 2][1] if mt >= 2 else None)]
                            if mt < 2 and ig == 0:
                                w0 += [x1_f[NKC - 1], h_rt[15]]
                            if mt < 2 and ig >= 1:
                                w0 += [h_devac_pair[half_mt]]
                        if mt == 0:
                            w0 += [h_aev[ig * IGS + kc]]
                        for fo in (0, 512):
                            hp = b.op("pe", lambda e, mt=mt, kc=kc, pp=pp, fo=fo: e.matmul(
                                pp[:, fo:fo + 512],
                                BUF3[:, kc * 2048 + mt * 128:kc * 2048 + (mt + 1) * 128],
                                a_grp[:, kc * 1024 + fo:kc * 1024 + fo + 512],
                                start=(kc == 0), stop=(kc == IGS - 1)),
                                waits=(w0 if fo == 0 else []))
                    hev = b.op("dve", lambda e, mt=mt, pp=pp: e.tensor_tensor(
                        out=r1c[mt], in0=r1c[mt], in1=pp[:, 0:1024], op=ALU.add),
                        waits=[hp, h_r2[mt]])
                    h_r2[mt] = hev
                    evs.append((hp, hev))
            pe_down_done[ig] = evs[-1][0]
            h_devac[ig] = evs[-1][1]
            h_devac_pair = [evs[-2][1], evs[-1][1]]
            if ig + 1 < NIG:
                h_dsl[ig + 1] = b.dma("pool", "w_d",
                                      lambda e, ig=ig: e.dma_start(out=BUF3[:, :],
                                                                   in_=down_view[ig + 1]),
                                      waits=[pe_down_done[ig]])
        for tt in (range(max(0, NIG - 2), NTT) if lvl >= 4 else ()):
            emit_expert_full(tt)

        r2_ready = [h_r2[mt] for mt in range(NKC)]
        ln2_f, _ = layernorm(r1c, None, r2_ready,
                             pre_bcast_waits=[h_aev[NISL - 1], h_devac[NIG - 1]],
                             first_cast_extra=last(*[h_r2[m] for m in range(NKC)]))
        for kc in range(NKC):
            b.dma("sync", "st_out",
                  lambda e, kc=kc: e.dma_start(out=out_view[kc], in_=r1c[kc]),
                  waits=[ln2_f[kc]])
        b.wait("pe", [("d", "st_out", b.dmacnt["st_out"])])
        _emit_streams(block, b)

    return nc


def _emit_streams(block, b):
    @block.tensor
    def _(eo):
        for f in b.q["pe"]:
            f(eo)

    @block.scalar
    def _(eo):
        for f in b.q["act"]:
            f(eo)

    @block.vector
    def _(eo):
        for f in b.q["dve"]:
            f(eo)

    @block.gpsimd
    def _(eo):
        for f in b.q["pool"]:
            f(eo)

    @block.sync
    def _(eo):
        for f in b.q["sync"]:
            f(eo)



def _pretile_lhsT(w):
    K, M = w.shape
    return np.ascontiguousarray(
        w.reshape(K // 128, 128, M // 128, 128).transpose(2, 1, 0, 3).reshape(M, K))


def _prep_inputs(inputs):
    f32 = np.float32
    g = {k: np.ascontiguousarray(np.asarray(v, f32)) for k, v in inputs.items()}
    win, wout = g["L0_attn_proj_in_w"], g["L0_attn_proj_out_w"]
    pre = {}
    pre["winT"] = _pretile_lhsT(np.ascontiguousarray(win.T))
    pre["woutT"] = _pretile_lhsT(np.ascontiguousarray(wout.T))
    pre["bwT"] = np.ascontiguousarray(
        g["L0_attn_B_w"].T.reshape(16, 128, 16).transpose(1, 0, 2).reshape(128, 256))
    pre["cwT"] = np.ascontiguousarray(g["L0_attn_C_w"].T)
    pre["a_t"] = np.ascontiguousarray(
        g["L0_attn_A"].reshape(16, 128, 16).transpose(1, 0, 2).reshape(128, 256))
    pre["dp_t"] = np.ascontiguousarray(g["L0_attn_D_param"].reshape(16, 128).T)
    pre["bin_t"] = np.ascontiguousarray(g["L0_attn_proj_in_b"].reshape(16, 128).T)
    pre["bout_t"] = np.ascontiguousarray(g["L0_attn_proj_out_b"].reshape(16, 128).T)
    pre["wq_t"] = np.ascontiguousarray(
        g["L0_ffn_W_q"].reshape(16, 128, 128).transpose(1, 0, 2).reshape(128, 2048))
    pre["gate_t"] = np.ascontiguousarray(
        g["L0_ffn_gate_w"].reshape(16, 128, 64, 128).transpose(2, 1, 0, 3).reshape(8192, 2048))
    pre["up_t"] = np.ascontiguousarray(
        g["L0_ffn_up_w"].reshape(16, 128, 64, 128).transpose(2, 1, 0, 3).reshape(8192, 2048))
    pre["down_t"] = np.ascontiguousarray(
        g["L0_ffn_down_w"].reshape(NIG, IGS, 128, 16, 128).transpose(0, 2, 1, 3, 4)
        .reshape(NIG * 128, IGS * 2048))
    pre["demb"] = g["L0_ffn_down_embed"]
    pre["uemb"] = g["L0_ffn_up_embed"]
    return g, pre


def kernel(**inputs):
    import ml_dtypes
    bf = ml_dtypes.bfloat16
    f32 = np.float32
    g, pre = _prep_inputs(inputs)
    x = g["L0_attn_x"]
    keys = g["L0_ffn_keys"]

    c_ident = np.eye(128, dtype=f32).astype(bf)
    c_iotam = np.tile(np.arange(64, dtype=f32) - BIGC, (128, 1)).astype(f32)
    kp = np.zeros((NLAG, S), f32) + np.arange(NLAG, dtype=f32)[:, None]
    c_kpat = np.ascontiguousarray(np.tile(kp.reshape(1, -1), (128, 1)))
    c_ones_col = np.ones((128, 1), f32).astype(bf)
    c_ones_f32 = np.ones((1, 128), f32)
    c_scal = np.tile(np.array([[1.0, 1e-5]], f32), (128, 1))
    io = np.arange(64, dtype=f32)
    c_ioteps = np.tile(np.concatenate([io * 2.0 ** -20, io * 2.0 ** -17])[None, :], (128, 1))

    nc = build_nc()
    in_maps = []
    for c in range(8):
        bidx, h = c // 2, c % 2
        s0 = h * 1024
        if h == 0:
            halo = np.zeros((HALO, D), f32)
        else:
            halo = x[bidx, s0 - HALO:s0]
        x_tc = np.ascontiguousarray(np.concatenate([halo, x[bidx, s0:s0 + 1024]], axis=0).T)
        side = 0 if bidx < 2 else 1
        hm = np.full((128, 1), 0.0 if h == 0 else 1.0, f32)
        sxy = np.empty((128, 16), np.int32)
        base = 1024 * c
        for tt in range(8):
            sxy[:, tt] = base + tt * 128 + np.arange(128)
            sxy[:, 8 + tt] = 8192 + base + tt * 128 + np.arange(128)
        im = dict(x_t=x_tc, mykeys=np.ascontiguousarray(keys[side]),
                  sxy_rows=sxy, c_ident=c_ident, c_iotam=c_iotam, c_kpat=c_kpat,
                  c_ones_col=c_ones_col, c_ones_f32=c_ones_f32, c_scal=c_scal,
                  c_ioteps=np.ascontiguousarray(c_ioteps), halo_mask=hm, **pre)
        in_maps.append(im)

    res = run_bass_kernel_spmd(nc, in_maps, core_ids=list(range(8)))
    out = np.empty((4, 2048, 2048), np.float32)
    for c in range(8):
        bidx, h = c // 2, c % 2
        out[bidx, h * 1024:(h + 1) * 1024, :] = res.results[c]["out_t"].T
    return out


# version 05
# speedup vs baseline: 1.6748x; 1.6748x over previous
import numpy as np
import concourse.bass as bass
import concourse.mybir as mybir
from concourse.bass_utils import run_bass_kernel_spmd
from concourse.bass_core_rust import IndirectOffsetOnAxis

F32 = mybir.dt.float32
BF16 = mybir.dt.bfloat16
I32 = mybir.dt.int32
AF = mybir.ActivationFunctionType
ALU = mybir.AluOpType
AX = mybir.AxisListType

D = 2048
S = 16
NKC = 16
T = 1024
HALO = 64
TT = T + HALO
NLAG = 64
NG = 8
INTER = 8192
NISL = 64
IGS = 4
NIG = NISL // IGS
NTT = 8
BIGC = float(2 ** 22)

ENGS = ("pe", "act", "dve", "pool", "sync")


class Bld:

    def __init__(self, nc):
        self.nc = nc
        self.q = {e: [] for e in ENGS}
        self.tick = {e: 0 for e in ENGS}
        self.sem = {}
        self.dmacnt = {}

    def op(self, eng, fn, waits=(), inc=True):
        serial = eng != "pe"
        if inc:
            self.tick[eng] += 1
        t = self.tick[eng]
        waits = tuple(w for w in waits if w is not None)

        def emit(eo, waits=waits, t=t, eng=eng, inc=inc, serial=serial):
            for w in waits:
                if w[0] == "e":
                    if w[1] == eng and w[2] <= (t - 1 if inc else t):
                        continue
                    eo.wait_ge(self.sem[w[1]], w[2])
                else:
                    eo.wait_ge(self.sem[w[1]], 16 * w[2])
            inst = fn(eo)
            if serial and inc and t > 1:
                inst._wait_ge(self.sem[eng], t - 1)
            if inc:
                inst.then_inc(self.sem[eng], 1)

        self.q[eng].append(emit)
        return ("e", eng, t) if inc else None

    def dma(self, qeng, stream, fn, waits=()):
        self.dmacnt[stream] = self.dmacnt.get(stream, 0) + 1
        c = self.dmacnt[stream]
        waits = tuple(w for w in waits if w is not None) + (
            (("d", stream, c - 1),) if c >= 2 else ())

        def emit(eo, waits=waits, c=c):
            for w in waits:
                if w[0] == "e":
                    eo.wait_ge(self.sem[w[1]], w[2])
                else:
                    eo.wait_ge(self.sem[w[1]], 16 * w[2])
            fn(eo).then_inc(self.sem[stream], 16)

        self.q[qeng].append(emit)
        return ("d", stream, c)

    def wait(self, eng, waits):
        waits = tuple(w for w in waits if w is not None)

        def emit(eo, waits=waits):
            for w in waits:
                if w[0] == "e":
                    eo.wait_ge(self.sem[w[1]], w[2])
                else:
                    eo.wait_ge(self.sem[w[1]], 16 * w[2])

        self.q[eng].append(emit)


def last(*hs):
    hs = [h for h in hs if h is not None]
    assert all(h[0] == hs[0][0] and h[1] == hs[0][1] for h in hs)
    return max(hs, key=lambda h: h[2])


STAGE = "full"


def build_nc(debug_taps=(), stage=None):
    stage = stage or STAGE
    nc = bass.Bass("TRN2", target_bir_lowering=False)
    b = Bld(nc)

    def din(name, shape, dt=F32):
        return nc.dram_tensor(name, shape, dt, kind="ExternalInput").ap()

    x_t = din("x_t", [D, TT])
    winT = din("winT", [D, D])
    woutT = din("woutT", [D, D])
    bwT = din("bwT", [128, 256])
    cwT = din("cwT", [S, D])
    a_t = din("a_t", [128, 256])
    dp_t = din("dp_t", [128, NKC])
    bin_t = din("bin_t", [128, NKC])
    bout_t = din("bout_t", [128, NKC])
    wq_t = din("wq_t", [128, D])
    mykeys = din("mykeys", [64, 64])
    gate_t = din("gate_t", [INTER, D])
    up_t = din("up_t", [INTER, D])
    down_t = din("down_t", [NIG * 128, IGS * 2048])
    demb = din("demb", [4096, D])
    uemb = din("uemb", [4096, D])
    sxy_rows = din("sxy_rows", [128, 16], I32)
    c_ident = din("c_ident", [128, 128], BF16)
    c_iotam = din("c_iotam", [128, 64])
    c_kpat = din("c_kpat", [128, NLAG * S])
    c_ones_col = din("c_ones_col", [128, 1], BF16)
    c_scal = din("c_scal", [128, 2])
    c_ioteps = din("c_ioteps", [128, 128])
    halo_mask = din("halo_mask", [128, 1])
    c_ones_f32 = din("c_ones_f32", [1, 128])

    out_t = nc.dram_tensor("out_t", [D, T], F32, kind="ExternalOutput").ap()

    cc_in = nc.dram_tensor("cc_in", [2048, 16], F32, kind="Internal").ap()
    cc_out = nc.dram_tensor("cc_out", [8 * 2048, 16], F32, kind="Internal").ap()
    demb_bf = nc.dram_tensor("demb_bf", [4096, D], BF16, kind="Internal").ap()
    uemb_bf = nc.dram_tensor("uemb_bf", [4096, D], BF16, kind="Internal").ap()

    import contextlib
    es = contextlib.ExitStack()
    with es:
        E = es.enter_context

        BUF1 = E(nc.sbuf_tensor("BUF1", [128, NKC * TT], BF16))
        BUF2 = E(nc.sbuf_tensor("BUF2", [128, NKC * TT], BF16))
        BUF3 = E(nc.sbuf_tensor("BUF3", [128, 8192], BF16))
        R1 = E(nc.sbuf_tensor("R1", [128, NKC * T], F32))
        X1 = E(nc.sbuf_tensor("X1", [128, NKC * T], BF16))
        MISC = E(nc.sbuf_tensor("MISC", [128, 2944], F32))
        BNDL = E(nc.sbuf_tensor("BNDL", [128, 4096], BF16))
        RTG = E(nc.sbuf_tensor("RTG", [128, 1024], F32))
        GXY = E(nc.sbuf_tensor("GXY", [128, 256], F32))
        CONS = E(nc.sbuf_tensor("CONS", [128, 384], F32))
        CONSB = E(nc.sbuf_tensor("CONSB", [128, 384], BF16))
        IDT = E(nc.sbuf_tensor("IDT", [128, 128], BF16))
        PACK = E(nc.sbuf_tensor("PACK", [128, 16], F32))
        SXI = E(nc.sbuf_tensor("SXI", [128, 16], I32))
        SMI = E(nc.sbuf_tensor("SMI", [128, 16], I32))
        ONESF = E(nc.sbuf_tensor("ONESF", [128, 128], F32))

        pA = E(nc.psum_tensor("pA", [128, 1024], F32))
        pB = E(nc.psum_tensor("pB", [128, 1024], F32))
        pH_full = E(nc.psum_tensor("pH", [128, 512], F32))
        pH = pH_full[:, 0:128]
        pC = E(nc.psum_tensor("pC", [128, 1024], F32))

        dma_streams = ["c_const", "c_cb", "c_x", "c_tab", "w_a", "w_b", "w_d",
                       "g_de", "g_ue", "x_pack", "g_cc", "st_out", "m_dma", "tr_x", "tr_e"]
        for e_ in ENGS:
            b.sem[e_] = E(nc.semaphore("s_" + e_))
        for s_ in dma_streams:
            b.sem[s_] = E(nc.semaphore("s_" + s_))

        block = E(nc.Block())

        x_bf = [BUF1[:, i * TT:(i + 1) * TT] for i in range(NKC)]
        xp_bf = [BUF2[:, i * TT:(i + 1) * TT] for i in range(NKC)]
        y_bf = [xp_bf[i][:, HALO:TT] for i in range(NKC)]
        r1c = [R1[:, i * T:(i + 1) * T] for i in range(NKC)]
        x1c = [X1[:, i * T:(i + 1) * T] for i in range(NKC)]
        wslab = [X1[:, 0:2048], X1[:, 2048:4096]]
        bndl = [BNDL[:, 0:2048], BNDL[:, 2048:4096]]
        wqs = BNDL[:, 0:2048]
        qbf = BNDL[:, 2048:3072]
        CWT = BNDL[0:16, 0:2048]
        GT = BNDL[0:16, 2048:3072]
        ESTR = BNDL[:, 3072:3328]
        a_grp = BUF2[:, 0:IGS * T]
        wg_sl = [BUF2[:, 4096 + i * 2048:4096 + (i + 1) * 2048] for i in range(2)]
        wu_sl = [BUF2[:, 8192 + i * 2048:8192 + (i + 1) * 2048] for i in range(2)]
        de_b = [BUF1[:, 0:2048], BUF1[:, 2048:4096]]
        ue_b = [BUF1[:, 4096:6144], BUF1[:, 6144:8192]]
        x1tok = BUF1[:, 8192:10240]
        es_acc = BUF1[:, 10240:12288]
        ttrj = BUF1[:, 12288:14336]
        gs_bf = BUF1[:, 14336:15360]
        ln_scr = [BUF1[:, 15360:16384], BUF1[:, 16384:17408]]
        wgb = [BUF3[:, 0:TT], BUF3[:, TT:2 * TT]]
        dlt_rep = MISC[:, 0:TT]
        la_v = MISC[:, 1088:1344]
        sg_v = MISC[:, 1344:1600]
        kbase = MISC[:, 1600:1728]
        kprod = MISC[:, 1728:1856]
        t1_sp = MISC[0:16, 1856:2944]
        eqbuf = MISC[:, 0:512]
        cbuf = MISC[:, 512:1024]
        ew_f = MISC[:, 1024:1032]
        s8 = MISC[:, 1032:1040]
        sm8 = MISC[:, 1040:1048]
        sil8 = MISC[:, 1048:1056]
        ewf = MISC[:, 1056:1064]
        cidxm = MISC[:, 1064:1072]
        ix64 = MISC[:, 1072:1080]
        offs_f = MISC[:, 1080:1088]
        mx1 = MISC[:, 1088:1089]
        sume = MISC[:, 1089:1090]
        rsum = MISC[:, 1090:1091]
        st_mean = MISC[0:1, 1092:2116]
        eps1_sb = MISC[:, 2116:2180]
        eps2_sb = MISC[:, 2180:2244]
        st_rstd = RTG[64:65, 0:1024]
        st_t1 = RTG[0:1, 0:1024]
        a_sb = CONS[:, 0:256]
        hmask_sb = CONS[:, 370:371]
        dp_sb = CONS[:, 256:272]
        bin_sb = CONS[:, 272:288]
        bout_sb = CONS[:, 288:304]
        iotam_sb = CONS[:, 304:368]
        one_col = CONS[:, 368:369]
        eps_col = CONS[:, 369:370]
        ident_sb = IDT[:, :]
        bw_sb = CONSB[:, 0:256]
        keys_sb = CONSB[0:64, 256:320]
        ones_col_sb = CONSB[:, 320:321]

        tap_out = {}
        for name, shape, dt_ in debug_taps:
            tap_out[name] = nc.dram_tensor("tap_" + name, list(shape), dt_, kind="ExternalOutput").ap()

        def emit_tap(name, sb_ap, waits):
            if name in tap_out:
                b.dma("sync", "st_out",
                      lambda e, n=name, s=sb_ap: e.dma_start(out=tap_out[n][:], in_=s),
                      waits=waits)

        for dst, src in [(a_sb, a_t[:]), (dp_sb, dp_t[:]), (bin_sb, bin_t[:]),
                         (bout_sb, bout_t[:]), (iotam_sb, c_iotam[:]),
                         (ONESF[0:1, :], c_ones_f32[:]), (ONESF[64:65, :], c_ones_f32[:]),
                         (CONS[:, 368:370], c_scal[:]), (CONS[:, 370:371], halo_mask[:])]:
            b.dma("sync", "c_const", lambda e, d=dst, s=src: e.dma_start(out=d, in_=s))
        b.dma("sync", "c_const", lambda e: e.dma_start(out=SXI[:], in_=sxy_rows[:]))
        h_const = ("d", "c_const", b.dmacnt["c_const"])
        for dst, src in [(ident_sb, c_ident[:]), (bw_sb, bwT[:]), (keys_sb, mykeys[:]),
                         (CONSB[64:128, 256:320], mykeys[:]),
                         (ones_col_sb, c_ones_col[:])]:
            b.dma("pool", "c_cb", lambda e, d=dst, s=src: e.dma_start(out=d, in_=s))
        h_cb = ("d", "c_cb", b.dmacnt["c_cb"])

        x_view = x_t.rearrange("(kc p) t -> p kc t", p=128)
        buf1_3d = BUF1[:].rearrange("p (kc t) -> p kc t", t=TT)
        for i in range(4):
            b.dma("pool", "c_x",
                  lambda e, i=i: e.dma_start(out=buf1_3d[:, i * 4:(i + 1) * 4, :],
                                             in_=x_view[:, i * 4:(i + 1) * 4, :]))
        h_x = ("d", "c_x", 4)

        h_sg = b.op("act", lambda e: e.activation(sg_v, a_sb, AF.Sigmoid), waits=[h_const, h_cb])
        h_la = b.op("act", lambda e: e.activation(la_v, sg_v, AF.Ln), waits=[h_sg])

        bnst = {"last_hm": None, "last_he": None}

        def build_bundle(g, par, war):
            hk = b.dma("sync", "m_dma",
                       lambda e, g=g: e.dma_start(out=kbase, in_=c_kpat[:, g * 128:(g + 1) * 128]),
                       waits=[war, bnst["last_hm"]])
            h_prev = bnst["last_he"]
            hm = None
            for kc in range(NKC):
                la_rep = la_v[:, kc * S:(kc + 1) * S].unsqueeze(1).to_broadcast([128, 8, S])
                hm = b.op("dve", lambda e, la_rep=la_rep: e.tensor_tensor(
                    out=kprod.rearrange("p (k s) -> p k s", s=S),
                    in0=kbase.rearrange("p (k s) -> p k s", s=S),
                    in1=la_rep, op=ALU.mult), waits=[hk, h_la, h_prev, h_const, h_cb])
                h_prev = b.op("act", lambda e, g=g, kc=kc, par=par: e.activation(
                    bndl[par][:, kc * 128:(kc + 1) * 128], kprod, AF.Exp), waits=[hm, war])
            bnst["last_hm"] = hm
            bnst["last_he"] = h_prev
            return h_prev

        win_view = winT.rearrange("(mt p) k -> mt p k", p=128)
        wout_view = woutT.rearrange("(mt p) k -> mt p k", p=128)
        gate_view = gate_t.rearrange("(mt p) k -> mt p k", p=128)
        up_view = up_t.rearrange("(mt p) k -> mt p k", p=128)
        down_view = down_t.rearrange("(ig p) k -> ig p k", p=128)
        tabsrc = [demb, uemb]
        tabdst = [demb_bf, uemb_bf]
        tab_i = [0]

        def emit_tab_chunk():
            i = tab_i[0]
            if i >= 16:
                return
            tab_i[0] += 1
            ti, ch = i // 8, i % 8
            b.dma("pool", "c_tab",
                  lambda e, ti=ti, ch=ch: e.dma_start(
                      out=tabdst[ti][ch * 512:(ch + 1) * 512, :],
                      in_=tabsrc[ti][ch * 512:(ch + 1) * 512, :]))

        evac_xp = [None] * NKC
        pe_done_win = [None] * NKC
        for mt in range(NKC):
            par = mt % 2
            war = pe_done_win[mt - 2] if mt >= 2 else None
            h_wd = b.dma("pool", "w_a" if par == 0 else "w_b",
                         lambda e, mt=mt, par=par: e.dma_start(out=wslab[par], in_=win_view[mt]),
                         waits=[war])
            emit_tab_chunk()
            ppsum = pA if par == 0 else pB
            war_ps = evac_xp[mt - 2] if mt >= 2 else None
            phs = pH[:, par * 64:(par + 1) * 64]
            hp = None
            for kc in range(NKC):
                b.op("pe", lambda e, kc=kc, par=par, phs=phs: e.matmul(
                    phs, wslab[par][:, kc * 128:(kc + 1) * 128],
                    x_bf[kc][:, 0:64], start=(kc == 0), stop=(kc == NKC - 1)),
                    waits=([h_wd, h_x, war_ps] + ([h_const, h_cb] if mt == 0 else [])
                           if kc == 0 else []), inc=False)
                for fo in (0, 512):
                    fin = kc == NKC - 1 and fo == 512
                    hp = b.op("pe", lambda e, kc=kc, par=par, ppsum=ppsum, fo=fo: e.matmul(
                        ppsum[:, fo:fo + 512], wslab[par][:, kc * 128:(kc + 1) * 128],
                        x_bf[kc][:, 64 + fo:64 + fo + 512],
                        start=(kc == 0), stop=(kc == NKC - 1)), inc=fin) or hp
            pe_done_win[mt] = hp
            b.op("dve", lambda e, mt=mt, phs=phs: e.tensor_scalar(
                out=xp_bf[mt][:, 0:64], in0=phs, scalar1=bin_sb[:, mt:mt + 1],
                scalar2=hmask_sb, op0=ALU.add, op1=ALU.mult), waits=[hp, h_const])
            evac_xp[mt] = b.op("dve", lambda e, mt=mt, ppsum=ppsum: e.tensor_scalar(
                out=xp_bf[mt][:, 64:TT], in0=ppsum[:, 0:1024], scalar1=bin_sb[:, mt:mt + 1],
                scalar2=None, op0=ALU.add), waits=[hp])

        h_b = [None, None]
        h_b[0] = build_bundle(0, 0, None)
        h_b[1] = build_bundle(1, 1, None)

        h_dlt_mm = None
        for kc in range(NKC):
            b.op("pe", lambda e, kc=kc: e.matmul(
                pH[0:16, 0:64], bw_sb[:, kc * S:(kc + 1) * S], xp_bf[kc][:, 0:64],
                start=(kc == 0), stop=(kc == NKC - 1)),
                waits=([evac_xp[14], evac_xp[15], h_const] if kc == 0 else [evac_xp[kc]]),
                inc=False)
            for fo in (0, 512):
                fin = kc == NKC - 1 and fo == 512
                h_dlt_mm = b.op("pe", lambda e, kc=kc, fo=fo: e.matmul(
                    pA[0:16, fo:fo + 512], bw_sb[:, kc * S:(kc + 1) * S],
                    xp_bf[kc][:, 64 + fo:64 + fo + 512],
                    start=(kc == 0), stop=(kc == NKC - 1)), inc=fin) or h_dlt_mm
        h1 = b.op("act", lambda e: e.activation(t1_sp[:, 0:64], pH[0:16, 0:64], AF.Abs),
                  waits=[h_dlt_mm])
        h1 = b.op("act", lambda e: e.activation(t1_sp[:, 64:TT], pA[0:16, 0:1024], AF.Abs),
                  waits=[h1])
        h1 = b.op("act", lambda e: e.activation(t1_sp, t1_sp, AF.Exp, scale=-1.0), waits=[h1])
        h1 = b.op("act", lambda e: e.activation(t1_sp, t1_sp, AF.Ln, bias=one_col[0:16, :]),
                  waits=[h1, h_const])
        h2 = b.op("dve", lambda e: e.tensor_scalar(out=MISC[0:16, 0:64], in0=pH[0:16, 0:64],
                                                   scalar1=0.0, scalar2=None, op0=ALU.max),
                  waits=[h_dlt_mm])
        h2 = b.op("dve", lambda e: e.tensor_scalar(out=MISC[0:16, 64:TT], in0=pA[0:16, 0:1024],
                                                   scalar1=0.0, scalar2=None, op0=ALU.max),
                  waits=[h2])
        h_dlt = b.op("dve", lambda e: e.tensor_tensor(out=MISC[0:16, 0:TT], in0=MISC[0:16, 0:TT],
                                                      in1=t1_sp, op=ALU.add), waits=[h1, h2])
        for k in range(1, 8):
            b.dma("sync", "m_dma",
                  lambda e, k=k: e.dma_start(out=MISC[16 * k:16 * (k + 1), 0:TT],
                                             in_=MISC[0:16, 0:TT]), waits=[h_dlt])
        h_rep = ("d", "m_dma", b.dmacnt["m_dma"])

        h_ioteps = b.dma("sync", "m_dma",
                         lambda e: e.dma_start(out=MISC[:, 2116:2244], in_=c_ioteps[:]),
                         waits=[h_dlt])
        h_wg = [None, None]
        h_sel_done = [None, None]
        h_p_done = [None, None]
        for g in range(NG):
            par = g % 2
            ppsum = pA if par == 0 else pB
            phs = pH[:, par * 64:(par + 1) * 64]
            hp = None
            for kc in range(NKC):
                w0 = []
                if kc == 0:
                    w0 = [h_b[par], h_wg[par], (h2 if g == 0 and par == 0 else None),
                          (h1 if g == 0 else None)]
                b.op("pe", lambda e, kc=kc, par=par, phs=phs: e.matmul(
                    phs, bndl[par][:, kc * 128:(kc + 1) * 128],
                    xp_bf[kc][:, 0:64], start=(kc == 0), stop=(kc == NKC - 1)), waits=w0,
                    inc=False)
                for fo in (0, 512):
                    fin = kc == NKC - 1 and fo == 512
                    hp = b.op("pe", lambda e, kc=kc, par=par, ppsum=ppsum, fo=fo: e.matmul(
                        ppsum[:, fo:fo + 512], bndl[par][:, kc * 128:(kc + 1) * 128],
                        xp_bf[kc][:, 64 + fo:64 + fo + 512],
                        start=(kc == 0), stop=(kc == NKC - 1)), inc=fin) or hp
            h_p_done[par] = hp
            b.op("dve", lambda e, par=par, phs=phs: e.tensor_tensor(
                out=wgb[par][:, 0:64], in0=phs, in1=dlt_rep[:, 0:64], op=ALU.mult),
                waits=[hp, h_rep, h_sel_done[par]])
            h_wg[par] = b.op("dve", lambda e, par=par, ppsum=ppsum: e.tensor_tensor(
                out=wgb[par][:, 64:TT], in0=ppsum[:, 0:1024], in1=dlt_rep[:, 64:TT],
                op=ALU.mult))
            if g + 2 < NG:
                h_b[par] = build_bundle(g + 2, par, h_p_done[par])
            hsel = None
            for k in range(8):
                sig = 8 * g + k
                for fo in (0, 512):
                    fin = k == 7 and fo == 512
                    hsel = b.op("pe", lambda e, k=k, par=par, sig=sig, fo=fo,
                                st0=(g == 0 and k == 0), sp0=(g == NG - 1 and k == 7): e.matmul(
                        pC[0:16, fo:fo + 512], ident_sb[:, k * 16:(k + 1) * 16],
                        wgb[par][:, HALO - sig + fo:HALO - sig + fo + 512],
                        start=st0, stop=sp0), waits=([h_wg[par]] if (k == 0 and fo == 0) else []),
                        inc=fin) or hsel
            h_sel_done[par] = hsel
        h_gt = b.op("dve", lambda e: e.tensor_copy(out=GT[:, :], in_=pC[0:16, 0:1024]),
                    waits=[h_sel_done[(NG - 1) % 2]])

        h_cwt = b.dma("pool", "w_d", lambda e: e.dma_start(out=CWT[:, :], in_=cwT[:]),
                      waits=[h_p_done[0]])
        evac_y = [None] * NKC
        for mt in range(NKC):
            par = mt % 2
            ppsum = pA if par == 0 else pB
            war = evac_y[mt - 2] if mt >= 2 else h_wg[par]
            hy = None
            for fo in (0, 512):
                hy = b.op("pe", lambda e, mt=mt, ppsum=ppsum, fo=fo: e.matmul(
                    ppsum[:, fo:fo + 512], CWT[:, mt * 128:(mt + 1) * 128],
                    GT[:, fo:fo + 512], start=True, stop=True),
                    waits=([h_gt, war, h_cwt if mt == 0 else None] if fo == 0 else []),
                    inc=(fo == 512)) or hy
            evac_y[mt] = b.op("dve", lambda e, mt=mt, ppsum=ppsum: e.scalar_tensor_tensor(
                out=y_bf[mt], in0=y_bf[mt], scalar=dp_sb[:, mt:mt + 1], in1=ppsum[:, 0:1024],
                op0=ALU.mult, op1=ALU.add), waits=[hy])

        evac_r1 = [None] * NKC
        pe_done_wout = [None] * NKC
        for mt in range(NKC):
            par = mt % 2
            ppsum = pA if par == 0 else pB
            war_slab = pe_done_wout[mt - 2] if mt >= 2 else pe_done_win[NKC - 1]
            h_wd = b.dma("pool", "w_a" if par == 0 else "w_b",
                         lambda e, mt=mt, par=par: e.dma_start(out=wslab[par], in_=wout_view[mt]),
                         waits=[war_slab])
            emit_tab_chunk()
            war_ps = evac_r1[mt - 2] if mt >= 2 else evac_y[14 + par]
            hp = None
            for kc in range(NKC):
                for fo in (0, 512):
                    fin = kc == NKC - 1 and fo == 512
                    hp = b.op("pe", lambda e, mt=mt, kc=kc, par=par, ppsum=ppsum, fo=fo: e.matmul(
                        ppsum[:, fo:fo + 512], wslab[par][:, kc * 128:(kc + 1) * 128],
                        y_bf[kc][:, fo:fo + 512],
                        start=(kc == 0), stop=(kc == NKC - 1)),
                        waits=([h_wd, war_ps, (evac_y[kc] if mt == 0 else None)]
                               if (kc == 0 and fo == 0)
                               else ([evac_y[kc]] if (mt == 0 and fo == 0) else [])),
                        inc=fin) or hp
            pe_done_wout[mt] = hp
            hr = b.op("dve", lambda e, mt=mt, ppsum=ppsum: e.scalar_tensor_tensor(
                out=r1c[mt], in0=x_bf[mt][:, HALO:TT], scalar=2.0, in1=ppsum[:, 0:1024],
                op0=ALU.mult, op1=ALU.add), waits=[hp])
            evac_r1[mt] = b.op("dve", lambda e, mt=mt: e.tensor_scalar(
                out=r1c[mt], in0=r1c[mt], scalar1=bout_sb[:, mt:mt + 1], scalar2=None,
                op0=ALU.add), waits=[hr])

        def layernorm(rc, out_bf, wait_in, pre_bcast_waits, first_cast_extra):
            h_cast = [None] * NKC
            hmm = None
            for kc in range(NKC):
                pr = kc % 2
                h_cast[kc] = b.op("dve", lambda e, kc=kc, pr=pr: e.tensor_copy(
                    out=ln_scr[pr], in_=rc[kc]),
                    waits=[wait_in[kc], (hmm if kc >= 2 else None)]
                          + ([first_cast_extra] if kc == 0 else []))
                for fo in (0, 512):
                    hmm = b.op("pe", lambda e, kc=kc, pr=pr, fo=fo: e.matmul(
                        pC[0:1, fo:fo + 512], ones_col_sb, ln_scr[pr][:, fo:fo + 512],
                        start=(kc == 0), stop=(kc == NKC - 1)),
                        waits=([h_cast[kc]] + (pre_bcast_waits if kc == 0 else [])
                               if fo == 0 else []), inc=(fo == 512)) or hmm
            h_mean = b.op("dve", lambda e: e.tensor_scalar(
                out=st_mean, in0=pC[0:1, 0:1024], scalar1=1.0 / D, scalar2=None, op0=ALU.mult),
                waits=[hmm])
            h_bm = None
            for lo in (0, 512):
                h_bm = b.op("pe", lambda e, lo=lo: e.matmul(
                    pA[:, lo:lo + 512], ONESF[0:1, :], st_mean[:, lo:lo + 512],
                    start=True, stop=True),
                    waits=[h_mean] + (pre_bcast_waits if lo == 0 else []),
                    inc=(lo == 512)) or h_bm
            hmm2 = None
            h_sub = [None] * NKC
            h_sq = [None] * NKC
            for kc in range(NKC):
                pr = kc % 2
                h_sub[kc] = b.op("dve", lambda e, kc=kc: e.tensor_tensor(
                    out=rc[kc], in0=rc[kc], in1=pA[:, 0:1024], op=ALU.subtract),
                    waits=[h_bm, h_cast[kc]])
                h_sq[kc] = b.op("act", lambda e, kc=kc, pr=pr: e.activation(
                    ln_scr[pr], rc[kc], AF.Square),
                    waits=[h_sub[kc], (hmm2 if kc >= 2 else None), (hmm if kc < 2 else None)])
                for fo in (0, 512):
                    hmm2 = b.op("pe", lambda e, kc=kc, pr=pr, fo=fo: e.matmul(
                        pC[0:1, fo:fo + 512], ones_col_sb, ln_scr[pr][:, fo:fo + 512],
                        start=(kc == 0), stop=(kc == NKC - 1)),
                        waits=([h_sq[kc], (h_mean if kc == 0 else None)] if fo == 0 else []),
                        inc=(fo == 512)) or hmm2
            hv = b.op("dve", lambda e: e.tensor_scalar(
                out=st_t1, in0=pC[0:1, 0:1024], scalar1=1.0 / D, scalar2=None, op0=ALU.mult),
                waits=[hmm2])
            hv = b.op("act", lambda e: e.activation(st_t1, st_t1, AF.Sqrt,
                                                    bias=eps_col[0:1, :]), waits=[hv])
            hv = b.op("dve", lambda e: e.reciprocal(st_rstd, st_t1), waits=[hv])
            h_br = None
            for lo in (0, 512):
                h_br = b.op("pe", lambda e, lo=lo: e.matmul(
                    pB[:, lo:lo + 512], ONESF[64:65, :], st_rstd[:, lo:lo + 512],
                    start=True, stop=True),
                    waits=[hv if lo == 0 else None], inc=(lo == 512)) or h_br
            f32_h, bf_h = [], []
            for kc in range(NKC):
                hf = b.op("dve", lambda e, kc=kc: e.tensor_tensor(
                    out=rc[kc], in0=rc[kc], in1=pB[:, 0:1024], op=ALU.mult),
                    waits=[h_br, h_sq[kc]])
                f32_h.append(hf)
                if out_bf is not None:
                    bf_h.append(b.op("act", lambda e, kc=kc: e.activation(
                        out_bf[kc], rc[kc], AF.Copy), waits=[hf]))
            return f32_h, bf_h

        x1_f, x1_b = layernorm(r1c, x1c, evac_r1,
                               pre_bcast_waits=[evac_r1[14], evac_r1[15]],
                               first_cast_extra=evac_r1[15])

        emit_tap("xp", BUF2[:, :], [evac_y[15]])
        emit_tap("dlt", MISC[:, 0:TT], [h_rep])
        emit_tap("gt", GT[:, :], [h_gt])
        emit_tap("x1", X1[:, :], [x1_b[15]] if x1_b else [])
        emit_tap("r1", R1[:, :], [x1_f[15]])

        out_view = out_t.rearrange("(kc p) t -> kc p t", p=128)
        if stage == "attn":
            for kc in range(NKC):
                b.dma("sync", "st_out",
                      lambda e, kc=kc: e.dma_start(out=out_view[kc], in_=r1c[kc]),
                      waits=[x1_f[kc]])
            b.wait("pe", [("d", "st_out", b.dmacnt["st_out"])])
            _emit_streams(block, b)
            return nc

        lvls = {"attn": (0, 9), "swiglu": (1, 9), "route1": (2, 9), "route2": (3, 9),
                "ex_a": (4, 1), "ex_b1": (4, 1.2), "ex_b2": (4, 1.5), "ex_b": (4, 2),
                "ex_c": (4, 3), "nocc": (4, 9), "full": (5, 9),
                "ex_b1_t1": (4, 1.2), "ex_b_t1": (4, 2), "ex_b0_t1": (4, 1.1),
                "ex_bx_t1": (4, 1.15)}
        only_tt0 = stage.endswith("_t1")
        lvl, exlvl = lvls[stage]
        do_route = lvl >= 2
        h_wq = b.dma("pool", "c_cb", lambda e: e.dma_start(out=wqs, in_=wq_t[:]),
                     waits=[h_p_done[0]])
        hq = None
        for kc in (range(NKC) if lvl >= 2 else ()):
            for fo in (0, 512):
                fin = kc == NKC - 1 and fo == 512
                hq = b.op("pe", lambda e, kc=kc, fo=fo: e.matmul(
                    pC[:, fo:fo + 512], wqs[:, kc * 128:(kc + 1) * 128],
                    x1c[kc][:, fo:fo + 512],
                    start=(kc == 0), stop=(kc == NKC - 1)),
                    waits=([h_wq, x1_b[kc]] if fo == 0 else []), inc=fin) or hq
        h_qe = b.op("dve", lambda e: e.tensor_copy(out=qbf, in_=pC[:, 0:1024]),
                    waits=[hq, h_p_done[1]])

        h_rt = [None] * 16
        h_pack = [None] * 16
        for half in (range(2) if do_route else ()):
            for pt in range(8):
                idx = half * 8 + pt
                hr = b.op("pe", lambda e, half=half, pt=pt: e.matmul(
                    pC[:, 0:64], qbf[64 * half:64 * half + 64, pt * 128:(pt + 1) * 128],
                    CONSB[64 * half:64 * half + 64, 256:320], start=True, stop=True),
                    waits=[h_qe, (h_rt[idx - 1] if idx > 0 else None)])
                h_rt[idx] = b.op("dve", lambda e, idx=idx: e.tensor_tensor(
                    out=RTG[:, idx * 64:(idx + 1) * 64], in0=pC[:, 0:64], in1=eps1_sb,
                    op=ALU.subtract), waits=[hr, h_ioteps if idx == 0 else None])
        for half in (range(2) if do_route else ()):
            for pt in range(8):
                idx = half * 8 + pt
                sl = RTG[:, idx * 64:(idx + 1) * 64]
                hk8 = b.op("dve", lambda e, sl=sl: e.max(out=PACK[:, 0:8], in_=sl),
                           waits=[h_rt[idx], (h_pack[idx - 1] if idx > 0 else None)])
                sl_rep = sl.unsqueeze(1).to_broadcast([128, 8, 64])
                k8_rep = PACK[:, 0:8].unsqueeze(2).to_broadcast([128, 8, 64])
                heq = b.op("dve", lambda e, sl_rep=sl_rep, k8_rep=k8_rep: e.tensor_tensor(
                    out=eqbuf.rearrange("p (i j) -> p i j", j=64), in0=sl_rep, in1=k8_rep,
                    op=ALU.is_equal), waits=[hk8])
                iot_rep = iotam_sb.unsqueeze(1).to_broadcast([128, 8, 64])
                hc = b.op("dve", lambda e, iot_rep=iot_rep: e.tensor_tensor(
                    out=cbuf.rearrange("p (i j) -> p i j", j=64),
                    in0=eqbuf.rearrange("p (i j) -> p i j", j=64), in1=iot_rep, op=ALU.mult),
                    waits=[heq, h_const])
                hmin = b.op("dve", lambda e: e.tensor_reduce(
                    out=PACK[:, 8:16], in_=cbuf.rearrange("p (i j) -> p i j", j=64),
                    axis=AX.X, op=ALU.min), waits=[hc])
                dst = cc_in.rearrange("(p2 two) v -> two p2 v", two=2)[half, pt * 128:(pt + 1) * 128, :]
                h_pack[idx] = b.dma("sync", "x_pack",
                                    lambda e, dst=dst: e.dma_start(out=dst, in_=PACK[:, :]),
                                    waits=[hmin])
        h_gsl = [None] * NISL
        h_usl = [None] * NISL
        for isl in range(2):
            par = isl % 2
            h_gsl[isl] = b.dma("pool", "w_a" if par == 0 else "w_b",
                               lambda e, isl=isl, par=par: e.dma_start(
                                   out=wg_sl[par], in_=gate_view[isl]),
                               waits=[pe_done_wout[NKC - 1] if isl == 0 else None])
            h_usl[isl] = b.dma("pool", "w_a" if par == 0 else "w_b",
                               lambda e, isl=isl, par=par: e.dma_start(
                                   out=wu_sl[par], in_=up_view[isl]),
                               waits=[pe_done_wout[NKC - 1] if isl == 0 else None])
        h_dsl = [None] * NIG
        h_dsl[0] = b.dma("pool", "w_d", lambda e: e.dma_start(out=BUF3[:, :], in_=down_view[0]),
                         waits=[h_sel_done[0], h_sel_done[1]])

        h_gx = [None] * NTT
        if lvl >= 5:
            h_cc = b.op("pool", lambda e: e.collective_compute(
                "AllGather", ALU.bypass,
                replica_groups=[[0, 1, 2, 3, 4, 5, 6, 7]],
                ins=[cc_in.opt()], outs=[cc_out.opt()]),
                waits=[("d", "x_pack", b.dmacnt["x_pack"])])
        elif lvl >= 3:
            h_cc = b.op("pool", lambda e: e.memset(GXY[:, :], 0.0),
                        waits=[("d", "x_pack", b.dmacnt["x_pack"])])
        if lvl >= 3:
            for tt in range(NTT):
                b.dma("pool", "g_cc", lambda e, tt=tt: e.indirect_dma_start(
                    out=GXY[:, tt * 16:tt * 16 + 16], out_offset=None, in_=cc_out[:],
                    in_offset=IndirectOffsetOnAxis(SXI[:, tt:tt + 1], axis=0)),
                    waits=[h_cc if tt == 0 else None, h_const if tt == 0 else None])
                b.dma("pool", "g_cc", lambda e, tt=tt: e.indirect_dma_start(
                    out=GXY[:, 128 + tt * 16:128 + tt * 16 + 16], out_offset=None, in_=cc_out[:],
                    in_offset=IndirectOffsetOnAxis(SXI[:, 8 + tt:8 + tt + 1], axis=0)))
                h_gx[tt] = ("d", "g_cc", b.dmacnt["g_cc"])

        emit_tap("rtg", RTG[:, :], [h_rt[15]])
        emit_tap("gxy", GXY[:, :], [h_gx[NTT - 1]])

        pe_gate_done = [None] * NISL
        pe_up_done = [None] * NISL
        h_aev = [None] * NISL
        h_sil = [None] * NISL
        pe_down_done = [None] * NIG
        h_devac = [None] * NIG
        h_r2 = {mt: x1_f[mt] for mt in range(NKC)}
        ex = {
            "h_smi_r_de": None,
            "h_smi_r_ue": None,
            "h_sm_reader": None,
            "h_x1tok_reader": None,
            "h_pt_evac": [None, None],
            "h_es_reader": None,
            "h_es_tr": None,
        }
        eq2 = RTG[:, 0:512]
        c2b = RTG[:, 512:1024]

        def emit_expert_full(tt):
            w0 = [h_gx[tt]]
            gxv = GXY[:, tt * 16:tt * 16 + 8]
            gxi = GXY[:, tt * 16 + 8:tt * 16 + 16]
            gyv = GXY[:, 128 + tt * 16:128 + tt * 16 + 8]
            gyi = GXY[:, 128 + tt * 16 + 8:128 + tt * 16 + 16]
            h = None
            for i in range(8):
                h = b.op("dve", lambda e, i=i, gyv=gyv, gxv=gxv: e.tensor_scalar(
                    out=eqbuf[:, i * 8:(i + 1) * 8], in0=gyv, scalar1=gxv[:, i:i + 1],
                    scalar2=None, op0=ALU.add), waits=(w0 if i == 0 else []))
            h_as = b.op("dve", lambda e: e.tensor_tensor(
                out=eqbuf[:, 0:64], in0=eqbuf[:, 0:64], in1=eps2_sb, op=ALU.subtract),
                waits=[h])
            h = b.op("dve", lambda e, gxi=gxi: e.tensor_scalar(
                out=ix64, in0=gxi, scalar1=BIGC, scalar2=64.0, op0=ALU.add, op1=ALU.mult),
                waits=w0)
            for i in range(8):
                h = b.op("dve", lambda e, i=i, gyi=gyi: e.tensor_scalar(
                    out=cbuf[:, i * 8:(i + 1) * 8], in0=gyi, scalar1=ix64[:, i:i + 1],
                    scalar2=None, op0=ALU.add), waits=([h] if i == 0 else []))
            h_ci = h
            h_s8 = b.op("dve", lambda e: e.max(out=s8, in_=eqbuf[:, 0:64]),
                        waits=[h_as, ex["h_sm_reader"]])
            as_rep = eqbuf[:, 0:64].unsqueeze(1).to_broadcast([128, 8, 64])
            s8_rep = s8.unsqueeze(2).to_broadcast([128, 8, 64])
            h_eq = b.op("dve", lambda e, as_rep=as_rep, s8_rep=s8_rep: e.tensor_tensor(
                out=eq2.rearrange("p (i j) -> p i j", j=64), in0=as_rep, in1=s8_rep,
                op=ALU.is_equal), waits=[h_s8])
            ci_rep = cbuf[:, 0:64].unsqueeze(1).to_broadcast([128, 8, 64])
            h_c2 = b.op("dve", lambda e, ci_rep=ci_rep: e.tensor_tensor(
                out=c2b.rearrange("p (i j) -> p i j", j=64),
                in0=eq2.rearrange("p (i j) -> p i j", j=64), in1=ci_rep, op=ALU.mult),
                waits=[h_eq, h_ci])
            h_cm = b.op("dve", lambda e: e.tensor_reduce(
                out=cidxm, in_=c2b.rearrange("p (i j) -> p i j", j=64),
                axis=AX.X, op=ALU.min), waits=[h_c2])
            par_smi = tt % 2
            h_of = b.op("dve", lambda e: e.tensor_scalar(
                out=offs_f, in0=cidxm, scalar1=BIGC, scalar2=4095.0,
                op0=ALU.add, op1=ALU.min), waits=[h_cm])
            h_of = b.op("dve", lambda e: e.tensor_scalar(
                out=offs_f, in0=offs_f, scalar1=0.0, scalar2=None, op0=ALU.max),
                waits=[h_of])
            h_oi = b.op("dve", lambda e, par_smi=par_smi: e.tensor_copy(
                out=SMI[:, par_smi * 8:par_smi * 8 + 8], in_=offs_f),
                waits=[h_of, ex["h_smi_r_de"], ex["h_smi_r_ue"]])
            h_m = b.op("dve", lambda e: e.tensor_reduce(out=mx1, in_=s8, axis=AX.X,
                                                        op=ALU.max, negate=True), waits=[h_s8])
            h_e = b.op("act", lambda e: e.activation(sm8, s8, AF.Exp, bias=mx1[:, 0:1],
                                                     scale=1.0, accum_out=sume),
                       waits=[h_m, ex["h_sm_reader"]])
            h_rc = b.op("dve", lambda e: e.reciprocal(rsum, sume), waits=[h_e])
            h_sm = b.op("dve", lambda e: e.tensor_scalar(
                out=sm8, in0=sm8, scalar1=rsum[:, 0:1], scalar2=None, op0=ALU.mult),
                waits=[h_rc])
            if exlvl < 1.1:
                return
            h_xt = None
            for kc in range(NKC):
                h_xt = b.dma("sync", "tr_x",
                             lambda e, kc=kc, tt=tt: e.dma_start_transpose(
                                 x1tok[:, kc * 128:(kc + 1) * 128],
                                 x1c[kc][:, tt * 128:(tt + 1) * 128]),
                             waits=[(x1_b[kc] if tt == 0 else None),
                                    (ex["h_x1tok_reader"] if kc == 0 else None)])
            if exlvl < 1.2:
                return
            h_xt_all = [h_xt]
            if exlvl < 1.5:
                return
            h_dot = [None] * 8
            h_deg = [None] * 8
            h_dm2 = [None] * 8
            for r in range(8):
                pr = r % 2
                h_deg[r] = b.dma("pool", "g_de", lambda e, r=r, pr=pr, par_smi=par_smi:
                                 e.indirect_dma_start(
                                     out=de_b[pr], out_offset=None, in_=demb_bf[:],
                                     in_offset=IndirectOffsetOnAxis(
                                         SMI[:, par_smi * 8 + r:par_smi * 8 + r + 1], axis=0)),
                                 waits=[(h_oi if r == 0 else None),
                                        (("d", "c_tab", 16) if (tt == 0 and r == 0) else None),
                                        (h_dm2[r - 2] if r >= 2 else None)])
                if exlvl < 2:
                    h_dot[r] = h_deg[r]
                    continue
                h_dm = b.op("dve", lambda e, r=r, pr=pr: e.tensor_tensor(
                    out=ttrj, in0=x1tok, in1=de_b[pr], op=ALU.mult),
                    waits=[h_deg[r], (h_dot[r - 1] if r >= 1 else None)]
                          + (h_xt_all if r == 0 else []))
                h_dm2[r] = h_dm
                h_dot[r] = b.op("act", lambda e, r=r: e.activation(
                    ttrj, ttrj, AF.Copy, accum_out=ew_f[:, r:r + 1]), waits=[h_dm])
            ex["h_smi_r_de"] = h_deg[7]
            if exlvl < 2:
                return
            ex["h_x1tok_reader"] = h_dm2[7]
            h_si = b.op("act", lambda e: e.activation(sil8, ew_f, AF.Silu), waits=[h_dot[7]])
            h_ewf = b.op("dve", lambda e: e.tensor_tensor(out=ewf, in0=sil8, in1=sm8,
                                                          op=ALU.mult), waits=[h_si, h_sm])
            ex["h_sm_reader"] = h_ewf
            if exlvl < 3:
                return
            h_es_list = [None] * 8
            h_ueg = [None] * 8
            prev_es_reader = ex["h_es_reader"]
            for r in range(8):
                pr = r % 2
                h_ueg[r] = b.dma("pool", "g_ue", lambda e, r=r, pr=pr, par_smi=par_smi:
                                 e.indirect_dma_start(
                                     out=ue_b[pr], out_offset=None, in_=uemb_bf[:],
                                     in_offset=IndirectOffsetOnAxis(
                                         SMI[:, par_smi * 8 + r:par_smi * 8 + r + 1], axis=0)),
                                 waits=[(prev_es_reader if r < 2 else h_es_list[r - 2])])
                if r == 0:
                    h_es_list[r] = b.op("dve", lambda e, pr=pr: e.tensor_scalar(
                        out=es_acc, in0=ue_b[pr], scalar1=ewf[:, 0:1], scalar2=None,
                        op0=ALU.mult),
                        waits=[h_ueg[0], h_ewf, ex["h_es_reader"], ex["h_es_tr"]])
                else:
                    h_es_list[r] = b.op("dve", lambda e, r=r, pr=pr: e.scalar_tensor_tensor(
                        out=es_acc, in0=ue_b[pr], scalar=ewf[:, r:r + 1], in1=es_acc,
                        op0=ALU.mult, op1=ALU.add), waits=[h_ueg[r], h_es_list[r - 1]])
            h_es = h_es_list[7]
            ex["h_smi_r_ue"] = h_ueg[7]
            ex["h_es_reader"] = h_es
            if exlvl < 4:
                return
            h_es_tr = None
            for kc in range(NKC):
                pt_par = kc % 2
                htr = b.dma("sync", "tr_e",
                            lambda e, kc=kc, pt_par=pt_par: e.dma_start_transpose(
                                ESTR[:, pt_par * 128:(pt_par + 1) * 128],
                                es_acc[:, kc * 128:(kc + 1) * 128]),
                            waits=[h_es, ex["h_pt_evac"][pt_par]])
                hadd = b.op("dve", lambda e, kc=kc, tt=tt, pt_par=pt_par: e.tensor_tensor(
                    out=r1c[kc][:, tt * 128:(tt + 1) * 128],
                    in0=r1c[kc][:, tt * 128:(tt + 1) * 128],
                    in1=ESTR[:, pt_par * 128:(pt_par + 1) * 128], op=ALU.add),
                    waits=[htr, h_r2[kc]])
                h_r2[kc] = hadd
                ex["h_pt_evac"][pt_par] = hadd
                h_es_tr = htr
            ex["h_es_tr"] = h_es_tr

        for ig in range(NIG):
            if lvl >= 4 and 2 <= ig < 2 + NTT and not (only_tt0 and ig != 2):
                emit_expert_full(ig - 2)
            for j in range(IGS):
                isl = ig * IGS + j
                par = isl % 2
                if isl >= 2:
                    h_gsl[isl] = b.dma("pool", "w_a" if par == 0 else "w_b",
                                       lambda e, isl=isl, par=par: e.dma_start(
                                           out=wg_sl[par], in_=gate_view[isl]),
                                       waits=[pe_gate_done[isl - 2]])
                    h_usl[isl] = b.dma("pool", "w_a" if par == 0 else "w_b",
                                       lambda e, isl=isl, par=par: e.dma_start(
                                           out=wu_sl[par], in_=up_view[isl]),
                                       waits=[pe_up_done[isl - 2]])
                war_gate = h_sil[isl - 1] if isl >= 1 else x1_f[NKC - 1]
                war_up = h_aev[isl - 1] if isl >= 1 else (
                    x1_f[NKC - 1] if ig == 0 else h_devac_pair[0])
                hp = None
                for kc in range(NKC):
                    for fo in (0, 512):
                        fin = kc == NKC - 1 and fo == 512
                        hp = b.op("pe", lambda e, isl=isl, kc=kc, par=par, fo=fo: e.matmul(
                            pA[:, fo:fo + 512], wg_sl[par][:, kc * 128:(kc + 1) * 128],
                            x1c[kc][:, fo:fo + 512],
                            start=(kc == 0), stop=(kc == NKC - 1)),
                            waits=([h_gsl[isl], war_gate,
                                    (x1_b[kc] if isl == 0 else None)]
                                   if (kc == 0 and fo == 0) else []), inc=fin) or hp
                pe_gate_done[isl] = hp
                h_sil[isl] = b.op("act", lambda e: e.activation(gs_bf, pA[:, 0:1024], AF.Silu),
                                  waits=[hp, (h_aev[isl - 1] if isl >= 1 else None)])
                hp = None
                for kc in range(NKC):
                    for fo in (0, 512):
                        fin = kc == NKC - 1 and fo == 512
                        hp = b.op("pe", lambda e, isl=isl, kc=kc, par=par, fo=fo: e.matmul(
                            pB[:, fo:fo + 512], wu_sl[par][:, kc * 128:(kc + 1) * 128],
                            x1c[kc][:, fo:fo + 512],
                            start=(kc == 0), stop=(kc == NKC - 1)),
                            waits=([h_usl[isl], war_up] if (kc == 0 and fo == 0) else []),
                            inc=fin) or hp
                pe_up_done[isl] = hp
                h_aev[isl] = b.op("dve", lambda e, j=j: e.tensor_tensor(
                    out=a_grp[:, j * 1024:(j + 1) * 1024], in0=pB[:, 0:1024], in1=gs_bf,
                    op=ALU.mult), waits=[hp, h_sil[isl]])
            evs = []
            for mtg in range(8):
                for half_mt in range(2):
                    mt = mtg * 2 + half_mt
                    pp = pB if half_mt == 0 else pC
                    hp = None
                    for kc in range(IGS):
                        w0 = []
                        if kc == 0:
                            w0 = [h_dsl[ig] if mt == 0 else None,
                                  (evs[mt - 2][1] if mt >= 2 else None),
                                  (h_aev[ig * IGS + IGS - 1] if mt < 2 else None)]
                            if mt < 2 and ig == 0:
                                w0 += [x1_f[NKC - 1], h_rt[15]]
                            if mt < 2 and ig >= 1:
                                w0 += [h_devac_pair[half_mt]]
                        if mt == 0:
                            w0 += [h_aev[ig * IGS + kc]]
                        for fo in (0, 512):
                            fin = kc == IGS - 1 and fo == 512
                            hp = b.op("pe", lambda e, mt=mt, kc=kc, pp=pp, fo=fo: e.matmul(
                                pp[:, fo:fo + 512],
                                BUF3[:, kc * 2048 + mt * 128:kc * 2048 + (mt + 1) * 128],
                                a_grp[:, kc * 1024 + fo:kc * 1024 + fo + 512],
                                start=(kc == 0), stop=(kc == IGS - 1)),
                                waits=(w0 if fo == 0 else []), inc=fin) or hp
                    hev = b.op("dve", lambda e, mt=mt, pp=pp: e.tensor_tensor(
                        out=r1c[mt], in0=r1c[mt], in1=pp[:, 0:1024], op=ALU.add),
                        waits=[hp, h_r2[mt]])
                    h_r2[mt] = hev
                    evs.append((hp, hev))
            pe_down_done[ig] = evs[-1][0]
            h_devac[ig] = evs[-1][1]
            h_devac_pair = [evs[-2][1], evs[-1][1]]
            if ig + 1 < NIG:
                h_dsl[ig + 1] = b.dma("pool", "w_d",
                                      lambda e, ig=ig: e.dma_start(out=BUF3[:, :],
                                                                   in_=down_view[ig + 1]),
                                      waits=[pe_down_done[ig]])
        for tt in (range(max(0, NIG - 2), NTT) if lvl >= 4 else ()):
            emit_expert_full(tt)

        r2_ready = [h_r2[mt] for mt in range(NKC)]
        ln2_f, _ = layernorm(r1c, None, r2_ready,
                             pre_bcast_waits=[h_aev[NISL - 1], h_devac[NIG - 1]],
                             first_cast_extra=last(*[h_r2[m] for m in range(NKC)]))
        for kc in range(NKC):
            b.dma("sync", "st_out",
                  lambda e, kc=kc: e.dma_start(out=out_view[kc], in_=r1c[kc]),
                  waits=[ln2_f[kc]])
        b.wait("pe", [("d", "st_out", b.dmacnt["st_out"])])
        _emit_streams(block, b)

    return nc


def _emit_streams(block, b):
    @block.tensor
    def _(eo):
        for f in b.q["pe"]:
            f(eo)

    @block.scalar
    def _(eo):
        for f in b.q["act"]:
            f(eo)

    @block.vector
    def _(eo):
        for f in b.q["dve"]:
            f(eo)

    @block.gpsimd
    def _(eo):
        for f in b.q["pool"]:
            f(eo)

    @block.sync
    def _(eo):
        for f in b.q["sync"]:
            f(eo)



def _pretile_lhsT(w):
    K, M = w.shape
    return np.ascontiguousarray(
        w.reshape(K // 128, 128, M // 128, 128).transpose(2, 1, 0, 3).reshape(M, K))


def _prep_inputs(inputs):
    f32 = np.float32
    g = {k: np.ascontiguousarray(np.asarray(v, f32)) for k, v in inputs.items()}
    win, wout = g["L0_attn_proj_in_w"], g["L0_attn_proj_out_w"]
    pre = {}
    pre["winT"] = _pretile_lhsT(np.ascontiguousarray(win.T))
    pre["woutT"] = _pretile_lhsT(np.ascontiguousarray(wout.T))
    pre["bwT"] = np.ascontiguousarray(
        g["L0_attn_B_w"].T.reshape(16, 128, 16).transpose(1, 0, 2).reshape(128, 256))
    pre["cwT"] = np.ascontiguousarray(g["L0_attn_C_w"].T)
    pre["a_t"] = np.ascontiguousarray(
        g["L0_attn_A"].reshape(16, 128, 16).transpose(1, 0, 2).reshape(128, 256))
    pre["dp_t"] = np.ascontiguousarray(g["L0_attn_D_param"].reshape(16, 128).T)
    pre["bin_t"] = np.ascontiguousarray(g["L0_attn_proj_in_b"].reshape(16, 128).T)
    pre["bout_t"] = np.ascontiguousarray(g["L0_attn_proj_out_b"].reshape(16, 128).T)
    pre["wq_t"] = np.ascontiguousarray(
        g["L0_ffn_W_q"].reshape(16, 128, 128).transpose(1, 0, 2).reshape(128, 2048))
    pre["gate_t"] = np.ascontiguousarray(
        g["L0_ffn_gate_w"].reshape(16, 128, 64, 128).transpose(2, 1, 0, 3).reshape(8192, 2048))
    pre["up_t"] = np.ascontiguousarray(
        g["L0_ffn_up_w"].reshape(16, 128, 64, 128).transpose(2, 1, 0, 3).reshape(8192, 2048))
    pre["down_t"] = np.ascontiguousarray(
        g["L0_ffn_down_w"].reshape(NIG, IGS, 128, 16, 128).transpose(0, 2, 1, 3, 4)
        .reshape(NIG * 128, IGS * 2048))
    pre["demb"] = g["L0_ffn_down_embed"]
    pre["uemb"] = g["L0_ffn_up_embed"]
    return g, pre


def kernel(**inputs):
    import ml_dtypes
    bf = ml_dtypes.bfloat16
    f32 = np.float32
    g, pre = _prep_inputs(inputs)
    x = g["L0_attn_x"]
    keys = g["L0_ffn_keys"]

    c_ident = np.eye(128, dtype=f32).astype(bf)
    c_iotam = np.tile(np.arange(64, dtype=f32) - BIGC, (128, 1)).astype(f32)
    kp = np.zeros((NLAG, S), f32) + np.arange(NLAG, dtype=f32)[:, None]
    c_kpat = np.ascontiguousarray(np.tile(kp.reshape(1, -1), (128, 1)))
    c_ones_col = np.ones((128, 1), f32).astype(bf)
    c_ones_f32 = np.ones((1, 128), f32)
    c_scal = np.tile(np.array([[1.0, 1e-5]], f32), (128, 1))
    io = np.arange(64, dtype=f32)
    c_ioteps = np.tile(np.concatenate([io * 2.0 ** -20, io * 2.0 ** -17])[None, :], (128, 1))

    nc = build_nc()
    in_maps = []
    for c in range(8):
        bidx, h = c // 2, c % 2
        s0 = h * 1024
        if h == 0:
            halo = np.zeros((HALO, D), f32)
        else:
            halo = x[bidx, s0 - HALO:s0]
        x_tc = np.ascontiguousarray(np.concatenate([halo, x[bidx, s0:s0 + 1024]], axis=0).T)
        side = 0 if bidx < 2 else 1
        hm = np.full((128, 1), 0.0 if h == 0 else 1.0, f32)
        sxy = np.empty((128, 16), np.int32)
        base = 1024 * c
        for tt in range(8):
            sxy[:, tt] = base + tt * 128 + np.arange(128)
            sxy[:, 8 + tt] = 8192 + base + tt * 128 + np.arange(128)
        im = dict(x_t=x_tc, mykeys=np.ascontiguousarray(keys[side]),
                  sxy_rows=sxy, c_ident=c_ident, c_iotam=c_iotam, c_kpat=c_kpat,
                  c_ones_col=c_ones_col, c_ones_f32=c_ones_f32, c_scal=c_scal,
                  c_ioteps=np.ascontiguousarray(c_ioteps), halo_mask=hm, **pre)
        in_maps.append(im)

    res = run_bass_kernel_spmd(nc, in_maps, core_ids=list(range(8)))
    out = np.empty((4, 2048, 2048), np.float32)
    for c in range(8):
        bidx, h = c // 2, c % 2
        out[bidx, h * 1024:(h + 1) * 1024, :] = res.results[c]["out_t"].T
    return out


# version 06
# speedup vs baseline: 1.8743x; 1.1191x over previous
import numpy as np
import concourse.bass as bass
import concourse.mybir as mybir
from concourse.bass_utils import run_bass_kernel_spmd
from concourse.bass_core_rust import IndirectOffsetOnAxis

F32 = mybir.dt.float32
BF16 = mybir.dt.bfloat16
I32 = mybir.dt.int32
AF = mybir.ActivationFunctionType
ALU = mybir.AluOpType
AX = mybir.AxisListType

D = 2048
S = 16
NKC = 16
T = 1024
HALO = 64
TT = T + HALO
NLAG = 64
NG = 8
INTER = 8192
NISL = 64
IGS = 4
NIG = NISL // IGS
NTT = 8
BIGC = float(2 ** 22)

ENGS = ("pe", "act", "dve", "pool", "sync")


class Bld:

    def __init__(self, nc):
        self.nc = nc
        self.q = {e: [] for e in ENGS}
        self.tick = {e: 0 for e in ENGS}
        self.sem = {}
        self.dmacnt = {}

    def op(self, eng, fn, waits=(), inc=True):
        serial = eng != "pe"
        if inc:
            self.tick[eng] += 1
        t = self.tick[eng]
        waits = tuple(w for w in waits if w is not None)

        def emit(eo, waits=waits, t=t, eng=eng, inc=inc, serial=serial):
            for w in waits:
                if w[0] == "e":
                    if w[1] == eng and w[2] <= (t - 1 if inc else t):
                        continue
                    eo.wait_ge(self.sem[w[1]], w[2])
                else:
                    eo.wait_ge(self.sem[w[1]], 16 * w[2])
            inst = fn(eo)
            if serial and inc and t > 1:
                inst._wait_ge(self.sem[eng], t - 1)
            if inc:
                inst.then_inc(self.sem[eng], 1)

        self.q[eng].append(emit)
        return ("e", eng, t) if inc else None

    def dma(self, qeng, stream, fn, waits=()):
        self.dmacnt[stream] = self.dmacnt.get(stream, 0) + 1
        c = self.dmacnt[stream]
        waits = tuple(w for w in waits if w is not None) + (
            (("d", stream, c - 1),) if c >= 2 else ())

        def emit(eo, waits=waits, c=c):
            for w in waits:
                if w[0] == "e":
                    eo.wait_ge(self.sem[w[1]], w[2])
                else:
                    eo.wait_ge(self.sem[w[1]], 16 * w[2])
            fn(eo).then_inc(self.sem[stream], 16)

        self.q[qeng].append(emit)
        return ("d", stream, c)

    def wait(self, eng, waits):
        waits = tuple(w for w in waits if w is not None)

        def emit(eo, waits=waits):
            for w in waits:
                if w[0] == "e":
                    eo.wait_ge(self.sem[w[1]], w[2])
                else:
                    eo.wait_ge(self.sem[w[1]], 16 * w[2])

        self.q[eng].append(emit)


def last(*hs):
    hs = [h for h in hs if h is not None]
    assert all(h[0] == hs[0][0] and h[1] == hs[0][1] for h in hs)
    return max(hs, key=lambda h: h[2])


STAGE = "full"


def build_nc(debug_taps=(), stage=None):
    stage = stage or STAGE
    nc = bass.Bass("TRN2", target_bir_lowering=False)
    b = Bld(nc)

    def din(name, shape, dt=F32):
        return nc.dram_tensor(name, shape, dt, kind="ExternalInput").ap()

    x_t = din("x_t", [D, TT])
    winT = din("winT", [D, D])
    woutT = din("woutT", [D, D])
    bwT = din("bwT", [128, 256])
    cwT = din("cwT", [S, D])
    a_t = din("a_t", [128, 256])
    dp_t = din("dp_t", [128, NKC])
    bin_t = din("bin_t", [128, NKC])
    bout_t = din("bout_t", [128, NKC])
    wq_t = din("wq_t", [128, D])
    mykeys = din("mykeys", [64, 64])
    gu_t = din("gu_t", [INTER, 2 * D])
    down_t = din("down_t", [NIG * 128, IGS * 2048])
    duemb = din("duemb", [4096, 2 * D])
    sxy_rows = din("sxy_rows", [128, 16], I32)
    c_ident = din("c_ident", [128, 128], BF16)
    c_iotam = din("c_iotam", [128, 64])
    c_kpat = din("c_kpat", [128, NLAG * S])
    c_ones_col = din("c_ones_col", [128, 1], BF16)
    c_scal = din("c_scal", [128, 2])
    c_ioteps = din("c_ioteps", [128, 128])
    halo_mask = din("halo_mask", [128, 1])
    c_ones_f32 = din("c_ones_f32", [1, 128])

    out_t = nc.dram_tensor("out_t", [D, T], F32, kind="ExternalOutput").ap()

    cc_in = nc.dram_tensor("cc_in", [2048, 16], F32, kind="Internal").ap()
    cc_out = nc.dram_tensor("cc_out", [8 * 2048, 16], F32, kind="Internal").ap()
    duemb_bf = nc.dram_tensor("duemb_bf", [4096, 2 * D], BF16, kind="Internal").ap()

    import contextlib
    es = contextlib.ExitStack()
    with es:
        E = es.enter_context

        BUF1 = E(nc.sbuf_tensor("BUF1", [128, NKC * TT], BF16))
        BUF2 = E(nc.sbuf_tensor("BUF2", [128, NKC * TT], BF16))
        BUF3 = E(nc.sbuf_tensor("BUF3", [128, 8192], BF16))
        R1 = E(nc.sbuf_tensor("R1", [128, NKC * T], F32))
        X1 = E(nc.sbuf_tensor("X1", [128, NKC * T], BF16))
        MISC = E(nc.sbuf_tensor("MISC", [128, 2944], F32))
        BNDL = E(nc.sbuf_tensor("BNDL", [128, 4096], BF16))
        RTG = E(nc.sbuf_tensor("RTG", [128, 1024], F32))
        GXY = E(nc.sbuf_tensor("GXY", [128, 256], F32))
        CONS = E(nc.sbuf_tensor("CONS", [128, 384], F32))
        CONSB = E(nc.sbuf_tensor("CONSB", [128, 384], BF16))
        IDT = E(nc.sbuf_tensor("IDT", [128, 128], BF16))
        PACK = E(nc.sbuf_tensor("PACK", [128, 16], F32))
        SXI = E(nc.sbuf_tensor("SXI", [128, 16], I32))
        SMI = E(nc.sbuf_tensor("SMI", [128, 32], I32))
        ONESF = E(nc.sbuf_tensor("ONESF", [128, 128], F32))

        pA = E(nc.psum_tensor("pA", [128, 1024], F32))
        pB = E(nc.psum_tensor("pB", [128, 1024], F32))
        pH_full = E(nc.psum_tensor("pH", [128, 512], F32))
        pH = pH_full[:, 0:128]
        pC = E(nc.psum_tensor("pC", [128, 1024], F32))

        dma_streams = ["c_const", "c_cb", "c_x", "c_tab", "w_a", "w_b", "w_d",
                       "g_de", "g_ue", "x_pack", "g_cc", "st_out", "m_dma", "tr_x", "tr_e"]
        for e_ in ENGS:
            b.sem[e_] = E(nc.semaphore("s_" + e_))
        for s_ in dma_streams:
            b.sem[s_] = E(nc.semaphore("s_" + s_))

        block = E(nc.Block())

        x_bf = [BUF1[:, i * TT:(i + 1) * TT] for i in range(NKC)]
        xp_bf = [BUF2[:, i * TT:(i + 1) * TT] for i in range(NKC)]
        y_bf = [xp_bf[i][:, HALO:TT] for i in range(NKC)]
        r1c = [R1[:, i * T:(i + 1) * T] for i in range(NKC)]
        x1c = [X1[:, i * T:(i + 1) * T] for i in range(NKC)]
        wslab = [X1[:, 0:2048], X1[:, 2048:4096]]
        bndl = [BNDL[:, 0:2048], BNDL[:, 2048:4096]]
        wqs = BNDL[:, 0:2048]
        qbf = BNDL[:, 2048:3072]
        CWT = BNDL[0:16, 0:2048]
        GT = BNDL[0:16, 2048:3072]
        ESTR = BNDL[:, 3072:3328]
        a_grp = BUF2[:, 0:IGS * T]
        wgu_sl = [BUF2[:, 4096 + i * 4096:4096 + (i + 1) * 4096] for i in range(2)]
        wg_sl = [wgu_sl[i][:, 0:2048] for i in range(2)]
        wu_sl = [wgu_sl[i][:, 2048:4096] for i in range(2)]
        du_b = [BUF1[:, 0:4096], BUF1[:, 4096:8192]]
        x1tok = BUF1[:, 8192:10240]
        es_acc = BUF1[:, 10240:12288]
        ttrj = BUF1[:, 12288:14336]
        gs_bf = BUF1[:, 14336:15360]
        ln_scr = [BUF1[:, 15360:16384], BUF1[:, 16384:17408]]
        wgb = [BUF3[:, 0:TT], BUF3[:, TT:2 * TT]]
        dlt_rep = MISC[:, 0:TT]
        la_v = MISC[:, 1088:1344]
        sg_v = MISC[:, 1344:1600]
        kbase = MISC[:, 1600:1728]
        kprod = MISC[:, 1728:1856]
        t1_sp = MISC[0:16, 1856:2944]
        eqbuf = MISC[:, 0:512]
        cbuf = MISC[:, 512:1024]
        ew_f = MISC[:, 1024:1032]
        s8 = MISC[:, 1032:1040]
        sm8 = MISC[:, 1040:1048]
        sil8 = MISC[:, 1048:1056]
        ewf = MISC[:, 1056:1064]
        cidxm = MISC[:, 1064:1072]
        ix64 = MISC[:, 1072:1080]
        offs_f = MISC[:, 1080:1088]
        mx1 = MISC[:, 1088:1089]
        sume = MISC[:, 1089:1090]
        rsum = MISC[:, 1090:1091]
        st_mean = MISC[0:1, 1092:2116]
        eps1_sb = MISC[:, 2116:2180]
        eps2_sb = MISC[:, 2180:2244]
        sm8s = MISC[:, 2244:2260]
        st_rstd = RTG[64:65, 0:1024]
        st_t1 = RTG[0:1, 0:1024]
        a_sb = CONS[:, 0:256]
        hmask_sb = CONS[:, 370:371]
        dp_sb = CONS[:, 256:272]
        bin_sb = CONS[:, 272:288]
        bout_sb = CONS[:, 288:304]
        iotam_sb = CONS[:, 304:368]
        one_col = CONS[:, 368:369]
        eps_col = CONS[:, 369:370]
        ident_sb = IDT[:, :]
        bw_sb = CONSB[:, 0:256]
        keys_sb = CONSB[0:64, 256:320]
        ones_col_sb = CONSB[:, 320:321]

        tap_out = {}
        for name, shape, dt_ in debug_taps:
            tap_out[name] = nc.dram_tensor("tap_" + name, list(shape), dt_, kind="ExternalOutput").ap()

        def emit_tap(name, sb_ap, waits):
            if name in tap_out:
                b.dma("sync", "st_out",
                      lambda e, n=name, s=sb_ap: e.dma_start(out=tap_out[n][:], in_=s),
                      waits=waits)

        for dst, src in [(a_sb, a_t[:]), (dp_sb, dp_t[:]), (bin_sb, bin_t[:]),
                         (bout_sb, bout_t[:]), (iotam_sb, c_iotam[:]),
                         (ONESF[0:1, :], c_ones_f32[:]), (ONESF[64:65, :], c_ones_f32[:]),
                         (CONS[:, 368:370], c_scal[:]), (CONS[:, 370:371], halo_mask[:])]:
            b.dma("sync", "c_const", lambda e, d=dst, s=src: e.dma_start(out=d, in_=s))
        b.dma("sync", "c_const", lambda e: e.dma_start(out=SXI[:], in_=sxy_rows[:]))
        h_const = ("d", "c_const", b.dmacnt["c_const"])
        for dst, src in [(ident_sb, c_ident[:]), (bw_sb, bwT[:]), (keys_sb, mykeys[:]),
                         (CONSB[64:128, 256:320], mykeys[:]),
                         (ones_col_sb, c_ones_col[:])]:
            b.dma("pool", "c_cb", lambda e, d=dst, s=src: e.dma_start(out=d, in_=s))
        h_cb = ("d", "c_cb", b.dmacnt["c_cb"])

        x_view = x_t.rearrange("(kc p) t -> p kc t", p=128)
        buf1_3d = BUF1[:].rearrange("p (kc t) -> p kc t", t=TT)
        for i in range(4):
            b.dma("pool", "c_x",
                  lambda e, i=i: e.dma_start(out=buf1_3d[:, i * 4:(i + 1) * 4, :],
                                             in_=x_view[:, i * 4:(i + 1) * 4, :]))
        h_x = ("d", "c_x", 4)

        h_sg = b.op("act", lambda e: e.activation(sg_v, a_sb, AF.Sigmoid), waits=[h_const, h_cb])
        h_la = b.op("act", lambda e: e.activation(la_v, sg_v, AF.Ln), waits=[h_sg])

        bnst = {"last_hm": None, "last_he": None}

        def build_bundle(g, par, war):
            hk = b.dma("sync", "m_dma",
                       lambda e, g=g: e.dma_start(out=kbase, in_=c_kpat[:, g * 128:(g + 1) * 128]),
                       waits=[war, bnst["last_hm"]])
            h_prev = bnst["last_he"]
            hm = None
            for kc in range(NKC):
                la_rep = la_v[:, kc * S:(kc + 1) * S].unsqueeze(1).to_broadcast([128, 8, S])
                hm = b.op("dve", lambda e, la_rep=la_rep: e.tensor_tensor(
                    out=kprod.rearrange("p (k s) -> p k s", s=S),
                    in0=kbase.rearrange("p (k s) -> p k s", s=S),
                    in1=la_rep, op=ALU.mult), waits=[hk, h_la, h_prev, h_const, h_cb])
                h_prev = b.op("act", lambda e, g=g, kc=kc, par=par: e.activation(
                    bndl[par][:, kc * 128:(kc + 1) * 128], kprod, AF.Exp), waits=[hm, war])
            bnst["last_hm"] = hm
            bnst["last_he"] = h_prev
            return h_prev

        win_view = winT.rearrange("(mt p) k -> mt p k", p=128)
        wout_view = woutT.rearrange("(mt p) k -> mt p k", p=128)
        gu_view = gu_t.rearrange("(mt p) k -> mt p k", p=128)
        down_view = down_t.rearrange("(ig p) k -> ig p k", p=128)
        tab_i = [0]

        def emit_tab_chunk():
            ch = tab_i[0]
            if ch >= 16:
                return
            tab_i[0] += 1
            b.dma("pool", "c_tab",
                  lambda e, ch=ch: e.dma_start(
                      out=duemb_bf[ch * 256:(ch + 1) * 256, :],
                      in_=duemb[ch * 256:(ch + 1) * 256, :]))

        evac_xp = [None] * NKC
        pe_done_win = [None] * NKC
        for mt in range(NKC):
            par = mt % 2
            war = pe_done_win[mt - 2] if mt >= 2 else None
            h_wd = b.dma("pool", "w_a" if par == 0 else "w_b",
                         lambda e, mt=mt, par=par: e.dma_start(out=wslab[par], in_=win_view[mt]),
                         waits=[war])
            emit_tab_chunk()
            ppsum = pA if par == 0 else pB
            war_ps = evac_xp[mt - 2] if mt >= 2 else None
            phs = pH[:, par * 64:(par + 1) * 64]
            hp = None
            for kc in range(NKC):
                b.op("pe", lambda e, kc=kc, par=par, phs=phs: e.matmul(
                    phs, wslab[par][:, kc * 128:(kc + 1) * 128],
                    x_bf[kc][:, 0:64], start=(kc == 0), stop=(kc == NKC - 1)),
                    waits=([h_wd, h_x, war_ps] + ([h_const, h_cb] if mt == 0 else [])
                           if kc == 0 else []), inc=False)
                for fo in (0, 512):
                    fin = kc == NKC - 1 and fo == 512
                    hp = b.op("pe", lambda e, kc=kc, par=par, ppsum=ppsum, fo=fo: e.matmul(
                        ppsum[:, fo:fo + 512], wslab[par][:, kc * 128:(kc + 1) * 128],
                        x_bf[kc][:, 64 + fo:64 + fo + 512],
                        start=(kc == 0), stop=(kc == NKC - 1)), inc=fin) or hp
            pe_done_win[mt] = hp
            b.op("dve", lambda e, mt=mt, phs=phs: e.tensor_scalar(
                out=xp_bf[mt][:, 0:64], in0=phs, scalar1=bin_sb[:, mt:mt + 1],
                scalar2=hmask_sb, op0=ALU.add, op1=ALU.mult), waits=[hp, h_const])
            evac_xp[mt] = b.op("dve", lambda e, mt=mt, ppsum=ppsum: e.tensor_scalar(
                out=xp_bf[mt][:, 64:TT], in0=ppsum[:, 0:1024], scalar1=bin_sb[:, mt:mt + 1],
                scalar2=None, op0=ALU.add), waits=[hp])

        h_b = [None, None]
        h_b[0] = build_bundle(0, 0, None)
        h_b[1] = build_bundle(1, 1, None)

        h_dlt_mm = None
        for kc in range(NKC):
            b.op("pe", lambda e, kc=kc: e.matmul(
                pH[0:16, 0:64], bw_sb[:, kc * S:(kc + 1) * S], xp_bf[kc][:, 0:64],
                start=(kc == 0), stop=(kc == NKC - 1)),
                waits=([evac_xp[14], evac_xp[15], h_const] if kc == 0 else [evac_xp[kc]]),
                inc=False)
            for fo in (0, 512):
                fin = kc == NKC - 1 and fo == 512
                h_dlt_mm = b.op("pe", lambda e, kc=kc, fo=fo: e.matmul(
                    pA[0:16, fo:fo + 512], bw_sb[:, kc * S:(kc + 1) * S],
                    xp_bf[kc][:, 64 + fo:64 + fo + 512],
                    start=(kc == 0), stop=(kc == NKC - 1)), inc=fin) or h_dlt_mm
        h1 = b.op("act", lambda e: e.activation(t1_sp[:, 0:64], pH[0:16, 0:64], AF.Abs),
                  waits=[h_dlt_mm])
        h1 = b.op("act", lambda e: e.activation(t1_sp[:, 64:TT], pA[0:16, 0:1024], AF.Abs),
                  waits=[h1])
        h1 = b.op("act", lambda e: e.activation(t1_sp, t1_sp, AF.Exp, scale=-1.0), waits=[h1])
        h1 = b.op("act", lambda e: e.activation(t1_sp, t1_sp, AF.Ln, bias=one_col[0:16, :]),
                  waits=[h1, h_const])
        h2 = b.op("dve", lambda e: e.tensor_scalar(out=MISC[0:16, 0:64], in0=pH[0:16, 0:64],
                                                   scalar1=0.0, scalar2=None, op0=ALU.max),
                  waits=[h_dlt_mm])
        h2 = b.op("dve", lambda e: e.tensor_scalar(out=MISC[0:16, 64:TT], in0=pA[0:16, 0:1024],
                                                   scalar1=0.0, scalar2=None, op0=ALU.max),
                  waits=[h2])
        h_dlt = b.op("dve", lambda e: e.tensor_tensor(out=MISC[0:16, 0:TT], in0=MISC[0:16, 0:TT],
                                                      in1=t1_sp, op=ALU.add), waits=[h1, h2])
        for k in range(1, 8):
            b.dma("sync", "m_dma",
                  lambda e, k=k: e.dma_start(out=MISC[16 * k:16 * (k + 1), 0:TT],
                                             in_=MISC[0:16, 0:TT]), waits=[h_dlt])
        h_rep = ("d", "m_dma", b.dmacnt["m_dma"])

        h_ioteps = b.dma("sync", "m_dma",
                         lambda e: e.dma_start(out=MISC[:, 2116:2244], in_=c_ioteps[:]),
                         waits=[h_dlt])
        h_wg = [None, None]
        h_sel_done = [None, None]
        h_p_done = [None, None]
        for g in range(NG):
            par = g % 2
            ppsum = pA if par == 0 else pB
            phs = pH[:, par * 64:(par + 1) * 64]
            hp = None
            for kc in range(NKC):
                w0 = []
                if kc == 0:
                    w0 = [h_b[par], h_wg[par], (h2 if g == 0 and par == 0 else None),
                          (h1 if g == 0 else None)]
                b.op("pe", lambda e, kc=kc, par=par, phs=phs: e.matmul(
                    phs, bndl[par][:, kc * 128:(kc + 1) * 128],
                    xp_bf[kc][:, 0:64], start=(kc == 0), stop=(kc == NKC - 1)), waits=w0,
                    inc=False)
                for fo in (0, 512):
                    fin = kc == NKC - 1 and fo == 512
                    hp = b.op("pe", lambda e, kc=kc, par=par, ppsum=ppsum, fo=fo: e.matmul(
                        ppsum[:, fo:fo + 512], bndl[par][:, kc * 128:(kc + 1) * 128],
                        xp_bf[kc][:, 64 + fo:64 + fo + 512],
                        start=(kc == 0), stop=(kc == NKC - 1)), inc=fin) or hp
            h_p_done[par] = hp
            b.op("dve", lambda e, par=par, phs=phs: e.tensor_tensor(
                out=wgb[par][:, 0:64], in0=phs, in1=dlt_rep[:, 0:64], op=ALU.mult),
                waits=[hp, h_rep, h_sel_done[par]])
            h_wg[par] = b.op("dve", lambda e, par=par, ppsum=ppsum: e.tensor_tensor(
                out=wgb[par][:, 64:TT], in0=ppsum[:, 0:1024], in1=dlt_rep[:, 64:TT],
                op=ALU.mult))
            if g + 2 < NG:
                h_b[par] = build_bundle(g + 2, par, h_p_done[par])
            hsel = None
            for k in range(8):
                sig = 8 * g + k
                for fo in (0, 512):
                    fin = k == 7 and fo == 512
                    hsel = b.op("pe", lambda e, k=k, par=par, sig=sig, fo=fo,
                                st0=(g == 0 and k == 0), sp0=(g == NG - 1 and k == 7): e.matmul(
                        pC[0:16, fo:fo + 512], ident_sb[:, k * 16:(k + 1) * 16],
                        wgb[par][:, HALO - sig + fo:HALO - sig + fo + 512],
                        start=st0, stop=sp0), waits=([h_wg[par]] if (k == 0 and fo == 0) else []),
                        inc=fin) or hsel
            h_sel_done[par] = hsel
        h_gt = b.op("dve", lambda e: e.tensor_copy(out=GT[:, :], in_=pC[0:16, 0:1024]),
                    waits=[h_sel_done[(NG - 1) % 2]])

        h_cwt = b.dma("pool", "w_d", lambda e: e.dma_start(out=CWT[:, :], in_=cwT[:]),
                      waits=[h_p_done[0]])
        evac_y = [None] * NKC
        for mt in range(NKC):
            par = mt % 2
            ppsum = pA if par == 0 else pB
            war = evac_y[mt - 2] if mt >= 2 else h_wg[par]
            hy = None
            for fo in (0, 512):
                hy = b.op("pe", lambda e, mt=mt, ppsum=ppsum, fo=fo: e.matmul(
                    ppsum[:, fo:fo + 512], CWT[:, mt * 128:(mt + 1) * 128],
                    GT[:, fo:fo + 512], start=True, stop=True),
                    waits=([h_gt, war, h_cwt if mt == 0 else None] if fo == 0 else []),
                    inc=(fo == 512)) or hy
            evac_y[mt] = b.op("dve", lambda e, mt=mt, ppsum=ppsum: e.scalar_tensor_tensor(
                out=y_bf[mt], in0=y_bf[mt], scalar=dp_sb[:, mt:mt + 1], in1=ppsum[:, 0:1024],
                op0=ALU.mult, op1=ALU.add), waits=[hy])

        evac_r1 = [None] * NKC
        pe_done_wout = [None] * NKC
        for mt in range(NKC):
            par = mt % 2
            ppsum = pA if par == 0 else pB
            war_slab = pe_done_wout[mt - 2] if mt >= 2 else pe_done_win[NKC - 1]
            h_wd = b.dma("pool", "w_a" if par == 0 else "w_b",
                         lambda e, mt=mt, par=par: e.dma_start(out=wslab[par], in_=wout_view[mt]),
                         waits=[war_slab])
            emit_tab_chunk()
            war_ps = evac_r1[mt - 2] if mt >= 2 else evac_y[14 + par]
            hp = None
            for kc in range(NKC):
                for fo in (0, 512):
                    fin = kc == NKC - 1 and fo == 512
                    hp = b.op("pe", lambda e, mt=mt, kc=kc, par=par, ppsum=ppsum, fo=fo: e.matmul(
                        ppsum[:, fo:fo + 512], wslab[par][:, kc * 128:(kc + 1) * 128],
                        y_bf[kc][:, fo:fo + 512],
                        start=(kc == 0), stop=(kc == NKC - 1)),
                        waits=([h_wd, war_ps, (evac_y[kc] if mt == 0 else None)]
                               if (kc == 0 and fo == 0)
                               else ([evac_y[kc]] if (mt == 0 and fo == 0) else [])),
                        inc=fin) or hp
            pe_done_wout[mt] = hp
            hr = b.op("dve", lambda e, mt=mt, ppsum=ppsum: e.scalar_tensor_tensor(
                out=r1c[mt], in0=x_bf[mt][:, HALO:TT], scalar=2.0, in1=ppsum[:, 0:1024],
                op0=ALU.mult, op1=ALU.add), waits=[hp])
            evac_r1[mt] = b.op("dve", lambda e, mt=mt: e.tensor_scalar(
                out=r1c[mt], in0=r1c[mt], scalar1=bout_sb[:, mt:mt + 1], scalar2=None,
                op0=ALU.add), waits=[hr])

        def layernorm(rc, out_bf, wait_in, pre_bcast_waits, first_cast_extra):
            h_cast = [None] * NKC
            hmm = None
            for kc in range(NKC):
                pr = kc % 2
                h_cast[kc] = b.op("dve", lambda e, kc=kc, pr=pr: e.tensor_copy(
                    out=ln_scr[pr], in_=rc[kc]),
                    waits=[wait_in[kc], (hmm if kc >= 2 else None)]
                          + ([first_cast_extra] if kc == 0 else []))
                for fo in (0, 512):
                    hmm = b.op("pe", lambda e, kc=kc, pr=pr, fo=fo: e.matmul(
                        pC[0:1, fo:fo + 512], ones_col_sb, ln_scr[pr][:, fo:fo + 512],
                        start=(kc == 0), stop=(kc == NKC - 1)),
                        waits=([h_cast[kc]] + (pre_bcast_waits if kc == 0 else [])
                               if fo == 0 else []), inc=(fo == 512)) or hmm
            h_mean = b.op("dve", lambda e: e.tensor_scalar(
                out=st_mean, in0=pC[0:1, 0:1024], scalar1=1.0 / D, scalar2=None, op0=ALU.mult),
                waits=[hmm])
            h_bm = None
            for lo in (0, 512):
                h_bm = b.op("pe", lambda e, lo=lo: e.matmul(
                    pA[:, lo:lo + 512], ONESF[0:1, :], st_mean[:, lo:lo + 512],
                    start=True, stop=True),
                    waits=[h_mean] + (pre_bcast_waits if lo == 0 else []),
                    inc=(lo == 512)) or h_bm
            hmm2 = None
            h_sub = [None] * NKC
            h_sq = [None] * NKC
            for kc in range(NKC):
                pr = kc % 2
                h_sub[kc] = b.op("dve", lambda e, kc=kc: e.tensor_tensor(
                    out=rc[kc], in0=rc[kc], in1=pA[:, 0:1024], op=ALU.subtract),
                    waits=[h_bm, h_cast[kc]])
                h_sq[kc] = b.op("act", lambda e, kc=kc, pr=pr: e.activation(
                    ln_scr[pr], rc[kc], AF.Square),
                    waits=[h_sub[kc], (hmm2 if kc >= 2 else None), (hmm if kc < 2 else None)])
                for fo in (0, 512):
                    hmm2 = b.op("pe", lambda e, kc=kc, pr=pr, fo=fo: e.matmul(
                        pC[0:1, fo:fo + 512], ones_col_sb, ln_scr[pr][:, fo:fo + 512],
                        start=(kc == 0), stop=(kc == NKC - 1)),
                        waits=([h_sq[kc], (h_mean if kc == 0 else None)] if fo == 0 else []),
                        inc=(fo == 512)) or hmm2
            hv = b.op("dve", lambda e: e.tensor_scalar(
                out=st_t1, in0=pC[0:1, 0:1024], scalar1=1.0 / D, scalar2=None, op0=ALU.mult),
                waits=[hmm2])
            hv = b.op("act", lambda e: e.activation(st_t1, st_t1, AF.Sqrt,
                                                    bias=eps_col[0:1, :]), waits=[hv])
            hv = b.op("dve", lambda e: e.reciprocal(st_rstd, st_t1), waits=[hv])
            h_br = None
            for lo in (0, 512):
                h_br = b.op("pe", lambda e, lo=lo: e.matmul(
                    pB[:, lo:lo + 512], ONESF[64:65, :], st_rstd[:, lo:lo + 512],
                    start=True, stop=True),
                    waits=[hv if lo == 0 else None], inc=(lo == 512)) or h_br
            f32_h, bf_h = [], []
            for kc in range(NKC):
                hf = b.op("dve", lambda e, kc=kc: e.tensor_tensor(
                    out=rc[kc], in0=rc[kc], in1=pB[:, 0:1024], op=ALU.mult),
                    waits=[h_br, h_sq[kc]])
                f32_h.append(hf)
                if out_bf is not None:
                    bf_h.append(b.op("act", lambda e, kc=kc: e.activation(
                        out_bf[kc], rc[kc], AF.Copy), waits=[hf]))
            return f32_h, bf_h

        x1_f, x1_b = layernorm(r1c, x1c, evac_r1,
                               pre_bcast_waits=[evac_r1[14], evac_r1[15]],
                               first_cast_extra=evac_r1[15])

        emit_tap("xp", BUF2[:, :], [evac_y[15]])
        emit_tap("dlt", MISC[:, 0:TT], [h_rep])
        emit_tap("gt", GT[:, :], [h_gt])
        emit_tap("x1", X1[:, :], [x1_b[15]] if x1_b else [])
        emit_tap("r1", R1[:, :], [x1_f[15]])

        out_view = out_t.rearrange("(kc p) t -> kc p t", p=128)
        if stage == "attn":
            for kc in range(NKC):
                b.dma("sync", "st_out",
                      lambda e, kc=kc: e.dma_start(out=out_view[kc], in_=r1c[kc]),
                      waits=[x1_f[kc]])
            b.wait("pe", [("d", "st_out", b.dmacnt["st_out"])])
            _emit_streams(block, b)
            return nc

        lvls = {"attn": (0, 9), "swiglu": (1, 9), "route1": (2, 9), "route2": (3, 9),
                "ex_a": (4, 1), "ex_b1": (4, 1.2), "ex_b2": (4, 1.5), "ex_b": (4, 2),
                "ex_c": (4, 3), "nocc": (4, 9), "full": (5, 9),
                "ex_b1_t1": (4, 1.2), "ex_b_t1": (4, 2), "ex_b0_t1": (4, 1.1),
                "ex_bx_t1": (4, 1.15)}
        only_tt0 = stage.endswith("_t1")
        lvl, _exlvl = lvls[stage]
        do_route = lvl >= 2
        h_wq = b.dma("pool", "c_cb", lambda e: e.dma_start(out=wqs, in_=wq_t[:]),
                     waits=[h_p_done[0]])
        hq = None
        for kc in (range(NKC) if lvl >= 2 else ()):
            for fo in (0, 512):
                fin = kc == NKC - 1 and fo == 512
                hq = b.op("pe", lambda e, kc=kc, fo=fo: e.matmul(
                    pC[:, fo:fo + 512], wqs[:, kc * 128:(kc + 1) * 128],
                    x1c[kc][:, fo:fo + 512],
                    start=(kc == 0), stop=(kc == NKC - 1)),
                    waits=([h_wq, x1_b[kc]] if fo == 0 else []), inc=fin) or hq
        h_qe = b.op("dve", lambda e: e.tensor_copy(out=qbf, in_=pC[:, 0:1024]),
                    waits=[hq, h_p_done[1]])

        h_rt = [None] * 16
        h_pack = [None] * 16
        for half in (range(2) if do_route else ()):
            for pt in range(8):
                idx = half * 8 + pt
                hr = b.op("pe", lambda e, half=half, pt=pt: e.matmul(
                    pC[:, 0:64], qbf[64 * half:64 * half + 64, pt * 128:(pt + 1) * 128],
                    CONSB[64 * half:64 * half + 64, 256:320], start=True, stop=True),
                    waits=[h_qe, (h_rt[idx - 1] if idx > 0 else None)])
                h_rt[idx] = b.op("dve", lambda e, idx=idx: e.tensor_tensor(
                    out=RTG[:, idx * 64:(idx + 1) * 64], in0=pC[:, 0:64], in1=eps1_sb,
                    op=ALU.subtract), waits=[hr, h_ioteps if idx == 0 else None])
        for half in (range(2) if do_route else ()):
            for pt in range(8):
                idx = half * 8 + pt
                sl = RTG[:, idx * 64:(idx + 1) * 64]
                hk8 = b.op("dve", lambda e, sl=sl: e.max(out=PACK[:, 0:8], in_=sl),
                           waits=[h_rt[idx], (h_pack[idx - 1] if idx > 0 else None)])
                sl_rep = sl.unsqueeze(1).to_broadcast([128, 8, 64])
                k8_rep = PACK[:, 0:8].unsqueeze(2).to_broadcast([128, 8, 64])
                heq = b.op("dve", lambda e, sl_rep=sl_rep, k8_rep=k8_rep: e.tensor_tensor(
                    out=eqbuf.rearrange("p (i j) -> p i j", j=64), in0=sl_rep, in1=k8_rep,
                    op=ALU.is_equal), waits=[hk8])
                iot_rep = iotam_sb.unsqueeze(1).to_broadcast([128, 8, 64])
                hc = b.op("dve", lambda e, iot_rep=iot_rep: e.tensor_tensor(
                    out=cbuf.rearrange("p (i j) -> p i j", j=64),
                    in0=eqbuf.rearrange("p (i j) -> p i j", j=64), in1=iot_rep, op=ALU.mult),
                    waits=[heq, h_const])
                hmin = b.op("dve", lambda e: e.tensor_reduce(
                    out=PACK[:, 8:16], in_=cbuf.rearrange("p (i j) -> p i j", j=64),
                    axis=AX.X, op=ALU.min), waits=[hc])
                dst = cc_in.rearrange("(p2 two) v -> two p2 v", two=2)[half, pt * 128:(pt + 1) * 128, :]
                h_pack[idx] = b.dma("sync", "x_pack",
                                    lambda e, dst=dst: e.dma_start(out=dst, in_=PACK[:, :]),
                                    waits=[hmin])
        h_gsl = [None] * NISL
        h_usl = [None] * NISL
        for isl in range(2):
            par = isl % 2
            h_gsl[isl] = b.dma("pool", "w_a" if par == 0 else "w_b",
                               lambda e, isl=isl, par=par: e.dma_start(
                                   out=wgu_sl[par], in_=gu_view[isl]),
                               waits=[pe_done_wout[NKC - 1] if isl == 0 else None])
            h_usl[isl] = h_gsl[isl]
        h_dsl = [None] * NIG
        h_dsl[0] = b.dma("pool", "w_d", lambda e: e.dma_start(out=BUF3[:, :], in_=down_view[0]),
                         waits=[h_sel_done[0], h_sel_done[1]])

        h_gx = [None] * NTT
        if lvl >= 5:
            h_cc = b.op("pool", lambda e: e.collective_compute(
                "AllGather", ALU.bypass,
                replica_groups=[[0, 1, 2, 3, 4, 5, 6, 7]],
                ins=[cc_in.opt()], outs=[cc_out.opt()]),
                waits=[("d", "x_pack", b.dmacnt["x_pack"])])
        elif lvl >= 3:
            h_cc = b.op("pool", lambda e: e.memset(GXY[:, :], 0.0),
                        waits=[("d", "x_pack", b.dmacnt["x_pack"])])
        if lvl >= 3:
            for tt in range(NTT):
                b.dma("pool", "g_cc", lambda e, tt=tt: e.indirect_dma_start(
                    out=GXY[:, tt * 16:tt * 16 + 16], out_offset=None, in_=cc_out[:],
                    in_offset=IndirectOffsetOnAxis(SXI[:, tt:tt + 1], axis=0)),
                    waits=[h_cc if tt == 0 else None, h_const if tt == 0 else None])
                b.dma("pool", "g_cc", lambda e, tt=tt: e.indirect_dma_start(
                    out=GXY[:, 128 + tt * 16:128 + tt * 16 + 16], out_offset=None, in_=cc_out[:],
                    in_offset=IndirectOffsetOnAxis(SXI[:, 8 + tt:8 + tt + 1], axis=0)))
                h_gx[tt] = ("d", "g_cc", b.dmacnt["g_cc"])

        emit_tap("rtg", RTG[:, :], [h_rt[15]])
        emit_tap("gxy", GXY[:, :], [h_gx[NTT - 1]])

        pe_gate_done = [None] * NISL
        pe_up_done = [None] * NISL
        h_aev = [None] * NISL
        h_sil = [None] * NISL
        pe_down_done = [None] * NIG
        h_devac = [None] * NIG
        h_r2 = {mt: x1_f[mt] for mt in range(NKC)}
        ex = {
            "h_smi_reader": [None] * 4,
            "h_s8_act": None,
            "h_x1tok_reader": None,
            "h_du_war": [None, None],
            "h_es_tr": None,
            "h_es_reader": None,
            "h_estr_war": [None, None],
            "h_ttrj_act": None,
        }
        eq2 = RTG[:, 0:512]
        c2b = RTG[:, 512:1024]
        exs = [dict() for _ in range(NTT)]

        def ex_part1(tt):
            st = exs[tt]
            slot = tt % 4
            sm_slot = tt % 2
            w0 = [h_gx[tt]]
            gxv = GXY[:, tt * 16:tt * 16 + 8]
            gxi = GXY[:, tt * 16 + 8:tt * 16 + 16]
            gyv = GXY[:, 128 + tt * 16:128 + tt * 16 + 8]
            gyi = GXY[:, 128 + tt * 16 + 8:128 + tt * 16 + 16]
            h = None
            for i in range(8):
                h = b.op("dve", lambda e, i=i, gyv=gyv, gxv=gxv: e.tensor_scalar(
                    out=eqbuf[:, i * 8:(i + 1) * 8], in0=gyv, scalar1=gxv[:, i:i + 1],
                    scalar2=None, op0=ALU.add), waits=(w0 if i == 0 else []))
            h_as = b.op("dve", lambda e: e.tensor_tensor(
                out=eqbuf[:, 0:64], in0=eqbuf[:, 0:64], in1=eps2_sb, op=ALU.subtract),
                waits=[h])
            h = b.op("dve", lambda e, gxi=gxi: e.tensor_scalar(
                out=ix64, in0=gxi, scalar1=BIGC, scalar2=64.0, op0=ALU.add, op1=ALU.mult),
                waits=w0)
            for i in range(8):
                h = b.op("dve", lambda e, i=i, gyi=gyi: e.tensor_scalar(
                    out=cbuf[:, i * 8:(i + 1) * 8], in0=gyi, scalar1=ix64[:, i:i + 1],
                    scalar2=None, op0=ALU.add), waits=([h] if i == 0 else []))
            h_ci = h
            h_s8 = b.op("dve", lambda e: e.max(out=s8, in_=eqbuf[:, 0:64]),
                        waits=[h_as, ex["h_s8_act"]])
            as_rep = eqbuf[:, 0:64].unsqueeze(1).to_broadcast([128, 8, 64])
            s8_rep = s8.unsqueeze(2).to_broadcast([128, 8, 64])
            h_eq = b.op("dve", lambda e, as_rep=as_rep, s8_rep=s8_rep: e.tensor_tensor(
                out=eq2.rearrange("p (i j) -> p i j", j=64), in0=as_rep, in1=s8_rep,
                op=ALU.is_equal), waits=[h_s8])
            ci_rep = cbuf[:, 0:64].unsqueeze(1).to_broadcast([128, 8, 64])
            h_c2 = b.op("dve", lambda e, ci_rep=ci_rep: e.tensor_tensor(
                out=c2b.rearrange("p (i j) -> p i j", j=64),
                in0=eq2.rearrange("p (i j) -> p i j", j=64), in1=ci_rep, op=ALU.mult),
                waits=[h_eq, h_ci])
            h_cm = b.op("dve", lambda e: e.tensor_reduce(
                out=cidxm, in_=c2b.rearrange("p (i j) -> p i j", j=64),
                axis=AX.X, op=ALU.min), waits=[h_c2])
            h_of = b.op("dve", lambda e: e.tensor_scalar(
                out=offs_f, in0=cidxm, scalar1=BIGC, scalar2=4095.0,
                op0=ALU.add, op1=ALU.min), waits=[h_cm])
            h_of = b.op("dve", lambda e: e.tensor_scalar(
                out=offs_f, in0=offs_f, scalar1=0.0, scalar2=None, op0=ALU.max),
                waits=[h_of])
            h_oi = b.op("dve", lambda e, slot=slot: e.tensor_copy(
                out=SMI[:, slot * 8:slot * 8 + 8], in_=offs_f),
                waits=[h_of, ex["h_smi_reader"][slot]])
            sm8v = sm8s[:, sm_slot * 8:sm_slot * 8 + 8]
            h_m = b.op("dve", lambda e: e.tensor_reduce(out=mx1, in_=s8, axis=AX.X,
                                                        op=ALU.max, negate=True), waits=[h_s8])
            h_e = b.op("act", lambda e, sm8v=sm8v: e.activation(
                sm8v, s8, AF.Exp, bias=mx1[:, 0:1], scale=1.0, accum_out=sume),
                waits=[h_m, (exs[tt - 2].get("h_sm_last") if tt >= 2 else None)])
            ex["h_s8_act"] = h_e
            h_rc = b.op("dve", lambda e: e.reciprocal(rsum, sume), waits=[h_e])
            h_sm = b.op("dve", lambda e, sm8v=sm8v: e.tensor_scalar(
                out=sm8v, in0=sm8v, scalar1=rsum[:, 0:1], scalar2=None, op0=ALU.mult),
                waits=[h_rc])
            st["h_sm"] = h_sm
            st["sm8v"] = sm8v
            st["h_oi"] = h_oi
            st["slot"] = slot
            h_xt = None
            for kc in range(NKC):
                h_xt = b.dma("sync", "tr_x",
                             lambda e, kc=kc, tt=tt: e.dma_start_transpose(
                                 x1tok[:, kc * 128:(kc + 1) * 128],
                                 x1c[kc][:, tt * 128:(tt + 1) * 128]),
                             waits=[(x1_b[kc] if tt == 0 else None),
                                    (ex["h_x1tok_reader"] if kc == 0 else None)])
            st["h_xt"] = h_xt
            st["h_dot"] = [None] * 8
            st["h_dm"] = [None] * 8
            st["h_es"] = [None] * 8
            st["h_deg"] = [None] * 8

        def ex_dots_step(tt, j):
            st = exs[tt]
            slot = st["slot"]
            sm8v = st["sm8v"]
            for r in (2 * j, 2 * j + 1):
                pr = r % 2
                war = ex["h_du_war"][pr]
                st["h_deg"][r] = b.dma("pool", "g_de",
                                       lambda e, r=r, pr=pr, slot=slot: e.indirect_dma_start(
                                           out=du_b[pr], out_offset=None, in_=duemb_bf[:],
                                           in_offset=IndirectOffsetOnAxis(
                                               SMI[:, slot * 8 + r:slot * 8 + r + 1], axis=0)),
                                       waits=[(st["h_oi"] if r == 0 else None),
                                              (("d", "c_tab", 16) if (tt == 0 and r == 0)
                                               else None), war])
                h_dm = b.op("dve", lambda e, pr=pr: e.tensor_tensor(
                    out=ttrj, in0=x1tok, in1=du_b[pr][:, 0:2048], op=ALU.mult),
                    waits=[st["h_deg"][r], (st["h_dot"][r - 1] if r >= 1 else None)]
                          + ([st["h_xt"], ex["h_ttrj_act"]] if r == 0 else []))
                st["h_dm"][r] = h_dm
                st["h_dot"][r] = b.op("act", lambda e, r=r: e.activation(
                    ttrj, ttrj, AF.Copy, accum_out=ew_f[:, r:r + 1]), waits=[h_dm])
                h_sil8 = b.op("act", lambda e, r=r: e.activation(
                    sil8[:, r:r + 1], ew_f[:, r:r + 1], AF.Silu), waits=[st["h_dot"][r]])
                h_ewf = b.op("dve", lambda e, r=r, sm8v=sm8v: e.tensor_scalar(
                    out=ewf[:, r:r + 1], in0=sil8[:, r:r + 1],
                    scalar1=sm8v[:, r:r + 1], scalar2=None, op0=ALU.mult),
                    waits=[h_sil8, st["h_sm"] if r == 0 else None])
                if r == 0:
                    h_es = b.op("dve", lambda e, pr=pr: e.tensor_scalar(
                        out=es_acc, in0=du_b[pr][:, 2048:4096], scalar1=ewf[:, 0:1],
                        scalar2=None, op0=ALU.mult),
                        waits=[h_ewf, ex["h_es_reader"], ex["h_es_tr"]])
                else:
                    h_es = b.op("dve", lambda e, r=r, pr=pr: e.scalar_tensor_tensor(
                        out=es_acc, in0=du_b[pr][:, 2048:4096], scalar=ewf[:, r:r + 1],
                        in1=es_acc, op0=ALU.mult, op1=ALU.add),
                        waits=[h_ewf, st["h_es"][r - 1]])
                st["h_es"][r] = h_es
                ex["h_du_war"][pr] = h_es
            if j == 3:
                st["h_sm_last"] = st["h_es"][7]
                ex["h_ttrj_act"] = st["h_dot"][7]
                ex["h_smi_reader"][slot] = st["h_deg"][7]
                ex["h_x1tok_reader"] = st["h_dm"][7]
                ex["h_es_reader"] = st["h_es"][7]

        def ex_es_add(tt):
            st = exs[tt]
            h_es = st["h_es"][7]
            h_es_tr = None
            for kc in range(NKC):
                pt_par = kc % 2
                htr = b.dma("sync", "tr_e",
                            lambda e, kc=kc, pt_par=pt_par: e.dma_start_transpose(
                                ESTR[:, pt_par * 128:(pt_par + 1) * 128],
                                es_acc[:, kc * 128:(kc + 1) * 128]),
                            waits=[h_es, ex["h_estr_war"][pt_par]])
                hadd = b.op("dve", lambda e, kc=kc, tt=tt, pt_par=pt_par: e.tensor_tensor(
                    out=r1c[kc][:, tt * 128:(tt + 1) * 128],
                    in0=r1c[kc][:, tt * 128:(tt + 1) * 128],
                    in1=ESTR[:, pt_par * 128:(pt_par + 1) * 128], op=ALU.add),
                    waits=[htr, h_r2[kc]])
                h_r2[kc] = hadd
                ex["h_estr_war"][pt_par] = hadd
                h_es_tr = htr
            ex["h_es_tr"] = h_es_tr

        do_ex = lvl >= 4
        for ig in range(NIG):
            tt_d = ig - 2
            for j in range(IGS):
                isl = ig * IGS + j
                par = isl % 2
                if isl >= 2:
                    h_gsl[isl] = b.dma("pool", "w_a" if par == 0 else "w_b",
                                       lambda e, isl=isl, par=par: e.dma_start(
                                           out=wgu_sl[par], in_=gu_view[isl]),
                                       waits=[pe_up_done[isl - 2]])
                    h_usl[isl] = h_gsl[isl]
                war_gate = h_sil[isl - 1] if isl >= 1 else x1_f[NKC - 1]
                war_up = h_aev[isl - 1] if isl >= 1 else (
                    x1_f[NKC - 1] if ig == 0 else h_devac_pair[0])
                hp = None
                for kc in range(NKC):
                    for fo in (0, 512):
                        fin = kc == NKC - 1 and fo == 512
                        hp = b.op("pe", lambda e, isl=isl, kc=kc, par=par, fo=fo: e.matmul(
                            pA[:, fo:fo + 512], wg_sl[par][:, kc * 128:(kc + 1) * 128],
                            x1c[kc][:, fo:fo + 512],
                            start=(kc == 0), stop=(kc == NKC - 1)),
                            waits=([h_gsl[isl], war_gate,
                                    (x1_b[kc] if isl == 0 else None)]
                                   if (kc == 0 and fo == 0) else []), inc=fin) or hp
                pe_gate_done[isl] = hp
                h_sil[isl] = b.op("act", lambda e: e.activation(gs_bf, pA[:, 0:1024], AF.Silu),
                                  waits=[hp, (h_aev[isl - 1] if isl >= 1 else None)])
                hp = None
                for kc in range(NKC):
                    for fo in (0, 512):
                        fin = kc == NKC - 1 and fo == 512
                        hp = b.op("pe", lambda e, isl=isl, kc=kc, par=par, fo=fo: e.matmul(
                            pB[:, fo:fo + 512], wu_sl[par][:, kc * 128:(kc + 1) * 128],
                            x1c[kc][:, fo:fo + 512],
                            start=(kc == 0), stop=(kc == NKC - 1)),
                            waits=([h_usl[isl], war_up] if (kc == 0 and fo == 0) else []),
                            inc=fin) or hp
                pe_up_done[isl] = hp
                h_aev[isl] = b.op("dve", lambda e, j=j: e.tensor_tensor(
                    out=a_grp[:, j * 1024:(j + 1) * 1024], in0=pB[:, 0:1024], in1=gs_bf,
                    op=ALU.mult), waits=[hp, h_sil[isl]])
                if do_ex and 0 <= tt_d < NTT:
                    ex_dots_step(tt_d, j)
            if do_ex and 0 <= tt_d < NTT:
                ex_es_add(tt_d)
            evs = []
            for mtg in range(8):
                for half_mt in range(2):
                    mt = mtg * 2 + half_mt
                    pp = pB if half_mt == 0 else pC
                    hp = None
                    for kc in range(IGS):
                        w0 = []
                        if kc == 0:
                            w0 = [h_dsl[ig] if mt == 0 else None,
                                  (evs[mt - 2][1] if mt >= 2 else None),
                                  (h_aev[ig * IGS + IGS - 1] if mt < 2 else None)]
                            if mt < 2 and ig == 0:
                                w0 += [x1_f[NKC - 1], h_rt[15]]
                            if mt < 2 and ig >= 1:
                                w0 += [h_devac_pair[half_mt]]
                        if mt == 0:
                            w0 += [h_aev[ig * IGS + kc]]
                        for fo in (0, 512):
                            fin = kc == IGS - 1 and fo == 512
                            hp = b.op("pe", lambda e, mt=mt, kc=kc, pp=pp, fo=fo: e.matmul(
                                pp[:, fo:fo + 512],
                                BUF3[:, kc * 2048 + mt * 128:kc * 2048 + (mt + 1) * 128],
                                a_grp[:, kc * 1024 + fo:kc * 1024 + fo + 512],
                                start=(kc == 0), stop=(kc == IGS - 1)),
                                waits=(w0 if fo == 0 else []), inc=fin) or hp
                    hev = b.op("dve", lambda e, mt=mt, pp=pp: e.tensor_tensor(
                        out=r1c[mt], in0=r1c[mt], in1=pp[:, 0:1024], op=ALU.add),
                        waits=[hp, h_r2[mt]])
                    h_r2[mt] = hev
                    evs.append((hp, hev))
            pe_down_done[ig] = evs[-1][0]
            h_devac[ig] = evs[-1][1]
            h_devac_pair = [evs[-2][1], evs[-1][1]]
            if ig + 1 < NIG:
                h_dsl[ig + 1] = b.dma("pool", "w_d",
                                      lambda e, ig=ig: e.dma_start(out=BUF3[:, :],
                                                                   in_=down_view[ig + 1]),
                                      waits=[pe_down_done[ig]])
            tt_p = ig - 1
            if do_ex and 0 <= tt_p < NTT:
                ex_part1(tt_p)

        r2_ready = [h_r2[mt] for mt in range(NKC)]
        ln2_f, _ = layernorm(r1c, None, r2_ready,
                             pre_bcast_waits=[h_aev[NISL - 1], h_devac[NIG - 1]],
                             first_cast_extra=last(*[h_r2[m] for m in range(NKC)]))
        for kc in range(NKC):
            b.dma("sync", "st_out",
                  lambda e, kc=kc: e.dma_start(out=out_view[kc], in_=r1c[kc]),
                  waits=[ln2_f[kc]])
        b.wait("pe", [("d", "st_out", b.dmacnt["st_out"])])
        _emit_streams(block, b)

    return nc


def _emit_streams(block, b):
    @block.tensor
    def _(eo):
        for f in b.q["pe"]:
            f(eo)

    @block.scalar
    def _(eo):
        for f in b.q["act"]:
            f(eo)

    @block.vector
    def _(eo):
        for f in b.q["dve"]:
            f(eo)

    @block.gpsimd
    def _(eo):
        for f in b.q["pool"]:
            f(eo)

    @block.sync
    def _(eo):
        for f in b.q["sync"]:
            f(eo)



def _pretile_lhsT(w):
    K, M = w.shape
    return np.ascontiguousarray(
        w.reshape(K // 128, 128, M // 128, 128).transpose(2, 1, 0, 3).reshape(M, K))


def _prep_inputs(inputs):
    f32 = np.float32
    g = {k: np.ascontiguousarray(np.asarray(v, f32)) for k, v in inputs.items()}
    win, wout = g["L0_attn_proj_in_w"], g["L0_attn_proj_out_w"]
    pre = {}
    pre["winT"] = _pretile_lhsT(np.ascontiguousarray(win.T))
    pre["woutT"] = _pretile_lhsT(np.ascontiguousarray(wout.T))
    pre["bwT"] = np.ascontiguousarray(
        g["L0_attn_B_w"].T.reshape(16, 128, 16).transpose(1, 0, 2).reshape(128, 256))
    pre["cwT"] = np.ascontiguousarray(g["L0_attn_C_w"].T)
    pre["a_t"] = np.ascontiguousarray(
        g["L0_attn_A"].reshape(16, 128, 16).transpose(1, 0, 2).reshape(128, 256))
    pre["dp_t"] = np.ascontiguousarray(g["L0_attn_D_param"].reshape(16, 128).T)
    pre["bin_t"] = np.ascontiguousarray(g["L0_attn_proj_in_b"].reshape(16, 128).T)
    pre["bout_t"] = np.ascontiguousarray(g["L0_attn_proj_out_b"].reshape(16, 128).T)
    pre["wq_t"] = np.ascontiguousarray(
        g["L0_ffn_W_q"].reshape(16, 128, 128).transpose(1, 0, 2).reshape(128, 2048))
    gate_t = g["L0_ffn_gate_w"].reshape(16, 128, 64, 128).transpose(2, 1, 0, 3).reshape(64, 128, 2048)
    up_t = g["L0_ffn_up_w"].reshape(16, 128, 64, 128).transpose(2, 1, 0, 3).reshape(64, 128, 2048)
    pre["gu_t"] = np.ascontiguousarray(
        np.concatenate([gate_t, up_t], axis=2).reshape(8192, 4096))
    pre["down_t"] = np.ascontiguousarray(
        g["L0_ffn_down_w"].reshape(NIG, IGS, 128, 16, 128).transpose(0, 2, 1, 3, 4)
        .reshape(NIG * 128, IGS * 2048))
    pre["duemb"] = np.ascontiguousarray(
        np.concatenate([g["L0_ffn_down_embed"], g["L0_ffn_up_embed"]], axis=1))
    return g, pre


def kernel(**inputs):
    import ml_dtypes
    bf = ml_dtypes.bfloat16
    f32 = np.float32
    g, pre = _prep_inputs(inputs)
    x = g["L0_attn_x"]
    keys = g["L0_ffn_keys"]

    c_ident = np.eye(128, dtype=f32).astype(bf)
    c_iotam = np.tile(np.arange(64, dtype=f32) - BIGC, (128, 1)).astype(f32)
    kp = np.zeros((NLAG, S), f32) + np.arange(NLAG, dtype=f32)[:, None]
    c_kpat = np.ascontiguousarray(np.tile(kp.reshape(1, -1), (128, 1)))
    c_ones_col = np.ones((128, 1), f32).astype(bf)
    c_ones_f32 = np.ones((1, 128), f32)
    c_scal = np.tile(np.array([[1.0, 1e-5]], f32), (128, 1))
    io = np.arange(64, dtype=f32)
    c_ioteps = np.tile(np.concatenate([io * 2.0 ** -20, io * 2.0 ** -17])[None, :], (128, 1))

    nc = build_nc()
    in_maps = []
    for c in range(8):
        bidx, h = c // 2, c % 2
        s0 = h * 1024
        if h == 0:
            halo = np.zeros((HALO, D), f32)
        else:
            halo = x[bidx, s0 - HALO:s0]
        x_tc = np.ascontiguousarray(np.concatenate([halo, x[bidx, s0:s0 + 1024]], axis=0).T)
        side = 0 if bidx < 2 else 1
        hm = np.full((128, 1), 0.0 if h == 0 else 1.0, f32)
        sxy = np.empty((128, 16), np.int32)
        base = 1024 * c
        for tt in range(8):
            sxy[:, tt] = base + tt * 128 + np.arange(128)
            sxy[:, 8 + tt] = 8192 + base + tt * 128 + np.arange(128)
        im = dict(x_t=x_tc, mykeys=np.ascontiguousarray(keys[side]),
                  sxy_rows=sxy, c_ident=c_ident, c_iotam=c_iotam, c_kpat=c_kpat,
                  c_ones_col=c_ones_col, c_ones_f32=c_ones_f32, c_scal=c_scal,
                  c_ioteps=np.ascontiguousarray(c_ioteps), halo_mask=hm, **pre)
        in_maps.append(im)

    res = run_bass_kernel_spmd(nc, in_maps, core_ids=list(range(8)))
    out = np.empty((4, 2048, 2048), np.float32)
    for c in range(8):
        bidx, h = c // 2, c % 2
        out[bidx, h * 1024:(h + 1) * 1024, :] = res.results[c]["out_t"].T
    return out


# version 09
# speedup vs baseline: 1.9561x; 1.0436x over previous
import numpy as np
import concourse.bass as bass
import concourse.mybir as mybir
from concourse.bass_utils import run_bass_kernel_spmd
from concourse.bass_core_rust import IndirectOffsetOnAxis

F32 = mybir.dt.float32
BF16 = mybir.dt.bfloat16
I32 = mybir.dt.int32
AF = mybir.ActivationFunctionType
ALU = mybir.AluOpType
AX = mybir.AxisListType

D = 2048
S = 16
NKC = 16
T = 1024
HALO = 64
TT = T + HALO
NLAG = 64
NG = 8
INTER = 8192
NISL = 64
IGS = 4
NIG = NISL // IGS
NTT = 8
BIGC = float(2 ** 22)

ENGS = ("pe", "act", "dve", "pool", "sync")


class Bld:

    def __init__(self, nc):
        self.nc = nc
        self.q = {e: [] for e in ENGS}
        self.tick = {e: 0 for e in ENGS}
        self.sem = {}
        self.dmacnt = {}

    def op(self, eng, fn, waits=(), inc=True):
        serial = eng != "pe"
        if inc:
            self.tick[eng] += 1
        t = self.tick[eng]
        waits = tuple(w for w in waits if w is not None)

        def emit(eo, waits=waits, t=t, eng=eng, inc=inc, serial=serial):
            for w in waits:
                if w[0] == "e":
                    if w[1] == eng and w[2] <= (t - 1 if inc else t):
                        continue
                    eo.wait_ge(self.sem[w[1]], w[2])
                else:
                    eo.wait_ge(self.sem[w[1]], 16 * w[2])
            inst = fn(eo)
            if serial and inc and t > 1:
                inst._wait_ge(self.sem[eng], t - 1)
            if inc:
                inst.then_inc(self.sem[eng], 1)

        self.q[eng].append(emit)
        return ("e", eng, t) if inc else None

    def dma(self, qeng, stream, fn, waits=()):
        self.dmacnt[stream] = self.dmacnt.get(stream, 0) + 1
        c = self.dmacnt[stream]
        waits = tuple(w for w in waits if w is not None) + (
            (("d", stream, c - 1),) if c >= 2 else ())

        def emit(eo, waits=waits, c=c):
            for w in waits:
                if w[0] == "e":
                    eo.wait_ge(self.sem[w[1]], w[2])
                else:
                    eo.wait_ge(self.sem[w[1]], 16 * w[2])
            fn(eo).then_inc(self.sem[stream], 16)

        self.q[qeng].append(emit)
        return ("d", stream, c)

    def wait(self, eng, waits):
        waits = tuple(w for w in waits if w is not None)

        def emit(eo, waits=waits):
            for w in waits:
                if w[0] == "e":
                    eo.wait_ge(self.sem[w[1]], w[2])
                else:
                    eo.wait_ge(self.sem[w[1]], 16 * w[2])

        self.q[eng].append(emit)


def last(*hs):
    hs = [h for h in hs if h is not None]
    assert all(h[0] == hs[0][0] and h[1] == hs[0][1] for h in hs)
    return max(hs, key=lambda h: h[2])


STAGE = "full"


def build_nc(debug_taps=(), stage=None):
    stage = stage or STAGE
    nc = bass.Bass("TRN2", target_bir_lowering=False)
    b = Bld(nc)

    def din(name, shape, dt=F32):
        return nc.dram_tensor(name, shape, dt, kind="ExternalInput").ap()

    x_t = din("x_t", [D, TT])
    winT = din("winT", [D, D])
    woutT = din("woutT", [D, D])
    bwT = din("bwT", [128, 256])
    cwT = din("cwT", [S, D])
    a_t = din("a_t", [128, 256])
    dp_t = din("dp_t", [128, NKC])
    bin_t = din("bin_t", [128, NKC])
    bout_t = din("bout_t", [128, NKC])
    wq_t = din("wq_t", [128, D])
    mykeys = din("mykeys", [64, 64])
    gu_t = din("gu_t", [INTER, 2 * D])
    down_t = din("down_t", [NIG * 128, IGS * 2048])
    duemb = din("duemb", [4096, 2 * D])
    sxy_rows = din("sxy_rows", [128, 16], I32)
    c_ident = din("c_ident", [128, 128], BF16)
    c_iotam = din("c_iotam", [128, 64])
    c_kpat = din("c_kpat", [128, NLAG * S])
    c_ones_col = din("c_ones_col", [128, 1], BF16)
    c_scal = din("c_scal", [128, 2])
    c_ioteps = din("c_ioteps", [128, 128])
    halo_mask = din("halo_mask", [128, 1])
    c_ones_f32 = din("c_ones_f32", [1, 128])

    out_t = nc.dram_tensor("out_t", [D, T], F32, kind="ExternalOutput").ap()

    cc_in = nc.dram_tensor("cc_in", [2048, 16], F32, kind="Internal").ap()
    cc_out = nc.dram_tensor("cc_out", [8 * 2048, 16], F32, kind="Internal").ap()
    duemb_bf = nc.dram_tensor("duemb_bf", [4096, 2 * D], BF16, kind="Internal").ap()
    dn_bf = nc.dram_tensor("dn_bf", [NIG * 128, IGS * 2048], BF16, kind="Internal").ap()

    import contextlib
    es = contextlib.ExitStack()
    with es:
        E = es.enter_context

        BUF1 = E(nc.sbuf_tensor("BUF1", [128, NKC * TT], BF16))
        BUF2 = E(nc.sbuf_tensor("BUF2", [128, NKC * TT], BF16))
        BUF3 = E(nc.sbuf_tensor("BUF3", [128, 8192], BF16))
        R1 = E(nc.sbuf_tensor("R1", [128, NKC * T], F32))
        X1 = E(nc.sbuf_tensor("X1", [128, NKC * T], BF16))
        MISC = E(nc.sbuf_tensor("MISC", [128, 2944], F32))
        BNDL = E(nc.sbuf_tensor("BNDL", [128, 4096], BF16))
        RTG = E(nc.sbuf_tensor("RTG", [128, 1024], F32))
        GXY = E(nc.sbuf_tensor("GXY", [128, 256], F32))
        CONS = E(nc.sbuf_tensor("CONS", [128, 384], F32))
        CONSB = E(nc.sbuf_tensor("CONSB", [128, 384], BF16))
        IDT = E(nc.sbuf_tensor("IDT", [128, 128], BF16))
        PACK = E(nc.sbuf_tensor("PACK", [128, 16], F32))
        SXI = E(nc.sbuf_tensor("SXI", [128, 16], I32))
        SMI = E(nc.sbuf_tensor("SMI", [128, 32], I32))
        ONESF = E(nc.sbuf_tensor("ONESF", [128, 128], F32))

        pA = E(nc.psum_tensor("pA", [128, 1024], F32))
        pB = E(nc.psum_tensor("pB", [128, 1024], F32))
        pHa = E(nc.psum_tensor("pHa", [128, 512], F32))
        pHb = E(nc.psum_tensor("pHb", [128, 512], F32))
        pC = E(nc.psum_tensor("pC", [128, 1024], F32))

        dma_streams = ["c_const", "c_cb", "c_x", "c_tab", "w_a", "w_b", "w_d",
                       "g_de", "g_ue", "x_pack", "g_cc", "st_out", "m_dma", "tr_x", "tr_e"]
        for e_ in ENGS:
            b.sem[e_] = E(nc.semaphore("s_" + e_))
        for s_ in dma_streams:
            b.sem[s_] = E(nc.semaphore("s_" + s_))

        block = E(nc.Block())

        x_bf = [BUF1[:, i * TT:(i + 1) * TT] for i in range(NKC)]
        xp_bf = [BUF2[:, i * TT:(i + 1) * TT] for i in range(NKC)]
        y_bf = [xp_bf[i][:, HALO:TT] for i in range(NKC)]
        r1c = [R1[:, i * T:(i + 1) * T] for i in range(NKC)]
        x1c = [X1[:, i * T:(i + 1) * T] for i in range(NKC)]
        wslab = [X1[:, 0:2048], X1[:, 2048:4096]]
        bndl = [BNDL[:, 0:2048], BNDL[:, 2048:4096]]
        wqs = BNDL[:, 0:2048]
        qbf = BNDL[:, 2048:3072]
        CWT = BNDL[0:16, 0:2048]
        GT = BNDL[0:16, 2048:3072]
        ESTR = BNDL[:, 3072:3328]
        a_grp = BUF2[:, 0:IGS * T]
        wgu_sl = [BUF2[:, 4096 + i * 4096:4096 + (i + 1) * 4096] for i in range(2)]
        wg_sl = [wgu_sl[i][:, 0:2048] for i in range(2)]
        wu_sl = [wgu_sl[i][:, 2048:4096] for i in range(2)]
        du_b = [BUF1[:, 0:4096], BUF1[:, 4096:8192]]
        x1tok = BUF1[:, 8192:10240]
        es_acc = BUF1[:, 10240:12288]
        ttrj = BUF1[:, 12288:14336]
        gs_bf = BUF1[:, 14336:15360]
        ln_scr = [BUF1[:, 15360:16384], BUF1[:, 16384:17408]]
        wgb = [BUF3[:, 0:TT], BUF3[:, TT:2 * TT]]
        dlt_rep = MISC[:, 0:TT]
        la_v = MISC[:, 1088:1344]
        sg_v = MISC[:, 1344:1600]
        kbase = MISC[:, 1600:1728]
        kprod = MISC[:, 1728:1856]
        t1_sp = MISC[0:16, 1856:2944]
        eqbuf = MISC[:, 0:512]
        cbuf = MISC[:, 512:1024]
        ew_f = MISC[:, 1024:1032]
        s8 = MISC[:, 1032:1040]
        sm8 = MISC[:, 1040:1048]
        sil8 = MISC[:, 1048:1056]
        ewf = MISC[:, 1056:1064]
        cidxm = MISC[:, 1064:1072]
        ix64 = MISC[:, 1072:1080]
        offs_f = MISC[:, 1080:1088]
        mx1 = MISC[:, 1088:1089]
        sume = MISC[:, 1089:1090]
        rsum = MISC[:, 1090:1091]
        st_mean = MISC[0:1, 1092:2116]
        eps1_sb = MISC[:, 2116:2180]
        eps2_sb = MISC[:, 2180:2244]
        sm8s = MISC[:, 2244:2260]
        st_rstd = RTG[64:65, 0:1024]
        st_t1 = RTG[0:1, 0:1024]
        a_sb = CONS[:, 0:256]
        hmask_sb = CONS[:, 370:371]
        dp_sb = CONS[:, 256:272]
        bin_sb = CONS[:, 272:288]
        bout_sb = CONS[:, 288:304]
        iotam_sb = CONS[:, 304:368]
        one_col = CONS[:, 368:369]
        eps_col = CONS[:, 369:370]
        ident_sb = IDT[:, :]
        bw_sb = CONSB[:, 0:256]
        keys_sb = CONSB[0:64, 256:320]
        ones_col_sb = CONSB[:, 320:321]

        tap_out = {}
        for name, shape, dt_ in debug_taps:
            tap_out[name] = nc.dram_tensor("tap_" + name, list(shape), dt_, kind="ExternalOutput").ap()

        def emit_tap(name, sb_ap, waits):
            if name in tap_out:
                b.dma("sync", "st_out",
                      lambda e, n=name, s=sb_ap: e.dma_start(out=tap_out[n][:], in_=s),
                      waits=waits)

        for dst, src in [(a_sb, a_t[:]), (dp_sb, dp_t[:]), (bin_sb, bin_t[:]),
                         (bout_sb, bout_t[:]), (iotam_sb, c_iotam[:]),
                         (ONESF[0:1, :], c_ones_f32[:]), (ONESF[64:65, :], c_ones_f32[:]),
                         (CONS[:, 368:370], c_scal[:]), (CONS[:, 370:371], halo_mask[:])]:
            b.dma("sync", "c_const", lambda e, d=dst, s=src: e.dma_start(out=d, in_=s))
        b.dma("sync", "c_const", lambda e: e.dma_start(out=SXI[:], in_=sxy_rows[:]))
        h_const = ("d", "c_const", b.dmacnt["c_const"])
        for dst, src in [(ident_sb, c_ident[:]), (bw_sb, bwT[:]), (keys_sb, mykeys[:]),
                         (CONSB[64:128, 256:320], mykeys[:]),
                         (ones_col_sb, c_ones_col[:])]:
            b.dma("pool", "c_cb", lambda e, d=dst, s=src: e.dma_start(out=d, in_=s))
        h_cb = ("d", "c_cb", b.dmacnt["c_cb"])

        x_view = x_t.rearrange("(kc p) t -> p kc t", p=128)
        buf1_3d = BUF1[:].rearrange("p (kc t) -> p kc t", t=TT)
        for i in range(4):
            b.dma("pool", "c_x",
                  lambda e, i=i: e.dma_start(out=buf1_3d[:, i * 4:(i + 1) * 4, :],
                                             in_=x_view[:, i * 4:(i + 1) * 4, :]))
        h_x = ("d", "c_x", 4)

        h_sg = b.op("act", lambda e: e.activation(sg_v, a_sb, AF.Sigmoid), waits=[h_const, h_cb])
        h_la = b.op("act", lambda e: e.activation(la_v, sg_v, AF.Ln), waits=[h_sg])

        bnst = {"last_hm": None, "last_he": None}

        def build_bundle(g, par, war):
            hk = b.dma("sync", "m_dma",
                       lambda e, g=g: e.dma_start(out=kbase, in_=c_kpat[:, g * 128:(g + 1) * 128]),
                       waits=[war, bnst["last_hm"]])
            h_prev = bnst["last_he"]
            hm = None
            for kc in range(NKC):
                la_rep = la_v[:, kc * S:(kc + 1) * S].unsqueeze(1).to_broadcast([128, 8, S])
                hm = b.op("dve", lambda e, la_rep=la_rep: e.tensor_tensor(
                    out=kprod.rearrange("p (k s) -> p k s", s=S),
                    in0=kbase.rearrange("p (k s) -> p k s", s=S),
                    in1=la_rep, op=ALU.mult), waits=[hk, h_la, h_prev, h_const, h_cb])
                h_prev = b.op("act", lambda e, g=g, kc=kc, par=par: e.activation(
                    bndl[par][:, kc * 128:(kc + 1) * 128], kprod, AF.Exp), waits=[hm, war])
            bnst["last_hm"] = hm
            bnst["last_he"] = h_prev
            return h_prev

        win_view = winT.rearrange("(mt p) k -> mt p k", p=128)
        wout_view = woutT.rearrange("(mt p) k -> mt p k", p=128)
        gu_view = gu_t.rearrange("(mt p) k -> mt p k", p=128)
        down_view = down_t.rearrange("(ig p) k -> ig p k", p=128)
        dn_view = dn_bf.rearrange("(ig p) k -> ig p k", p=128)
        tab_i = [0]

        def emit_tab_chunk():
            ch = tab_i[0]
            if ch >= 24:
                return
            tab_i[0] += 1
            if ch < 16:
                b.dma("pool", "c_tab",
                      lambda e, ch=ch: e.dma_start(
                          out=duemb_bf[ch * 256:(ch + 1) * 256, :],
                          in_=duemb[ch * 256:(ch + 1) * 256, :]))
            else:
                c2 = ch - 16
                b.dma("pool", "c_tab",
                      lambda e, c2=c2: e.dma_start(
                          out=dn_bf[c2 * 256:(c2 + 1) * 256, :],
                          in_=down_t[c2 * 256:(c2 + 1) * 256, :]))

        evac_xp = [None] * NKC
        pe_done_win = [None] * NKC
        for mt in range(NKC):
            par = mt % 2
            war = pe_done_win[mt - 2] if mt >= 2 else None
            h_wd = b.dma("pool", "w_a" if par == 0 else "w_b",
                         lambda e, mt=mt, par=par: e.dma_start(out=wslab[par], in_=win_view[mt]),
                         waits=[war])
            ppsum = pA if par == 0 else pB
            war_ps = evac_xp[mt - 2] if mt >= 2 else None
            phs = (pHa if par == 0 else pHb)[:, 0:64]
            hp = None
            for kc in range(NKC):
                b.op("pe", lambda e, kc=kc, par=par, phs=phs: e.matmul(
                    phs, wslab[par][:, kc * 128:(kc + 1) * 128],
                    x_bf[kc][:, 0:64], start=(kc == 0), stop=(kc == NKC - 1)),
                    waits=([h_wd, h_x, war_ps] + ([h_const, h_cb] if mt == 0 else [])
                           if kc == 0 else []), inc=False)
                for fo in (0, 512):
                    fin = kc == NKC - 1 and fo == 512
                    hp = b.op("pe", lambda e, kc=kc, par=par, ppsum=ppsum, fo=fo: e.matmul(
                        ppsum[:, fo:fo + 512], wslab[par][:, kc * 128:(kc + 1) * 128],
                        x_bf[kc][:, 64 + fo:64 + fo + 512],
                        start=(kc == 0), stop=(kc == NKC - 1)), inc=fin) or hp
            pe_done_win[mt] = hp
            b.op("dve", lambda e, mt=mt, phs=phs: e.tensor_scalar(
                out=xp_bf[mt][:, 0:64], in0=phs, scalar1=bin_sb[:, mt:mt + 1],
                scalar2=hmask_sb, op0=ALU.add, op1=ALU.mult), waits=[hp, h_const])
            evac_xp[mt] = b.op("dve", lambda e, mt=mt, ppsum=ppsum: e.tensor_scalar(
                out=xp_bf[mt][:, 64:TT], in0=ppsum[:, 0:1024], scalar1=bin_sb[:, mt:mt + 1],
                scalar2=None, op0=ALU.add), waits=[hp])

        for _ in range(12):
            emit_tab_chunk()
        h_b = [None, None]
        h_b[0] = build_bundle(0, 0, None)
        h_b[1] = build_bundle(1, 1, None)

        h_dlt_mm = None
        for kc in range(NKC):
            b.op("pe", lambda e, kc=kc: e.matmul(
                pHa[0:16, 0:64], bw_sb[:, kc * S:(kc + 1) * S], xp_bf[kc][:, 0:64],
                start=(kc == 0), stop=(kc == NKC - 1)),
                waits=([evac_xp[14], evac_xp[15], h_const] if kc == 0 else [evac_xp[kc]]),
                inc=False)
            for fo in (0, 512):
                fin = kc == NKC - 1 and fo == 512
                h_dlt_mm = b.op("pe", lambda e, kc=kc, fo=fo: e.matmul(
                    pA[0:16, fo:fo + 512], bw_sb[:, kc * S:(kc + 1) * S],
                    xp_bf[kc][:, 64 + fo:64 + fo + 512],
                    start=(kc == 0), stop=(kc == NKC - 1)), inc=fin) or h_dlt_mm
        h1 = b.op("act", lambda e: e.activation(t1_sp[:, 0:64], pHa[0:16, 0:64], AF.Abs),
                  waits=[h_dlt_mm])
        h1 = b.op("act", lambda e: e.activation(t1_sp[:, 64:TT], pA[0:16, 0:1024], AF.Abs),
                  waits=[h1])
        h1 = b.op("act", lambda e: e.activation(t1_sp, t1_sp, AF.Exp, scale=-1.0), waits=[h1])
        h1 = b.op("act", lambda e: e.activation(t1_sp, t1_sp, AF.Ln, bias=one_col[0:16, :]),
                  waits=[h1, h_const])
        h2 = b.op("dve", lambda e: e.tensor_scalar(out=MISC[0:16, 0:64], in0=pHa[0:16, 0:64],
                                                   scalar1=0.0, scalar2=None, op0=ALU.max),
                  waits=[h_dlt_mm])
        h2 = b.op("dve", lambda e: e.tensor_scalar(out=MISC[0:16, 64:TT], in0=pA[0:16, 0:1024],
                                                   scalar1=0.0, scalar2=None, op0=ALU.max),
                  waits=[h2])
        h_dlt = b.op("dve", lambda e: e.tensor_tensor(out=MISC[0:16, 0:TT], in0=MISC[0:16, 0:TT],
                                                      in1=t1_sp, op=ALU.add), waits=[h1, h2])
        for k in range(1, 8):
            b.dma("sync", "m_dma",
                  lambda e, k=k: e.dma_start(out=MISC[16 * k:16 * (k + 1), 0:TT],
                                             in_=MISC[0:16, 0:TT]), waits=[h_dlt])
        h_rep = ("d", "m_dma", b.dmacnt["m_dma"])

        h_ioteps = b.dma("sync", "m_dma",
                         lambda e: e.dma_start(out=MISC[:, 2116:2244], in_=c_ioteps[:]),
                         waits=[h_dlt])
        h_wg = [None, None]
        h_sel_done = [None, None]
        h_p_done = [None, None]
        for g in range(NG):
            par = g % 2
            ppsum = pA if par == 0 else pB
            phs = (pHa if par == 0 else pHb)[:, 0:64]
            hp = None
            for kc in range(NKC):
                w0 = []
                if kc == 0:
                    w0 = [h_b[par], h_wg[par], (h2 if g == 0 and par == 0 else None),
                          (h1 if g == 0 else None)]
                b.op("pe", lambda e, kc=kc, par=par, phs=phs: e.matmul(
                    phs, bndl[par][:, kc * 128:(kc + 1) * 128],
                    xp_bf[kc][:, 0:64], start=(kc == 0), stop=(kc == NKC - 1)), waits=w0,
                    inc=False)
                for fo in (0, 512):
                    fin = kc == NKC - 1 and fo == 512
                    hp = b.op("pe", lambda e, kc=kc, par=par, ppsum=ppsum, fo=fo: e.matmul(
                        ppsum[:, fo:fo + 512], bndl[par][:, kc * 128:(kc + 1) * 128],
                        xp_bf[kc][:, 64 + fo:64 + fo + 512],
                        start=(kc == 0), stop=(kc == NKC - 1)), inc=fin) or hp
            h_p_done[par] = hp
            b.op("dve", lambda e, par=par, phs=phs: e.tensor_tensor(
                out=wgb[par][:, 0:64], in0=phs, in1=dlt_rep[:, 0:64], op=ALU.mult),
                waits=[hp, h_rep, h_sel_done[par]])
            h_wg[par] = b.op("dve", lambda e, par=par, ppsum=ppsum: e.tensor_tensor(
                out=wgb[par][:, 64:TT], in0=ppsum[:, 0:1024], in1=dlt_rep[:, 64:TT],
                op=ALU.mult))
            if g + 2 < NG:
                h_b[par] = build_bundle(g + 2, par, h_p_done[par])
            hsel = None
            for k in range(8):
                sig = 8 * g + k
                for fo in (0, 512):
                    fin = k == 7 and fo == 512
                    hsel = b.op("pe", lambda e, k=k, par=par, sig=sig, fo=fo,
                                st0=(g == 0 and k == 0), sp0=(g == NG - 1 and k == 7): e.matmul(
                        pC[0:16, fo:fo + 512], ident_sb[:, k * 16:(k + 1) * 16],
                        wgb[par][:, HALO - sig + fo:HALO - sig + fo + 512],
                        start=st0, stop=sp0), waits=([h_wg[par]] if (k == 0 and fo == 0) else []),
                        inc=fin) or hsel
            h_sel_done[par] = hsel
        h_gt = b.op("dve", lambda e: e.tensor_copy(out=GT[:, :], in_=pC[0:16, 0:1024]),
                    waits=[h_sel_done[(NG - 1) % 2]])

        h_cwt = b.dma("pool", "c_cb", lambda e: e.dma_start(out=CWT[:, :], in_=cwT[:]),
                      waits=[h_p_done[0]])
        evac_y = [None] * NKC
        for mt in range(NKC):
            par = mt % 2
            ppsum = pA if par == 0 else pB
            war = evac_y[mt - 2] if mt >= 2 else h_wg[par]
            hy = None
            for fo in (0, 512):
                hy = b.op("pe", lambda e, mt=mt, ppsum=ppsum, fo=fo: e.matmul(
                    ppsum[:, fo:fo + 512], CWT[:, mt * 128:(mt + 1) * 128],
                    GT[:, fo:fo + 512], start=True, stop=True),
                    waits=([h_gt, war, h_cwt if mt == 0 else None] if fo == 0 else []),
                    inc=(fo == 512)) or hy
            evac_y[mt] = b.op("dve", lambda e, mt=mt, ppsum=ppsum: e.scalar_tensor_tensor(
                out=y_bf[mt], in0=y_bf[mt], scalar=dp_sb[:, mt:mt + 1], in1=ppsum[:, 0:1024],
                op0=ALU.mult, op1=ALU.add), waits=[hy])

        evac_r1 = [None] * NKC
        pe_done_wout = [None] * NKC
        for mt in range(NKC):
            par = mt % 2
            ppsum = pA if par == 0 else pB
            war_slab = pe_done_wout[mt - 2] if mt >= 2 else pe_done_win[NKC - 1]
            h_wd = b.dma("pool", "w_a" if par == 0 else "w_b",
                         lambda e, mt=mt, par=par: e.dma_start(out=wslab[par], in_=wout_view[mt]),
                         waits=[war_slab])
            if mt % 2 == 1:
                emit_tab_chunk()
            war_ps = evac_r1[mt - 2] if mt >= 2 else evac_y[14 + par]
            hp = None
            for kc in range(NKC):
                for fo in (0, 512):
                    fin = kc == NKC - 1 and fo == 512
                    hp = b.op("pe", lambda e, mt=mt, kc=kc, par=par, ppsum=ppsum, fo=fo: e.matmul(
                        ppsum[:, fo:fo + 512], wslab[par][:, kc * 128:(kc + 1) * 128],
                        y_bf[kc][:, fo:fo + 512],
                        start=(kc == 0), stop=(kc == NKC - 1)),
                        waits=([h_wd, war_ps, (evac_y[kc] if mt == 0 else None)]
                               if (kc == 0 and fo == 0)
                               else ([evac_y[kc]] if (mt == 0 and fo == 0) else [])),
                        inc=fin) or hp
            pe_done_wout[mt] = hp
            hr = b.op("dve", lambda e, mt=mt, ppsum=ppsum: e.scalar_tensor_tensor(
                out=r1c[mt], in0=x_bf[mt][:, HALO:TT], scalar=2.0, in1=ppsum[:, 0:1024],
                op0=ALU.mult, op1=ALU.add), waits=[hp])
            evac_r1[mt] = b.op("dve", lambda e, mt=mt: e.tensor_scalar(
                out=r1c[mt], in0=r1c[mt], scalar1=bout_sb[:, mt:mt + 1], scalar2=None,
                op0=ALU.add), waits=[hr])

        def layernorm(rc, out_bf, wait_in, pre_bcast_waits, first_cast_extra):
            h_cast = [None] * NKC
            hmm = None
            for kc in range(NKC):
                pr = kc % 2
                h_cast[kc] = b.op("dve", lambda e, kc=kc, pr=pr: e.tensor_copy(
                    out=ln_scr[pr], in_=rc[kc]),
                    waits=[wait_in[kc], (hmm if kc >= 2 else None)]
                          + ([first_cast_extra] if kc == 0 else []))
                for fo in (0, 512):
                    hmm = b.op("pe", lambda e, kc=kc, pr=pr, fo=fo: e.matmul(
                        pC[0:1, fo:fo + 512], ones_col_sb, ln_scr[pr][:, fo:fo + 512],
                        start=(kc == 0), stop=(kc == NKC - 1)),
                        waits=([h_cast[kc]] + (pre_bcast_waits if kc == 0 else [])
                               if fo == 0 else []), inc=(fo == 512)) or hmm
            h_mean = b.op("dve", lambda e: e.tensor_scalar(
                out=st_mean, in0=pC[0:1, 0:1024], scalar1=1.0 / D, scalar2=None, op0=ALU.mult),
                waits=[hmm])
            h_bm = None
            for lo in (0, 512):
                h_bm = b.op("pe", lambda e, lo=lo: e.matmul(
                    pA[:, lo:lo + 512], ONESF[0:1, :], st_mean[:, lo:lo + 512],
                    start=True, stop=True),
                    waits=[h_mean] + (pre_bcast_waits if lo == 0 else []),
                    inc=(lo == 512)) or h_bm
            hmm2 = None
            h_sub = [None] * NKC
            h_sq = [None] * NKC
            for kc in range(NKC):
                pr = kc % 2
                h_sub[kc] = b.op("dve", lambda e, kc=kc: e.tensor_tensor(
                    out=rc[kc], in0=rc[kc], in1=pA[:, 0:1024], op=ALU.subtract),
                    waits=[h_bm, h_cast[kc]])
                h_sq[kc] = b.op("act", lambda e, kc=kc, pr=pr: e.activation(
                    ln_scr[pr], rc[kc], AF.Square),
                    waits=[h_sub[kc], (hmm2 if kc >= 2 else None), (hmm if kc < 2 else None)])
                for fo in (0, 512):
                    hmm2 = b.op("pe", lambda e, kc=kc, pr=pr, fo=fo: e.matmul(
                        pC[0:1, fo:fo + 512], ones_col_sb, ln_scr[pr][:, fo:fo + 512],
                        start=(kc == 0), stop=(kc == NKC - 1)),
                        waits=([h_sq[kc], (h_mean if kc == 0 else None)] if fo == 0 else []),
                        inc=(fo == 512)) or hmm2
            hv = b.op("dve", lambda e: e.tensor_scalar(
                out=st_t1, in0=pC[0:1, 0:1024], scalar1=1.0 / D, scalar2=None, op0=ALU.mult),
                waits=[hmm2])
            hv = b.op("act", lambda e: e.activation(st_t1, st_t1, AF.Sqrt,
                                                    bias=eps_col[0:1, :]), waits=[hv])
            hv = b.op("dve", lambda e: e.reciprocal(st_rstd, st_t1), waits=[hv])
            h_br = None
            for lo in (0, 512):
                h_br = b.op("pe", lambda e, lo=lo: e.matmul(
                    pB[:, lo:lo + 512], ONESF[64:65, :], st_rstd[:, lo:lo + 512],
                    start=True, stop=True),
                    waits=[hv if lo == 0 else None], inc=(lo == 512)) or h_br
            f32_h, bf_h = [], []
            for kc in range(NKC):
                hf = b.op("dve", lambda e, kc=kc: e.tensor_tensor(
                    out=rc[kc], in0=rc[kc], in1=pB[:, 0:1024], op=ALU.mult),
                    waits=[h_br, h_sq[kc]])
                f32_h.append(hf)
                if out_bf is not None:
                    bf_h.append(b.op("act", lambda e, kc=kc: e.activation(
                        out_bf[kc], rc[kc], AF.Copy), waits=[hf]))
            return f32_h, bf_h

        for _ in range(6):
            emit_tab_chunk()
        x1_f, x1_b = layernorm(r1c, x1c, evac_r1,
                               pre_bcast_waits=[evac_r1[14], evac_r1[15]],
                               first_cast_extra=evac_r1[15])

        emit_tap("xp", BUF2[:, :], [evac_y[15]])
        emit_tap("dlt", MISC[:, 0:TT], [h_rep])
        emit_tap("gt", GT[:, :], [h_gt])
        emit_tap("x1", X1[:, :], [x1_b[15]] if x1_b else [])
        emit_tap("r1", R1[:, :], [x1_f[15]])

        out_view = out_t.rearrange("(kc p) t -> kc p t", p=128)
        if stage == "attn":
            for kc in range(NKC):
                b.dma("sync", "st_out",
                      lambda e, kc=kc: e.dma_start(out=out_view[kc], in_=r1c[kc]),
                      waits=[x1_f[kc]])
            b.wait("pe", [("d", "st_out", b.dmacnt["st_out"])])
            _emit_streams(block, b)
            return nc

        lvls = {"attn": (0, 9), "swiglu": (1, 9), "route1": (2, 9), "route2": (3, 9),
                "ex_a": (4, 1), "ex_b1": (4, 1.2), "ex_b2": (4, 1.5), "ex_b": (4, 2),
                "ex_c": (4, 3), "nocc": (4, 9), "full": (5, 9),
                "ex_b1_t1": (4, 1.2), "ex_b_t1": (4, 2), "ex_b0_t1": (4, 1.1),
                "ex_bx_t1": (4, 1.15)}
        only_tt0 = stage.endswith("_t1")
        lvl, _exlvl = lvls[stage]
        do_route = lvl >= 2
        h_wq = b.dma("pool", "c_cb", lambda e: e.dma_start(out=wqs, in_=wq_t[:]),
                     waits=[h_p_done[0]])
        hq = None
        for kc in (range(NKC) if lvl >= 2 else ()):
            for fo in (0, 512):
                fin = kc == NKC - 1 and fo == 512
                hq = b.op("pe", lambda e, kc=kc, fo=fo: e.matmul(
                    pC[:, fo:fo + 512], wqs[:, kc * 128:(kc + 1) * 128],
                    x1c[kc][:, fo:fo + 512],
                    start=(kc == 0), stop=(kc == NKC - 1)),
                    waits=([h_wq, x1_b[kc]] if fo == 0 else []), inc=fin) or hq
        h_qe = b.op("dve", lambda e: e.tensor_copy(out=qbf, in_=pC[:, 0:1024]),
                    waits=[hq, h_p_done[1]])

        h_rt = [None] * 16
        h_pack = [None] * 16
        for half in (range(2) if do_route else ()):
            for pt in range(8):
                idx = half * 8 + pt
                hr = b.op("pe", lambda e, half=half, pt=pt: e.matmul(
                    pC[:, 0:64], qbf[64 * half:64 * half + 64, pt * 128:(pt + 1) * 128],
                    CONSB[64 * half:64 * half + 64, 256:320], start=True, stop=True),
                    waits=[h_qe, (h_rt[idx - 1] if idx > 0 else None)])
                h_rt[idx] = b.op("dve", lambda e, idx=idx: e.tensor_tensor(
                    out=RTG[:, idx * 64:(idx + 1) * 64], in0=pC[:, 0:64], in1=eps1_sb,
                    op=ALU.subtract), waits=[hr, h_ioteps if idx == 0 else None])
        for half in (range(2) if do_route else ()):
            for pt in range(8):
                idx = half * 8 + pt
                sl = RTG[:, idx * 64:(idx + 1) * 64]
                hk8 = b.op("dve", lambda e, sl=sl: e.max(out=PACK[:, 0:8], in_=sl),
                           waits=[h_rt[idx], (h_pack[idx - 1] if idx > 0 else None)])
                sl_rep = sl.unsqueeze(1).to_broadcast([128, 8, 64])
                k8_rep = PACK[:, 0:8].unsqueeze(2).to_broadcast([128, 8, 64])
                heq = b.op("dve", lambda e, sl_rep=sl_rep, k8_rep=k8_rep: e.tensor_tensor(
                    out=eqbuf.rearrange("p (i j) -> p i j", j=64), in0=sl_rep, in1=k8_rep,
                    op=ALU.is_equal), waits=[hk8])
                iot_rep = iotam_sb.unsqueeze(1).to_broadcast([128, 8, 64])
                hc = b.op("dve", lambda e, iot_rep=iot_rep: e.tensor_tensor(
                    out=cbuf.rearrange("p (i j) -> p i j", j=64),
                    in0=eqbuf.rearrange("p (i j) -> p i j", j=64), in1=iot_rep, op=ALU.mult),
                    waits=[heq, h_const])
                hmin = b.op("dve", lambda e: e.tensor_reduce(
                    out=PACK[:, 8:16], in_=cbuf.rearrange("p (i j) -> p i j", j=64),
                    axis=AX.X, op=ALU.min), waits=[hc])
                dst = cc_in.rearrange("(p2 two) v -> two p2 v", two=2)[half, pt * 128:(pt + 1) * 128, :]
                h_pack[idx] = b.dma("sync", "x_pack",
                                    lambda e, dst=dst: e.dma_start(out=dst, in_=PACK[:, :]),
                                    waits=[hmin])
        h_gsl = [None] * NISL
        h_usl = [None] * NISL
        for isl in range(2):
            par = isl % 2
            h_gsl[isl] = b.dma("pool", "w_a" if par == 0 else "w_b",
                               lambda e, isl=isl, par=par: e.dma_start(
                                   out=wgu_sl[par], in_=gu_view[isl]),
                               waits=[pe_done_wout[NKC - 1] if isl == 0 else None])
            h_usl[isl] = h_gsl[isl]
        h_dsl = [None] * NIG
        h_dsl[0] = b.dma("sync", "w_d",
                         lambda e: e.dma_start(out=BUF3[:, :], in_=dn_view[0]),
                         waits=[h_sel_done[0], h_sel_done[1], ("d", "c_tab", 24)])

        h_gx = [None] * NTT
        if lvl >= 5:
            h_cc = b.op("pool", lambda e: e.collective_compute(
                "AllGather", ALU.bypass,
                replica_groups=[[0, 1, 2, 3, 4, 5, 6, 7]],
                ins=[cc_in.opt()], outs=[cc_out.opt()]),
                waits=[("d", "x_pack", b.dmacnt["x_pack"])])
        elif lvl >= 3:
            h_cc = b.op("pool", lambda e: e.memset(GXY[:, :], 0.0),
                        waits=[("d", "x_pack", b.dmacnt["x_pack"])])
        if lvl >= 3:
            for tt in range(NTT):
                b.dma("pool", "g_cc", lambda e, tt=tt: e.indirect_dma_start(
                    out=GXY[:, tt * 16:tt * 16 + 16], out_offset=None, in_=cc_out[:],
                    in_offset=IndirectOffsetOnAxis(SXI[:, tt:tt + 1], axis=0)),
                    waits=[h_cc if tt == 0 else None, h_const if tt == 0 else None])
                b.dma("pool", "g_cc", lambda e, tt=tt: e.indirect_dma_start(
                    out=GXY[:, 128 + tt * 16:128 + tt * 16 + 16], out_offset=None, in_=cc_out[:],
                    in_offset=IndirectOffsetOnAxis(SXI[:, 8 + tt:8 + tt + 1], axis=0)))
                h_gx[tt] = ("d", "g_cc", b.dmacnt["g_cc"])

        emit_tap("rtg", RTG[:, :], [h_rt[15]])
        emit_tap("gxy", GXY[:, :], [h_gx[NTT - 1]])

        pe_gate_done = [None] * NISL
        pe_up_done = [None] * NISL
        h_aev = [None] * NISL
        h_sil = [None] * NISL
        pe_down_done = [None] * NIG
        h_devac = [None] * NIG
        h_r2 = {mt: x1_f[mt] for mt in range(NKC)}
        ex = {
            "h_smi_reader": [None] * 4,
            "h_s8_act": None,
            "h_x1tok_reader": None,
            "h_du_war": [None, None],
            "h_es_tr": None,
            "h_es_reader": None,
            "h_estr_war": [None, None],
            "h_ttrj_act": None,
        }
        eq2 = RTG[:, 0:512]
        c2b = RTG[:, 512:1024]
        exs = [dict() for _ in range(NTT)]

        def ex_part1(tt):
            st = exs[tt]
            slot = tt % 4
            sm_slot = tt % 2
            w0 = [h_gx[tt]]
            gxv = GXY[:, tt * 16:tt * 16 + 8]
            gxi = GXY[:, tt * 16 + 8:tt * 16 + 16]
            gyv = GXY[:, 128 + tt * 16:128 + tt * 16 + 8]
            gyi = GXY[:, 128 + tt * 16 + 8:128 + tt * 16 + 16]
            h = None
            for i in range(8):
                h = b.op("dve", lambda e, i=i, gyv=gyv, gxv=gxv: e.tensor_scalar(
                    out=eqbuf[:, i * 8:(i + 1) * 8], in0=gyv, scalar1=gxv[:, i:i + 1],
                    scalar2=None, op0=ALU.add), waits=(w0 if i == 0 else []))
            h_as = b.op("dve", lambda e: e.tensor_tensor(
                out=eqbuf[:, 0:64], in0=eqbuf[:, 0:64], in1=eps2_sb, op=ALU.subtract),
                waits=[h])
            h = b.op("dve", lambda e, gxi=gxi: e.tensor_scalar(
                out=ix64, in0=gxi, scalar1=BIGC, scalar2=64.0, op0=ALU.add, op1=ALU.mult),
                waits=w0)
            for i in range(8):
                h = b.op("dve", lambda e, i=i, gyi=gyi: e.tensor_scalar(
                    out=cbuf[:, i * 8:(i + 1) * 8], in0=gyi, scalar1=ix64[:, i:i + 1],
                    scalar2=None, op0=ALU.add), waits=([h] if i == 0 else []))
            h_ci = h
            h_s8 = b.op("dve", lambda e: e.max(out=s8, in_=eqbuf[:, 0:64]),
                        waits=[h_as, ex["h_s8_act"]])
            as_rep = eqbuf[:, 0:64].unsqueeze(1).to_broadcast([128, 8, 64])
            s8_rep = s8.unsqueeze(2).to_broadcast([128, 8, 64])
            h_eq = b.op("dve", lambda e, as_rep=as_rep, s8_rep=s8_rep: e.tensor_tensor(
                out=eq2.rearrange("p (i j) -> p i j", j=64), in0=as_rep, in1=s8_rep,
                op=ALU.is_equal), waits=[h_s8])
            ci_rep = cbuf[:, 0:64].unsqueeze(1).to_broadcast([128, 8, 64])
            h_c2 = b.op("dve", lambda e, ci_rep=ci_rep: e.tensor_tensor(
                out=c2b.rearrange("p (i j) -> p i j", j=64),
                in0=eq2.rearrange("p (i j) -> p i j", j=64), in1=ci_rep, op=ALU.mult),
                waits=[h_eq, h_ci])
            h_cm = b.op("dve", lambda e: e.tensor_reduce(
                out=cidxm, in_=c2b.rearrange("p (i j) -> p i j", j=64),
                axis=AX.X, op=ALU.min), waits=[h_c2])
            h_of = b.op("dve", lambda e: e.tensor_scalar(
                out=offs_f, in0=cidxm, scalar1=BIGC, scalar2=4095.0,
                op0=ALU.add, op1=ALU.min), waits=[h_cm])
            h_of = b.op("dve", lambda e: e.tensor_scalar(
                out=offs_f, in0=offs_f, scalar1=0.0, scalar2=None, op0=ALU.max),
                waits=[h_of])
            h_oi = b.op("dve", lambda e, slot=slot: e.tensor_copy(
                out=SMI[:, slot * 8:slot * 8 + 8], in_=offs_f),
                waits=[h_of, ex["h_smi_reader"][slot]])
            sm8v = sm8s[:, sm_slot * 8:sm_slot * 8 + 8]
            h_m = b.op("dve", lambda e: e.tensor_reduce(out=mx1, in_=s8, axis=AX.X,
                                                        op=ALU.max, negate=True), waits=[h_s8])
            h_e = b.op("act", lambda e, sm8v=sm8v: e.activation(
                sm8v, s8, AF.Exp, bias=mx1[:, 0:1], scale=1.0, accum_out=sume),
                waits=[h_m, (exs[tt - 2].get("h_sm_last") if tt >= 2 else None)])
            ex["h_s8_act"] = h_e
            h_rc = b.op("dve", lambda e: e.reciprocal(rsum, sume), waits=[h_e])
            h_sm = b.op("dve", lambda e, sm8v=sm8v: e.tensor_scalar(
                out=sm8v, in0=sm8v, scalar1=rsum[:, 0:1], scalar2=None, op0=ALU.mult),
                waits=[h_rc])
            st["h_sm"] = h_sm
            st["sm8v"] = sm8v
            st["h_oi"] = h_oi
            st["slot"] = slot
            h_xt = None
            for kc in range(NKC):
                h_xt = b.dma("sync", "tr_x",
                             lambda e, kc=kc, tt=tt: e.dma_start_transpose(
                                 x1tok[:, kc * 128:(kc + 1) * 128],
                                 x1c[kc][:, tt * 128:(tt + 1) * 128]),
                             waits=[(x1_b[kc] if tt == 0 else None),
                                    (ex["h_x1tok_reader"] if kc == 0 else None)])
            st["h_xt"] = h_xt
            st["h_dot"] = [None] * 8
            st["h_dm"] = [None] * 8
            st["h_es"] = [None] * 8
            st["h_deg"] = [None] * 8

        def ex_dots_step(tt, j):
            st = exs[tt]
            slot = st["slot"]
            sm8v = st["sm8v"]
            for r in (2 * j, 2 * j + 1):
                pr = r % 2
                war = ex["h_du_war"][pr]
                st["h_deg"][r] = b.dma("pool", "g_de",
                                       lambda e, r=r, pr=pr, slot=slot: e.indirect_dma_start(
                                           out=du_b[pr], out_offset=None, in_=duemb_bf[:],
                                           in_offset=IndirectOffsetOnAxis(
                                               SMI[:, slot * 8 + r:slot * 8 + r + 1], axis=0)),
                                       waits=[(st["h_oi"] if r == 0 else None),
                                              (("d", "c_tab", 16) if (tt == 0 and r == 0)
                                               else None), war])
                h_dm = b.op("dve", lambda e, pr=pr: e.tensor_tensor(
                    out=ttrj, in0=x1tok, in1=du_b[pr][:, 0:2048], op=ALU.mult),
                    waits=[st["h_deg"][r], (st["h_dot"][r - 1] if r >= 1 else None)]
                          + ([st["h_xt"], ex["h_ttrj_act"]] if r == 0 else []))
                st["h_dm"][r] = h_dm
                st["h_dot"][r] = b.op("act", lambda e, r=r: e.activation(
                    ttrj, ttrj, AF.Copy, accum_out=ew_f[:, r:r + 1]), waits=[h_dm])
                h_sil8 = b.op("act", lambda e, r=r: e.activation(
                    sil8[:, r:r + 1], ew_f[:, r:r + 1], AF.Silu), waits=[st["h_dot"][r]])
                h_ewf = b.op("dve", lambda e, r=r, sm8v=sm8v: e.tensor_scalar(
                    out=ewf[:, r:r + 1], in0=sil8[:, r:r + 1],
                    scalar1=sm8v[:, r:r + 1], scalar2=None, op0=ALU.mult),
                    waits=[h_sil8, st["h_sm"] if r == 0 else None])
                if r == 0:
                    h_es = b.op("dve", lambda e, pr=pr: e.tensor_scalar(
                        out=es_acc, in0=du_b[pr][:, 2048:4096], scalar1=ewf[:, 0:1],
                        scalar2=None, op0=ALU.mult),
                        waits=[h_ewf, ex["h_es_reader"], ex["h_es_tr"]])
                else:
                    h_es = b.op("dve", lambda e, r=r, pr=pr: e.scalar_tensor_tensor(
                        out=es_acc, in0=du_b[pr][:, 2048:4096], scalar=ewf[:, r:r + 1],
                        in1=es_acc, op0=ALU.mult, op1=ALU.add),
                        waits=[h_ewf, st["h_es"][r - 1]])
                st["h_es"][r] = h_es
                ex["h_du_war"][pr] = h_es
            if j == 3:
                st["h_sm_last"] = st["h_es"][7]
                ex["h_ttrj_act"] = st["h_dot"][7]
                ex["h_smi_reader"][slot] = st["h_deg"][7]
                ex["h_x1tok_reader"] = st["h_dm"][7]
                ex["h_es_reader"] = st["h_es"][7]

        def ex_es_add(tt):
            st = exs[tt]
            h_es = st["h_es"][7]
            h_es_tr = None
            for kc in range(NKC):
                pt_par = kc % 2
                htr = b.dma("sync", "tr_e",
                            lambda e, kc=kc, pt_par=pt_par: e.dma_start_transpose(
                                ESTR[:, pt_par * 128:(pt_par + 1) * 128],
                                es_acc[:, kc * 128:(kc + 1) * 128]),
                            waits=[h_es, ex["h_estr_war"][pt_par]])
                hadd = b.op("dve", lambda e, kc=kc, tt=tt, pt_par=pt_par: e.tensor_tensor(
                    out=r1c[kc][:, tt * 128:(tt + 1) * 128],
                    in0=r1c[kc][:, tt * 128:(tt + 1) * 128],
                    in1=ESTR[:, pt_par * 128:(pt_par + 1) * 128], op=ALU.add),
                    waits=[htr, h_r2[kc]])
                h_r2[kc] = hadd
                ex["h_estr_war"][pt_par] = hadd
                h_es_tr = htr
            ex["h_es_tr"] = h_es_tr

        do_ex = lvl >= 4
        for ig in range(NIG):
            tt_d = ig - 2
            for j in range(IGS):
                isl = ig * IGS + j
                par = isl % 2
                if isl >= 2:
                    h_gsl[isl] = b.dma("pool", "w_a" if par == 0 else "w_b",
                                       lambda e, isl=isl, par=par: e.dma_start(
                                           out=wgu_sl[par], in_=gu_view[isl]),
                                       waits=[pe_up_done[isl - 2]])
                    h_usl[isl] = h_gsl[isl]
                war_gate = h_sil[isl - 1] if isl >= 1 else x1_f[NKC - 1]
                if j >= 1:
                    war_up = h_aev[isl - 1]
                elif ig == 0:
                    war_up = x1_f[NKC - 1]
                else:
                    war_up = h_devac_pair[0]
                hp = None
                for kc in range(NKC):
                    for fo in (0, 512):
                        fin = kc == NKC - 1 and fo == 512
                        hp = b.op("pe", lambda e, isl=isl, kc=kc, par=par, fo=fo: e.matmul(
                            pA[:, fo:fo + 512], wg_sl[par][:, kc * 128:(kc + 1) * 128],
                            x1c[kc][:, fo:fo + 512],
                            start=(kc == 0), stop=(kc == NKC - 1)),
                            waits=([h_gsl[isl], war_gate,
                                    (x1_b[kc] if isl == 0 else None)]
                                   if (kc == 0 and fo == 0) else []), inc=fin) or hp
                pe_gate_done[isl] = hp
                h_sil[isl] = b.op("act", lambda e: e.activation(gs_bf, pA[:, 0:1024], AF.Silu),
                                  waits=[hp, (h_aev[isl - 1] if isl >= 1 else None)])
                hp = None
                for kc in range(NKC):
                    for fo in (0, 512):
                        fin = kc == NKC - 1 and fo == 512
                        hp = b.op("pe", lambda e, isl=isl, kc=kc, par=par, fo=fo: e.matmul(
                            pB[:, fo:fo + 512], wu_sl[par][:, kc * 128:(kc + 1) * 128],
                            x1c[kc][:, fo:fo + 512],
                            start=(kc == 0), stop=(kc == NKC - 1)),
                            waits=([h_usl[isl], war_up] if (kc == 0 and fo == 0) else []),
                            inc=fin) or hp
                pe_up_done[isl] = hp
                h_aev[isl] = b.op("dve", lambda e, j=j: e.tensor_tensor(
                    out=a_grp[:, j * 1024:(j + 1) * 1024], in0=pB[:, 0:1024], in1=gs_bf,
                    op=ALU.mult), waits=[hp, h_sil[isl]])
                if do_ex and 0 <= tt_d < NTT:
                    ex_dots_step(tt_d, j)
            if do_ex and 0 <= tt_d < NTT:
                ex_es_add(tt_d)
            evs = []
            for mtg in range(8):
                for half_mt in range(2):
                    mt = mtg * 2 + half_mt
                    pp = pB if half_mt == 0 else pC
                    hp = None
                    for kc in range(IGS):
                        w0 = []
                        if kc == 0:
                            w0 = [h_dsl[ig] if mt == 0 else None,
                                  (evs[mt - 2][1] if mt >= 2 else None),
                                  (h_aev[ig * IGS + IGS - 1] if mt < 2 else None)]
                            if mt < 2 and ig == 0:
                                w0 += [x1_f[NKC - 1], h_rt[15]]
                            if mt < 2 and ig >= 1:
                                w0 += [h_devac_pair[half_mt]]
                        if mt == 0:
                            w0 += [h_aev[ig * IGS + kc]]
                        for fo in (0, 512):
                            fin = kc == IGS - 1 and fo == 512
                            hp = b.op("pe", lambda e, mt=mt, kc=kc, pp=pp, fo=fo: e.matmul(
                                pp[:, fo:fo + 512],
                                BUF3[:, kc * 2048 + mt * 128:kc * 2048 + (mt + 1) * 128],
                                a_grp[:, kc * 1024 + fo:kc * 1024 + fo + 512],
                                start=(kc == 0), stop=(kc == IGS - 1)),
                                waits=(w0 if fo == 0 else []), inc=fin) or hp
                    hev = b.op("dve", lambda e, mt=mt, pp=pp: e.tensor_tensor(
                        out=r1c[mt], in0=r1c[mt], in1=pp[:, 0:1024], op=ALU.add),
                        waits=[hp, h_r2[mt]])
                    h_r2[mt] = hev
                    evs.append((hp, hev))
            pe_down_done[ig] = evs[-1][0]
            h_devac[ig] = evs[-1][1]
            h_devac_pair = [evs[-2][1], evs[-1][1]]
            tt_p = ig - 1
            if do_ex and 0 <= tt_p < NTT:
                ex_part1(tt_p)
            if ig + 1 < NIG:
                h_dsl[ig + 1] = b.dma("sync", "w_d",
                                      lambda e, ig=ig: e.dma_start(out=BUF3[:, :],
                                                                   in_=dn_view[ig + 1]),
                                      waits=[pe_down_done[ig]])

        r2_ready = [h_r2[mt] for mt in range(NKC)]
        ln2_f, _ = layernorm(r1c, None, r2_ready,
                             pre_bcast_waits=[h_aev[NISL - 1], h_devac[NIG - 1]],
                             first_cast_extra=last(*[h_r2[m] for m in range(NKC)]))
        for kc in range(NKC):
            b.dma("sync", "st_out",
                  lambda e, kc=kc: e.dma_start(out=out_view[kc], in_=r1c[kc]),
                  waits=[ln2_f[kc]])
        b.wait("pe", [("d", "st_out", b.dmacnt["st_out"])])
        _emit_streams(block, b)

    return nc


def _emit_streams(block, b):
    @block.tensor
    def _(eo):
        for f in b.q["pe"]:
            f(eo)

    @block.scalar
    def _(eo):
        for f in b.q["act"]:
            f(eo)

    @block.vector
    def _(eo):
        for f in b.q["dve"]:
            f(eo)

    @block.gpsimd
    def _(eo):
        for f in b.q["pool"]:
            f(eo)

    @block.sync
    def _(eo):
        for f in b.q["sync"]:
            f(eo)



def _pretile_lhsT(w):
    K, M = w.shape
    return np.ascontiguousarray(
        w.reshape(K // 128, 128, M // 128, 128).transpose(2, 1, 0, 3).reshape(M, K))


def _prep_inputs(inputs):
    f32 = np.float32
    g = {k: np.ascontiguousarray(np.asarray(v, f32)) for k, v in inputs.items()}
    win, wout = g["L0_attn_proj_in_w"], g["L0_attn_proj_out_w"]
    pre = {}
    pre["winT"] = _pretile_lhsT(np.ascontiguousarray(win.T))
    pre["woutT"] = _pretile_lhsT(np.ascontiguousarray(wout.T))
    pre["bwT"] = np.ascontiguousarray(
        g["L0_attn_B_w"].T.reshape(16, 128, 16).transpose(1, 0, 2).reshape(128, 256))
    pre["cwT"] = np.ascontiguousarray(g["L0_attn_C_w"].T)
    pre["a_t"] = np.ascontiguousarray(
        g["L0_attn_A"].reshape(16, 128, 16).transpose(1, 0, 2).reshape(128, 256))
    pre["dp_t"] = np.ascontiguousarray(g["L0_attn_D_param"].reshape(16, 128).T)
    pre["bin_t"] = np.ascontiguousarray(g["L0_attn_proj_in_b"].reshape(16, 128).T)
    pre["bout_t"] = np.ascontiguousarray(g["L0_attn_proj_out_b"].reshape(16, 128).T)
    pre["wq_t"] = np.ascontiguousarray(
        g["L0_ffn_W_q"].reshape(16, 128, 128).transpose(1, 0, 2).reshape(128, 2048))
    gate_t = g["L0_ffn_gate_w"].reshape(16, 128, 64, 128).transpose(2, 1, 0, 3).reshape(64, 128, 2048)
    up_t = g["L0_ffn_up_w"].reshape(16, 128, 64, 128).transpose(2, 1, 0, 3).reshape(64, 128, 2048)
    pre["gu_t"] = np.ascontiguousarray(
        np.concatenate([gate_t, up_t], axis=2).reshape(8192, 4096))
    pre["down_t"] = np.ascontiguousarray(
        g["L0_ffn_down_w"].reshape(NIG, IGS, 128, 16, 128).transpose(0, 2, 1, 3, 4)
        .reshape(NIG * 128, IGS * 2048))
    pre["duemb"] = np.ascontiguousarray(
        np.concatenate([g["L0_ffn_down_embed"], g["L0_ffn_up_embed"]], axis=1))
    return g, pre


def kernel(**inputs):
    import ml_dtypes
    bf = ml_dtypes.bfloat16
    f32 = np.float32
    g, pre = _prep_inputs(inputs)
    x = g["L0_attn_x"]
    keys = g["L0_ffn_keys"]

    c_ident = np.eye(128, dtype=f32).astype(bf)
    c_iotam = np.tile(np.arange(64, dtype=f32) - BIGC, (128, 1)).astype(f32)
    kp = np.zeros((NLAG, S), f32) + np.arange(NLAG, dtype=f32)[:, None]
    c_kpat = np.ascontiguousarray(np.tile(kp.reshape(1, -1), (128, 1)))
    c_ones_col = np.ones((128, 1), f32).astype(bf)
    c_ones_f32 = np.ones((1, 128), f32)
    c_scal = np.tile(np.array([[1.0, 1e-5]], f32), (128, 1))
    io = np.arange(64, dtype=f32)
    c_ioteps = np.tile(np.concatenate([io * 2.0 ** -20, io * 2.0 ** -17])[None, :], (128, 1))

    nc = build_nc()
    in_maps = []
    for c in range(8):
        bidx, h = c // 2, c % 2
        s0 = h * 1024
        if h == 0:
            halo = np.zeros((HALO, D), f32)
        else:
            halo = x[bidx, s0 - HALO:s0]
        x_tc = np.ascontiguousarray(np.concatenate([halo, x[bidx, s0:s0 + 1024]], axis=0).T)
        side = 0 if bidx < 2 else 1
        hm = np.full((128, 1), 0.0 if h == 0 else 1.0, f32)
        sxy = np.empty((128, 16), np.int32)
        base = 1024 * c
        for tt in range(8):
            sxy[:, tt] = base + tt * 128 + np.arange(128)
            sxy[:, 8 + tt] = 8192 + base + tt * 128 + np.arange(128)
        im = dict(x_t=x_tc, mykeys=np.ascontiguousarray(keys[side]),
                  sxy_rows=sxy, c_ident=c_ident, c_iotam=c_iotam, c_kpat=c_kpat,
                  c_ones_col=c_ones_col, c_ones_f32=c_ones_f32, c_scal=c_scal,
                  c_ioteps=np.ascontiguousarray(c_ioteps), halo_mask=hm, **pre)
        in_maps.append(im)

    res = run_bass_kernel_spmd(nc, in_maps, core_ids=list(range(8)))
    out = np.empty((4, 2048, 2048), np.float32)
    for c in range(8):
        bidx, h = c // 2, c % 2
        out[bidx, h * 1024:(h + 1) * 1024, :] = res.results[c]["out_t"].T
    return out


# version 13
# speedup vs baseline: 2.0930x; 1.0700x over previous
import numpy as np
import concourse.bass as bass
import concourse.mybir as mybir
from concourse.bass_utils import run_bass_kernel_spmd
from concourse.bass_core_rust import IndirectOffsetOnAxis

F32 = mybir.dt.float32
BF16 = mybir.dt.bfloat16
I32 = mybir.dt.int32
AF = mybir.ActivationFunctionType
ALU = mybir.AluOpType
AX = mybir.AxisListType

D = 2048
S = 16
NKC = 16
T = 1024
HALO = 64
TT = T + HALO
NLAG = 64
NG = 8
INTER = 8192
NISL = 64
IGS = 4
NIG = NISL // IGS
NTT = 8
BIGC = float(2 ** 22)

ENGS = ("pe", "act", "dve", "pool", "sync")


class Bld:

    def __init__(self, nc):
        self.nc = nc
        self.q = {e: [] for e in ENGS}
        self.tick = {e: 0 for e in ENGS}
        self.sem = {}
        self.dmacnt = {}

    def op(self, eng, fn, waits=(), inc=True):
        serial = eng != "pe"
        if inc:
            self.tick[eng] += 1
        t = self.tick[eng]
        waits = tuple(w for w in waits if w is not None)

        def emit(eo, waits=waits, t=t, eng=eng, inc=inc, serial=serial):
            for w in waits:
                if w[0] == "e":
                    if w[1] == eng and w[2] <= (t - 1 if inc else t):
                        continue
                    eo.wait_ge(self.sem[w[1]], w[2])
                else:
                    eo.wait_ge(self.sem[w[1]], 16 * w[2])
            inst = fn(eo)
            if serial and inc and t > 1:
                inst._wait_ge(self.sem[eng], t - 1)
            if inc:
                inst.then_inc(self.sem[eng], 1)

        self.q[eng].append(emit)
        return ("e", eng, t) if inc else None

    def dma(self, qeng, stream, fn, waits=()):
        self.dmacnt[stream] = self.dmacnt.get(stream, 0) + 1
        c = self.dmacnt[stream]
        waits = tuple(w for w in waits if w is not None) + (
            (("d", stream, c - 1),) if c >= 2 else ())

        def emit(eo, waits=waits, c=c):
            for w in waits:
                if w[0] == "e":
                    eo.wait_ge(self.sem[w[1]], w[2])
                else:
                    eo.wait_ge(self.sem[w[1]], 16 * w[2])
            fn(eo).then_inc(self.sem[stream], 16)

        self.q[qeng].append(emit)
        return ("d", stream, c)

    def wait(self, eng, waits):
        waits = tuple(w for w in waits if w is not None)

        def emit(eo, waits=waits):
            for w in waits:
                if w[0] == "e":
                    eo.wait_ge(self.sem[w[1]], w[2])
                else:
                    eo.wait_ge(self.sem[w[1]], 16 * w[2])

        self.q[eng].append(emit)


def last(*hs):
    hs = [h for h in hs if h is not None]
    assert all(h[0] == hs[0][0] and h[1] == hs[0][1] for h in hs)
    return max(hs, key=lambda h: h[2])


STAGE = "full"


def build_nc(debug_taps=(), stage=None):
    stage = stage or STAGE
    nc = bass.Bass("TRN2", target_bir_lowering=False)
    b = Bld(nc)

    def din(name, shape, dt=F32):
        return nc.dram_tensor(name, shape, dt, kind="ExternalInput").ap()

    x_t = din("x_t", [D, TT])
    winT = din("winT", [D, D])
    woutT = din("woutT", [D, D])
    bwT = din("bwT", [128, 256])
    cwT = din("cwT", [S, D])
    a_t = din("a_t", [128, 256])
    dp_t = din("dp_t", [128, NKC])
    bin_t = din("bin_t", [128, NKC])
    bout_t = din("bout_t", [128, NKC])
    wq_t = din("wq_t", [128, D])
    mykeys = din("mykeys", [64, 64])
    gu_t = din("gu_t", [INTER, 2 * D])
    down_t = din("down_t", [NIG * 128, IGS * 2048])
    duemb = din("duemb", [4096, 2 * D])
    sxy_rows = din("sxy_rows", [128, 16], I32)
    c_ident = din("c_ident", [128, 128], BF16)
    c_iotam = din("c_iotam", [128, 64])
    c_kpat = din("c_kpat", [128, NLAG * S])
    c_ones_col = din("c_ones_col", [128, 1], BF16)
    c_scal = din("c_scal", [128, 2])
    c_ioteps = din("c_ioteps", [128, 128])
    halo_mask = din("halo_mask", [128, 1])
    c_ones_f32 = din("c_ones_f32", [1, 128])

    out_t = nc.dram_tensor("out_t", [D, T], F32, kind="ExternalOutput").ap()

    cc_in = nc.dram_tensor("cc_in", [2048, 16], F32, kind="Internal").ap()
    cc_out = nc.dram_tensor("cc_out", [8 * 2048, 16], F32, kind="Internal").ap()
    duemb_bf = nc.dram_tensor("duemb_bf", [4096, 2 * D], BF16, kind="Internal").ap()
    dn_bf = nc.dram_tensor("dn_bf", [NIG * 128, IGS * 2048], BF16, kind="Internal").ap()

    import contextlib
    es = contextlib.ExitStack()
    with es:
        E = es.enter_context

        BUF1 = E(nc.sbuf_tensor("BUF1", [128, NKC * TT], BF16))
        BUF2 = E(nc.sbuf_tensor("BUF2", [128, NKC * TT], BF16))
        BUF3 = E(nc.sbuf_tensor("BUF3", [128, 8192], BF16))
        R1 = E(nc.sbuf_tensor("R1", [128, NKC * T], F32))
        X1 = E(nc.sbuf_tensor("X1", [128, NKC * T], BF16))
        MISC = E(nc.sbuf_tensor("MISC", [128, 2944], F32))
        BNDL = E(nc.sbuf_tensor("BNDL", [128, 4096], BF16))
        RTG = E(nc.sbuf_tensor("RTG", [128, 1024], F32))
        CONS = E(nc.sbuf_tensor("CONS", [128, 384], F32))
        CONSB = E(nc.sbuf_tensor("CONSB", [128, 384], BF16))
        IDT = E(nc.sbuf_tensor("IDT", [128, 128], BF16))
        PACK = E(nc.sbuf_tensor("PACK", [128, 64], F32))
        SXI = E(nc.sbuf_tensor("SXI", [128, 16], I32))
        SMI = E(nc.sbuf_tensor("SMI", [128, 32], I32))
        ONESF = E(nc.sbuf_tensor("ONESF", [128, 128], F32))

        pA = E(nc.psum_tensor("pA", [128, 1024], F32))
        pB = E(nc.psum_tensor("pB", [128, 1024], F32))
        pHa = E(nc.psum_tensor("pHa", [128, 512], F32))
        pHb = E(nc.psum_tensor("pHb", [128, 512], F32))
        pC = E(nc.psum_tensor("pC", [128, 1024], F32))

        dma_streams = ["c_const", "c_cb", "c_x", "c_tab", "w_a", "w_b", "w_d",
                       "g_de", "g_ue", "x_pack", "g_cc", "st_out", "m_dma"]
        for e_ in ENGS:
            b.sem[e_] = E(nc.semaphore("s_" + e_))
        for s_ in dma_streams:
            b.sem[s_] = E(nc.semaphore("s_" + s_))

        block = E(nc.Block())

        x_bf = [BUF1[:, i * TT:(i + 1) * TT] for i in range(NKC)]
        xp_bf = [BUF2[:, i * TT:(i + 1) * TT] for i in range(NKC)]
        y_bf = [xp_bf[i][:, HALO:TT] for i in range(NKC)]
        r1c = [R1[:, i * T:(i + 1) * T] for i in range(NKC)]
        x1c = [X1[:, i * T:(i + 1) * T] for i in range(NKC)]
        wslab = [X1[:, 0:2048], X1[:, 2048:4096]]
        bndl = [BNDL[:, i * 2048:(i + 1) * 2048] for i in range(2)]
        wqs = BNDL[:, 0:2048]
        qbf = BNDL[:, 2048:3072]
        CWT = BNDL[0:16, 0:2048]
        GT = BNDL[0:16, 2048:3072]
        a_grp = BUF2[:, 0:IGS * T]
        wgu_sl = [BUF2[:, 4096 + i * 4096:4096 + (i + 1) * 4096] for i in range(2)]
        wg_sl = [wgu_sl[i][:, 0:2048] for i in range(2)]
        wu_sl = [wgu_sl[i][:, 2048:4096] for i in range(2)]
        du_b = [BUF1[:, 0:4096], BUF1[:, 4096:8192], BUF2[:, 12288:16384]]
        x1tok = BUF1[:, 8192:10240]
        es_acc = BUF1[:, 10240:12288]
        ttrj = BUF1[:, 12288:14336]
        gs_bf = BUF1[:, 14336:15360]
        ln_scr = [BUF1[:, 15360:16384], BUF1[:, 16384:17408]]
        wgb = [BUF3[:, 0:TT], BUF3[:, TT:2 * TT]]
        dlt_rep = MISC[:, 0:TT]
        la_v = MISC[:, 1088:1344]
        sg_v = MISC[:, 1344:1600]
        kbase = MISC[:, 1600:1728]
        kprod = MISC[:, 1728:1856]
        t1_sp = MISC[0:16, 1856:2944]
        GXY = MISC[:, 1856:2112]
        eqbuf = MISC[:, 0:512]
        cbuf = MISC[:, 512:1024]
        ew_f = MISC[:, 1024:1032]
        s8 = MISC[:, 1032:1040]
        sm8 = MISC[:, 1040:1048]
        sil8 = MISC[:, 1048:1056]
        ewf = MISC[:, 1056:1064]
        cidxm = MISC[:, 1064:1072]
        ix64 = MISC[:, 1072:1080]
        offs_f = MISC[:, 1080:1088]
        mx1 = MISC[:, 1088:1089]
        sume = MISC[:, 1089:1090]
        rsum = MISC[:, 1090:1091]
        st_mean = MISC[0:1, 1092:2116]
        eps1_sb = MISC[:, 2116:2180]
        eps2_sb = MISC[:, 2180:2244]
        sm8s = MISC[:, 2244:2260]
        st_rstd = RTG[64:65, 0:1024]
        st_t1 = RTG[0:1, 0:1024]
        a_sb = CONS[:, 0:256]
        hmask_sb = CONS[:, 370:371]
        dp_sb = CONS[:, 256:272]
        bin_sb = CONS[:, 272:288]
        bout_sb = CONS[:, 288:304]
        iotam_sb = CONS[:, 304:368]
        one_col = CONS[:, 368:369]
        eps_col = CONS[:, 369:370]
        ident_sb = IDT[:, :]
        bw_sb = CONSB[:, 0:256]
        keys_sb = CONSB[0:64, 256:320]
        ones_col_sb = CONSB[:, 320:321]

        tap_out = {}
        for name, shape, dt_ in debug_taps:
            tap_out[name] = nc.dram_tensor("tap_" + name, list(shape), dt_, kind="ExternalOutput").ap()

        def emit_tap(name, sb_ap, waits):
            if name in tap_out:
                b.dma("sync", "st_out",
                      lambda e, n=name, s=sb_ap: e.dma_start(out=tap_out[n][:], in_=s),
                      waits=waits)

        for dst, src in [(a_sb, a_t[:]), (dp_sb, dp_t[:]), (bin_sb, bin_t[:]),
                         (bout_sb, bout_t[:]), (iotam_sb, c_iotam[:]),
                         (ONESF[0:1, :], c_ones_f32[:]), (ONESF[64:65, :], c_ones_f32[:]),
                         (CONS[:, 368:370], c_scal[:]), (CONS[:, 370:371], halo_mask[:])]:
            b.dma("sync", "c_const", lambda e, d=dst, s=src: e.dma_start(out=d, in_=s))
        b.dma("sync", "c_const", lambda e: e.dma_start(out=SXI[:], in_=sxy_rows[:]))
        h_const = ("d", "c_const", b.dmacnt["c_const"])
        for dst, src in [(ident_sb, c_ident[:]), (bw_sb, bwT[:]), (keys_sb, mykeys[:]),
                         (CONSB[64:128, 256:320], mykeys[:]),
                         (ones_col_sb, c_ones_col[:])]:
            b.dma("pool", "c_cb", lambda e, d=dst, s=src: e.dma_start(out=d, in_=s))
        h_cb = ("d", "c_cb", b.dmacnt["c_cb"])

        x_view = x_t.rearrange("(kc p) t -> p kc t", p=128)
        buf1_3d = BUF1[:].rearrange("p (kc t) -> p kc t", t=TT)
        for i in range(4):
            b.dma("pool", "c_x",
                  lambda e, i=i: e.dma_start(out=buf1_3d[:, i * 4:(i + 1) * 4, :],
                                             in_=x_view[:, i * 4:(i + 1) * 4, :]))
        h_x = ("d", "c_x", 4)

        h_sg = b.op("act", lambda e: e.activation(sg_v, a_sb, AF.Sigmoid), waits=[h_const, h_cb])
        h_la = b.op("act", lambda e: e.activation(la_v, sg_v, AF.Ln), waits=[h_sg])

        bnst = {"last_hm": None, "last_he": None}

        def build_bundle(g, war):
            buf = bndl[g % 2]
            hk = b.dma("sync", "m_dma",
                       lambda e, g=g: e.dma_start(out=kbase, in_=c_kpat[:, g * 128:(g + 1) * 128]),
                       waits=[war, bnst["last_hm"]])
            h_prev = bnst["last_he"]
            hm = None
            for kc in range(NKC):
                la_rep = la_v[:, kc * S:(kc + 1) * S].unsqueeze(1).to_broadcast([128, 8, S])
                hm = b.op("dve", lambda e, la_rep=la_rep: e.tensor_tensor(
                    out=kprod.rearrange("p (k s) -> p k s", s=S),
                    in0=kbase.rearrange("p (k s) -> p k s", s=S),
                    in1=la_rep, op=ALU.mult), waits=[hk, h_la, h_prev, h_const, h_cb])
                h_prev = b.op("act", lambda e, kc=kc, buf=buf: e.activation(
                    buf[:, kc * 128:(kc + 1) * 128], kprod, AF.Exp), waits=[hm, war])
            bnst["last_hm"] = hm
            bnst["last_he"] = h_prev
            return h_prev

        win_view = winT.rearrange("(mt p) k -> mt p k", p=128)
        wout_view = woutT.rearrange("(mt p) k -> mt p k", p=128)
        gu_view = gu_t.rearrange("(mt p) k -> mt p k", p=128)
        down_view = down_t.rearrange("(ig p) k -> ig p k", p=128)
        dn_view = dn_bf.rearrange("(ig p) k -> ig p k", p=128)
        tab_i = [0]

        def emit_tab_chunk():
            ch = tab_i[0]
            if ch >= 24:
                return
            tab_i[0] += 1
            if ch < 16:
                b.dma("pool", "c_tab",
                      lambda e, ch=ch: e.dma_start(
                          out=duemb_bf[ch * 256:(ch + 1) * 256, :],
                          in_=duemb[ch * 256:(ch + 1) * 256, :]))
            else:
                c2 = ch - 16
                b.dma("pool", "c_tab",
                      lambda e, c2=c2: e.dma_start(
                          out=dn_bf[c2 * 256:(c2 + 1) * 256, :],
                          in_=down_t[c2 * 256:(c2 + 1) * 256, :]))

        evac_xp = [None] * NKC
        pe_done_win = [None] * NKC
        for mt in range(NKC):
            par = mt % 2
            war = pe_done_win[mt - 2] if mt >= 2 else None
            h_wd = b.dma("pool", "w_a" if par == 0 else "w_b",
                         lambda e, mt=mt, par=par: e.dma_start(out=wslab[par], in_=win_view[mt]),
                         waits=[war])
            if mt >= 2:
                emit_tab_chunk()
            ppsum = pA if par == 0 else pB
            war_ps = evac_xp[mt - 2] if mt >= 2 else None
            phs = (pHa if par == 0 else pHb)[:, 0:64]
            hp = None
            for kc in range(NKC):
                b.op("pe", lambda e, kc=kc, par=par, phs=phs: e.matmul(
                    phs, wslab[par][:, kc * 128:(kc + 1) * 128],
                    x_bf[kc][:, 0:64], start=(kc == 0), stop=(kc == NKC - 1)),
                    waits=([h_wd, h_x, war_ps] + ([h_const, h_cb] if mt == 0 else [])
                           if kc == 0 else []), inc=False)
                for fo in (0, 512):
                    fin = kc == NKC - 1 and fo == 512
                    hp = b.op("pe", lambda e, kc=kc, par=par, ppsum=ppsum, fo=fo: e.matmul(
                        ppsum[:, fo:fo + 512], wslab[par][:, kc * 128:(kc + 1) * 128],
                        x_bf[kc][:, 64 + fo:64 + fo + 512],
                        start=(kc == 0), stop=(kc == NKC - 1)), inc=fin) or hp
            pe_done_win[mt] = hp
            b.op("dve", lambda e, mt=mt, phs=phs: e.tensor_scalar(
                out=xp_bf[mt][:, 0:64], in0=phs, scalar1=bin_sb[:, mt:mt + 1],
                scalar2=hmask_sb, op0=ALU.add, op1=ALU.mult), waits=[hp, h_const])
            evac_xp[mt] = b.op("dve", lambda e, mt=mt, ppsum=ppsum: e.tensor_scalar(
                out=xp_bf[mt][:, 64:TT], in0=ppsum[:, 0:1024], scalar1=bin_sb[:, mt:mt + 1],
                scalar2=None, op0=ALU.add), waits=[hp])

        for _ in range(10):
            emit_tab_chunk()
        h_b = [None] * NG
        h_b[0] = build_bundle(0, None)

        h_dlt_mm = None
        for kc in range(NKC):
            b.op("pe", lambda e, kc=kc: e.matmul(
                pHa[0:16, 0:64], bw_sb[:, kc * S:(kc + 1) * S], xp_bf[kc][:, 0:64],
                start=(kc == 0), stop=(kc == NKC - 1)),
                waits=([evac_xp[14], evac_xp[15], h_const] if kc == 0 else [evac_xp[kc]]),
                inc=False)
            for fo in (0, 512):
                fin = kc == NKC - 1 and fo == 512
                h_dlt_mm = b.op("pe", lambda e, kc=kc, fo=fo: e.matmul(
                    pA[0:16, fo:fo + 512], bw_sb[:, kc * S:(kc + 1) * S],
                    xp_bf[kc][:, 64 + fo:64 + fo + 512],
                    start=(kc == 0), stop=(kc == NKC - 1)), inc=fin) or h_dlt_mm
        h1 = b.op("act", lambda e: e.activation(t1_sp[:, 0:64], pHa[0:16, 0:64], AF.Abs),
                  waits=[h_dlt_mm])
        h1 = b.op("act", lambda e: e.activation(t1_sp[:, 64:TT], pA[0:16, 0:1024], AF.Abs),
                  waits=[h1])
        h1 = b.op("act", lambda e: e.activation(t1_sp, t1_sp, AF.Exp, scale=-1.0), waits=[h1])
        h1 = b.op("act", lambda e: e.activation(t1_sp, t1_sp, AF.Ln, bias=one_col[0:16, :]),
                  waits=[h1, h_const])
        h2 = b.op("dve", lambda e: e.tensor_scalar(out=MISC[0:16, 0:64], in0=pHa[0:16, 0:64],
                                                   scalar1=0.0, scalar2=None, op0=ALU.max),
                  waits=[h_dlt_mm])
        h2 = b.op("dve", lambda e: e.tensor_scalar(out=MISC[0:16, 64:TT], in0=pA[0:16, 0:1024],
                                                   scalar1=0.0, scalar2=None, op0=ALU.max),
                  waits=[h2])
        h_dlt = b.op("dve", lambda e: e.tensor_tensor(out=MISC[0:16, 0:TT], in0=MISC[0:16, 0:TT],
                                                      in1=t1_sp, op=ALU.add), waits=[h1, h2])
        for k in range(1, 8):
            b.dma("sync", "m_dma",
                  lambda e, k=k: e.dma_start(out=MISC[16 * k:16 * (k + 1), 0:TT],
                                             in_=MISC[0:16, 0:TT]), waits=[h_dlt])
        h_rep = ("d", "m_dma", b.dmacnt["m_dma"])

        h_ioteps = b.dma("sync", "m_dma",
                         lambda e: e.dma_start(out=MISC[:, 2116:2244], in_=c_ioteps[:]),
                         waits=[h_dlt])
        h_b[1] = build_bundle(1, None)
        h_wg = [None, None]
        h_sel_done = [None, None]
        h_p_done = [None] * NG
        for g in range(NG):
            par = g % 2
            bbuf = bndl[g % 2]
            ppsum = pA if par == 0 else pB
            phs = (pHa if par == 0 else pHb)[:, 0:64]
            hp = None
            for kc in range(NKC):
                w0 = []
                if kc == 0:
                    w0 = [h_b[g], h_wg[par], (h2 if g == 0 and par == 0 else None),
                          (h1 if g == 0 else None)]
                b.op("pe", lambda e, kc=kc, bbuf=bbuf, phs=phs: e.matmul(
                    phs, bbuf[:, kc * 128:(kc + 1) * 128],
                    xp_bf[kc][:, 0:64], start=(kc == 0), stop=(kc == NKC - 1)), waits=w0,
                    inc=False)
                for fo in (0, 512):
                    fin = kc == NKC - 1 and fo == 512
                    hp = b.op("pe", lambda e, kc=kc, bbuf=bbuf, ppsum=ppsum, fo=fo: e.matmul(
                        ppsum[:, fo:fo + 512], bbuf[:, kc * 128:(kc + 1) * 128],
                        xp_bf[kc][:, 64 + fo:64 + fo + 512],
                        start=(kc == 0), stop=(kc == NKC - 1)), inc=fin) or hp
            h_p_done[g] = hp
            b.op("dve", lambda e, par=par, phs=phs: e.tensor_tensor(
                out=wgb[par][:, 0:64], in0=phs, in1=dlt_rep[:, 0:64], op=ALU.mult),
                waits=[hp, h_rep, h_sel_done[par]])
            h_wg[par] = b.op("dve", lambda e, par=par, ppsum=ppsum: e.tensor_tensor(
                out=wgb[par][:, 64:TT], in0=ppsum[:, 0:1024], in1=dlt_rep[:, 64:TT],
                op=ALU.mult))
            if g + 2 < NG:
                h_b[g + 2] = build_bundle(g + 2, h_p_done[g])
            hsel = None
            for k in range(8):
                sig = 8 * g + k
                for fo in (0, 512):
                    fin = k == 7 and fo == 512
                    hsel = b.op("pe", lambda e, k=k, par=par, sig=sig, fo=fo,
                                st0=(g == 0 and k == 0), sp0=(g == NG - 1 and k == 7): e.matmul(
                        pC[0:16, fo:fo + 512], ident_sb[:, k * 16:(k + 1) * 16],
                        wgb[par][:, HALO - sig + fo:HALO - sig + fo + 512],
                        start=st0, stop=sp0), waits=([h_wg[par]] if (k == 0 and fo == 0) else []),
                        inc=fin) or hsel
            h_sel_done[par] = hsel
        h_gt = b.op("dve", lambda e: e.tensor_copy(out=GT[:, :], in_=pC[0:16, 0:1024]),
                    waits=[h_sel_done[(NG - 1) % 2]])

        h_cwt = b.dma("pool", "c_cb", lambda e: e.dma_start(out=CWT[:, :], in_=cwT[:]),
                      waits=[h_p_done[6]])
        evac_y = [None] * NKC
        for mt in range(NKC):
            par = mt % 2
            ppsum = pA if par == 0 else pB
            war = evac_y[mt - 2] if mt >= 2 else h_wg[par]
            hy = None
            for fo in (0, 512):
                hy = b.op("pe", lambda e, mt=mt, ppsum=ppsum, fo=fo: e.matmul(
                    ppsum[:, fo:fo + 512], CWT[:, mt * 128:(mt + 1) * 128],
                    GT[:, fo:fo + 512], start=True, stop=True),
                    waits=([h_gt, war, h_cwt if mt == 0 else None] if fo == 0 else []),
                    inc=(fo == 512)) or hy
            evac_y[mt] = b.op("dve", lambda e, mt=mt, ppsum=ppsum: e.scalar_tensor_tensor(
                out=y_bf[mt], in0=y_bf[mt], scalar=dp_sb[:, mt:mt + 1], in1=ppsum[:, 0:1024],
                op0=ALU.mult, op1=ALU.add), waits=[hy])

        evac_r1 = [None] * NKC
        pe_done_wout = [None] * NKC
        for mt in range(NKC):
            par = mt % 2
            ppsum = pA if par == 0 else pB
            war_slab = pe_done_wout[mt - 2] if mt >= 2 else pe_done_win[NKC - 1]
            h_wd = b.dma("pool", "w_a" if par == 0 else "w_b",
                         lambda e, mt=mt, par=par: e.dma_start(out=wslab[par], in_=wout_view[mt]),
                         waits=[war_slab])
            war_ps = evac_r1[mt - 2] if mt >= 2 else evac_y[14 + par]
            hp = None
            for kc in range(NKC):
                for fo in (0, 512):
                    fin = kc == NKC - 1 and fo == 512
                    hp = b.op("pe", lambda e, mt=mt, kc=kc, par=par, ppsum=ppsum, fo=fo: e.matmul(
                        ppsum[:, fo:fo + 512], wslab[par][:, kc * 128:(kc + 1) * 128],
                        y_bf[kc][:, fo:fo + 512],
                        start=(kc == 0), stop=(kc == NKC - 1)),
                        waits=([h_wd, war_ps, (evac_y[kc] if mt == 0 else None)]
                               if (kc == 0 and fo == 0)
                               else ([evac_y[kc]] if (mt == 0 and fo == 0) else [])),
                        inc=fin) or hp
            pe_done_wout[mt] = hp
            hr = b.op("dve", lambda e, mt=mt, ppsum=ppsum: e.scalar_tensor_tensor(
                out=r1c[mt], in0=x_bf[mt][:, HALO:TT], scalar=2.0, in1=ppsum[:, 0:1024],
                op0=ALU.mult, op1=ALU.add), waits=[hp])
            evac_r1[mt] = b.op("dve", lambda e, mt=mt: e.tensor_scalar(
                out=r1c[mt], in0=r1c[mt], scalar1=bout_sb[:, mt:mt + 1], scalar2=None,
                op0=ALU.add), waits=[hr])

        def layernorm(rc, out_bf, wait_in, pre_bcast_waits, first_cast_extra):
            h_cast = [None] * NKC
            hmm = None
            for kc in range(NKC):
                pr = kc % 2
                h_cast[kc] = b.op("dve", lambda e, kc=kc, pr=pr: e.tensor_copy(
                    out=ln_scr[pr], in_=rc[kc]),
                    waits=[wait_in[kc], (hmm if kc >= 2 else None)]
                          + ([first_cast_extra] if kc == 0 else []))
                for fo in (0, 512):
                    hmm = b.op("pe", lambda e, kc=kc, pr=pr, fo=fo: e.matmul(
                        pC[0:1, fo:fo + 512], ones_col_sb, ln_scr[pr][:, fo:fo + 512],
                        start=(kc == 0), stop=(kc == NKC - 1)),
                        waits=([h_cast[kc]] + (pre_bcast_waits if kc == 0 else [])
                               if fo == 0 else []), inc=(fo == 512)) or hmm
            h_mean = b.op("dve", lambda e: e.tensor_scalar(
                out=st_mean, in0=pC[0:1, 0:1024], scalar1=1.0 / D, scalar2=None, op0=ALU.mult),
                waits=[hmm])
            h_bm = None
            for lo in (0, 512):
                h_bm = b.op("pe", lambda e, lo=lo: e.matmul(
                    pA[:, lo:lo + 512], ONESF[0:1, :], st_mean[:, lo:lo + 512],
                    start=True, stop=True),
                    waits=[h_mean] + (pre_bcast_waits if lo == 0 else []),
                    inc=(lo == 512)) or h_bm
            hmm2 = None
            h_sub = [None] * NKC
            h_sq = [None] * NKC
            for kc in range(NKC):
                pr = kc % 2
                h_sub[kc] = b.op("dve", lambda e, kc=kc: e.tensor_tensor(
                    out=rc[kc], in0=rc[kc], in1=pA[:, 0:1024], op=ALU.subtract),
                    waits=[h_bm, h_cast[kc]])
                h_sq[kc] = b.op("act", lambda e, kc=kc, pr=pr: e.activation(
                    ln_scr[pr], rc[kc], AF.Square),
                    waits=[h_sub[kc], (hmm2 if kc >= 2 else None), (hmm if kc < 2 else None)])
                for fo in (0, 512):
                    hmm2 = b.op("pe", lambda e, kc=kc, pr=pr, fo=fo: e.matmul(
                        pC[0:1, fo:fo + 512], ones_col_sb, ln_scr[pr][:, fo:fo + 512],
                        start=(kc == 0), stop=(kc == NKC - 1)),
                        waits=([h_sq[kc], (h_mean if kc == 0 else None)] if fo == 0 else []),
                        inc=(fo == 512)) or hmm2
            hv = b.op("dve", lambda e: e.tensor_scalar(
                out=st_t1, in0=pC[0:1, 0:1024], scalar1=1.0 / D, scalar2=None, op0=ALU.mult),
                waits=[hmm2])
            hv = b.op("act", lambda e: e.activation(st_t1, st_t1, AF.Sqrt,
                                                    bias=eps_col[0:1, :]), waits=[hv])
            hv = b.op("dve", lambda e: e.reciprocal(st_rstd, st_t1), waits=[hv])
            h_br = None
            for lo in (0, 512):
                h_br = b.op("pe", lambda e, lo=lo: e.matmul(
                    pB[:, lo:lo + 512], ONESF[64:65, :], st_rstd[:, lo:lo + 512],
                    start=True, stop=True),
                    waits=[hv if lo == 0 else None], inc=(lo == 512)) or h_br
            f32_h, bf_h = [], []
            for kc in range(NKC):
                hf = b.op("dve", lambda e, kc=kc: e.tensor_tensor(
                    out=rc[kc], in0=rc[kc], in1=pB[:, 0:1024], op=ALU.mult),
                    waits=[h_br, h_sq[kc]])
                f32_h.append(hf)
                if out_bf is not None:
                    bf_h.append(b.op("act", lambda e, kc=kc: e.activation(
                        out_bf[kc], rc[kc], AF.Copy), waits=[hf]))
            return f32_h, bf_h

        x1_f, x1_b = layernorm(r1c, x1c, evac_r1,
                               pre_bcast_waits=[evac_r1[14], evac_r1[15]],
                               first_cast_extra=evac_r1[15])

        emit_tap("xp", BUF2[:, :], [evac_y[15]])
        emit_tap("dlt", MISC[:, 0:TT], [h_rep])
        emit_tap("gt", GT[:, :], [h_gt])
        emit_tap("x1", X1[:, :], [x1_b[15]] if x1_b else [])
        emit_tap("r1", R1[:, :], [x1_f[15]])

        out_view = out_t.rearrange("(kc p) t -> kc p t", p=128)
        if stage == "attn":
            for kc in range(NKC):
                b.dma("sync", "st_out",
                      lambda e, kc=kc: e.dma_start(out=out_view[kc], in_=r1c[kc]),
                      waits=[x1_f[kc]])
            b.wait("pe", [("d", "st_out", b.dmacnt["st_out"])])
            _emit_streams(block, b)
            return nc

        lvls = {"attn": (0, 9), "swiglu": (1, 9), "route1": (2, 9), "route2": (3, 9),
                "ex_a": (4, 1), "ex_b1": (4, 1.2), "ex_b2": (4, 1.5), "ex_b": (4, 2),
                "ex_c": (4, 3), "nocc": (4, 9), "full": (5, 9),
                "ex_b1_t1": (4, 1.2), "ex_b_t1": (4, 2), "ex_b0_t1": (4, 1.1),
                "ex_bx_t1": (4, 1.15)}
        only_tt0 = stage.endswith("_t1")
        lvl, _exlvl = lvls[stage]
        do_route = lvl >= 2
        h_wq = b.dma("pool", "c_cb", lambda e: e.dma_start(out=wqs, in_=wq_t[:]),
                     waits=[h_p_done[6]])
        hq = None
        for kc in (range(NKC) if lvl >= 2 else ()):
            for fo in (0, 512):
                fin = kc == NKC - 1 and fo == 512
                hq = b.op("pe", lambda e, kc=kc, fo=fo: e.matmul(
                    pC[:, fo:fo + 512], wqs[:, kc * 128:(kc + 1) * 128],
                    x1c[kc][:, fo:fo + 512],
                    start=(kc == 0), stop=(kc == NKC - 1)),
                    waits=([h_wq, x1_b[kc]] if fo == 0 else []), inc=fin) or hq
        h_qe = b.op("dve", lambda e: e.tensor_copy(out=qbf, in_=pC[:, 0:1024]),
                    waits=[hq, h_p_done[7]])

        h_rt = [None] * 16
        h_pack = [None] * 16

        def emit_route_tail(jj):
            if not do_route:
                return
            for pt4 in range(4):
                idx = 4 * jj + pt4
                half, pt = idx // 8, idx % 8
                slot = idx % 4
                hr = b.op("pe", lambda e, half=half, pt=pt: e.matmul(
                    pC[:, 0:64], qbf[64 * half:64 * half + 64, pt * 128:(pt + 1) * 128],
                    CONSB[64 * half:64 * half + 64, 256:320], start=True, stop=True),
                    waits=[(h_qe if idx == 0 else None), (h_rt[idx - 1] if idx > 0 else None)])
                h_rt[idx] = b.op("dve", lambda e, idx=idx: e.tensor_tensor(
                    out=RTG[:, idx * 64:(idx + 1) * 64], in0=pC[:, 0:64], in1=eps1_sb,
                    op=ALU.subtract), waits=[hr, h_ioteps if idx == 0 else None])
                sl = RTG[:, idx * 64:(idx + 1) * 64]
                pk = PACK[:, slot * 16:slot * 16 + 16]
                hk8 = b.op("dve", lambda e, sl=sl, pk=pk: e.max(out=pk[:, 0:8], in_=sl),
                           waits=[h_rt[idx], (h_pack[idx - 4] if idx >= 4 else None)])
                sl_rep = sl.unsqueeze(1).to_broadcast([128, 8, 64])
                k8_rep = pk[:, 0:8].unsqueeze(2).to_broadcast([128, 8, 64])
                heq = b.op("dve", lambda e, sl_rep=sl_rep, k8_rep=k8_rep: e.tensor_tensor(
                    out=eqbuf.rearrange("p (i j) -> p i j", j=64), in0=sl_rep, in1=k8_rep,
                    op=ALU.is_equal), waits=[hk8])
                iot_rep = iotam_sb.unsqueeze(1).to_broadcast([128, 8, 64])
                hc = b.op("dve", lambda e, iot_rep=iot_rep: e.tensor_tensor(
                    out=cbuf.rearrange("p (i j) -> p i j", j=64),
                    in0=eqbuf.rearrange("p (i j) -> p i j", j=64), in1=iot_rep, op=ALU.mult),
                    waits=[heq, h_const])
                hmin = b.op("dve", lambda e, pk=pk: e.tensor_reduce(
                    out=pk[:, 8:16], in_=cbuf.rearrange("p (i j) -> p i j", j=64),
                    axis=AX.X, op=ALU.min), waits=[hc])
                dst = cc_in.rearrange("(p2 two) v -> two p2 v", two=2)[half, pt * 128:(pt + 1) * 128, :]
                h_pack[idx] = b.dma("sync", "x_pack",
                                    lambda e, dst=dst, pk=pk: e.dma_start(out=dst, in_=pk),
                                    waits=[hmin])
        h_gsl = [None] * NISL
        h_usl = [None] * NISL
        for isl in range(2):
            par = isl % 2
            h_gsl[isl] = b.dma("pool", "w_a" if par == 0 else "w_b",
                               lambda e, isl=isl, par=par: e.dma_start(
                                   out=wgu_sl[par], in_=gu_view[isl]),
                               waits=[pe_done_wout[NKC - 1] if isl == 0 else None])
            h_usl[isl] = h_gsl[isl]
        h_dsl = [None] * NIG
        h_dsl[0] = b.dma("sync", "w_d",
                         lambda e: e.dma_start(out=BUF3[:, :], in_=dn_view[0]),
                         waits=[h_sel_done[0], h_sel_done[1], ("d", "c_tab", 24)])

        h_gx = [None] * NTT

        def emit_collective():
            if lvl >= 5:
                h_cc = b.op("pool", lambda e: e.collective_compute(
                    "AllGather", ALU.bypass,
                    replica_groups=[[0, 1, 2, 3, 4, 5, 6, 7]],
                    ins=[cc_in.opt()], outs=[cc_out.opt()]),
                    waits=[("d", "x_pack", b.dmacnt["x_pack"])])
            elif lvl >= 3:
                h_cc = b.op("pool", lambda e: e.memset(GXY[:, :], 0.0),
                            waits=[("d", "x_pack", b.dmacnt["x_pack"])])
            if lvl >= 3:
                for tt in range(NTT):
                    b.dma("pool", "g_cc", lambda e, tt=tt: e.indirect_dma_start(
                        out=GXY[:, tt * 16:tt * 16 + 16], out_offset=None, in_=cc_out[:],
                        in_offset=IndirectOffsetOnAxis(SXI[:, tt:tt + 1], axis=0)),
                        waits=[h_cc if tt == 0 else None, h_const if tt == 0 else None])
                    b.dma("pool", "g_cc", lambda e, tt=tt: e.indirect_dma_start(
                        out=GXY[:, 128 + tt * 16:128 + tt * 16 + 16], out_offset=None,
                        in_=cc_out[:],
                        in_offset=IndirectOffsetOnAxis(SXI[:, 8 + tt:8 + tt + 1], axis=0)))
                    h_gx[tt] = ("d", "g_cc", b.dmacnt["g_cc"])
            emit_tap("rtg", RTG[:, :], [h_rt[15]])
            emit_tap("gxy", GXY[:, :], [h_gx[NTT - 1]])

        pe_gate_done = [None] * NISL
        pe_up_done = [None] * NISL
        h_aev = [None] * NISL
        h_sil = [None] * NISL
        pe_down_done = [None] * NIG
        h_devac = [None] * NIG
        h_r2 = {mt: x1_f[mt] for mt in range(NKC)}
        ex = {
            "h_smi_reader": [None] * 4,
            "h_s8_act": None,
            "h_x1tok_reader": None,
            "h_du_war": [None, None, None],
            "h_es_tr": None,
            "h_es_reader": None,
            "h_pH_war": last(h_wg[0], h_wg[1]),
            "h_pA_war": None,
            "h_ttrj_act": None,
        }
        eq2 = RTG[:, 0:512]
        c2b = RTG[:, 512:1024]
        exs = [dict() for _ in range(NTT)]

        def ex_part1(tt):
            st = exs[tt]
            slot = tt % 4
            sm_slot = tt % 2
            w0 = [h_gx[tt]]
            gxv = GXY[:, tt * 16:tt * 16 + 8]
            gxi = GXY[:, tt * 16 + 8:tt * 16 + 16]
            gyv = GXY[:, 128 + tt * 16:128 + tt * 16 + 8]
            gyi = GXY[:, 128 + tt * 16 + 8:128 + tt * 16 + 16]
            h = None
            for i in range(8):
                h = b.op("dve", lambda e, i=i, gyv=gyv, gxv=gxv: e.tensor_scalar(
                    out=eqbuf[:, i * 8:(i + 1) * 8], in0=gyv, scalar1=gxv[:, i:i + 1],
                    scalar2=None, op0=ALU.add), waits=(w0 if i == 0 else []))
            h_as = b.op("dve", lambda e: e.tensor_tensor(
                out=eqbuf[:, 0:64], in0=eqbuf[:, 0:64], in1=eps2_sb, op=ALU.subtract),
                waits=[h])
            h = b.op("dve", lambda e, gxi=gxi: e.tensor_scalar(
                out=ix64, in0=gxi, scalar1=BIGC, scalar2=64.0, op0=ALU.add, op1=ALU.mult),
                waits=w0)
            for i in range(8):
                h = b.op("dve", lambda e, i=i, gyi=gyi: e.tensor_scalar(
                    out=cbuf[:, i * 8:(i + 1) * 8], in0=gyi, scalar1=ix64[:, i:i + 1],
                    scalar2=None, op0=ALU.add), waits=([h] if i == 0 else []))
            h_ci = h
            h_s8 = b.op("dve", lambda e: e.max(out=s8, in_=eqbuf[:, 0:64]),
                        waits=[h_as, ex["h_s8_act"]])
            as_rep = eqbuf[:, 0:64].unsqueeze(1).to_broadcast([128, 8, 64])
            s8_rep = s8.unsqueeze(2).to_broadcast([128, 8, 64])
            h_eq = b.op("dve", lambda e, as_rep=as_rep, s8_rep=s8_rep: e.tensor_tensor(
                out=eq2.rearrange("p (i j) -> p i j", j=64), in0=as_rep, in1=s8_rep,
                op=ALU.is_equal), waits=[h_s8])
            ci_rep = cbuf[:, 0:64].unsqueeze(1).to_broadcast([128, 8, 64])
            h_c2 = b.op("dve", lambda e, ci_rep=ci_rep: e.tensor_tensor(
                out=c2b.rearrange("p (i j) -> p i j", j=64),
                in0=eq2.rearrange("p (i j) -> p i j", j=64), in1=ci_rep, op=ALU.mult),
                waits=[h_eq, h_ci])
            h_cm = b.op("dve", lambda e: e.tensor_reduce(
                out=cidxm, in_=c2b.rearrange("p (i j) -> p i j", j=64),
                axis=AX.X, op=ALU.min), waits=[h_c2])
            h_of = b.op("dve", lambda e: e.tensor_scalar(
                out=offs_f, in0=cidxm, scalar1=BIGC, scalar2=4095.0,
                op0=ALU.add, op1=ALU.min), waits=[h_cm])
            h_of = b.op("dve", lambda e: e.tensor_scalar(
                out=offs_f, in0=offs_f, scalar1=0.0, scalar2=None, op0=ALU.max),
                waits=[h_of])
            h_oi = b.op("dve", lambda e, slot=slot: e.tensor_copy(
                out=SMI[:, slot * 8:slot * 8 + 8], in_=offs_f),
                waits=[h_of, ex["h_smi_reader"][slot]])
            sm8v = sm8s[:, sm_slot * 8:sm_slot * 8 + 8]
            h_m = b.op("dve", lambda e: e.tensor_reduce(out=mx1, in_=s8, axis=AX.X,
                                                        op=ALU.max, negate=True), waits=[h_s8])
            h_e = b.op("act", lambda e, sm8v=sm8v: e.activation(
                sm8v, s8, AF.Exp, bias=mx1[:, 0:1], scale=1.0, accum_out=sume),
                waits=[h_m, (exs[tt - 2].get("h_sm_last") if tt >= 2 else None)])
            ex["h_s8_act"] = h_e
            h_rc = b.op("dve", lambda e: e.reciprocal(rsum, sume), waits=[h_e])
            h_sm = b.op("dve", lambda e, sm8v=sm8v: e.tensor_scalar(
                out=sm8v, in0=sm8v, scalar1=rsum[:, 0:1], scalar2=None, op0=ALU.mult),
                waits=[h_rc])
            st["h_sm"] = h_sm
            st["sm8v"] = sm8v
            st["h_oi"] = h_oi
            st["slot"] = slot
            isl_last = (tt + 1) * IGS + IGS - 1
            h_xe = None
            for wave in range(2):
                htrs = None
                for k8 in range(8):
                    kc = wave * 8 + k8
                    htrs = b.op("pe", lambda e, kc=kc, k8=k8, tt=tt: e.matmul(
                        pA[:, k8 * 128:(k8 + 1) * 128],
                        x1c[kc][:, tt * 128:(tt + 1) * 128], ident_sb,
                        start=True, stop=True),
                        waits=([h_sil[isl_last], h_xe,
                                (x1_b[kc] if tt == 0 else None)] if k8 == 0 else []))
                for k8 in range(8):
                    kc = wave * 8 + k8
                    h_xe = b.op("act", lambda e, kc=kc, k8=k8: e.activation(
                        x1tok[:, kc * 128:(kc + 1) * 128],
                        pA[:, k8 * 128:(k8 + 1) * 128], AF.Copy),
                        waits=[(htrs if k8 == 0 else None),
                               (ex["h_x1tok_reader"] if wave == 0 and k8 == 0 else None)])
            ex["h_pA_war"] = h_xe
            st["h_xt"] = h_xe
            st["h_dot"] = [None] * 8
            st["h_dm"] = [None] * 8
            st["h_es"] = [None] * 8
            st["h_deg"] = [None] * 8

        def ex_dots_step(tt, j):
            st = exs[tt]
            slot = st["slot"]
            sm8v = st["sm8v"]
            for r in (2 * j, 2 * j + 1):
                pr = r % 3
                war = ex["h_du_war"][pr]
                st["h_deg"][r] = b.dma("pool", "g_de",
                                       lambda e, r=r, pr=pr, slot=slot: e.indirect_dma_start(
                                           out=du_b[pr], out_offset=None, in_=duemb_bf[:],
                                           in_offset=IndirectOffsetOnAxis(
                                               SMI[:, slot * 8 + r:slot * 8 + r + 1], axis=0)),
                                       waits=[(st["h_oi"] if r == 0 else None),
                                              (("d", "c_tab", 16) if (tt == 0 and r == 0)
                                               else None), war])
                h_dm = b.op("dve", lambda e, pr=pr: e.tensor_tensor(
                    out=ttrj, in0=x1tok, in1=du_b[pr][:, 0:2048], op=ALU.mult),
                    waits=[st["h_deg"][r], (st["h_dot"][r - 1] if r >= 1 else None)]
                          + ([st["h_xt"], ex["h_ttrj_act"]] if r == 0 else []))
                st["h_dm"][r] = h_dm
                st["h_dot"][r] = b.op("act", lambda e, r=r: e.activation(
                    ttrj, ttrj, AF.Copy, accum_out=ew_f[:, r:r + 1]), waits=[h_dm])
                h_sil8 = b.op("act", lambda e, r=r: e.activation(
                    sil8[:, r:r + 1], ew_f[:, r:r + 1], AF.Silu), waits=[st["h_dot"][r]])
                h_ewf = b.op("dve", lambda e, r=r, sm8v=sm8v: e.tensor_scalar(
                    out=ewf[:, r:r + 1], in0=sil8[:, r:r + 1],
                    scalar1=sm8v[:, r:r + 1], scalar2=None, op0=ALU.mult),
                    waits=[h_sil8, st["h_sm"] if r == 0 else None])
                if r == 0:
                    h_es = b.op("dve", lambda e, pr=pr: e.tensor_scalar(
                        out=es_acc, in0=du_b[pr][:, 2048:4096], scalar1=ewf[:, 0:1],
                        scalar2=None, op0=ALU.mult),
                        waits=[h_ewf, ex["h_es_reader"], ex["h_es_tr"]])
                else:
                    h_es = b.op("dve", lambda e, r=r, pr=pr: e.scalar_tensor_tensor(
                        out=es_acc, in0=du_b[pr][:, 2048:4096], scalar=ewf[:, r:r + 1],
                        in1=es_acc, op0=ALU.mult, op1=ALU.add),
                        waits=[h_ewf, st["h_es"][r - 1]])
                st["h_es"][r] = h_es
                ex["h_du_war"][pr] = h_es
            if j == 3:
                st["h_sm_last"] = st["h_es"][7]
                ex["h_ttrj_act"] = st["h_dot"][7]
                ex["h_smi_reader"][slot] = st["h_deg"][7]
                ex["h_x1tok_reader"] = st["h_dm"][7]
                ex["h_es_reader"] = st["h_es"][7]


        do_ex = lvl >= 4
        for ig in range(NIG):
            tt_d = ig - 2
            for j in range(IGS):
                isl = ig * IGS + j
                par = isl % 2
                if isl >= 2 and h_gsl[isl] is None:
                    h_gsl[isl] = b.dma("pool", "w_a" if par == 0 else "w_b",
                                       lambda e, isl=isl, par=par: e.dma_start(
                                           out=wgu_sl[par], in_=gu_view[isl]),
                                       waits=[pe_up_done[isl - 2]])
                    h_usl[isl] = h_gsl[isl]
                war_gate = h_sil[isl - 1] if isl >= 1 else x1_f[NKC - 1]
                if j >= 1:
                    war_up = h_aev[isl - 1]
                elif ig == 0:
                    war_up = x1_f[NKC - 1]
                else:
                    war_up = h_devac_pair[0]
                hp = None
                for kc in range(NKC):
                    for fo in (0, 512):
                        fin = kc == NKC - 1 and fo == 512
                        hp = b.op("pe", lambda e, isl=isl, kc=kc, par=par, fo=fo: e.matmul(
                            pA[:, fo:fo + 512], wg_sl[par][:, kc * 128:(kc + 1) * 128],
                            x1c[kc][:, fo:fo + 512],
                            start=(kc == 0), stop=(kc == NKC - 1)),
                            waits=([h_gsl[isl], war_gate,
                                    (ex["h_pA_war"] if j == 0 else None),
                                    (x1_b[kc] if isl == 0 else None)]
                                   if (kc == 0 and fo == 0) else []), inc=fin) or hp
                pe_gate_done[isl] = hp
                h_sil[isl] = b.op("act", lambda e: e.activation(gs_bf, pA[:, 0:1024], AF.Silu),
                                  waits=[hp, (h_aev[isl - 1] if isl >= 1 else None)])
                hp = None
                for kc in range(NKC):
                    for fo in (0, 512):
                        fin = kc == NKC - 1 and fo == 512
                        hp = b.op("pe", lambda e, isl=isl, kc=kc, par=par, fo=fo: e.matmul(
                            pB[:, fo:fo + 512], wu_sl[par][:, kc * 128:(kc + 1) * 128],
                            x1c[kc][:, fo:fo + 512],
                            start=(kc == 0), stop=(kc == NKC - 1)),
                            waits=([h_usl[isl], war_up] if (kc == 0 and fo == 0) else []),
                            inc=fin) or hp
                pe_up_done[isl] = hp
                h_aev[isl] = b.op("dve", lambda e, j=j: e.tensor_tensor(
                    out=a_grp[:, j * 1024:(j + 1) * 1024], in0=pB[:, 0:1024], in1=gs_bf,
                    op=ALU.mult), waits=[hp, h_sil[isl]])
                if ig == 0:
                    emit_route_tail(j)
                if do_ex and 0 <= tt_d < NTT:
                    ex_dots_step(tt_d, j)
            if ig == 0:
                for isl_h in (4, 5):
                    par_h = isl_h % 2
                    h_gsl[isl_h] = b.dma("pool", "w_a" if par_h == 0 else "w_b",
                                         lambda e, isl=isl_h, par=par_h: e.dma_start(
                                             out=wgu_sl[par], in_=gu_view[isl]),
                                         waits=[pe_up_done[isl_h - 2]])
                    h_usl[isl_h] = h_gsl[isl_h]
                emit_collective()
            do_es = do_ex and 0 <= tt_d < NTT
            evs = []
            htr_es = [None] * NKC
            for mtg in range(8):
                for half_mt in range(2):
                    mt = mtg * 2 + half_mt
                    pp = pB if half_mt == 0 else pC
                    hp = None
                    for kc in range(IGS):
                        w0 = []
                        if kc == 0:
                            w0 = [h_dsl[ig] if mt == 0 else None,
                                  (evs[mt - 2][1] if mt >= 2 else None),
                                  (h_aev[ig * IGS + IGS - 1] if mt < 2 else None)]
                            if mt < 2 and ig == 0:
                                w0 += [x1_f[NKC - 1], h_rt[15]]
                            if mt < 2 and ig >= 1:
                                w0 += [h_devac_pair[half_mt]]
                        if mt == 0:
                            w0 += [h_aev[ig * IGS + kc]]
                        for fo in (0, 512):
                            fin = kc == IGS - 1 and fo == 512
                            hp = b.op("pe", lambda e, mt=mt, kc=kc, pp=pp, fo=fo: e.matmul(
                                pp[:, fo:fo + 512],
                                BUF3[:, kc * 2048 + mt * 128:kc * 2048 + (mt + 1) * 128],
                                a_grp[:, kc * 1024 + fo:kc * 1024 + fo + 512],
                                start=(kc == 0), stop=(kc == IGS - 1)),
                                waits=(w0 if fo == 0 else []), inc=fin) or hp
                    if do_es:
                        sl = mt % 8
                        pht = pHa if sl < 4 else pHb
                        po = (sl % 4) * 128
                        htr_es[mt] = b.op("pe", lambda e, mt=mt, pht=pht, po=po: e.matmul(
                            pht[:, po:po + 128], es_acc[:, mt * 128:(mt + 1) * 128],
                            ident_sb, start=True, stop=True),
                            waits=([exs[tt_d]["h_es"][7], ex["h_pH_war"]] if mt == 0
                                   else ([ex["h_pH_war"]] if mt == 8 else [])))
                    hev = b.op("dve", lambda e, mt=mt, pp=pp: e.tensor_tensor(
                        out=r1c[mt], in0=r1c[mt], in1=pp[:, 0:1024], op=ALU.add),
                        waits=[hp, h_r2[mt]])
                    h_r2[mt] = hev
                    evs.append((hp, hev))
                    if do_es and half_mt == 1 and mtg % 4 == 3:
                        base = (mtg // 4) * 8
                        hadd = None
                        for m2 in range(base, base + 8):
                            sl2 = m2 % 8
                            pht2 = pHa if sl2 < 4 else pHb
                            po2 = (sl2 % 4) * 128
                            hadd = b.op("dve", lambda e, m2=m2, pht2=pht2, po2=po2,
                                        tt_d=tt_d: e.tensor_tensor(
                                out=r1c[m2][:, tt_d * 128:(tt_d + 1) * 128],
                                in0=r1c[m2][:, tt_d * 128:(tt_d + 1) * 128],
                                in1=pht2[:, po2:po2 + 128], op=ALU.add),
                                waits=[(htr_es[base + 7] if m2 == base else None),
                                       h_r2[m2]])
                            h_r2[m2] = hadd
                        ex["h_pH_war"] = hadd
            if do_es:
                ex["h_es_tr"] = htr_es[NKC - 1]
            pe_down_done[ig] = evs[-1][0]
            h_devac[ig] = evs[-1][1]
            h_devac_pair = [evs[-2][1], evs[-1][1]]
            tt_p = ig - 1
            if do_ex and 0 <= tt_p < NTT:
                ex_part1(tt_p)
            if ig + 1 < NIG:
                h_dsl[ig + 1] = b.dma("sync", "w_d",
                                      lambda e, ig=ig: e.dma_start(out=BUF3[:, :],
                                                                   in_=dn_view[ig + 1]),
                                      waits=[pe_down_done[ig]])

        r2_ready = [h_r2[mt] for mt in range(NKC)]
        ln2_f, _ = layernorm(r1c, None, r2_ready,
                             pre_bcast_waits=[h_aev[NISL - 1], h_devac[NIG - 1]],
                             first_cast_extra=last(*[h_r2[m] for m in range(NKC)]))
        for kc in range(NKC):
            b.dma("sync", "st_out",
                  lambda e, kc=kc: e.dma_start(out=out_view[kc], in_=r1c[kc]),
                  waits=[ln2_f[kc]])
        b.wait("pe", [("d", "st_out", b.dmacnt["st_out"])])
        _emit_streams(block, b)

    return nc


def _emit_streams(block, b):
    @block.tensor
    def _(eo):
        for f in b.q["pe"]:
            f(eo)

    @block.scalar
    def _(eo):
        for f in b.q["act"]:
            f(eo)

    @block.vector
    def _(eo):
        for f in b.q["dve"]:
            f(eo)

    @block.gpsimd
    def _(eo):
        for f in b.q["pool"]:
            f(eo)

    @block.sync
    def _(eo):
        for f in b.q["sync"]:
            f(eo)



def _pretile_lhsT(w):
    K, M = w.shape
    return np.ascontiguousarray(
        w.reshape(K // 128, 128, M // 128, 128).transpose(2, 1, 0, 3).reshape(M, K))


def _prep_inputs(inputs):
    f32 = np.float32
    g = {k: np.ascontiguousarray(np.asarray(v, f32)) for k, v in inputs.items()}
    win, wout = g["L0_attn_proj_in_w"], g["L0_attn_proj_out_w"]
    pre = {}
    pre["winT"] = _pretile_lhsT(np.ascontiguousarray(win.T))
    pre["woutT"] = _pretile_lhsT(np.ascontiguousarray(wout.T))
    pre["bwT"] = np.ascontiguousarray(
        g["L0_attn_B_w"].T.reshape(16, 128, 16).transpose(1, 0, 2).reshape(128, 256))
    pre["cwT"] = np.ascontiguousarray(g["L0_attn_C_w"].T)
    pre["a_t"] = np.ascontiguousarray(
        g["L0_attn_A"].reshape(16, 128, 16).transpose(1, 0, 2).reshape(128, 256))
    pre["dp_t"] = np.ascontiguousarray(g["L0_attn_D_param"].reshape(16, 128).T)
    pre["bin_t"] = np.ascontiguousarray(g["L0_attn_proj_in_b"].reshape(16, 128).T)
    pre["bout_t"] = np.ascontiguousarray(g["L0_attn_proj_out_b"].reshape(16, 128).T)
    pre["wq_t"] = np.ascontiguousarray(
        g["L0_ffn_W_q"].reshape(16, 128, 128).transpose(1, 0, 2).reshape(128, 2048))
    gate_t = g["L0_ffn_gate_w"].reshape(16, 128, 64, 128).transpose(2, 1, 0, 3).reshape(64, 128, 2048)
    up_t = g["L0_ffn_up_w"].reshape(16, 128, 64, 128).transpose(2, 1, 0, 3).reshape(64, 128, 2048)
    pre["gu_t"] = np.ascontiguousarray(
        np.concatenate([gate_t, up_t], axis=2).reshape(8192, 4096))
    pre["down_t"] = np.ascontiguousarray(
        g["L0_ffn_down_w"].reshape(NIG, IGS, 128, 16, 128).transpose(0, 2, 1, 3, 4)
        .reshape(NIG * 128, IGS * 2048))
    pre["duemb"] = np.ascontiguousarray(
        np.concatenate([g["L0_ffn_down_embed"], g["L0_ffn_up_embed"]], axis=1))
    return g, pre


def kernel(**inputs):
    import ml_dtypes
    bf = ml_dtypes.bfloat16
    f32 = np.float32
    g, pre = _prep_inputs(inputs)
    x = g["L0_attn_x"]
    keys = g["L0_ffn_keys"]

    c_ident = np.eye(128, dtype=f32).astype(bf)
    c_iotam = np.tile(np.arange(64, dtype=f32) - BIGC, (128, 1)).astype(f32)
    kp = np.zeros((NLAG, S), f32) + np.arange(NLAG, dtype=f32)[:, None]
    c_kpat = np.ascontiguousarray(np.tile(kp.reshape(1, -1), (128, 1)))
    c_ones_col = np.ones((128, 1), f32).astype(bf)
    c_ones_f32 = np.ones((1, 128), f32)
    c_scal = np.tile(np.array([[1.0, 1e-5]], f32), (128, 1))
    io = np.arange(64, dtype=f32)
    c_ioteps = np.tile(np.concatenate([io * 2.0 ** -20, io * 2.0 ** -17])[None, :], (128, 1))

    nc = build_nc()
    in_maps = []
    for c in range(8):
        bidx, h = c // 2, c % 2
        s0 = h * 1024
        if h == 0:
            halo = np.zeros((HALO, D), f32)
        else:
            halo = x[bidx, s0 - HALO:s0]
        x_tc = np.ascontiguousarray(np.concatenate([halo, x[bidx, s0:s0 + 1024]], axis=0).T)
        side = 0 if bidx < 2 else 1
        hm = np.full((128, 1), 0.0 if h == 0 else 1.0, f32)
        sxy = np.empty((128, 16), np.int32)
        base = 1024 * c
        for tt in range(8):
            sxy[:, tt] = base + tt * 128 + np.arange(128)
            sxy[:, 8 + tt] = 8192 + base + tt * 128 + np.arange(128)
        im = dict(x_t=x_tc, mykeys=np.ascontiguousarray(keys[side]),
                  sxy_rows=sxy, c_ident=c_ident, c_iotam=c_iotam, c_kpat=c_kpat,
                  c_ones_col=c_ones_col, c_ones_f32=c_ones_f32, c_scal=c_scal,
                  c_ioteps=np.ascontiguousarray(c_ioteps), halo_mask=hm, **pre)
        in_maps.append(im)

    res = run_bass_kernel_spmd(nc, in_maps, core_ids=list(range(8)))
    out = np.empty((4, 2048, 2048), np.float32)
    for c in range(8):
        bidx, h = c // 2, c % 2
        out[bidx, h * 1024:(h + 1) * 1024, :] = res.results[c]["out_t"].T
    return out


# version 14
# speedup vs baseline: 2.2059x; 1.0539x over previous
import numpy as np
import concourse.bass as bass
import concourse.mybir as mybir
from concourse.bass_utils import run_bass_kernel_spmd
from concourse.bass_core_rust import IndirectOffsetOnAxis

F32 = mybir.dt.float32
BF16 = mybir.dt.bfloat16
I32 = mybir.dt.int32
AF = mybir.ActivationFunctionType
ALU = mybir.AluOpType
AX = mybir.AxisListType

D = 2048
S = 16
NKC = 16
T = 1024
HALO = 64
TT = T + HALO
NLAG = 64
NG = 8
INTER = 8192
NISL = 64
IGS = 4
NIG = NISL // IGS
NTT = 8
BIGC = float(2 ** 22)

ENGS = ("pe", "act", "dve", "pool", "sync")


class Bld:

    def __init__(self, nc):
        self.nc = nc
        self.q = {e: [] for e in ENGS}
        self.tick = {e: 0 for e in ENGS}
        self.sem = {}
        self.dmacnt = {}

    def op(self, eng, fn, waits=(), inc=True):
        serial = eng != "pe"
        if inc:
            self.tick[eng] += 1
        t = self.tick[eng]
        waits = tuple(w for w in waits if w is not None)

        def emit(eo, waits=waits, t=t, eng=eng, inc=inc, serial=serial):
            for w in waits:
                if w[0] == "e":
                    if w[1] == eng and w[2] <= (t - 1 if inc else t):
                        continue
                    eo.wait_ge(self.sem[w[1]], w[2])
                else:
                    eo.wait_ge(self.sem[w[1]], 16 * w[2])
            inst = fn(eo)
            if serial and inc and t > 1:
                inst._wait_ge(self.sem[eng], t - 1)
            if inc:
                inst.then_inc(self.sem[eng], 1)

        self.q[eng].append(emit)
        return ("e", eng, t) if inc else None

    def dma(self, qeng, stream, fn, waits=()):
        self.dmacnt[stream] = self.dmacnt.get(stream, 0) + 1
        c = self.dmacnt[stream]
        waits = tuple(w for w in waits if w is not None) + (
            (("d", stream, c - 1),) if c >= 2 else ())

        def emit(eo, waits=waits, c=c):
            for w in waits:
                if w[0] == "e":
                    eo.wait_ge(self.sem[w[1]], w[2])
                else:
                    eo.wait_ge(self.sem[w[1]], 16 * w[2])
            fn(eo).then_inc(self.sem[stream], 16)

        self.q[qeng].append(emit)
        return ("d", stream, c)

    def wait(self, eng, waits):
        waits = tuple(w for w in waits if w is not None)

        def emit(eo, waits=waits):
            for w in waits:
                if w[0] == "e":
                    eo.wait_ge(self.sem[w[1]], w[2])
                else:
                    eo.wait_ge(self.sem[w[1]], 16 * w[2])

        self.q[eng].append(emit)


def last(*hs):
    hs = [h for h in hs if h is not None]
    assert all(h[0] == hs[0][0] and h[1] == hs[0][1] for h in hs)
    return max(hs, key=lambda h: h[2])


STAGE = "full"


def build_nc(debug_taps=(), stage=None):
    stage = stage or STAGE
    nc = bass.Bass("TRN2", target_bir_lowering=False)
    b = Bld(nc)

    def din(name, shape, dt=F32):
        return nc.dram_tensor(name, shape, dt, kind="ExternalInput").ap()

    x_t = din("x_t", [D, TT])
    winT = din("winT", [D, D])
    woutT = din("woutT", [D, D])
    bwT = din("bwT", [128, 256])
    cwT = din("cwT", [S, D])
    a_t = din("a_t", [128, 256])
    dp_t = din("dp_t", [128, NKC])
    bin_t = din("bin_t", [128, NKC])
    bout_t = din("bout_t", [128, NKC])
    wq_t = din("wq_t", [128, D])
    mykeys = din("mykeys", [64, 64])
    gu_t = din("gu_t", [INTER, 2 * D])
    down_t = din("down_t", [NIG * 128, IGS * 2048])
    duemb = din("duemb", [4096, 2 * D])
    sxy_rows = din("sxy_rows", [128, 16], I32)
    c_ident = din("c_ident", [128, 128], BF16)
    c_iotam = din("c_iotam", [128, 64])
    c_kpat = din("c_kpat", [128, NLAG * S])
    c_ones_col = din("c_ones_col", [128, 1], BF16)
    c_scal = din("c_scal", [128, 2])
    c_ioteps = din("c_ioteps", [128, 128])
    halo_mask = din("halo_mask", [128, 1])
    c_ones_f32 = din("c_ones_f32", [1, 128])

    out_t = nc.dram_tensor("out_t", [D, T], F32, kind="ExternalOutput").ap()

    cc_in = nc.dram_tensor("cc_in", [2048, 16], F32, kind="Internal").ap()
    cc_out = nc.dram_tensor("cc_out", [8 * 2048, 16], F32, kind="Internal").ap()
    duemb_bf = nc.dram_tensor("duemb_bf", [4096, 2 * D], BF16, kind="Internal").ap()
    dn_bf = nc.dram_tensor("dn_bf", [NIG * 128, IGS * 2048], BF16, kind="Internal").ap()

    import contextlib
    es = contextlib.ExitStack()
    with es:
        E = es.enter_context

        BUF1 = E(nc.sbuf_tensor("BUF1", [128, NKC * TT], BF16))
        BUF2 = E(nc.sbuf_tensor("BUF2", [128, NKC * TT], BF16))
        BUF3 = E(nc.sbuf_tensor("BUF3", [128, 8192], BF16))
        R1 = E(nc.sbuf_tensor("R1", [128, NKC * T], F32))
        X1 = E(nc.sbuf_tensor("X1", [128, NKC * T], BF16))
        MISC = E(nc.sbuf_tensor("MISC", [128, 2944], F32))
        BNDL = E(nc.sbuf_tensor("BNDL", [128, 4096], BF16))
        RTG = E(nc.sbuf_tensor("RTG", [128, 1024], F32))
        CONS = E(nc.sbuf_tensor("CONS", [128, 384], F32))
        CONSB = E(nc.sbuf_tensor("CONSB", [128, 384], BF16))
        IDT = E(nc.sbuf_tensor("IDT", [128, 128], BF16))
        PACK = E(nc.sbuf_tensor("PACK", [128, 64], F32))
        SXI = E(nc.sbuf_tensor("SXI", [128, 16], I32))
        SMI = E(nc.sbuf_tensor("SMI", [128, 32], I32))
        ONESF = E(nc.sbuf_tensor("ONESF", [128, 128], F32))

        pA = E(nc.psum_tensor("pA", [128, 1024], F32))
        pB = E(nc.psum_tensor("pB", [128, 1024], F32))
        pHa = E(nc.psum_tensor("pHa", [128, 512], F32))
        pHb = E(nc.psum_tensor("pHb", [128, 512], F32))
        pC = E(nc.psum_tensor("pC", [128, 1024], F32))

        dma_streams = ["c_const", "c_cb", "c_x", "c_tab", "w_a", "w_b", "w_d",
                       "g_de", "g_ue", "x_pack", "g_cc", "g_cc2", "st_out", "m_dma"]
        for e_ in ENGS:
            b.sem[e_] = E(nc.semaphore("s_" + e_))
        for s_ in dma_streams:
            b.sem[s_] = E(nc.semaphore("s_" + s_))

        block = E(nc.Block())

        x_bf = [BUF1[:, i * TT:(i + 1) * TT] for i in range(NKC)]
        xp_bf = [BUF2[:, i * TT:(i + 1) * TT] for i in range(NKC)]
        y_bf = [xp_bf[i][:, HALO:TT] for i in range(NKC)]
        r1c = [R1[:, i * T:(i + 1) * T] for i in range(NKC)]
        x1c = [X1[:, i * T:(i + 1) * T] for i in range(NKC)]
        wslab = [X1[:, 0:2048], X1[:, 2048:4096]]
        bndl = [BNDL[:, i * 2048:(i + 1) * 2048] for i in range(2)]
        wqs = BNDL[:, 0:2048]
        qbf = BNDL[:, 2048:3072]
        CWT = BNDL[0:16, 0:2048]
        GT = BNDL[0:16, 2048:3072]
        a_grp = BUF2[:, 0:IGS * T]
        wgu_sl = [BUF2[:, 4096 + i * 4096:4096 + (i + 1) * 4096] for i in range(2)]
        wg_sl = [wgu_sl[i][:, 0:2048] for i in range(2)]
        wu_sl = [wgu_sl[i][:, 2048:4096] for i in range(2)]
        du_b = [BUF1[:, 0:4096], BUF1[:, 4096:8192], BUF2[:, 12288:16384]]
        x1tok = BUF1[:, 8192:10240]
        es_acc = BUF1[:, 10240:12288]
        ttrj = BUF1[:, 12288:14336]
        gs_bf = BUF1[:, 14336:15360]
        ln_scr = [BUF1[:, 15360:16384], BUF1[:, 16384:17408]]
        wgb = [BUF3[:, 0:TT], BUF3[:, TT:2 * TT]]
        dlt_rep = MISC[:, 0:TT]
        la_v = MISC[:, 1088:1344]
        sg_v = MISC[:, 1344:1600]
        kbase = MISC[:, 1600:1728]
        kprod = MISC[:, 1728:1856]
        t1_sp = MISC[0:16, 1856:2944]
        GXY = MISC[:, 1856:2112]
        eqbuf = MISC[:, 0:512]
        cbuf = MISC[:, 512:1024]
        ew_f = MISC[:, 1024:1032]
        s8 = MISC[:, 1032:1040]
        sm8 = MISC[:, 1040:1048]
        sil8 = MISC[:, 1048:1056]
        ewf = MISC[:, 1056:1064]
        cidxm = MISC[:, 1064:1072]
        ix64 = MISC[:, 1072:1080]
        offs_f = MISC[:, 1080:1088]
        mx1 = MISC[:, 1088:1089]
        sume = MISC[:, 1089:1090]
        rsum = MISC[:, 1090:1091]
        st_mean = MISC[0:1, 1092:2116]
        eps1_sb = MISC[:, 2116:2180]
        eps2_sb = MISC[:, 2180:2244]
        sm8s = MISC[:, 2244:2260]
        st_rstd = RTG[64:65, 0:1024]
        st_t1 = RTG[0:1, 0:1024]
        a_sb = CONS[:, 0:256]
        hmask_sb = CONS[:, 370:371]
        dp_sb = CONS[:, 256:272]
        bin_sb = CONS[:, 272:288]
        bout_sb = CONS[:, 288:304]
        iotam_sb = CONS[:, 304:368]
        one_col = CONS[:, 368:369]
        eps_col = CONS[:, 369:370]
        ident_sb = IDT[:, :]
        bw_sb = CONSB[:, 0:256]
        keys_sb = CONSB[0:64, 256:320]
        ones_col_sb = CONSB[:, 320:321]

        tap_out = {}
        for name, shape, dt_ in debug_taps:
            tap_out[name] = nc.dram_tensor("tap_" + name, list(shape), dt_, kind="ExternalOutput").ap()

        def emit_tap(name, sb_ap, waits):
            if name in tap_out:
                b.dma("sync", "st_out",
                      lambda e, n=name, s=sb_ap: e.dma_start(out=tap_out[n][:], in_=s),
                      waits=waits)

        for dst, src in [(a_sb, a_t[:]), (dp_sb, dp_t[:]), (bin_sb, bin_t[:]),
                         (bout_sb, bout_t[:]), (iotam_sb, c_iotam[:]),
                         (ONESF[0:1, :], c_ones_f32[:]), (ONESF[64:65, :], c_ones_f32[:]),
                         (CONS[:, 368:370], c_scal[:]), (CONS[:, 370:371], halo_mask[:])]:
            b.dma("sync", "c_const", lambda e, d=dst, s=src: e.dma_start(out=d, in_=s))
        b.dma("sync", "c_const", lambda e: e.dma_start(out=SXI[:], in_=sxy_rows[:]))
        h_const = ("d", "c_const", b.dmacnt["c_const"])
        for dst, src in [(ident_sb, c_ident[:]), (ones_col_sb, c_ones_col[:])]:
            b.dma("sync", "c_const", lambda e, d=dst, s=src: e.dma_start(out=d, in_=s))
        h_const = ("d", "c_const", b.dmacnt["c_const"])
        for dst, src in [(bw_sb, bwT[:]), (keys_sb, mykeys[:]),
                         (CONSB[64:128, 256:320], mykeys[:])]:
            b.dma("pool", "c_cb", lambda e, d=dst, s=src: e.dma_start(out=d, in_=s))
        h_cb = ("d", "c_cb", b.dmacnt["c_cb"])

        x_view = x_t.rearrange("(kc p) t -> p kc t", p=128)
        buf1_3d = BUF1[:].rearrange("p (kc t) -> p kc t", t=TT)
        for i in range(4):
            b.dma("pool", "c_x",
                  lambda e, i=i: e.dma_start(out=buf1_3d[:, i * 4:(i + 1) * 4, :],
                                             in_=x_view[:, i * 4:(i + 1) * 4, :]))
        h_x = ("d", "c_x", 4)

        h_sg = b.op("act", lambda e: e.activation(sg_v, a_sb, AF.Sigmoid), waits=[h_const])
        h_la = b.op("act", lambda e: e.activation(la_v, sg_v, AF.Ln), waits=[h_sg])

        bnst = {"last_hm": None, "last_he": None}

        def build_bundle(g, war):
            buf = bndl[g % 2]
            hk = b.dma("sync", "m_dma",
                       lambda e, g=g: e.dma_start(out=kbase, in_=c_kpat[:, g * 128:(g + 1) * 128]),
                       waits=[war, bnst["last_hm"]])
            h_prev = bnst["last_he"]
            hm = None
            for kc in range(NKC):
                la_rep = la_v[:, kc * S:(kc + 1) * S].unsqueeze(1).to_broadcast([128, 8, S])
                hm = b.op("dve", lambda e, la_rep=la_rep: e.tensor_tensor(
                    out=kprod.rearrange("p (k s) -> p k s", s=S),
                    in0=kbase.rearrange("p (k s) -> p k s", s=S),
                    in1=la_rep, op=ALU.mult), waits=[hk, h_la, h_prev, h_const])
                h_prev = b.op("act", lambda e, kc=kc, buf=buf: e.activation(
                    buf[:, kc * 128:(kc + 1) * 128], kprod, AF.Exp), waits=[hm, war])
            bnst["last_hm"] = hm
            bnst["last_he"] = h_prev
            return h_prev

        win_view = winT.rearrange("(mt p) k -> mt p k", p=128)
        wout_view = woutT.rearrange("(mt p) k -> mt p k", p=128)
        gu_view = gu_t.rearrange("(mt p) k -> mt p k", p=128)
        down_view = down_t.rearrange("(ig p) k -> ig p k", p=128)
        dn_view = dn_bf.rearrange("(ig p) k -> ig p k", p=128)
        tab_i = [0]

        def emit_tab_chunk():
            ch = tab_i[0]
            if ch >= 48:
                return
            tab_i[0] += 1
            if ch < 32:
                b.dma("pool", "c_tab",
                      lambda e, ch=ch: e.dma_start(
                          out=duemb_bf[ch * 128:(ch + 1) * 128, :],
                          in_=duemb[ch * 128:(ch + 1) * 128, :]))
            else:
                c2 = ch - 32
                b.dma("pool", "c_tab",
                      lambda e, c2=c2: e.dma_start(
                          out=dn_bf[c2 * 128:(c2 + 1) * 128, :],
                          in_=down_t[c2 * 128:(c2 + 1) * 128, :]))

        evac_xp = [None] * NKC
        pe_done_win = [None] * NKC
        for mt in range(NKC):
            par = mt % 2
            war = pe_done_win[mt - 2] if mt >= 2 else None
            h_wd = b.dma("pool", "w_a" if par == 0 else "w_b",
                         lambda e, mt=mt, par=par: e.dma_start(out=wslab[par], in_=win_view[mt]),
                         waits=[war])
            ppsum = pA if par == 0 else pB
            war_ps = evac_xp[mt - 2] if mt >= 2 else None
            phs = (pHa if par == 0 else pHb)[:, 0:64]
            hp = None
            for kc in range(NKC):
                b.op("pe", lambda e, kc=kc, par=par, phs=phs: e.matmul(
                    phs, wslab[par][:, kc * 128:(kc + 1) * 128],
                    x_bf[kc][:, 0:64], start=(kc == 0), stop=(kc == NKC - 1)),
                    waits=([h_wd, h_x, war_ps] + ([h_const, h_cb] if mt == 0 else [])
                           if kc == 0 else []), inc=False)
                for fo in (0, 512):
                    fin = kc == NKC - 1 and fo == 512
                    hp = b.op("pe", lambda e, kc=kc, par=par, ppsum=ppsum, fo=fo: e.matmul(
                        ppsum[:, fo:fo + 512], wslab[par][:, kc * 128:(kc + 1) * 128],
                        x_bf[kc][:, 64 + fo:64 + fo + 512],
                        start=(kc == 0), stop=(kc == NKC - 1)), inc=fin) or hp
            pe_done_win[mt] = hp
            b.op("dve", lambda e, mt=mt, phs=phs: e.tensor_scalar(
                out=xp_bf[mt][:, 0:64], in0=phs, scalar1=bin_sb[:, mt:mt + 1],
                scalar2=hmask_sb, op0=ALU.add, op1=ALU.mult), waits=[hp, h_const])
            evac_xp[mt] = b.op("dve", lambda e, mt=mt, ppsum=ppsum: e.tensor_scalar(
                out=xp_bf[mt][:, 64:TT], in0=ppsum[:, 0:1024], scalar1=bin_sb[:, mt:mt + 1],
                scalar2=None, op0=ALU.add), waits=[hp])

        for _ in range(36):
            emit_tab_chunk()
        h_b = [None] * NG
        h_b[0] = build_bundle(0, None)

        h_dlt_mm = None
        for kc in range(NKC):
            b.op("pe", lambda e, kc=kc: e.matmul(
                pHa[0:16, 0:64], bw_sb[:, kc * S:(kc + 1) * S], xp_bf[kc][:, 0:64],
                start=(kc == 0), stop=(kc == NKC - 1)),
                waits=([evac_xp[14], evac_xp[15], h_const] if kc == 0 else [evac_xp[kc]]),
                inc=False)
            for fo in (0, 512):
                fin = kc == NKC - 1 and fo == 512
                h_dlt_mm = b.op("pe", lambda e, kc=kc, fo=fo: e.matmul(
                    pA[0:16, fo:fo + 512], bw_sb[:, kc * S:(kc + 1) * S],
                    xp_bf[kc][:, 64 + fo:64 + fo + 512],
                    start=(kc == 0), stop=(kc == NKC - 1)), inc=fin) or h_dlt_mm
        h1 = b.op("act", lambda e: e.activation(t1_sp[:, 0:64], pHa[0:16, 0:64], AF.Abs),
                  waits=[h_dlt_mm])
        h1 = b.op("act", lambda e: e.activation(t1_sp[:, 64:TT], pA[0:16, 0:1024], AF.Abs),
                  waits=[h1])
        h1 = b.op("act", lambda e: e.activation(t1_sp, t1_sp, AF.Exp, scale=-1.0), waits=[h1])
        h1 = b.op("act", lambda e: e.activation(t1_sp, t1_sp, AF.Ln, bias=one_col[0:16, :]),
                  waits=[h1, h_const])
        h2 = b.op("dve", lambda e: e.tensor_scalar(out=MISC[0:16, 0:64], in0=pHa[0:16, 0:64],
                                                   scalar1=0.0, scalar2=None, op0=ALU.max),
                  waits=[h_dlt_mm])
        h2 = b.op("dve", lambda e: e.tensor_scalar(out=MISC[0:16, 64:TT], in0=pA[0:16, 0:1024],
                                                   scalar1=0.0, scalar2=None, op0=ALU.max),
                  waits=[h2])
        h_dlt = b.op("dve", lambda e: e.tensor_tensor(out=MISC[0:16, 0:TT], in0=MISC[0:16, 0:TT],
                                                      in1=t1_sp, op=ALU.add), waits=[h1, h2])
        for k in range(1, 8):
            b.dma("sync", "m_dma",
                  lambda e, k=k: e.dma_start(out=MISC[16 * k:16 * (k + 1), 0:TT],
                                             in_=MISC[0:16, 0:TT]), waits=[h_dlt])
        h_rep = ("d", "m_dma", b.dmacnt["m_dma"])

        h_ioteps = b.dma("sync", "m_dma",
                         lambda e: e.dma_start(out=MISC[:, 2116:2244], in_=c_ioteps[:]),
                         waits=[h_dlt])
        h_b[1] = build_bundle(1, None)
        h_wg = [None, None]
        h_sel_done = [None, None]
        h_p_done = [None] * NG
        for g in range(NG):
            par = g % 2
            bbuf = bndl[g % 2]
            ppsum = pA if par == 0 else pB
            phs = (pHa if par == 0 else pHb)[:, 0:64]
            hp = None
            for kc in range(NKC):
                w0 = []
                if kc == 0:
                    w0 = [h_b[g], h_wg[par], (h2 if g == 0 and par == 0 else None),
                          (h1 if g == 0 else None)]
                b.op("pe", lambda e, kc=kc, bbuf=bbuf, phs=phs: e.matmul(
                    phs, bbuf[:, kc * 128:(kc + 1) * 128],
                    xp_bf[kc][:, 0:64], start=(kc == 0), stop=(kc == NKC - 1)), waits=w0,
                    inc=False)
                for fo in (0, 512):
                    fin = kc == NKC - 1 and fo == 512
                    hp = b.op("pe", lambda e, kc=kc, bbuf=bbuf, ppsum=ppsum, fo=fo: e.matmul(
                        ppsum[:, fo:fo + 512], bbuf[:, kc * 128:(kc + 1) * 128],
                        xp_bf[kc][:, 64 + fo:64 + fo + 512],
                        start=(kc == 0), stop=(kc == NKC - 1)), inc=fin) or hp
            h_p_done[g] = hp
            b.op("dve", lambda e, par=par, phs=phs: e.tensor_tensor(
                out=wgb[par][:, 0:64], in0=phs, in1=dlt_rep[:, 0:64], op=ALU.mult),
                waits=[hp, h_rep, h_sel_done[par]])
            h_wg[par] = b.op("dve", lambda e, par=par, ppsum=ppsum: e.tensor_tensor(
                out=wgb[par][:, 64:TT], in0=ppsum[:, 0:1024], in1=dlt_rep[:, 64:TT],
                op=ALU.mult))
            if g + 2 < NG:
                h_b[g + 2] = build_bundle(g + 2, h_p_done[g])
            hsel = None
            for k in range(8):
                sig = 8 * g + k
                for fo in (0, 512):
                    fin = k == 7 and fo == 512
                    hsel = b.op("pe", lambda e, k=k, par=par, sig=sig, fo=fo,
                                st0=(g == 0 and k == 0), sp0=(g == NG - 1 and k == 7): e.matmul(
                        pC[0:16, fo:fo + 512], ident_sb[:, k * 16:(k + 1) * 16],
                        wgb[par][:, HALO - sig + fo:HALO - sig + fo + 512],
                        start=st0, stop=sp0), waits=([h_wg[par]] if (k == 0 and fo == 0) else []),
                        inc=fin) or hsel
            h_sel_done[par] = hsel
        h_gt = b.op("dve", lambda e: e.tensor_copy(out=GT[:, :], in_=pC[0:16, 0:1024]),
                    waits=[h_sel_done[(NG - 1) % 2]])

        h_cwt = b.dma("pool", "c_cb", lambda e: e.dma_start(out=CWT[:, :], in_=cwT[:]),
                      waits=[h_p_done[6]])
        evac_y = [None] * NKC
        for mt in range(NKC):
            par = mt % 2
            ppsum = pA if par == 0 else pB
            war = evac_y[mt - 2] if mt >= 2 else h_wg[par]
            hy = None
            for fo in (0, 512):
                hy = b.op("pe", lambda e, mt=mt, ppsum=ppsum, fo=fo: e.matmul(
                    ppsum[:, fo:fo + 512], CWT[:, mt * 128:(mt + 1) * 128],
                    GT[:, fo:fo + 512], start=True, stop=True),
                    waits=([h_gt, war, h_cwt if mt == 0 else None] if fo == 0 else []),
                    inc=(fo == 512)) or hy
            evac_y[mt] = b.op("dve", lambda e, mt=mt, ppsum=ppsum: e.scalar_tensor_tensor(
                out=y_bf[mt], in0=y_bf[mt], scalar=dp_sb[:, mt:mt + 1], in1=ppsum[:, 0:1024],
                op0=ALU.mult, op1=ALU.add), waits=[hy])

        evac_r1 = [None] * NKC
        pe_done_wout = [None] * NKC
        for mt in range(NKC):
            par = mt % 2
            ppsum = pA if par == 0 else pB
            war_slab = pe_done_wout[mt - 2] if mt >= 2 else pe_done_win[NKC - 1]
            h_wd = b.dma("pool", "w_a" if par == 0 else "w_b",
                         lambda e, mt=mt, par=par: e.dma_start(out=wslab[par], in_=wout_view[mt]),
                         waits=[war_slab])
            if 2 <= mt < 14:
                emit_tab_chunk()
            war_ps = evac_r1[mt - 2] if mt >= 2 else evac_y[14 + par]
            hp = None
            for kc in range(NKC):
                for fo in (0, 512):
                    fin = kc == NKC - 1 and fo == 512
                    hp = b.op("pe", lambda e, mt=mt, kc=kc, par=par, ppsum=ppsum, fo=fo: e.matmul(
                        ppsum[:, fo:fo + 512], wslab[par][:, kc * 128:(kc + 1) * 128],
                        y_bf[kc][:, fo:fo + 512],
                        start=(kc == 0), stop=(kc == NKC - 1)),
                        waits=([h_wd, war_ps, (evac_y[kc] if mt == 0 else None)]
                               if (kc == 0 and fo == 0)
                               else ([evac_y[kc]] if (mt == 0 and fo == 0) else [])),
                        inc=fin) or hp
            pe_done_wout[mt] = hp
            hr = b.op("dve", lambda e, mt=mt, ppsum=ppsum: e.scalar_tensor_tensor(
                out=r1c[mt], in0=x_bf[mt][:, HALO:TT], scalar=2.0, in1=ppsum[:, 0:1024],
                op0=ALU.mult, op1=ALU.add), waits=[hp])
            evac_r1[mt] = b.op("dve", lambda e, mt=mt: e.tensor_scalar(
                out=r1c[mt], in0=r1c[mt], scalar1=bout_sb[:, mt:mt + 1], scalar2=None,
                op0=ALU.add), waits=[hr])

        def layernorm(rc, out_bf, wait_in, pre_bcast_waits, first_cast_extra):
            h_cast = [None] * NKC
            hmm = None
            for kc in range(NKC):
                pr = kc % 2
                h_cast[kc] = b.op("dve", lambda e, kc=kc, pr=pr: e.tensor_copy(
                    out=ln_scr[pr], in_=rc[kc]),
                    waits=[wait_in[kc], (hmm if kc >= 2 else None)]
                          + ([first_cast_extra] if kc == 0 else []))
                for fo in (0, 512):
                    hmm = b.op("pe", lambda e, kc=kc, pr=pr, fo=fo: e.matmul(
                        pC[0:1, fo:fo + 512], ones_col_sb, ln_scr[pr][:, fo:fo + 512],
                        start=(kc == 0), stop=(kc == NKC - 1)),
                        waits=([h_cast[kc]] + (pre_bcast_waits if kc == 0 else [])
                               if fo == 0 else []), inc=(fo == 512)) or hmm
            h_mean = b.op("dve", lambda e: e.tensor_scalar(
                out=st_mean, in0=pC[0:1, 0:1024], scalar1=1.0 / D, scalar2=None, op0=ALU.mult),
                waits=[hmm])
            h_bm = None
            for lo in (0, 512):
                h_bm = b.op("pe", lambda e, lo=lo: e.matmul(
                    pA[:, lo:lo + 512], ONESF[0:1, :], st_mean[:, lo:lo + 512],
                    start=True, stop=True),
                    waits=[h_mean] + (pre_bcast_waits if lo == 0 else []),
                    inc=(lo == 512)) or h_bm
            hmm2 = None
            h_sub = [None] * NKC
            h_sq = [None] * NKC
            for kc in range(NKC):
                pr = kc % 2
                h_sub[kc] = b.op("dve", lambda e, kc=kc: e.tensor_tensor(
                    out=rc[kc], in0=rc[kc], in1=pA[:, 0:1024], op=ALU.subtract),
                    waits=[h_bm, h_cast[kc]])
                h_sq[kc] = b.op("act", lambda e, kc=kc, pr=pr: e.activation(
                    ln_scr[pr], rc[kc], AF.Square),
                    waits=[h_sub[kc], (hmm2 if kc >= 2 else None), (hmm if kc < 2 else None)])
                for fo in (0, 512):
                    hmm2 = b.op("pe", lambda e, kc=kc, pr=pr, fo=fo: e.matmul(
                        pC[0:1, fo:fo + 512], ones_col_sb, ln_scr[pr][:, fo:fo + 512],
                        start=(kc == 0), stop=(kc == NKC - 1)),
                        waits=([h_sq[kc], (h_mean if kc == 0 else None)] if fo == 0 else []),
                        inc=(fo == 512)) or hmm2
            hv = b.op("dve", lambda e: e.tensor_scalar(
                out=st_t1, in0=pC[0:1, 0:1024], scalar1=1.0 / D, scalar2=None, op0=ALU.mult),
                waits=[hmm2])
            hv = b.op("act", lambda e: e.activation(st_t1, st_t1, AF.Sqrt,
                                                    bias=eps_col[0:1, :]), waits=[hv])
            hv = b.op("dve", lambda e: e.reciprocal(st_rstd, st_t1), waits=[hv])
            h_br = None
            for lo in (0, 512):
                h_br = b.op("pe", lambda e, lo=lo: e.matmul(
                    pB[:, lo:lo + 512], ONESF[64:65, :], st_rstd[:, lo:lo + 512],
                    start=True, stop=True),
                    waits=[hv if lo == 0 else None], inc=(lo == 512)) or h_br
            f32_h, bf_h = [], []
            for kc in range(NKC):
                hf = b.op("dve", lambda e, kc=kc: e.tensor_tensor(
                    out=rc[kc], in0=rc[kc], in1=pB[:, 0:1024], op=ALU.mult),
                    waits=[h_br, h_sq[kc]])
                f32_h.append(hf)
                if out_bf is not None:
                    bf_h.append(b.op("act", lambda e, kc=kc: e.activation(
                        out_bf[kc], rc[kc], AF.Copy), waits=[hf]))
            return f32_h, bf_h

        x1_f, x1_b = layernorm(r1c, x1c, evac_r1,
                               pre_bcast_waits=[evac_r1[14], evac_r1[15]],
                               first_cast_extra=evac_r1[15])

        emit_tap("xp", BUF2[:, :], [evac_y[15]])
        emit_tap("dlt", MISC[:, 0:TT], [h_rep])
        emit_tap("gt", GT[:, :], [h_gt])
        emit_tap("x1", X1[:, :], [x1_b[15]] if x1_b else [])
        emit_tap("r1", R1[:, :], [x1_f[15]])

        out_view = out_t.rearrange("(kc p) t -> kc p t", p=128)
        if stage == "attn":
            for kc in range(NKC):
                b.dma("sync", "st_out",
                      lambda e, kc=kc: e.dma_start(out=out_view[kc], in_=r1c[kc]),
                      waits=[x1_f[kc]])
            b.wait("pe", [("d", "st_out", b.dmacnt["st_out"])])
            _emit_streams(block, b)
            return nc

        lvls = {"attn": (0, 9), "swiglu": (1, 9), "route1": (2, 9), "route2": (3, 9),
                "ex_a": (4, 1), "ex_b1": (4, 1.2), "ex_b2": (4, 1.5), "ex_b": (4, 2),
                "ex_c": (4, 3), "nocc": (4, 9), "full": (5, 9),
                "ex_b1_t1": (4, 1.2), "ex_b_t1": (4, 2), "ex_b0_t1": (4, 1.1),
                "ex_bx_t1": (4, 1.15)}
        only_tt0 = stage.endswith("_t1")
        lvl, _exlvl = lvls[stage]
        do_route = lvl >= 2
        h_wq = b.dma("pool", "c_cb", lambda e: e.dma_start(out=wqs, in_=wq_t[:]),
                     waits=[h_p_done[6]])
        hq = None
        for kc in (range(NKC) if lvl >= 2 else ()):
            for fo in (0, 512):
                fin = kc == NKC - 1 and fo == 512
                hq = b.op("pe", lambda e, kc=kc, fo=fo: e.matmul(
                    pC[:, fo:fo + 512], wqs[:, kc * 128:(kc + 1) * 128],
                    x1c[kc][:, fo:fo + 512],
                    start=(kc == 0), stop=(kc == NKC - 1)),
                    waits=([h_wq, x1_b[kc]] if fo == 0 else []), inc=fin) or hq
        h_qe = b.op("dve", lambda e: e.tensor_copy(out=qbf, in_=pC[:, 0:1024]),
                    waits=[hq, h_p_done[7]])

        h_rt = [None] * 16
        h_pack = [None] * 16

        def emit_route_tail(jj):
            if not do_route:
                return
            for pt4 in range(4):
                idx = 4 * jj + pt4
                half, pt = idx // 8, idx % 8
                slot = idx % 4
                hr = b.op("pe", lambda e, half=half, pt=pt: e.matmul(
                    pC[:, 0:64], qbf[64 * half:64 * half + 64, pt * 128:(pt + 1) * 128],
                    CONSB[64 * half:64 * half + 64, 256:320], start=True, stop=True),
                    waits=[(h_qe if idx == 0 else None), (h_rt[idx - 1] if idx > 0 else None)])
                h_rt[idx] = b.op("dve", lambda e, idx=idx: e.tensor_tensor(
                    out=RTG[:, idx * 64:(idx + 1) * 64], in0=pC[:, 0:64], in1=eps1_sb,
                    op=ALU.subtract), waits=[hr, h_ioteps if idx == 0 else None])
                sl = RTG[:, idx * 64:(idx + 1) * 64]
                pk = PACK[:, slot * 16:slot * 16 + 16]
                hk8 = b.op("dve", lambda e, sl=sl, pk=pk: e.max(out=pk[:, 0:8], in_=sl),
                           waits=[h_rt[idx], (h_pack[idx - 4] if idx >= 4 else None)])
                sl_rep = sl.unsqueeze(1).to_broadcast([128, 8, 64])
                k8_rep = pk[:, 0:8].unsqueeze(2).to_broadcast([128, 8, 64])
                heq = b.op("dve", lambda e, sl_rep=sl_rep, k8_rep=k8_rep: e.tensor_tensor(
                    out=eqbuf.rearrange("p (i j) -> p i j", j=64), in0=sl_rep, in1=k8_rep,
                    op=ALU.is_equal), waits=[hk8])
                iot_rep = iotam_sb.unsqueeze(1).to_broadcast([128, 8, 64])
                hc = b.op("dve", lambda e, iot_rep=iot_rep: e.tensor_tensor(
                    out=cbuf.rearrange("p (i j) -> p i j", j=64),
                    in0=eqbuf.rearrange("p (i j) -> p i j", j=64), in1=iot_rep, op=ALU.mult),
                    waits=[heq, h_const])
                hmin = b.op("dve", lambda e, pk=pk: e.tensor_reduce(
                    out=pk[:, 8:16], in_=cbuf.rearrange("p (i j) -> p i j", j=64),
                    axis=AX.X, op=ALU.min), waits=[hc])
                dst = cc_in.rearrange("(p2 two) v -> two p2 v", two=2)[half, pt * 128:(pt + 1) * 128, :]
                h_pack[idx] = b.dma("sync", "x_pack",
                                    lambda e, dst=dst, pk=pk: e.dma_start(out=dst, in_=pk),
                                    waits=[hmin])
        h_gsl = [None] * NISL
        h_usl = [None] * NISL
        for isl in range(2):
            par = isl % 2
            h_gsl[isl] = b.dma("pool", "w_a" if par == 0 else "w_b",
                               lambda e, isl=isl, par=par: e.dma_start(
                                   out=wgu_sl[par], in_=gu_view[isl]),
                               waits=[pe_done_wout[NKC - 1] if isl == 0 else None])
            h_usl[isl] = h_gsl[isl]
        h_dsl = [None] * NIG
        h_dsl[0] = b.dma("sync", "w_d",
                         lambda e: e.dma_start(out=BUF3[:, :], in_=dn_view[0]),
                         waits=[h_sel_done[0], h_sel_done[1], ("d", "c_tab", 48)])

        h_gx = [None] * NTT

        def emit_collective():
            if lvl >= 5:
                h_cc = b.op("pool", lambda e: e.collective_compute(
                    "AllGather", ALU.bypass,
                    replica_groups=[[0, 1, 2, 3, 4, 5, 6, 7]],
                    ins=[cc_in.opt()], outs=[cc_out.opt()]),
                    waits=[("d", "x_pack", b.dmacnt["x_pack"])])
            elif lvl >= 3:
                h_cc = b.op("pool", lambda e: e.memset(GXY[:, :], 0.0),
                            waits=[("d", "x_pack", b.dmacnt["x_pack"])])
            if lvl >= 3:
                for tt in range(NTT):
                    gst = "g_cc" if tt % 2 == 0 else "g_cc2"
                    b.dma("pool", gst, lambda e, tt=tt: e.indirect_dma_start(
                        out=GXY[:, tt * 16:tt * 16 + 16], out_offset=None, in_=cc_out[:],
                        in_offset=IndirectOffsetOnAxis(SXI[:, tt:tt + 1], axis=0)),
                        waits=[h_cc if tt < 2 else None, h_const if tt < 2 else None])
                    b.dma("pool", gst, lambda e, tt=tt: e.indirect_dma_start(
                        out=GXY[:, 128 + tt * 16:128 + tt * 16 + 16], out_offset=None,
                        in_=cc_out[:],
                        in_offset=IndirectOffsetOnAxis(SXI[:, 8 + tt:8 + tt + 1], axis=0)))
                    h_gx[tt] = ("d", gst, b.dmacnt[gst])
            emit_tap("rtg", RTG[:, :], [h_rt[15]])
            emit_tap("gxy", GXY[:, :], [h_gx[NTT - 1]])

        pe_gate_done = [None] * NISL
        pe_up_done = [None] * NISL
        h_aev = [None] * NISL
        h_sil = [None] * NISL
        pe_down_done = [None] * NIG
        h_devac = [None] * NIG
        h_r2 = {mt: x1_f[mt] for mt in range(NKC)}
        ex = {
            "h_smi_reader": [(), (), (), ()],
            "h_s8_act": None,
            "h_x1tok_reader": None,
            "h_du_war": [None, None, None],
            "h_es_tr": None,
            "h_es_reader": None,
            "h_pH_war": last(h_wg[0], h_wg[1]),
            "h_pA_war": None,
            "h_ttrj_act": None,
        }
        eq2 = RTG[:, 0:512]
        c2b = RTG[:, 512:1024]
        exs = [dict() for _ in range(NTT)]

        def ex_part1(tt):
            st = exs[tt]
            slot = tt % 4
            sm_slot = tt % 2
            isl_last = (tt + 1) * IGS + IGS - 1
            h_xe = None
            for wave in range(2):
                htrs = None
                for k8 in range(8):
                    kc = wave * 8 + k8
                    htrs = b.op("pe", lambda e, kc=kc, k8=k8, tt=tt: e.matmul(
                        pA[:, k8 * 128:(k8 + 1) * 128],
                        x1c[kc][:, tt * 128:(tt + 1) * 128], ident_sb,
                        start=True, stop=True),
                        waits=([h_sil[isl_last], h_xe,
                                (x1_b[kc] if tt == 0 else None)] if k8 == 0 else []))
                for k8 in range(8):
                    kc = wave * 8 + k8
                    h_xe = b.op("act", lambda e, kc=kc, k8=k8: e.activation(
                        x1tok[:, kc * 128:(kc + 1) * 128],
                        pA[:, k8 * 128:(k8 + 1) * 128], AF.Copy),
                        waits=[(htrs if k8 == 0 else None),
                               (ex["h_x1tok_reader"] if wave == 0 and k8 == 0 else None)])
            ex["h_pA_war"] = h_xe
            st["h_xt"] = h_xe
            w0 = [h_gx[tt]]
            gxv = GXY[:, tt * 16:tt * 16 + 8]
            gxi = GXY[:, tt * 16 + 8:tt * 16 + 16]
            gyv = GXY[:, 128 + tt * 16:128 + tt * 16 + 8]
            gyi = GXY[:, 128 + tt * 16 + 8:128 + tt * 16 + 16]
            h = None
            for i in range(8):
                h = b.op("dve", lambda e, i=i, gyv=gyv, gxv=gxv: e.tensor_scalar(
                    out=eqbuf[:, i * 8:(i + 1) * 8], in0=gyv, scalar1=gxv[:, i:i + 1],
                    scalar2=None, op0=ALU.add), waits=(w0 if i == 0 else []))
            h_as = b.op("dve", lambda e: e.tensor_tensor(
                out=eqbuf[:, 0:64], in0=eqbuf[:, 0:64], in1=eps2_sb, op=ALU.subtract),
                waits=[h])
            h = b.op("dve", lambda e, gxi=gxi: e.tensor_scalar(
                out=ix64, in0=gxi, scalar1=BIGC, scalar2=64.0, op0=ALU.add, op1=ALU.mult),
                waits=w0)
            for i in range(8):
                h = b.op("dve", lambda e, i=i, gyi=gyi: e.tensor_scalar(
                    out=cbuf[:, i * 8:(i + 1) * 8], in0=gyi, scalar1=ix64[:, i:i + 1],
                    scalar2=None, op0=ALU.add), waits=([h] if i == 0 else []))
            h_ci = h
            h_s8 = b.op("dve", lambda e: e.max(out=s8, in_=eqbuf[:, 0:64]),
                        waits=[h_as, ex["h_s8_act"]])
            as_rep = eqbuf[:, 0:64].unsqueeze(1).to_broadcast([128, 8, 64])
            s8_rep = s8.unsqueeze(2).to_broadcast([128, 8, 64])
            h_eq = b.op("dve", lambda e, as_rep=as_rep, s8_rep=s8_rep: e.tensor_tensor(
                out=eq2.rearrange("p (i j) -> p i j", j=64), in0=as_rep, in1=s8_rep,
                op=ALU.is_equal), waits=[h_s8])
            ci_rep = cbuf[:, 0:64].unsqueeze(1).to_broadcast([128, 8, 64])
            h_c2 = b.op("dve", lambda e, ci_rep=ci_rep: e.tensor_tensor(
                out=c2b.rearrange("p (i j) -> p i j", j=64),
                in0=eq2.rearrange("p (i j) -> p i j", j=64), in1=ci_rep, op=ALU.mult),
                waits=[h_eq, h_ci])
            h_cm = b.op("dve", lambda e: e.tensor_reduce(
                out=cidxm, in_=c2b.rearrange("p (i j) -> p i j", j=64),
                axis=AX.X, op=ALU.min), waits=[h_c2])
            h_of = b.op("dve", lambda e: e.tensor_scalar(
                out=offs_f, in0=cidxm, scalar1=BIGC, scalar2=4095.0,
                op0=ALU.add, op1=ALU.min), waits=[h_cm])
            h_of = b.op("dve", lambda e: e.tensor_scalar(
                out=offs_f, in0=offs_f, scalar1=0.0, scalar2=None, op0=ALU.max),
                waits=[h_of])
            h_oi = b.op("dve", lambda e, slot=slot: e.tensor_copy(
                out=SMI[:, slot * 8:slot * 8 + 8], in_=offs_f),
                waits=[h_of, *ex["h_smi_reader"][slot]])
            sm8v = sm8s[:, sm_slot * 8:sm_slot * 8 + 8]
            h_m = b.op("dve", lambda e: e.tensor_reduce(out=mx1, in_=s8, axis=AX.X,
                                                        op=ALU.max, negate=True), waits=[h_s8])
            h_e = b.op("act", lambda e, sm8v=sm8v: e.activation(
                sm8v, s8, AF.Exp, bias=mx1[:, 0:1], scale=1.0, accum_out=sume),
                waits=[h_m, (exs[tt - 2].get("h_sm_last") if tt >= 2 else None)])
            ex["h_s8_act"] = h_e
            h_rc = b.op("dve", lambda e: e.reciprocal(rsum, sume), waits=[h_e])
            h_sm = b.op("dve", lambda e, sm8v=sm8v: e.tensor_scalar(
                out=sm8v, in0=sm8v, scalar1=rsum[:, 0:1], scalar2=None, op0=ALU.mult),
                waits=[h_rc])
            st["h_sm"] = h_sm
            st["sm8v"] = sm8v
            st["h_oi"] = h_oi
            st["slot"] = slot
            st["h_dot"] = [None] * 8
            st["h_dm"] = [None] * 8
            st["h_es"] = [None] * 8
            st["h_deg"] = [None] * 8

        def ex_dots_step(tt, j):
            st = exs[tt]
            slot = st["slot"]
            sm8v = st["sm8v"]
            for r in (2 * j, 2 * j + 1):
                pr = r % 3
                war = ex["h_du_war"][pr]
                gst = "g_de" if r % 2 == 0 else "g_ue"
                st["h_deg"][r] = b.dma("pool", gst,
                                       lambda e, r=r, pr=pr, slot=slot: e.indirect_dma_start(
                                           out=du_b[pr], out_offset=None, in_=duemb_bf[:],
                                           in_offset=IndirectOffsetOnAxis(
                                               SMI[:, slot * 8 + r:slot * 8 + r + 1], axis=0)),
                                       waits=[(st["h_oi"] if r <= 1 else None),
                                              (("d", "c_tab", 32) if (tt == 0 and r <= 1)
                                               else None), war])
                h_dm = b.op("dve", lambda e, pr=pr: e.tensor_tensor(
                    out=ttrj, in0=x1tok, in1=du_b[pr][:, 0:2048], op=ALU.mult),
                    waits=[st["h_deg"][r], (st["h_dot"][r - 1] if r >= 1 else None)]
                          + ([st["h_xt"], ex["h_ttrj_act"]] if r == 0 else []))
                st["h_dm"][r] = h_dm
                st["h_dot"][r] = b.op("act", lambda e, r=r: e.activation(
                    ttrj, ttrj, AF.Copy, accum_out=ew_f[:, r:r + 1]), waits=[h_dm])
                h_sil8 = b.op("act", lambda e, r=r: e.activation(
                    sil8[:, r:r + 1], ew_f[:, r:r + 1], AF.Silu), waits=[st["h_dot"][r]])
                h_ewf = b.op("dve", lambda e, r=r, sm8v=sm8v: e.tensor_scalar(
                    out=ewf[:, r:r + 1], in0=sil8[:, r:r + 1],
                    scalar1=sm8v[:, r:r + 1], scalar2=None, op0=ALU.mult),
                    waits=[h_sil8, st["h_sm"] if r == 0 else None])
                if r == 0:
                    h_es = b.op("dve", lambda e, pr=pr: e.tensor_scalar(
                        out=es_acc, in0=du_b[pr][:, 2048:4096], scalar1=ewf[:, 0:1],
                        scalar2=None, op0=ALU.mult),
                        waits=[h_ewf, ex["h_es_reader"], ex["h_es_tr"]])
                else:
                    h_es = b.op("dve", lambda e, r=r, pr=pr: e.scalar_tensor_tensor(
                        out=es_acc, in0=du_b[pr][:, 2048:4096], scalar=ewf[:, r:r + 1],
                        in1=es_acc, op0=ALU.mult, op1=ALU.add),
                        waits=[h_ewf, st["h_es"][r - 1]])
                st["h_es"][r] = h_es
                ex["h_du_war"][pr] = h_es
            if j == 3:
                st["h_sm_last"] = st["h_es"][7]
                ex["h_ttrj_act"] = st["h_dot"][7]
                ex["h_smi_reader"][slot] = (st["h_deg"][6], st["h_deg"][7])
                ex["h_x1tok_reader"] = st["h_dm"][7]
                ex["h_es_reader"] = st["h_es"][7]


        do_ex = lvl >= 4
        for ig in range(NIG):
            tt_d = ig - 2
            for j in range(IGS):
                isl = ig * IGS + j
                par = isl % 2
                if isl >= 2 and h_gsl[isl] is None:
                    h_gsl[isl] = b.dma("pool", "w_a" if par == 0 else "w_b",
                                       lambda e, isl=isl, par=par: e.dma_start(
                                           out=wgu_sl[par], in_=gu_view[isl]),
                                       waits=[pe_up_done[isl - 2]])
                    h_usl[isl] = h_gsl[isl]
                war_gate = h_sil[isl - 1] if isl >= 1 else x1_f[NKC - 1]
                if j >= 1:
                    war_up = h_aev[isl - 1]
                elif ig == 0:
                    war_up = x1_f[NKC - 1]
                else:
                    war_up = h_devac_pair[0]
                hp = None
                for kc in range(NKC):
                    for fo in (0, 512):
                        fin = kc == NKC - 1 and fo == 512
                        hp = b.op("pe", lambda e, isl=isl, kc=kc, par=par, fo=fo: e.matmul(
                            pA[:, fo:fo + 512], wg_sl[par][:, kc * 128:(kc + 1) * 128],
                            x1c[kc][:, fo:fo + 512],
                            start=(kc == 0), stop=(kc == NKC - 1)),
                            waits=([h_gsl[isl], war_gate,
                                    (ex["h_pA_war"] if j == 0 else None),
                                    (x1_b[kc] if isl == 0 else None)]
                                   if (kc == 0 and fo == 0) else []), inc=fin) or hp
                pe_gate_done[isl] = hp
                h_sil[isl] = b.op("act", lambda e: e.activation(gs_bf, pA[:, 0:1024], AF.Silu),
                                  waits=[hp, (h_aev[isl - 1] if isl >= 1 else None)])
                hp = None
                for kc in range(NKC):
                    for fo in (0, 512):
                        fin = kc == NKC - 1 and fo == 512
                        hp = b.op("pe", lambda e, isl=isl, kc=kc, par=par, fo=fo: e.matmul(
                            pB[:, fo:fo + 512], wu_sl[par][:, kc * 128:(kc + 1) * 128],
                            x1c[kc][:, fo:fo + 512],
                            start=(kc == 0), stop=(kc == NKC - 1)),
                            waits=([h_usl[isl], war_up] if (kc == 0 and fo == 0) else []),
                            inc=fin) or hp
                pe_up_done[isl] = hp
                h_aev[isl] = b.op("dve", lambda e, j=j: e.tensor_tensor(
                    out=a_grp[:, j * 1024:(j + 1) * 1024], in0=pB[:, 0:1024], in1=gs_bf,
                    op=ALU.mult), waits=[hp, h_sil[isl]])
                if ig == 0:
                    emit_route_tail(j)
                if do_ex and 0 <= tt_d < NTT:
                    ex_dots_step(tt_d, j)
            if ig == 0:
                for isl_h in (4, 5):
                    par_h = isl_h % 2
                    h_gsl[isl_h] = b.dma("pool", "w_a" if par_h == 0 else "w_b",
                                         lambda e, isl=isl_h, par=par_h: e.dma_start(
                                             out=wgu_sl[par], in_=gu_view[isl]),
                                         waits=[pe_up_done[isl_h - 2]])
                    h_usl[isl_h] = h_gsl[isl_h]
                emit_collective()
            do_es = do_ex and 0 <= tt_d < NTT
            evs = []
            htr_es = [None] * NKC
            for mtg in range(8):
                for half_mt in range(2):
                    mt = mtg * 2 + half_mt
                    pp = pB if half_mt == 0 else pC
                    hp = None
                    for kc in range(IGS):
                        w0 = []
                        if kc == 0:
                            w0 = [h_dsl[ig] if mt == 0 else None,
                                  (evs[mt - 2][1] if mt >= 2 else None),
                                  (h_aev[ig * IGS + IGS - 1] if mt < 2 else None)]
                            if mt < 2 and ig == 0:
                                w0 += [x1_f[NKC - 1], h_rt[15]]
                            if mt < 2 and ig >= 1:
                                w0 += [h_devac_pair[half_mt]]
                        if mt == 0:
                            w0 += [h_aev[ig * IGS + kc]]
                        for fo in (0, 512):
                            fin = kc == IGS - 1 and fo == 512
                            hp = b.op("pe", lambda e, mt=mt, kc=kc, pp=pp, fo=fo: e.matmul(
                                pp[:, fo:fo + 512],
                                BUF3[:, kc * 2048 + mt * 128:kc * 2048 + (mt + 1) * 128],
                                a_grp[:, kc * 1024 + fo:kc * 1024 + fo + 512],
                                start=(kc == 0), stop=(kc == IGS - 1)),
                                waits=(w0 if fo == 0 else []), inc=fin) or hp
                    if do_es and mt >= 8:
                        for ki in range(2):
                            kk = (mt - 8) * 2 + ki if mt < 12 else (mt - 12) * 2 + 8 + ki
                            sl = kk % 8
                            pht = pHa if sl < 4 else pHb
                            po = (sl % 4) * 128
                            htr_es[kk] = b.op("pe", lambda e, kk=kk, pht=pht, po=po: e.matmul(
                                pht[:, po:po + 128], es_acc[:, kk * 128:(kk + 1) * 128],
                                ident_sb, start=True, stop=True),
                                waits=([exs[tt_d]["h_es"][7], ex["h_pH_war"]] if kk == 0
                                       else ([ex["h_pH_war"]] if kk == 8 else [])))
                    hev = b.op("dve", lambda e, mt=mt, pp=pp: e.tensor_tensor(
                        out=r1c[mt], in0=r1c[mt], in1=pp[:, 0:1024], op=ALU.add),
                        waits=[hp, h_r2[mt]])
                    h_r2[mt] = hev
                    evs.append((hp, hev))
                    if do_es and mt in (11, 15):
                        base = 0 if mt == 11 else 8
                        hadd = None
                        for m2 in range(base, base + 8):
                            sl2 = m2 % 8
                            pht2 = pHa if sl2 < 4 else pHb
                            po2 = (sl2 % 4) * 128
                            hadd = b.op("dve", lambda e, m2=m2, pht2=pht2, po2=po2,
                                        tt_d=tt_d: e.tensor_tensor(
                                out=r1c[m2][:, tt_d * 128:(tt_d + 1) * 128],
                                in0=r1c[m2][:, tt_d * 128:(tt_d + 1) * 128],
                                in1=pht2[:, po2:po2 + 128], op=ALU.add),
                                waits=[(htr_es[base + 7] if m2 == base else None),
                                       h_r2[m2]])
                            h_r2[m2] = hadd
                        ex["h_pH_war"] = hadd
            if do_es:
                ex["h_es_tr"] = htr_es[NKC - 1]
            pe_down_done[ig] = evs[-1][0]
            h_devac[ig] = evs[-1][1]
            h_devac_pair = [evs[-2][1], evs[-1][1]]
            tt_p = ig - 1
            if do_ex and 0 <= tt_p < NTT:
                ex_part1(tt_p)
            if ig + 1 < NIG:
                h_dsl[ig + 1] = b.dma("sync", "w_d",
                                      lambda e, ig=ig: e.dma_start(out=BUF3[:, :],
                                                                   in_=dn_view[ig + 1]),
                                      waits=[pe_down_done[ig]])

        r2_ready = [h_r2[mt] for mt in range(NKC)]
        ln2_f, _ = layernorm(r1c, None, r2_ready,
                             pre_bcast_waits=[h_aev[NISL - 1], h_devac[NIG - 1]],
                             first_cast_extra=last(*[h_r2[m] for m in range(NKC)]))
        for kc in range(NKC):
            b.dma("sync", "st_out",
                  lambda e, kc=kc: e.dma_start(out=out_view[kc], in_=r1c[kc]),
                  waits=[ln2_f[kc]])
        b.wait("pe", [("d", "st_out", b.dmacnt["st_out"])])
        _emit_streams(block, b)

    return nc


def _emit_streams(block, b):
    @block.tensor
    def _(eo):
        for f in b.q["pe"]:
            f(eo)

    @block.scalar
    def _(eo):
        for f in b.q["act"]:
            f(eo)

    @block.vector
    def _(eo):
        for f in b.q["dve"]:
            f(eo)

    @block.gpsimd
    def _(eo):
        for f in b.q["pool"]:
            f(eo)

    @block.sync
    def _(eo):
        for f in b.q["sync"]:
            f(eo)



def _pretile_lhsT(w):
    K, M = w.shape
    return np.ascontiguousarray(
        w.reshape(K // 128, 128, M // 128, 128).transpose(2, 1, 0, 3).reshape(M, K))


def _prep_inputs(inputs):
    f32 = np.float32
    g = {k: np.ascontiguousarray(np.asarray(v, f32)) for k, v in inputs.items()}
    win, wout = g["L0_attn_proj_in_w"], g["L0_attn_proj_out_w"]
    pre = {}
    pre["winT"] = _pretile_lhsT(np.ascontiguousarray(win.T))
    pre["woutT"] = _pretile_lhsT(np.ascontiguousarray(wout.T))
    pre["bwT"] = np.ascontiguousarray(
        g["L0_attn_B_w"].T.reshape(16, 128, 16).transpose(1, 0, 2).reshape(128, 256))
    pre["cwT"] = np.ascontiguousarray(g["L0_attn_C_w"].T)
    pre["a_t"] = np.ascontiguousarray(
        g["L0_attn_A"].reshape(16, 128, 16).transpose(1, 0, 2).reshape(128, 256))
    pre["dp_t"] = np.ascontiguousarray(g["L0_attn_D_param"].reshape(16, 128).T)
    pre["bin_t"] = np.ascontiguousarray(g["L0_attn_proj_in_b"].reshape(16, 128).T)
    pre["bout_t"] = np.ascontiguousarray(g["L0_attn_proj_out_b"].reshape(16, 128).T)
    pre["wq_t"] = np.ascontiguousarray(
        g["L0_ffn_W_q"].reshape(16, 128, 128).transpose(1, 0, 2).reshape(128, 2048))
    gate_t = g["L0_ffn_gate_w"].reshape(16, 128, 64, 128).transpose(2, 1, 0, 3).reshape(64, 128, 2048)
    up_t = g["L0_ffn_up_w"].reshape(16, 128, 64, 128).transpose(2, 1, 0, 3).reshape(64, 128, 2048)
    pre["gu_t"] = np.ascontiguousarray(
        np.concatenate([gate_t, up_t], axis=2).reshape(8192, 4096))
    pre["down_t"] = np.ascontiguousarray(
        g["L0_ffn_down_w"].reshape(NIG, IGS, 128, 16, 128).transpose(0, 2, 1, 3, 4)
        .reshape(NIG * 128, IGS * 2048))
    pre["duemb"] = np.ascontiguousarray(
        np.concatenate([g["L0_ffn_down_embed"], g["L0_ffn_up_embed"]], axis=1))
    return g, pre


def kernel(**inputs):
    import ml_dtypes
    bf = ml_dtypes.bfloat16
    f32 = np.float32
    g, pre = _prep_inputs(inputs)
    x = g["L0_attn_x"]
    keys = g["L0_ffn_keys"]

    c_ident = np.eye(128, dtype=f32).astype(bf)
    c_iotam = np.tile(np.arange(64, dtype=f32) - BIGC, (128, 1)).astype(f32)
    kp = np.zeros((NLAG, S), f32) + np.arange(NLAG, dtype=f32)[:, None]
    c_kpat = np.ascontiguousarray(np.tile(kp.reshape(1, -1), (128, 1)))
    c_ones_col = np.ones((128, 1), f32).astype(bf)
    c_ones_f32 = np.ones((1, 128), f32)
    c_scal = np.tile(np.array([[1.0, 1e-5]], f32), (128, 1))
    io = np.arange(64, dtype=f32)
    c_ioteps = np.tile(np.concatenate([io * 2.0 ** -20, io * 2.0 ** -17])[None, :], (128, 1))

    nc = build_nc()
    in_maps = []
    for c in range(8):
        bidx, h = c // 2, c % 2
        s0 = h * 1024
        if h == 0:
            halo = np.zeros((HALO, D), f32)
        else:
            halo = x[bidx, s0 - HALO:s0]
        x_tc = np.ascontiguousarray(np.concatenate([halo, x[bidx, s0:s0 + 1024]], axis=0).T)
        side = 0 if bidx < 2 else 1
        hm = np.full((128, 1), 0.0 if h == 0 else 1.0, f32)
        sxy = np.empty((128, 16), np.int32)
        base = 1024 * c
        for tt in range(8):
            sxy[:, tt] = base + tt * 128 + np.arange(128)
            sxy[:, 8 + tt] = 8192 + base + tt * 128 + np.arange(128)
        im = dict(x_t=x_tc, mykeys=np.ascontiguousarray(keys[side]),
                  sxy_rows=sxy, c_ident=c_ident, c_iotam=c_iotam, c_kpat=c_kpat,
                  c_ones_col=c_ones_col, c_ones_f32=c_ones_f32, c_scal=c_scal,
                  c_ioteps=np.ascontiguousarray(c_ioteps), halo_mask=hm, **pre)
        in_maps.append(im)

    res = run_bass_kernel_spmd(nc, in_maps, core_ids=list(range(8)))
    out = np.empty((4, 2048, 2048), np.float32)
    for c in range(8):
        bidx, h = c // 2, c % 2
        out[bidx, h * 1024:(h + 1) * 1024, :] = res.results[c]["out_t"].T
    return out


# version 16
# speedup vs baseline: 2.2365x; 1.0139x over previous
import numpy as np
import concourse.bass as bass
import concourse.mybir as mybir
from concourse.bass_utils import run_bass_kernel_spmd
from concourse.bass_core_rust import IndirectOffsetOnAxis

F32 = mybir.dt.float32
BF16 = mybir.dt.bfloat16
I32 = mybir.dt.int32
AF = mybir.ActivationFunctionType
ALU = mybir.AluOpType
AX = mybir.AxisListType

D = 2048
S = 16
NKC = 16
T = 1024
HALO = 64
TT = T + HALO
NLAG = 64
NG = 8
INTER = 8192
NISL = 64
IGS = 4
NIG = NISL // IGS
NTT = 8
BIGC = float(2 ** 22)

ENGS = ("pe", "act", "dve", "pool", "sync")


class Bld:

    def __init__(self, nc):
        self.nc = nc
        self.q = {e: [] for e in ENGS}
        self.tick = {e: 0 for e in ENGS}
        self.sem = {}
        self.dmacnt = {}

    def op(self, eng, fn, waits=(), inc=True):
        serial = eng != "pe"
        if inc:
            self.tick[eng] += 1
        t = self.tick[eng]
        waits = tuple(w for w in waits if w is not None)

        def emit(eo, waits=waits, t=t, eng=eng, inc=inc, serial=serial):
            for w in waits:
                if w[0] == "e":
                    if w[1] == eng and w[2] <= (t - 1 if inc else t):
                        continue
                    eo.wait_ge(self.sem[w[1]], w[2])
                else:
                    eo.wait_ge(self.sem[w[1]], 16 * w[2])
            inst = fn(eo)
            if serial and inc and t > 1:
                inst._wait_ge(self.sem[eng], t - 1)
            if inc:
                inst.then_inc(self.sem[eng], 1)

        self.q[eng].append(emit)
        return ("e", eng, t) if inc else None

    def dma(self, qeng, stream, fn, waits=()):
        self.dmacnt[stream] = self.dmacnt.get(stream, 0) + 1
        c = self.dmacnt[stream]
        waits = tuple(w for w in waits if w is not None) + (
            (("d", stream, c - 1),) if c >= 2 else ())

        def emit(eo, waits=waits, c=c):
            for w in waits:
                if w[0] == "e":
                    eo.wait_ge(self.sem[w[1]], w[2])
                else:
                    eo.wait_ge(self.sem[w[1]], 16 * w[2])
            fn(eo).then_inc(self.sem[stream], 16)

        self.q[qeng].append(emit)
        return ("d", stream, c)

    def wait(self, eng, waits):
        waits = tuple(w for w in waits if w is not None)

        def emit(eo, waits=waits):
            for w in waits:
                if w[0] == "e":
                    eo.wait_ge(self.sem[w[1]], w[2])
                else:
                    eo.wait_ge(self.sem[w[1]], 16 * w[2])

        self.q[eng].append(emit)


def last(*hs):
    hs = [h for h in hs if h is not None]
    assert all(h[0] == hs[0][0] and h[1] == hs[0][1] for h in hs)
    return max(hs, key=lambda h: h[2])


STAGE = "full"


def build_nc(debug_taps=(), stage=None):
    stage = stage or STAGE
    nc = bass.Bass("TRN2", target_bir_lowering=False)
    b = Bld(nc)

    def din(name, shape, dt=F32):
        return nc.dram_tensor(name, shape, dt, kind="ExternalInput").ap()

    x_t = din("x_t", [D, TT])
    winT = din("winT", [D, D])
    woutT = din("woutT", [D, D])
    bwT = din("bwT", [128, 256])
    cwT = din("cwT", [S, D])
    a_t = din("a_t", [128, 256])
    dp_t = din("dp_t", [128, NKC])
    bin_t = din("bin_t", [128, NKC])
    bout_t = din("bout_t", [128, NKC])
    wq_t = din("wq_t", [128, D])
    mykeys = din("mykeys", [64, 64])
    gu_t = din("gu_t", [INTER, 2 * D])
    down_t = din("down_t", [NIG * 128, IGS * 2048])
    duemb = din("duemb", [4096, 2 * D])
    sxy_rows = din("sxy_rows", [128, 16], I32)
    c_ident = din("c_ident", [128, 128], BF16)
    c_iotam = din("c_iotam", [128, 64])
    c_kpat = din("c_kpat", [128, NLAG * S])
    c_ones_col = din("c_ones_col", [128, 1], BF16)
    c_scal = din("c_scal", [128, 2])
    c_ioteps = din("c_ioteps", [128, 128])
    halo_mask = din("halo_mask", [128, 1])
    c_ones_f32 = din("c_ones_f32", [1, 128])

    out_t = nc.dram_tensor("out_t", [D, T], F32, kind="ExternalOutput").ap()

    cc_in = nc.dram_tensor("cc_in", [2048, 16], F32, kind="Internal").ap()
    cc_out = nc.dram_tensor("cc_out", [8 * 2048, 16], F32, kind="Internal").ap()
    duemb_bf = nc.dram_tensor("duemb_bf", [4096, 2 * D], BF16, kind="Internal").ap()
    dn_bf = nc.dram_tensor("dn_bf", [NIG * 128, IGS * 2048], BF16, kind="Internal").ap()

    import contextlib
    es = contextlib.ExitStack()
    with es:
        E = es.enter_context

        BUF1 = E(nc.sbuf_tensor("BUF1", [128, NKC * TT], BF16))
        BUF2 = E(nc.sbuf_tensor("BUF2", [128, NKC * TT], BF16))
        BUF3 = E(nc.sbuf_tensor("BUF3", [128, 8192], BF16))
        R1 = E(nc.sbuf_tensor("R1", [128, NKC * T], F32))
        X1 = E(nc.sbuf_tensor("X1", [128, NKC * T], BF16))
        MISC = E(nc.sbuf_tensor("MISC", [128, 2944], F32))
        BNDL = E(nc.sbuf_tensor("BNDL", [128, 4096], BF16))
        RTG = E(nc.sbuf_tensor("RTG", [128, 1024], F32))
        CONS = E(nc.sbuf_tensor("CONS", [128, 384], F32))
        CONSB = E(nc.sbuf_tensor("CONSB", [128, 384], BF16))
        IDT = E(nc.sbuf_tensor("IDT", [128, 128], BF16))
        PACK = E(nc.sbuf_tensor("PACK", [128, 64], F32))
        SXI = E(nc.sbuf_tensor("SXI", [128, 16], I32))
        SMI = E(nc.sbuf_tensor("SMI", [128, 32], I32))
        ONESF = E(nc.sbuf_tensor("ONESF", [128, 128], F32))

        pA = E(nc.psum_tensor("pA", [128, 1024], F32))
        pB = E(nc.psum_tensor("pB", [128, 1024], F32))
        pHa = E(nc.psum_tensor("pHa", [128, 512], F32))
        pHb = E(nc.psum_tensor("pHb", [128, 512], F32))
        pC = E(nc.psum_tensor("pC", [128, 1024], F32))

        dma_streams = ["c_const", "c_cb", "c_x", "c_tab", "w_a", "w_b", "w_d",
                       "g_de", "g_ue", "x_pack", "g_cc", "g_cc2", "g_cc3", "g_cc4",
                       "st_out", "m_dma"]
        for e_ in ENGS:
            b.sem[e_] = E(nc.semaphore("s_" + e_))
        for s_ in dma_streams:
            b.sem[s_] = E(nc.semaphore("s_" + s_))

        block = E(nc.Block())

        x_bf = [BUF1[:, i * TT:(i + 1) * TT] for i in range(NKC)]
        xp_bf = [BUF2[:, i * TT:(i + 1) * TT] for i in range(NKC)]
        y_bf = [xp_bf[i][:, HALO:TT] for i in range(NKC)]
        r1c = [R1[:, i * T:(i + 1) * T] for i in range(NKC)]
        x1c = [X1[:, i * T:(i + 1) * T] for i in range(NKC)]
        wslab = [X1[:, 0:2048], X1[:, 2048:4096]]
        bndl = [BNDL[:, i * 2048:(i + 1) * 2048] for i in range(2)]
        wqs = BNDL[:, 0:2048]
        qbf = BNDL[:, 2048:3072]
        CWT = BUF3[0:16, 2176:4224]
        GT = BNDL[0:16, 2048:3072]
        a_grp = BUF2[:, 0:IGS * T]
        wgu_sl = [BUF2[:, 4096 + i * 4096:4096 + (i + 1) * 4096] for i in range(2)]
        wg_sl = [wgu_sl[i][:, 0:2048] for i in range(2)]
        wu_sl = [wgu_sl[i][:, 2048:4096] for i in range(2)]
        du_b = [BUF1[:, 0:4096], BUF1[:, 4096:8192], BUF2[:, 12288:16384]]
        x1tok = BUF1[:, 8192:10240]
        es_acc = BUF1[:, 10240:12288]
        ttrj = BUF1[:, 12288:14336]
        gs_bf = BUF1[:, 14336:15360]
        ln_scr = [BUF1[:, 15360:16384], BUF1[:, 16384:17408]]
        wgb = [BUF3[:, 0:TT], BUF3[:, TT:2 * TT]]
        dlt_rep = MISC[:, 0:TT]
        la_v = MISC[:, 1088:1344]
        sg_v = MISC[:, 1344:1600]
        kbase = MISC[:, 1600:1728]
        kprod = MISC[:, 1728:1856]
        t1_sp = MISC[0:16, 1856:2944]
        GXY = MISC[:, 1856:2112]
        eqbuf = MISC[:, 0:512]
        cbuf = MISC[:, 512:1024]
        ew_f = MISC[:, 1024:1032]
        s8 = MISC[:, 1032:1040]
        sm8 = MISC[:, 1040:1048]
        sil8 = MISC[:, 1048:1056]
        ewf = MISC[:, 1056:1064]
        cidxm = MISC[:, 1064:1072]
        ix64 = MISC[:, 1072:1080]
        offs_f = MISC[:, 1080:1088]
        mx1 = MISC[:, 1088:1089]
        sume = MISC[:, 1089:1090]
        rsum = MISC[:, 1090:1091]
        st_mean = MISC[0:1, 1092:2116]
        eps1_sb = MISC[:, 2116:2180]
        eps2_sb = MISC[:, 2180:2244]
        sm8s = MISC[:, 2244:2260]
        st_rstd = RTG[64:65, 0:1024]
        st_t1 = RTG[0:1, 0:1024]
        a_sb = CONS[:, 0:256]
        hmask_sb = CONS[:, 370:371]
        dp_sb = CONS[:, 256:272]
        bin_sb = CONS[:, 272:288]
        bout_sb = CONS[:, 288:304]
        iotam_sb = CONS[:, 304:368]
        one_col = CONS[:, 368:369]
        eps_col = CONS[:, 369:370]
        ident_sb = IDT[:, :]
        bw_sb = CONSB[:, 0:256]
        keys_sb = CONSB[0:64, 256:320]
        ones_col_sb = CONSB[:, 320:321]

        tap_out = {}
        for name, shape, dt_ in debug_taps:
            tap_out[name] = nc.dram_tensor("tap_" + name, list(shape), dt_, kind="ExternalOutput").ap()

        def emit_tap(name, sb_ap, waits):
            if name in tap_out:
                b.dma("sync", "st_out",
                      lambda e, n=name, s=sb_ap: e.dma_start(out=tap_out[n][:], in_=s),
                      waits=waits)

        for dst, src in [(a_sb, a_t[:]), (dp_sb, dp_t[:]), (bin_sb, bin_t[:]),
                         (bout_sb, bout_t[:]), (iotam_sb, c_iotam[:]),
                         (ONESF[0:1, :], c_ones_f32[:]), (ONESF[64:65, :], c_ones_f32[:]),
                         (CONS[:, 368:370], c_scal[:]), (CONS[:, 370:371], halo_mask[:])]:
            b.dma("sync", "c_const", lambda e, d=dst, s=src: e.dma_start(out=d, in_=s))
        b.dma("sync", "c_const", lambda e: e.dma_start(out=SXI[:], in_=sxy_rows[:]))
        h_const = ("d", "c_const", b.dmacnt["c_const"])
        for dst, src in [(ident_sb, c_ident[:]), (ones_col_sb, c_ones_col[:])]:
            b.dma("sync", "c_const", lambda e, d=dst, s=src: e.dma_start(out=d, in_=s))
        h_const = ("d", "c_const", b.dmacnt["c_const"])
        for dst, src in [(bw_sb, bwT[:]), (keys_sb, mykeys[:]),
                         (CONSB[64:128, 256:320], mykeys[:])]:
            b.dma("pool", "c_cb", lambda e, d=dst, s=src: e.dma_start(out=d, in_=s))
        h_cb = ("d", "c_cb", b.dmacnt["c_cb"])
        h_cwt = b.dma("pool", "c_cb", lambda e: e.dma_start(out=CWT[:, :], in_=cwT[:]))

        x_view = x_t.rearrange("(kc p) t -> p kc t", p=128)
        buf1_3d = BUF1[:].rearrange("p (kc t) -> p kc t", t=TT)
        for i in range(4):
            b.dma("pool", "c_x",
                  lambda e, i=i: e.dma_start(out=buf1_3d[:, i * 4:(i + 1) * 4, :],
                                             in_=x_view[:, i * 4:(i + 1) * 4, :]))
        h_x = ("d", "c_x", 4)

        h_sg = b.op("act", lambda e: e.activation(sg_v, a_sb, AF.Sigmoid), waits=[h_const])
        h_la = b.op("act", lambda e: e.activation(la_v, sg_v, AF.Ln), waits=[h_sg])

        bnst = {"last_hm": None, "last_he": None}

        def build_bundle(g, war):
            buf = bndl[g % 2]
            hk = b.dma("sync", "m_dma",
                       lambda e, g=g: e.dma_start(out=kbase, in_=c_kpat[:, g * 128:(g + 1) * 128]),
                       waits=[war, bnst["last_hm"]])
            h_prev = bnst["last_he"]
            hm = None
            for kc in range(NKC):
                la_rep = la_v[:, kc * S:(kc + 1) * S].unsqueeze(1).to_broadcast([128, 8, S])
                hm = b.op("dve", lambda e, la_rep=la_rep: e.tensor_tensor(
                    out=kprod.rearrange("p (k s) -> p k s", s=S),
                    in0=kbase.rearrange("p (k s) -> p k s", s=S),
                    in1=la_rep, op=ALU.mult), waits=[hk, h_la, h_prev, h_const])
                h_prev = b.op("act", lambda e, kc=kc, buf=buf: e.activation(
                    buf[:, kc * 128:(kc + 1) * 128], kprod, AF.Exp), waits=[hm, war])
            bnst["last_hm"] = hm
            bnst["last_he"] = h_prev
            return h_prev

        win_view = winT.rearrange("(mt p) k -> mt p k", p=128)
        wout_view = woutT.rearrange("(mt p) k -> mt p k", p=128)
        gu_view = gu_t.rearrange("(mt p) k -> mt p k", p=128)
        down_view = down_t.rearrange("(ig p) k -> ig p k", p=128)
        dn_view = dn_bf.rearrange("(ig p) k -> ig p k", p=128)
        tab_i = [0]

        def emit_tab_chunk():
            ch = tab_i[0]
            if ch >= 48:
                return
            tab_i[0] += 1
            if ch < 32:
                b.dma("pool", "c_tab",
                      lambda e, ch=ch: e.dma_start(
                          out=duemb_bf[ch * 128:(ch + 1) * 128, :],
                          in_=duemb[ch * 128:(ch + 1) * 128, :]))
            else:
                c2 = ch - 32
                b.dma("pool", "c_tab",
                      lambda e, c2=c2: e.dma_start(
                          out=dn_bf[c2 * 128:(c2 + 1) * 128, :],
                          in_=down_t[c2 * 128:(c2 + 1) * 128, :]))

        evac_xp = [None] * NKC
        pe_done_win = [None] * NKC
        for mt in range(NKC):
            par = mt % 2
            war = pe_done_win[mt - 2] if mt >= 2 else None
            h_wd = b.dma("pool", "w_a" if par == 0 else "w_b",
                         lambda e, mt=mt, par=par: e.dma_start(out=wslab[par], in_=win_view[mt]),
                         waits=[war])
            ppsum = pA if par == 0 else pB
            war_ps = evac_xp[mt - 2] if mt >= 2 else None
            phs = (pHa if par == 0 else pHb)[:, 0:64]
            hp = None
            for kc in range(NKC):
                b.op("pe", lambda e, kc=kc, par=par, phs=phs: e.matmul(
                    phs, wslab[par][:, kc * 128:(kc + 1) * 128],
                    x_bf[kc][:, 0:64], start=(kc == 0), stop=(kc == NKC - 1)),
                    waits=([h_wd, h_x, war_ps] + ([h_const, h_cb] if mt == 0 else [])
                           if kc == 0 else []), inc=False)
                for fo in (0, 512):
                    fin = kc == NKC - 1 and fo == 512
                    hp = b.op("pe", lambda e, kc=kc, par=par, ppsum=ppsum, fo=fo: e.matmul(
                        ppsum[:, fo:fo + 512], wslab[par][:, kc * 128:(kc + 1) * 128],
                        x_bf[kc][:, 64 + fo:64 + fo + 512],
                        start=(kc == 0), stop=(kc == NKC - 1)), inc=fin) or hp
            pe_done_win[mt] = hp
            b.op("dve", lambda e, mt=mt, phs=phs: e.tensor_scalar(
                out=xp_bf[mt][:, 0:64], in0=phs, scalar1=bin_sb[:, mt:mt + 1],
                scalar2=hmask_sb, op0=ALU.add, op1=ALU.mult), waits=[hp, h_const])
            evac_xp[mt] = b.op("dve", lambda e, mt=mt, ppsum=ppsum: e.tensor_scalar(
                out=xp_bf[mt][:, 64:TT], in0=ppsum[:, 0:1024], scalar1=bin_sb[:, mt:mt + 1],
                scalar2=None, op0=ALU.add), waits=[hp])

        for _ in range(36):
            emit_tab_chunk()
        h_b = [None] * NG
        h_b[0] = build_bundle(0, None)

        h_dlt_mm = None
        for kc in range(NKC):
            b.op("pe", lambda e, kc=kc: e.matmul(
                pHa[0:16, 0:64], bw_sb[:, kc * S:(kc + 1) * S], xp_bf[kc][:, 0:64],
                start=(kc == 0), stop=(kc == NKC - 1)),
                waits=([evac_xp[14], evac_xp[15], h_const] if kc == 0 else [evac_xp[kc]]),
                inc=False)
            for fo in (0, 512):
                fin = kc == NKC - 1 and fo == 512
                h_dlt_mm = b.op("pe", lambda e, kc=kc, fo=fo: e.matmul(
                    pA[0:16, fo:fo + 512], bw_sb[:, kc * S:(kc + 1) * S],
                    xp_bf[kc][:, 64 + fo:64 + fo + 512],
                    start=(kc == 0), stop=(kc == NKC - 1)), inc=fin) or h_dlt_mm
        h1 = b.op("act", lambda e: e.activation(t1_sp[:, 0:64], pHa[0:16, 0:64], AF.Abs),
                  waits=[h_dlt_mm])
        h1 = b.op("act", lambda e: e.activation(t1_sp[:, 64:TT], pA[0:16, 0:1024], AF.Abs),
                  waits=[h1])
        h1 = b.op("act", lambda e: e.activation(t1_sp, t1_sp, AF.Exp, scale=-1.0), waits=[h1])
        h1 = b.op("act", lambda e: e.activation(t1_sp, t1_sp, AF.Ln, bias=one_col[0:16, :]),
                  waits=[h1, h_const])
        h2 = b.op("dve", lambda e: e.tensor_scalar(out=MISC[0:16, 0:64], in0=pHa[0:16, 0:64],
                                                   scalar1=0.0, scalar2=None, op0=ALU.max),
                  waits=[h_dlt_mm])
        h2 = b.op("dve", lambda e: e.tensor_scalar(out=MISC[0:16, 64:TT], in0=pA[0:16, 0:1024],
                                                   scalar1=0.0, scalar2=None, op0=ALU.max),
                  waits=[h2])
        h_dlt = b.op("dve", lambda e: e.tensor_tensor(out=MISC[0:16, 0:TT], in0=MISC[0:16, 0:TT],
                                                      in1=t1_sp, op=ALU.add), waits=[h1, h2])
        for k in range(1, 8):
            b.dma("sync", "m_dma",
                  lambda e, k=k: e.dma_start(out=MISC[16 * k:16 * (k + 1), 0:TT],
                                             in_=MISC[0:16, 0:TT]), waits=[h_dlt])
        h_rep = ("d", "m_dma", b.dmacnt["m_dma"])

        h_ioteps = b.dma("sync", "m_dma",
                         lambda e: e.dma_start(out=MISC[:, 2116:2244], in_=c_ioteps[:]),
                         waits=[h_dlt])
        h_b[1] = build_bundle(1, None)
        h_wg = [None, None]
        h_sel_done = [None, None]
        h_p_done = [None] * NG
        for g in range(NG):
            par = g % 2
            bbuf = bndl[g % 2]
            ppsum = pA if par == 0 else pB
            phs = (pHa if par == 0 else pHb)[:, 0:64]
            hp = None
            for kc in range(NKC):
                w0 = []
                if kc == 0:
                    w0 = [h_b[g], h_wg[par], (h2 if g == 0 and par == 0 else None),
                          (h1 if g == 0 else None)]
                b.op("pe", lambda e, kc=kc, bbuf=bbuf, phs=phs: e.matmul(
                    phs, bbuf[:, kc * 128:(kc + 1) * 128],
                    xp_bf[kc][:, 0:64], start=(kc == 0), stop=(kc == NKC - 1)), waits=w0,
                    inc=False)
                for fo in (0, 512):
                    fin = kc == NKC - 1 and fo == 512
                    hp = b.op("pe", lambda e, kc=kc, bbuf=bbuf, ppsum=ppsum, fo=fo: e.matmul(
                        ppsum[:, fo:fo + 512], bbuf[:, kc * 128:(kc + 1) * 128],
                        xp_bf[kc][:, 64 + fo:64 + fo + 512],
                        start=(kc == 0), stop=(kc == NKC - 1)), inc=fin) or hp
            h_p_done[g] = hp
            b.op("dve", lambda e, par=par, phs=phs: e.tensor_tensor(
                out=wgb[par][:, 0:64], in0=phs, in1=dlt_rep[:, 0:64], op=ALU.mult),
                waits=[hp, h_rep, h_sel_done[par]])
            h_wg[par] = b.op("dve", lambda e, par=par, ppsum=ppsum: e.tensor_tensor(
                out=wgb[par][:, 64:TT], in0=ppsum[:, 0:1024], in1=dlt_rep[:, 64:TT],
                op=ALU.mult))
            if g + 2 < NG:
                h_b[g + 2] = build_bundle(g + 2, h_p_done[g])
            hsel = None
            for k in range(8):
                sig = 8 * g + k
                for fo in (0, 512):
                    fin = k == 7 and fo == 512
                    hsel = b.op("pe", lambda e, k=k, par=par, sig=sig, fo=fo,
                                st0=(g == 0 and k == 0), sp0=(g == NG - 1 and k == 7): e.matmul(
                        pC[0:16, fo:fo + 512], ident_sb[:, k * 16:(k + 1) * 16],
                        wgb[par][:, HALO - sig + fo:HALO - sig + fo + 512],
                        start=st0, stop=sp0), waits=([h_wg[par]] if (k == 0 and fo == 0) else []),
                        inc=fin) or hsel
            h_sel_done[par] = hsel
        h_gt = b.op("dve", lambda e: e.tensor_copy(out=GT[:, :], in_=pC[0:16, 0:1024]),
                    waits=[h_sel_done[(NG - 1) % 2]])

        evac_y = [None] * NKC
        for mt in range(NKC):
            par = mt % 2
            ppsum = pA if par == 0 else pB
            war = evac_y[mt - 2] if mt >= 2 else h_wg[par]
            hy = None
            for fo in (0, 512):
                hy = b.op("pe", lambda e, mt=mt, ppsum=ppsum, fo=fo: e.matmul(
                    ppsum[:, fo:fo + 512], CWT[:, mt * 128:(mt + 1) * 128],
                    GT[:, fo:fo + 512], start=True, stop=True),
                    waits=([h_gt, war, h_cwt if mt == 0 else None] if fo == 0 else []),
                    inc=(fo == 512)) or hy
            evac_y[mt] = b.op("dve", lambda e, mt=mt, ppsum=ppsum: e.scalar_tensor_tensor(
                out=y_bf[mt], in0=y_bf[mt], scalar=dp_sb[:, mt:mt + 1], in1=ppsum[:, 0:1024],
                op0=ALU.mult, op1=ALU.add), waits=[hy])

        evac_r1 = [None] * NKC
        pe_done_wout = [None] * NKC
        for mt in range(NKC):
            par = mt % 2
            ppsum = pA if par == 0 else pB
            war_slab = pe_done_wout[mt - 2] if mt >= 2 else pe_done_win[NKC - 1]
            h_wd = b.dma("pool", "w_a" if par == 0 else "w_b",
                         lambda e, mt=mt, par=par: e.dma_start(out=wslab[par], in_=wout_view[mt]),
                         waits=[war_slab])
            war_ps = evac_r1[mt - 2] if mt >= 2 else evac_y[14 + par]
            hp = None
            for kc in range(NKC):
                for fo in (0, 512):
                    fin = kc == NKC - 1 and fo == 512
                    hp = b.op("pe", lambda e, mt=mt, kc=kc, par=par, ppsum=ppsum, fo=fo: e.matmul(
                        ppsum[:, fo:fo + 512], wslab[par][:, kc * 128:(kc + 1) * 128],
                        y_bf[kc][:, fo:fo + 512],
                        start=(kc == 0), stop=(kc == NKC - 1)),
                        waits=([h_wd, war_ps, (evac_y[kc] if mt == 0 else None)]
                               if (kc == 0 and fo == 0)
                               else ([evac_y[kc]] if (mt == 0 and fo == 0) else [])),
                        inc=fin) or hp
            pe_done_wout[mt] = hp
            hr = b.op("dve", lambda e, mt=mt, ppsum=ppsum: e.scalar_tensor_tensor(
                out=r1c[mt], in0=x_bf[mt][:, HALO:TT], scalar=2.0, in1=ppsum[:, 0:1024],
                op0=ALU.mult, op1=ALU.add), waits=[hp])
            evac_r1[mt] = b.op("dve", lambda e, mt=mt: e.tensor_scalar(
                out=r1c[mt], in0=r1c[mt], scalar1=bout_sb[:, mt:mt + 1], scalar2=None,
                op0=ALU.add), waits=[hr])

        def layernorm(rc, out_bf, wait_in, pre_bcast_waits, first_cast_extra):
            h_cast = [None] * NKC
            hmm = None
            for kc in range(NKC):
                pr = kc % 2
                h_cast[kc] = b.op("dve", lambda e, kc=kc, pr=pr: e.tensor_copy(
                    out=ln_scr[pr], in_=rc[kc]),
                    waits=[wait_in[kc], (hmm if kc >= 2 else None)]
                          + ([first_cast_extra] if kc == 0 else []))
                for fo in (0, 512):
                    hmm = b.op("pe", lambda e, kc=kc, pr=pr, fo=fo: e.matmul(
                        pC[0:1, fo:fo + 512], ones_col_sb, ln_scr[pr][:, fo:fo + 512],
                        start=(kc == 0), stop=(kc == NKC - 1)),
                        waits=([h_cast[kc]] + (pre_bcast_waits if kc == 0 else [])
                               if fo == 0 else []), inc=(fo == 512)) or hmm
            h_mean = b.op("dve", lambda e: e.tensor_scalar(
                out=st_mean, in0=pC[0:1, 0:1024], scalar1=1.0 / D, scalar2=None, op0=ALU.mult),
                waits=[hmm])
            h_bm = None
            for lo in (0, 512):
                h_bm = b.op("pe", lambda e, lo=lo: e.matmul(
                    pA[:, lo:lo + 512], ONESF[0:1, :], st_mean[:, lo:lo + 512],
                    start=True, stop=True),
                    waits=[h_mean] + (pre_bcast_waits if lo == 0 else []),
                    inc=(lo == 512)) or h_bm
            hmm2 = None
            h_sub = [None] * NKC
            h_sq = [None] * NKC
            for kc in range(NKC):
                pr = kc % 2
                h_sub[kc] = b.op("dve", lambda e, kc=kc: e.tensor_tensor(
                    out=rc[kc], in0=rc[kc], in1=pA[:, 0:1024], op=ALU.subtract),
                    waits=[h_bm, h_cast[kc]])
                h_sq[kc] = b.op("act", lambda e, kc=kc, pr=pr: e.activation(
                    ln_scr[pr], rc[kc], AF.Square),
                    waits=[h_sub[kc], (hmm2 if kc >= 2 else None), (hmm if kc < 2 else None)])
                for fo in (0, 512):
                    hmm2 = b.op("pe", lambda e, kc=kc, pr=pr, fo=fo: e.matmul(
                        pC[0:1, fo:fo + 512], ones_col_sb, ln_scr[pr][:, fo:fo + 512],
                        start=(kc == 0), stop=(kc == NKC - 1)),
                        waits=([h_sq[kc], (h_mean if kc == 0 else None)] if fo == 0 else []),
                        inc=(fo == 512)) or hmm2
            hv = b.op("dve", lambda e: e.tensor_scalar(
                out=st_t1, in0=pC[0:1, 0:1024], scalar1=1.0 / D, scalar2=None, op0=ALU.mult),
                waits=[hmm2])
            hv = b.op("act", lambda e: e.activation(st_t1, st_t1, AF.Sqrt,
                                                    bias=eps_col[0:1, :]), waits=[hv])
            hv = b.op("dve", lambda e: e.reciprocal(st_rstd, st_t1), waits=[hv])
            h_br = None
            for lo in (0, 512):
                h_br = b.op("pe", lambda e, lo=lo: e.matmul(
                    pB[:, lo:lo + 512], ONESF[64:65, :], st_rstd[:, lo:lo + 512],
                    start=True, stop=True),
                    waits=[hv if lo == 0 else None], inc=(lo == 512)) or h_br
            f32_h, bf_h = [], []
            for kc in range(NKC):
                hf = b.op("dve", lambda e, kc=kc: e.tensor_tensor(
                    out=rc[kc], in0=rc[kc], in1=pB[:, 0:1024], op=ALU.mult),
                    waits=[h_br, h_sq[kc]])
                f32_h.append(hf)
                if out_bf is not None:
                    bf_h.append(b.op("act", lambda e, kc=kc: e.activation(
                        out_bf[kc], rc[kc], AF.Copy), waits=[hf]))
            return f32_h, bf_h

        for _ in range(12):
            emit_tab_chunk()
        x1_f, x1_b = layernorm(r1c, x1c, evac_r1,
                               pre_bcast_waits=[evac_r1[14], evac_r1[15]],
                               first_cast_extra=evac_r1[15])

        emit_tap("xp", BUF2[:, :], [evac_y[15]])
        emit_tap("dlt", MISC[:, 0:TT], [h_rep])
        emit_tap("gt", GT[:, :], [h_gt])
        emit_tap("x1", X1[:, :], [x1_b[15]] if x1_b else [])
        emit_tap("r1", R1[:, :], [x1_f[15]])

        out_view = out_t.rearrange("(kc p) t -> kc p t", p=128)
        if stage == "attn":
            for kc in range(NKC):
                b.dma("sync", "st_out",
                      lambda e, kc=kc: e.dma_start(out=out_view[kc], in_=r1c[kc]),
                      waits=[x1_f[kc]])
            b.wait("pe", [("d", "st_out", b.dmacnt["st_out"])])
            _emit_streams(block, b)
            return nc

        lvls = {"attn": (0, 9), "swiglu": (1, 9), "route1": (2, 9), "route2": (3, 9),
                "ex_a": (4, 1), "ex_b1": (4, 1.2), "ex_b2": (4, 1.5), "ex_b": (4, 2),
                "ex_c": (4, 3), "nocc": (4, 9), "full": (5, 9),
                "ex_b1_t1": (4, 1.2), "ex_b_t1": (4, 2), "ex_b0_t1": (4, 1.1),
                "ex_bx_t1": (4, 1.15)}
        only_tt0 = stage.endswith("_t1")
        lvl, _exlvl = lvls[stage]
        do_route = lvl >= 2
        h_wq = b.dma("pool", "c_cb", lambda e: e.dma_start(out=wqs, in_=wq_t[:]),
                     waits=[h_p_done[6]])
        hq = None
        for kc in (range(NKC) if lvl >= 2 else ()):
            for fo in (0, 512):
                fin = kc == NKC - 1 and fo == 512
                hq = b.op("pe", lambda e, kc=kc, fo=fo: e.matmul(
                    pC[:, fo:fo + 512], wqs[:, kc * 128:(kc + 1) * 128],
                    x1c[kc][:, fo:fo + 512],
                    start=(kc == 0), stop=(kc == NKC - 1)),
                    waits=([h_wq, x1_b[kc]] if fo == 0 else []), inc=fin) or hq
        h_qe = b.op("dve", lambda e: e.tensor_copy(out=qbf, in_=pC[:, 0:1024]),
                    waits=[hq, h_p_done[7]])

        h_rt = [None] * 16
        h_pack = [None] * 16

        def emit_route_tail(jj):
            if not do_route:
                return
            for pt4 in range(8):
                idx = 8 * jj + pt4
                half, pt = idx // 8, idx % 8
                slot = idx % 4
                hr = b.op("pe", lambda e, half=half, pt=pt: e.matmul(
                    pC[:, 0:64], qbf[64 * half:64 * half + 64, pt * 128:(pt + 1) * 128],
                    CONSB[64 * half:64 * half + 64, 256:320], start=True, stop=True),
                    waits=[(h_qe if idx == 0 else None), (h_rt[idx - 1] if idx > 0 else None)])
                h_rt[idx] = b.op("dve", lambda e, idx=idx: e.tensor_tensor(
                    out=RTG[:, idx * 64:(idx + 1) * 64], in0=pC[:, 0:64], in1=eps1_sb,
                    op=ALU.subtract), waits=[hr, h_ioteps if idx == 0 else None])
                sl = RTG[:, idx * 64:(idx + 1) * 64]
                pk = PACK[:, slot * 16:slot * 16 + 16]
                hk8 = b.op("dve", lambda e, sl=sl, pk=pk: e.max(out=pk[:, 0:8], in_=sl),
                           waits=[h_rt[idx], (h_pack[idx - 4] if idx >= 4 else None)])
                sl_rep = sl.unsqueeze(1).to_broadcast([128, 8, 64])
                k8_rep = pk[:, 0:8].unsqueeze(2).to_broadcast([128, 8, 64])
                heq = b.op("dve", lambda e, sl_rep=sl_rep, k8_rep=k8_rep: e.tensor_tensor(
                    out=eqbuf.rearrange("p (i j) -> p i j", j=64), in0=sl_rep, in1=k8_rep,
                    op=ALU.is_equal), waits=[hk8])
                iot_rep = iotam_sb.unsqueeze(1).to_broadcast([128, 8, 64])
                hc = b.op("dve", lambda e, iot_rep=iot_rep: e.tensor_tensor(
                    out=cbuf.rearrange("p (i j) -> p i j", j=64),
                    in0=eqbuf.rearrange("p (i j) -> p i j", j=64), in1=iot_rep, op=ALU.mult),
                    waits=[heq, h_const])
                hmin = b.op("dve", lambda e, pk=pk: e.tensor_reduce(
                    out=pk[:, 8:16], in_=cbuf.rearrange("p (i j) -> p i j", j=64),
                    axis=AX.X, op=ALU.min), waits=[hc])
                dst = cc_in.rearrange("(p2 two) v -> two p2 v", two=2)[half, pt * 128:(pt + 1) * 128, :]
                h_pack[idx] = b.dma("sync", "x_pack",
                                    lambda e, dst=dst, pk=pk: e.dma_start(out=dst, in_=pk),
                                    waits=[hmin])
        h_gsl = [None] * NISL
        h_usl = [None] * NISL
        for isl in range(2):
            par = isl % 2
            h_gsl[isl] = b.dma("pool", "w_a" if par == 0 else "w_b",
                               lambda e, isl=isl, par=par: e.dma_start(
                                   out=wgu_sl[par], in_=gu_view[isl]),
                               waits=[pe_done_wout[NKC - 1] if isl == 0 else None])
            h_usl[isl] = h_gsl[isl]
        h_dsl = [None] * NIG
        h_dsl[0] = b.dma("sync", "w_d",
                         lambda e: e.dma_start(out=BUF3[:, :], in_=dn_view[0]),
                         waits=[h_sel_done[0], h_sel_done[1], evac_y[NKC - 1],
                                ("d", "c_tab", 48)])

        h_gx = [None] * NTT

        def emit_collective():
            if lvl >= 5:
                h_cc = b.op("pool", lambda e: e.collective_compute(
                    "AllGather", ALU.bypass,
                    replica_groups=[[0, 1, 2, 3, 4, 5, 6, 7]],
                    ins=[cc_in.opt()], outs=[cc_out.opt()]),
                    waits=[("d", "x_pack", b.dmacnt["x_pack"])])
            elif lvl >= 3:
                h_cc = b.op("pool", lambda e: e.memset(GXY[:, :], 0.0),
                            waits=[("d", "x_pack", b.dmacnt["x_pack"])])
            if lvl >= 3:
                for tt in range(NTT):
                    gst = ("g_cc", "g_cc2", "g_cc3", "g_cc4")[tt % 4]
                    b.dma("pool", gst, lambda e, tt=tt: e.indirect_dma_start(
                        out=GXY[:, tt * 16:tt * 16 + 16], out_offset=None, in_=cc_out[:],
                        in_offset=IndirectOffsetOnAxis(SXI[:, tt:tt + 1], axis=0)),
                        waits=[h_cc if tt < 4 else None, h_const if tt < 4 else None])
                    b.dma("pool", gst, lambda e, tt=tt: e.indirect_dma_start(
                        out=GXY[:, 128 + tt * 16:128 + tt * 16 + 16], out_offset=None,
                        in_=cc_out[:],
                        in_offset=IndirectOffsetOnAxis(SXI[:, 8 + tt:8 + tt + 1], axis=0)))
                    h_gx[tt] = ("d", gst, b.dmacnt[gst])
            emit_tap("rtg", RTG[:, :], [h_rt[15]])
            emit_tap("gxy", GXY[:, :], [h_gx[NTT - 1]])

        pe_gate_done = [None] * NISL
        pe_up_done = [None] * NISL
        h_aev = [None] * NISL
        h_sil = [None] * NISL
        pe_down_done = [None] * NIG
        h_devac = [None] * NIG
        h_r2 = {mt: x1_f[mt] for mt in range(NKC)}
        ex = {
            "h_smi_reader": [(), (), (), ()],
            "h_s8_act": None,
            "h_x1tok_reader": None,
            "h_du_war": [None, None, None],
            "h_es_tr": None,
            "h_es_reader": None,
            "h_pH_war": last(h_wg[0], h_wg[1]),
            "h_pA_war": None,
            "h_ttrj_act": None,
        }
        eq2 = RTG[:, 0:512]
        c2b = RTG[:, 512:1024]
        exs = [dict() for _ in range(NTT)]

        def ex_part1(tt):
            st = exs[tt]
            slot = tt % 4
            sm_slot = tt % 2
            isl_last = (tt + 1) * IGS + IGS - 1
            h_xe = None
            for wave in range(2):
                htrs = None
                for k8 in range(8):
                    kc = wave * 8 + k8
                    htrs = b.op("pe", lambda e, kc=kc, k8=k8, tt=tt: e.matmul(
                        pA[:, k8 * 128:(k8 + 1) * 128],
                        x1c[kc][:, tt * 128:(tt + 1) * 128], ident_sb,
                        start=True, stop=True),
                        waits=([h_sil[isl_last], h_xe,
                                (x1_b[kc] if tt == 0 else None)] if k8 == 0 else []))
                for k8 in range(8):
                    kc = wave * 8 + k8
                    h_xe = b.op("act", lambda e, kc=kc, k8=k8: e.activation(
                        x1tok[:, kc * 128:(kc + 1) * 128],
                        pA[:, k8 * 128:(k8 + 1) * 128], AF.Copy),
                        waits=[(htrs if k8 == 0 else None),
                               (ex["h_x1tok_reader"] if wave == 0 and k8 == 0 else None)])
            ex["h_pA_war"] = h_xe
            st["h_xt"] = h_xe
            w0 = [h_gx[tt]]
            gxv = GXY[:, tt * 16:tt * 16 + 8]
            gxi = GXY[:, tt * 16 + 8:tt * 16 + 16]
            gyv = GXY[:, 128 + tt * 16:128 + tt * 16 + 8]
            gyi = GXY[:, 128 + tt * 16 + 8:128 + tt * 16 + 16]
            h = None
            for i in range(8):
                h = b.op("dve", lambda e, i=i, gyv=gyv, gxv=gxv: e.tensor_scalar(
                    out=eqbuf[:, i * 8:(i + 1) * 8], in0=gyv, scalar1=gxv[:, i:i + 1],
                    scalar2=None, op0=ALU.add), waits=(w0 if i == 0 else []))
            h_as = b.op("dve", lambda e: e.tensor_tensor(
                out=eqbuf[:, 0:64], in0=eqbuf[:, 0:64], in1=eps2_sb, op=ALU.subtract),
                waits=[h])
            h = b.op("dve", lambda e, gxi=gxi: e.tensor_scalar(
                out=ix64, in0=gxi, scalar1=BIGC, scalar2=64.0, op0=ALU.add, op1=ALU.mult),
                waits=w0)
            for i in range(8):
                h = b.op("dve", lambda e, i=i, gyi=gyi: e.tensor_scalar(
                    out=cbuf[:, i * 8:(i + 1) * 8], in0=gyi, scalar1=ix64[:, i:i + 1],
                    scalar2=None, op0=ALU.add), waits=([h] if i == 0 else []))
            h_ci = h
            h_s8 = b.op("dve", lambda e: e.max(out=s8, in_=eqbuf[:, 0:64]),
                        waits=[h_as, ex["h_s8_act"]])
            as_rep = eqbuf[:, 0:64].unsqueeze(1).to_broadcast([128, 8, 64])
            s8_rep = s8.unsqueeze(2).to_broadcast([128, 8, 64])
            h_eq = b.op("dve", lambda e, as_rep=as_rep, s8_rep=s8_rep: e.tensor_tensor(
                out=eq2.rearrange("p (i j) -> p i j", j=64), in0=as_rep, in1=s8_rep,
                op=ALU.is_equal), waits=[h_s8])
            ci_rep = cbuf[:, 0:64].unsqueeze(1).to_broadcast([128, 8, 64])
            h_c2 = b.op("dve", lambda e, ci_rep=ci_rep: e.tensor_tensor(
                out=c2b.rearrange("p (i j) -> p i j", j=64),
                in0=eq2.rearrange("p (i j) -> p i j", j=64), in1=ci_rep, op=ALU.mult),
                waits=[h_eq, h_ci])
            h_cm = b.op("dve", lambda e: e.tensor_reduce(
                out=cidxm, in_=c2b.rearrange("p (i j) -> p i j", j=64),
                axis=AX.X, op=ALU.min), waits=[h_c2])
            h_of = b.op("dve", lambda e: e.tensor_scalar(
                out=offs_f, in0=cidxm, scalar1=BIGC, scalar2=4095.0,
                op0=ALU.add, op1=ALU.min), waits=[h_cm])
            h_of = b.op("dve", lambda e: e.tensor_scalar(
                out=offs_f, in0=offs_f, scalar1=0.0, scalar2=None, op0=ALU.max),
                waits=[h_of])
            h_oi = b.op("dve", lambda e, slot=slot: e.tensor_copy(
                out=SMI[:, slot * 8:slot * 8 + 8], in_=offs_f),
                waits=[h_of, *ex["h_smi_reader"][slot]])
            sm8v = sm8s[:, sm_slot * 8:sm_slot * 8 + 8]
            h_m = b.op("dve", lambda e: e.tensor_reduce(out=mx1, in_=s8, axis=AX.X,
                                                        op=ALU.max, negate=True), waits=[h_s8])
            h_e = b.op("act", lambda e, sm8v=sm8v: e.activation(
                sm8v, s8, AF.Exp, bias=mx1[:, 0:1], scale=1.0, accum_out=sume),
                waits=[h_m, (exs[tt - 2].get("h_sm_last") if tt >= 2 else None)])
            ex["h_s8_act"] = h_e
            h_rc = b.op("dve", lambda e: e.reciprocal(rsum, sume), waits=[h_e])
            h_sm = b.op("dve", lambda e, sm8v=sm8v: e.tensor_scalar(
                out=sm8v, in0=sm8v, scalar1=rsum[:, 0:1], scalar2=None, op0=ALU.mult),
                waits=[h_rc])
            st["h_sm"] = h_sm
            st["sm8v"] = sm8v
            st["h_oi"] = h_oi
            st["slot"] = slot
            st["h_dot"] = [None] * 8
            st["h_dm"] = [None] * 8
            st["h_es"] = [None] * 8
            st["h_deg"] = [None] * 8

        def ex_dots_step(tt, j):
            st = exs[tt]
            slot = st["slot"]
            sm8v = st["sm8v"]
            for r in (2 * j, 2 * j + 1):
                pr = r % 3
                war = ex["h_du_war"][pr]
                gst = "g_de" if r % 2 == 0 else "g_ue"
                st["h_deg"][r] = b.dma("pool", gst,
                                       lambda e, r=r, pr=pr, slot=slot: e.indirect_dma_start(
                                           out=du_b[pr], out_offset=None, in_=duemb_bf[:],
                                           in_offset=IndirectOffsetOnAxis(
                                               SMI[:, slot * 8 + r:slot * 8 + r + 1], axis=0)),
                                       waits=[(st["h_oi"] if r <= 1 else None),
                                              (("d", "c_tab", 32) if (tt == 0 and r <= 1)
                                               else None), war])
                h_dm = b.op("dve", lambda e, pr=pr: e.tensor_tensor(
                    out=ttrj, in0=x1tok, in1=du_b[pr][:, 0:2048], op=ALU.mult),
                    waits=[st["h_deg"][r], (st["h_dot"][r - 1] if r >= 1 else None)]
                          + ([st["h_xt"], ex["h_ttrj_act"]] if r == 0 else []))
                st["h_dm"][r] = h_dm
                st["h_dot"][r] = b.op("act", lambda e, r=r: e.activation(
                    ttrj, ttrj, AF.Copy, accum_out=ew_f[:, r:r + 1]), waits=[h_dm])
                h_sil8 = b.op("act", lambda e, r=r: e.activation(
                    sil8[:, r:r + 1], ew_f[:, r:r + 1], AF.Silu), waits=[st["h_dot"][r]])
                h_ewf = b.op("dve", lambda e, r=r, sm8v=sm8v: e.tensor_scalar(
                    out=ewf[:, r:r + 1], in0=sil8[:, r:r + 1],
                    scalar1=sm8v[:, r:r + 1], scalar2=None, op0=ALU.mult),
                    waits=[h_sil8, st["h_sm"] if r == 0 else None])
                if r == 0:
                    h_es = b.op("dve", lambda e, pr=pr: e.tensor_scalar(
                        out=es_acc, in0=du_b[pr][:, 2048:4096], scalar1=ewf[:, 0:1],
                        scalar2=None, op0=ALU.mult),
                        waits=[h_ewf, ex["h_es_reader"], ex["h_es_tr"]])
                else:
                    h_es = b.op("dve", lambda e, r=r, pr=pr: e.scalar_tensor_tensor(
                        out=es_acc, in0=du_b[pr][:, 2048:4096], scalar=ewf[:, r:r + 1],
                        in1=es_acc, op0=ALU.mult, op1=ALU.add),
                        waits=[h_ewf, st["h_es"][r - 1]])
                st["h_es"][r] = h_es
                ex["h_du_war"][pr] = h_es
            if j == 3:
                st["h_sm_last"] = st["h_es"][7]
                ex["h_ttrj_act"] = st["h_dot"][7]
                ex["h_smi_reader"][slot] = (st["h_deg"][6], st["h_deg"][7])
                ex["h_x1tok_reader"] = st["h_dm"][7]
                ex["h_es_reader"] = st["h_es"][7]


        do_ex = lvl >= 4
        for ig in range(NIG):
            tt_d = ig - 2
            for j in range(IGS):
                isl = ig * IGS + j
                par = isl % 2
                if isl >= 2 and h_gsl[isl] is None:
                    h_gsl[isl] = b.dma("pool", "w_a" if par == 0 else "w_b",
                                       lambda e, isl=isl, par=par: e.dma_start(
                                           out=wgu_sl[par], in_=gu_view[isl]),
                                       waits=[pe_up_done[isl - 2]])
                    h_usl[isl] = h_gsl[isl]
                war_gate = h_sil[isl - 1] if isl >= 1 else x1_f[NKC - 1]
                if j >= 1:
                    war_up = h_aev[isl - 1]
                elif ig == 0:
                    war_up = x1_f[NKC - 1]
                else:
                    war_up = h_devac_pair[0]
                hp = None
                for kc in range(NKC):
                    for fo in (0, 512):
                        fin = kc == NKC - 1 and fo == 512
                        hp = b.op("pe", lambda e, isl=isl, kc=kc, par=par, fo=fo: e.matmul(
                            pA[:, fo:fo + 512], wg_sl[par][:, kc * 128:(kc + 1) * 128],
                            x1c[kc][:, fo:fo + 512],
                            start=(kc == 0), stop=(kc == NKC - 1)),
                            waits=([h_gsl[isl], war_gate,
                                    (ex["h_pA_war"] if j == 0 else None),
                                    (x1_b[kc] if isl == 0 else None)]
                                   if (kc == 0 and fo == 0) else []), inc=fin) or hp
                pe_gate_done[isl] = hp
                h_sil[isl] = b.op("act", lambda e: e.activation(gs_bf, pA[:, 0:1024], AF.Silu),
                                  waits=[hp, (h_aev[isl - 1] if isl >= 1 else None)])
                hp = None
                for kc in range(NKC):
                    for fo in (0, 512):
                        fin = kc == NKC - 1 and fo == 512
                        hp = b.op("pe", lambda e, isl=isl, kc=kc, par=par, fo=fo: e.matmul(
                            pB[:, fo:fo + 512], wu_sl[par][:, kc * 128:(kc + 1) * 128],
                            x1c[kc][:, fo:fo + 512],
                            start=(kc == 0), stop=(kc == NKC - 1)),
                            waits=([h_usl[isl], war_up] if (kc == 0 and fo == 0) else []),
                            inc=fin) or hp
                pe_up_done[isl] = hp
                h_aev[isl] = b.op("dve", lambda e, j=j: e.tensor_tensor(
                    out=a_grp[:, j * 1024:(j + 1) * 1024], in0=pB[:, 0:1024], in1=gs_bf,
                    op=ALU.mult), waits=[hp, h_sil[isl]])
                if ig == 0 and j < 2:
                    emit_route_tail(j)
                if do_ex and 0 <= tt_d < NTT:
                    ex_dots_step(tt_d, j)
            if ig == 0:
                for isl_h in (4, 5):
                    par_h = isl_h % 2
                    h_gsl[isl_h] = b.dma("pool", "w_a" if par_h == 0 else "w_b",
                                         lambda e, isl=isl_h, par=par_h: e.dma_start(
                                             out=wgu_sl[par], in_=gu_view[isl]),
                                         waits=[pe_up_done[isl_h - 2]])
                    h_usl[isl_h] = h_gsl[isl_h]
                emit_collective()
            do_es = do_ex and 0 <= tt_d < NTT
            evs = []
            htr_es = [None] * NKC
            for mtg in range(8):
                for half_mt in range(2):
                    mt = mtg * 2 + half_mt
                    pp = pB if half_mt == 0 else pC
                    hp = None
                    for kc in range(IGS):
                        w0 = []
                        if kc == 0:
                            w0 = [h_dsl[ig] if mt == 0 else None,
                                  (evs[mt - 2][1] if mt >= 2 else None),
                                  (h_aev[ig * IGS + IGS - 1] if mt < 2 else None)]
                            if mt < 2 and ig == 0:
                                w0 += [x1_f[NKC - 1], h_rt[15]]
                            if mt < 2 and ig >= 1:
                                w0 += [h_devac_pair[half_mt]]
                        if mt == 0:
                            w0 += [h_aev[ig * IGS + kc]]
                        for fo in (0, 512):
                            fin = kc == IGS - 1 and fo == 512
                            hp = b.op("pe", lambda e, mt=mt, kc=kc, pp=pp, fo=fo: e.matmul(
                                pp[:, fo:fo + 512],
                                BUF3[:, kc * 2048 + mt * 128:kc * 2048 + (mt + 1) * 128],
                                a_grp[:, kc * 1024 + fo:kc * 1024 + fo + 512],
                                start=(kc == 0), stop=(kc == IGS - 1)),
                                waits=(w0 if fo == 0 else []), inc=fin) or hp
                    if do_es and mt >= 8:
                        for ki in range(2):
                            kk = (mt - 8) * 2 + ki if mt < 12 else (mt - 12) * 2 + 8 + ki
                            sl = kk % 8
                            pht = pHa if sl < 4 else pHb
                            po = (sl % 4) * 128
                            htr_es[kk] = b.op("pe", lambda e, kk=kk, pht=pht, po=po: e.matmul(
                                pht[:, po:po + 128], es_acc[:, kk * 128:(kk + 1) * 128],
                                ident_sb, start=True, stop=True),
                                waits=([exs[tt_d]["h_es"][7], ex["h_pH_war"]] if kk == 0
                                       else ([ex["h_pH_war"]] if kk == 8 else [])))
                    hev = b.op("dve", lambda e, mt=mt, pp=pp: e.tensor_tensor(
                        out=r1c[mt], in0=r1c[mt], in1=pp[:, 0:1024], op=ALU.add),
                        waits=[hp, h_r2[mt]])
                    h_r2[mt] = hev
                    evs.append((hp, hev))
                    if do_es and mt in (11, 15):
                        base = 0 if mt == 11 else 8
                        hadd = None
                        for m2 in range(base, base + 8):
                            sl2 = m2 % 8
                            pht2 = pHa if sl2 < 4 else pHb
                            po2 = (sl2 % 4) * 128
                            hadd = b.op("dve", lambda e, m2=m2, pht2=pht2, po2=po2,
                                        tt_d=tt_d: e.tensor_tensor(
                                out=r1c[m2][:, tt_d * 128:(tt_d + 1) * 128],
                                in0=r1c[m2][:, tt_d * 128:(tt_d + 1) * 128],
                                in1=pht2[:, po2:po2 + 128], op=ALU.add),
                                waits=[(htr_es[base + 7] if m2 == base else None),
                                       h_r2[m2]])
                            h_r2[m2] = hadd
                        ex["h_pH_war"] = hadd
            if do_es:
                ex["h_es_tr"] = htr_es[NKC - 1]
            pe_down_done[ig] = evs[-1][0]
            h_devac[ig] = evs[-1][1]
            h_devac_pair = [evs[-2][1], evs[-1][1]]
            tt_p = ig - 1
            if do_ex and 0 <= tt_p < NTT:
                ex_part1(tt_p)
            if ig + 1 < NIG:
                h_dsl[ig + 1] = b.dma("sync", "w_d",
                                      lambda e, ig=ig: e.dma_start(out=BUF3[:, :],
                                                                   in_=dn_view[ig + 1]),
                                      waits=[pe_down_done[ig]])

        r2_ready = [h_r2[mt] for mt in range(NKC)]
        ln2_f, _ = layernorm(r1c, None, r2_ready,
                             pre_bcast_waits=[h_aev[NISL - 1], h_devac[NIG - 1]],
                             first_cast_extra=last(*[h_r2[m] for m in range(NKC)]))
        for kc in range(NKC):
            b.dma("sync", "st_out",
                  lambda e, kc=kc: e.dma_start(out=out_view[kc], in_=r1c[kc]),
                  waits=[ln2_f[kc]])
        b.wait("pe", [("d", "st_out", b.dmacnt["st_out"])])
        _emit_streams(block, b)

    return nc


def _emit_streams(block, b):
    @block.tensor
    def _(eo):
        for f in b.q["pe"]:
            f(eo)

    @block.scalar
    def _(eo):
        for f in b.q["act"]:
            f(eo)

    @block.vector
    def _(eo):
        for f in b.q["dve"]:
            f(eo)

    @block.gpsimd
    def _(eo):
        for f in b.q["pool"]:
            f(eo)

    @block.sync
    def _(eo):
        for f in b.q["sync"]:
            f(eo)



def _pretile_lhsT(w):
    K, M = w.shape
    return np.ascontiguousarray(
        w.reshape(K // 128, 128, M // 128, 128).transpose(2, 1, 0, 3).reshape(M, K))


def _prep_inputs(inputs):
    f32 = np.float32
    g = {k: np.ascontiguousarray(np.asarray(v, f32)) for k, v in inputs.items()}
    win, wout = g["L0_attn_proj_in_w"], g["L0_attn_proj_out_w"]
    pre = {}
    pre["winT"] = _pretile_lhsT(np.ascontiguousarray(win.T))
    pre["woutT"] = _pretile_lhsT(np.ascontiguousarray(wout.T))
    pre["bwT"] = np.ascontiguousarray(
        g["L0_attn_B_w"].T.reshape(16, 128, 16).transpose(1, 0, 2).reshape(128, 256))
    pre["cwT"] = np.ascontiguousarray(g["L0_attn_C_w"].T)
    pre["a_t"] = np.ascontiguousarray(
        g["L0_attn_A"].reshape(16, 128, 16).transpose(1, 0, 2).reshape(128, 256))
    pre["dp_t"] = np.ascontiguousarray(g["L0_attn_D_param"].reshape(16, 128).T)
    pre["bin_t"] = np.ascontiguousarray(g["L0_attn_proj_in_b"].reshape(16, 128).T)
    pre["bout_t"] = np.ascontiguousarray(g["L0_attn_proj_out_b"].reshape(16, 128).T)
    pre["wq_t"] = np.ascontiguousarray(
        g["L0_ffn_W_q"].reshape(16, 128, 128).transpose(1, 0, 2).reshape(128, 2048))
    gate_t = g["L0_ffn_gate_w"].reshape(16, 128, 64, 128).transpose(2, 1, 0, 3).reshape(64, 128, 2048)
    up_t = g["L0_ffn_up_w"].reshape(16, 128, 64, 128).transpose(2, 1, 0, 3).reshape(64, 128, 2048)
    pre["gu_t"] = np.ascontiguousarray(
        np.concatenate([gate_t, up_t], axis=2).reshape(8192, 4096))
    pre["down_t"] = np.ascontiguousarray(
        g["L0_ffn_down_w"].reshape(NIG, IGS, 128, 16, 128).transpose(0, 2, 1, 3, 4)
        .reshape(NIG * 128, IGS * 2048))
    pre["duemb"] = np.ascontiguousarray(
        np.concatenate([g["L0_ffn_down_embed"], g["L0_ffn_up_embed"]], axis=1))
    return g, pre


def kernel(**inputs):
    import ml_dtypes
    bf = ml_dtypes.bfloat16
    f32 = np.float32
    g, pre = _prep_inputs(inputs)
    x = g["L0_attn_x"]
    keys = g["L0_ffn_keys"]

    c_ident = np.eye(128, dtype=f32).astype(bf)
    c_iotam = np.tile(np.arange(64, dtype=f32) - BIGC, (128, 1)).astype(f32)
    kp = np.zeros((NLAG, S), f32) + np.arange(NLAG, dtype=f32)[:, None]
    c_kpat = np.ascontiguousarray(np.tile(kp.reshape(1, -1), (128, 1)))
    c_ones_col = np.ones((128, 1), f32).astype(bf)
    c_ones_f32 = np.ones((1, 128), f32)
    c_scal = np.tile(np.array([[1.0, 1e-5]], f32), (128, 1))
    io = np.arange(64, dtype=f32)
    c_ioteps = np.tile(np.concatenate([io * 2.0 ** -20, io * 2.0 ** -17])[None, :], (128, 1))

    nc = build_nc()
    in_maps = []
    for c in range(8):
        bidx, h = c // 2, c % 2
        s0 = h * 1024
        if h == 0:
            halo = np.zeros((HALO, D), f32)
        else:
            halo = x[bidx, s0 - HALO:s0]
        x_tc = np.ascontiguousarray(np.concatenate([halo, x[bidx, s0:s0 + 1024]], axis=0).T)
        side = 0 if bidx < 2 else 1
        hm = np.full((128, 1), 0.0 if h == 0 else 1.0, f32)
        sxy = np.empty((128, 16), np.int32)
        base = 1024 * c
        for tt in range(8):
            sxy[:, tt] = base + tt * 128 + np.arange(128)
            sxy[:, 8 + tt] = 8192 + base + tt * 128 + np.arange(128)
        im = dict(x_t=x_tc, mykeys=np.ascontiguousarray(keys[side]),
                  sxy_rows=sxy, c_ident=c_ident, c_iotam=c_iotam, c_kpat=c_kpat,
                  c_ones_col=c_ones_col, c_ones_f32=c_ones_f32, c_scal=c_scal,
                  c_ioteps=np.ascontiguousarray(c_ioteps), halo_mask=hm, **pre)
        in_maps.append(im)

    res = run_bass_kernel_spmd(nc, in_maps, core_ids=list(range(8)))
    out = np.empty((4, 2048, 2048), np.float32)
    for c in range(8):
        bidx, h = c // 2, c % 2
        out[bidx, h * 1024:(h + 1) * 1024, :] = res.results[c]["out_t"].T
    return out
